# version 15
# baseline (speedup 1.0000x reference)
import numpy as np
import ml_dtypes
import bass_rust
import concourse.bass as bass
import concourse.mybir as mybir
from concourse.alu_op_type import AluOpType as OP
from concourse.bass_utils import run_bass_kernel_spmd
from contextlib import ExitStack
from concourse import library_config

F32 = mybir.dt.float32
BF16 = mybir.dt.bfloat16
AF = mybir.ActivationFunctionType
AX = mybir.AxisListType

S = 4096
DM = 1024
TB = 512
NT = S // TB
NCH = DM // 128
NDIAG = 33

TAPS = ()
P1C_HP = range(8)
P1C_IB = range(8)
SKIP_P1 = False
FFN_W = range(NT)


class Ctr:
    def __init__(self, nc, es, name):
        self.sem = es.enter_context(nc.semaphore(name))
        self.v = 0
        self.name = name


class Prog:
    ENGS = ("sync", "gpsimd", "tensor", "vector", "scalar")

    def __init__(self, nc, es):
        self.nc = nc
        self.es = es
        self.ops = {e: [] for e in self.ENGS}
        self.all_ctrs = []
        self.c = {n: self.ctr("c_" + n) for n in ("t", "v", "s", "g")}
        self.engc = {"tensor": self.c["t"], "vector": self.c["v"],
                     "scalar": self.c["s"], "gpsimd": self.c["g"]}

    def ctr(self, name):
        c = Ctr(self.nc, self.es, name)
        self.all_ctrs.append(c)
        return c

    def dpool(self, name, n):
        return [self.ctr(f"{name}{i}") for i in range(n)]

    def emit(self, eng, fn, wait=(), inc=()):
        wait = [(c.sem, v) for c, v in wait if v > 0]
        for c, a in inc:
            c.v += a
        incs = [(c.sem, a) for c, a in inc]

        def run(h, _w=wait, _i=incs, _f=fn):
            for s_, v_ in _w:
                h.wait_ge(s_, v_)
            inst = _f(h)
            for s_, a_ in _i:
                inst.then_inc(s_, a_)
        self.ops[eng].append(run)

    def dma(self, q, c, out, in_, wait=(), slow_ok=False):
        if slow_ok:
            self.emit(q, lambda h: h.dma_start(out=out, in_=in_,
                                               allow_slow_non_contiguous=True),
                      wait=wait, inc=[(c, 16)])
        else:
            self.emit(q, lambda h: h.dma_start(out=out, in_=in_), wait=wait, inc=[(c, 16)])
        return c.v

    def op(self, eng, fn, wait=(), inc=False, drain=None):
        c = self.engc[eng]
        if drain is None:
            drain = eng == "vector"
        self.emit(eng, fn, wait=wait, inc=[(c, 1)] if inc else [])
        if drain:
            self.ops[eng].append(lambda h: h.drain())
        return c.v if inc else None

    def barrier(self, tag=""):
        vals = [(c, c.v) for c in self.all_ctrs]
        for e in self.ENGS:
            cc = self.engc.get(e)
            if e != "sync":
                self.ops[e].append(lambda h: h.drain())
            if cc is None:
                self.emit(e, lambda h: h.nop(), wait=vals)
            else:
                self.emit(e, lambda h: h.nop(), wait=vals, inc=[(cc, 1)])

    def replay_into(self, block):
        for e in self.ENGS:
            ops = self.ops[e]

            def make(ops_):
                def body(h):
                    for f in ops_:
                        f(h)
                return body
            getattr(block, e)(make(ops))


def fm_view(dram_ap, i0=None, i1=None):
    v = dram_ap.rearrange("(c p) t -> p c t", p=128)
    if i0 is not None:
        v = v[:, :, i0:i1]
    return v


def sb3(sbuf_tensor):
    return sbuf_tensor[:].rearrange("p (c t) -> p c t", t=TB)


class Build:
    def __init__(self, taps=()):
        self.taps = taps
        nc = bass.Bass("TRN2", target_bir_lowering=False)
        self.nc = nc
        self.es = ExitStack()

    def sbuf(self, name, shape, dt, scope=None):
        return (scope or self.es).enter_context(self.nc.sbuf_tensor(name, list(shape), dt))

    def build(self):
        nc = self.nc
        es = self.es

        def din(name, shape, dt=F32):
            return nc.dram_tensor(name, list(shape), dt, kind="ExternalInput").ap()

        def dsc(name, shape, dt=F32):
            return nc.dram_tensor(name, list(shape), dt, kind="Internal").ap()

        self.inp = {
            "xT": din("xT", (DM, S)),
            "w_in": din("w_in", (DM, 3 * DM)),
            "w_gate": din("w_gate", (DM, 2 * DM)),
            "b_gate": din("b_gate", (128, 16)),
            "tisa": din("tisa", (3, 16, 21)),
            "wqkv": din("wqkv", (2, DM, 1536)),
            "qk_scale": din("qk_scale", (2, 2, 64)),
            "wo0": din("wo0", (2, 512, DM)),
            "ln_g": din("ln_g", (2, 128, NCH)),
            "ln_b": din("ln_b", (2, 128, NCH)),
            "w1f": din("w1f", (2, DM, 5632)),
            "b1f": din("b1f", (2, 128, 44)),
            "w2f": din("w2f", (2, 2816, DM)),
            "b2f": din("b2f", (2, 128, NCH)),
            "WkT": din("WkT", (DM, DM)),
            "WkaT": din("WkaT", (DM, DM)),
            "WvT": din("WvT", (DM, DM)),
            "bkv": din("bkv", (3, 128, NCH)),
            "gamma": din("gamma", (128, NCH)),
            "beta": din("beta", (128, NCH)),
            "Wp1": din("Wp1", (DM, 128)),
            "bp1": din("bp1", (128, 1)),
            "Wp2": din("Wp2", (128, 8)),
            "bp2": din("bp2", (8,)),
            "We1": din("We1", (8, DM, 2048)),
            "be1": din("be1", (8, 2048)),
            "We2": din("We2", (8, 2048, DM)),
            "be2": din("be2", (8, DM)),
            "WoT1": din("WoT1", (DM, DM)),
            "bo1": din("bo1", (128, NCH)),
        }
        self.out = nc.dram_tensor("out", [DM, S], F32, kind="ExternalOutput").ap()

        sc = {}
        sc["xn"] = dsc("sc_xn", (DM, S), BF16)
        sc["xnf"] = dsc("sc_xnf", (DM, S), F32)
        sc["qT"] = dsc("sc_qT", (DM, S), BF16)
        sc["kT"] = dsc("sc_kT", (DM, S), BF16)
        sc["attn"] = dsc("sc_attn", (DM, S), BF16)
        sc["glu"] = dsc("sc_glu", (DM, S), F32)
        sc["vh"] = dsc("sc_vh", (16, 8448), BF16)
        sc["ffnx"] = dsc("sc_ffnx", (DM, S), F32)
        sc["ffnxb"] = dsc("sc_ffnxb", (DM, S), BF16)
        sc["ffno"] = dsc("sc_ffno", (DM, S), F32)
        sc["xmid"] = dsc("sc_xmid", (DM, S), F32)
        sc["x1"] = dsc("sc_x1", (DM, S), F32)
        sc["xn2"] = dsc("sc_xn2", (DM, S), BF16)
        sc["xn2f"] = dsc("sc_xn2f", (DM, S), F32)
        sc["h2"] = dsc("sc_h2", (DM, S), BF16)
        sc["acc1"] = dsc("sc_acc1", (DM, S), F32)
        sc["small"] = dsc("sc_small", (4096,), F32)
        sc["stats"] = dsc("sc_stats", (2, S), F32)
        self.sc = sc

        self.tap_outs = {}
        for t in self.taps:
            a = sc[t]
            self.tap_outs[t] = nc.dram_tensor("tap_" + t, list(a.shape), a.dtype,
                                              kind="ExternalOutput").ap()

        with es:
            self.P = Prog(nc, es)
            P = self.P
            self.ones_f = self.sbuf("ones_f", [1, 128], F32)
            self.onesP = self.sbuf("onesP", [128, 128], F32)
            self.ones_col_b = self.sbuf("ones_col_b", [128, 1], BF16)
            self.ident_b = self.sbuf("ident_b", [128, 128], BF16)
            self.ident_f = self.sbuf("ident_f", [128, 128], F32)
            self.st_mu = self.sbuf("st_mu", [1, 2 * TB], F32)
            self.st_r = self.sbuf("st_r", [1, 2 * TB], F32)
            self.st_tmp = self.sbuf("st_tmp", [1, 4 * TB], F32)
            self.rrow = self.sbuf("rrow", [128, TB], F32)
            self.bcb = self.sbuf("bcb", [128, TB], F32)
            self.ps = [es.enter_context(nc.psum_tensor(f"ps{i}", [128, 512], F32))
                       for i in range(8)]

            self.init_consts()
            P.barrier("consts")
            self.phase_ln(self.inp["xT"], sc["xn"], self.st_mu, self.st_r,
                          dst2_f32=sc["xnf"])
            P.barrier("p0")
            if not getattr(self, "stop_after_p0", False):
                if not SKIP_P1:
                    self.phase_l0_attn()
                    P.barrier("p1")
                    self.phase_gate()
                    P.barrier("p2")
                self.phase_l0_ffn()
                P.barrier("p3")
                self.phase_p4()
                P.barrier("p4")
                self.phase_l1_attn()
                P.barrier("p5b")
                self.phase_l1_ffn()
                P.barrier("p6")

            fin = P.ctr("fin")
            fv = 0
            for t in self.taps:
                fv = P.dma("sync", fin, self.tap_outs[t], sc[t])
            if fv:
                P.emit("sync", lambda h: h.nop(), wait=[(fin, fv)])

            block = es.enter_context(nc.Block())
            P.replay_into(block)
        return nc

    def bcast(self, out_ap, row_ap, n, ps, wait=()):
        P = self.P
        tb = P.op("tensor", lambda h: h.matmul(
            ps[:, 0:n], self.ones_f[:], row_ap, start=True, stop=True),
            wait=wait, inc=True)
        return P.op("vector", lambda h: h.tensor_copy(out_ap, ps[:, 0:n]),
                    wait=[(P.c["t"], tb)], inc=True)

    def init_consts(self):
        P, h = self.P, None
        P.op("gpsimd", lambda h: h.memset(self.ones_f[:], 1.0), inc=True)
        P.op("gpsimd", lambda h: h.memset(self.onesP[:], 1.0), inc=True)
        P.op("gpsimd", lambda h: h.memset(self.ones_col_b[:], 1.0), inc=True)
        gp = P.op("gpsimd", lambda h: h.memset(self.ident_b[:], 1.0), inc=True)
        P.op("gpsimd", lambda h: h.affine_select(
            out=self.ident_b[:], in_=self.ident_b[:], pattern=[[-1, 128]],
            compare_op=OP.is_equal, fill=0.0, base=0, channel_multiplier=1),
            wait=[(P.c["g"], gp)], inc=True)
        gq = P.op("gpsimd", lambda h: h.affine_select(
            out=self.ident_f[:], in_=self.onesP[:], pattern=[[-1, 128]],
            compare_op=OP.is_equal, fill=0.0, base=0, channel_multiplier=1),
            wait=[(P.c["g"], P.c["g"].v)], inc=True)

    def rsqrt_chain(self, t_slots, var_ap, out_ap, wait_v):
        P = self.P
        t0, t1, t2 = t_slots
        sv = P.op("scalar", lambda h: h.activation(t0, var_ap, AF.Sqrt),
                  wait=wait_v, inc=True)
        P.op("vector", lambda h: h.reciprocal(out_ap, t0), wait=[(P.c["s"], sv)])
        P.op("vector", lambda h: h.tensor_tensor(t1, out_ap, out_ap, op=OP.mult))
        P.op("vector", lambda h: h.tensor_tensor(t1, var_ap, t1, op=OP.mult))
        P.op("vector", lambda h: h.tensor_scalar(t1, t1, -0.5, 1.5, op0=OP.mult, op1=OP.add))
        return P.op("vector", lambda h: h.tensor_tensor(out_ap, out_ap, t1, op=OP.mult), inc=True)

    def phase_ln(self, src_f32, dst_bf, mu_row, r_row, dst2_f32=None, affine=None,
                 add_srcs=(), dst_sum_f32=None):
        nc, P = self.nc, self.P
        self.ln_id = getattr(self, "ln_id", 0) + 1
        pref = f"ln{self.ln_id}_"
        with ExitStack() as ph:
            sb = lambda n, shp, dt: self.sbuf(pref + n, shp, dt, ph)
            nadd = len(add_srcs)
            xb = [sb(f"ln_xb{i}", [128, NCH * TB], F32) for i in range(2)]
            ab = [[sb(f"ln_ab{j}_{i}", [128, NCH * TB], F32) for i in range(2)]
                  for j in range(nadd)]
            xbb = [sb(f"ln_xbb{i}", [128, NCH * TB], BF16) for i in range(2)]
            xout = [sb(f"ln_xo{i}", [128, NCH * TB], BF16) for i in range(2)]
            xof = [sb(f"ln_xof{i}", [128, NCH * TB], F32) for i in range(2)] if dst2_f32 is not None else None
            xsq = [sb(f"ln_xsq{i}", [128, NCH * TB], BF16) for i in range(2)]
            tmpfull = sb("ln_tmpf", [128, NCH * TB], F32)
            gam = None
            ld = P.dpool(pref + "ld", 2)
            sto = P.dpool(pref + "st", 2)
            onc = P.ctr(pref + "onc")
            gw = 0
            if affine is not None:
                gam = sb("ln_gam", [128, 2 * NCH], F32)
                P.dma("sync", onc, gam[:, 0:NCH], affine[0])
                gw = P.dma("sync", onc, gam[:, NCH:2 * NCH], affine[1])
            ps_stats = [(self.ps[0], self.ps[1]), (self.ps[2], self.ps[3])]
            ps_bc = [(self.ps[4], self.ps[5]), (self.ps[6], self.ps[7])]
            loads, casts, sums, stores, apply_done = {}, {}, {}, {}, {}
            for i in range(NT):
                pr = i % 2
                wv = []
                if i >= 2:
                    wv = [(P.c["v"], apply_done[i - 2]), (P.c["s"], casts[i - 2]),
                          (P.c["t"], sums[i - 2]), (sto[i % 2], stores[i - 2])]
                loads[i] = P.dma("sync", ld[pr], sb3(xb[pr]), fm_view(src_f32, i * TB, (i + 1) * TB), wait=wv)
                for j, asrc in enumerate(add_srcs):
                    loads[i] = P.dma("sync", ld[pr], sb3(ab[j][pr]),
                                     fm_view(asrc, i * TB, (i + 1) * TB), wait=wv)
                vw = [(ld[pr], loads[i])]
                if nadd:
                    vsum = None
                    for j in range(nadd):
                        vsum = P.op("vector", lambda h, p_=pr, j_=j: h.tensor_tensor(
                            xb[p_][:], xb[p_][:], ab[j_][p_][:], op=OP.add),
                            wait=vw, inc=(j == nadd - 1))
                    vw = [(P.c["v"], vsum)]
                    if dst_sum_f32 is not None:
                        stores[i] = P.dma("sync", sto[pr], fm_view(dst_sum_f32, i * TB, (i + 1) * TB),
                                          sb3(xb[pr]), wait=vw)
                casts[i] = P.op("scalar", lambda h, p_=pr: h.activation(
                    xbb[p_][:], xb[p_][:], AF.Copy), wait=vw, inc=True, drain=True)
                sq = P.op("scalar", lambda h, p_=pr: h.activation(
                    xsq[p_][:], xbb[p_][:], AF.Square), inc=True)
                for c in range(NCH):
                    P.op("tensor", lambda h, p_=pr, c_=c: h.matmul(
                        ps_stats[p_][0][0:1, 0:TB], self.ones_col_b[:],
                        xbb[p_][:, c_ * TB:(c_ + 1) * TB],
                        start=(c_ == 0), stop=(c_ == NCH - 1)),
                        wait=[(P.c["s"], sq)] if c == 0 else ())
                for c in range(NCH):
                    sums[i] = P.op("tensor", lambda h, p_=pr, c_=c: h.matmul(
                        ps_stats[p_][1][0:1, 0:TB], self.ones_col_b[:],
                        xsq[p_][:, c_ * TB:(c_ + 1) * TB],
                        start=(c_ == 0), stop=(c_ == NCH - 1)), inc=(c == NCH - 1))
                stt = self.st_tmp
                t0, t1, t2, t3 = (stt[0:1, k * TB:(k + 1) * TB] for k in range(4))
                mu_b = mu_row[0:1, pr * TB:(pr + 1) * TB]
                r_b = r_row[0:1, pr * TB:(pr + 1) * TB]
                P.op("vector", lambda h, p_=pr, m_=mu_b: h.tensor_scalar(
                    m_, ps_stats[p_][0][0:1, 0:TB], 1.0 / DM, None, op0=OP.mult),
                    wait=[(P.c["t"], sums[i])])
                P.op("vector", lambda h, p_=pr: h.tensor_scalar(
                    t0, ps_stats[p_][1][0:1, 0:TB], 1.0 / DM, None, op0=OP.mult))
                P.op("vector", lambda h, m_=mu_b: h.tensor_tensor(t1, m_, m_, op=OP.mult))
                P.op("vector", lambda h: h.scalar_tensor_tensor(
                    t2, t0, 1.0, t1, op0=OP.mult, op1=OP.subtract))
                vvar = P.op("vector", lambda h: h.tensor_scalar(
                    t2, t2, 1e-5, None, op0=OP.add), inc=True)
                vr = self.rsqrt_chain((t0, t1, t3), t2, r_b, [(P.c["v"], vvar)])
                bca, bcb = ps_bc[pr]
                P.op("tensor", lambda h, m_=mu_b, b_=bca: h.matmul(
                    b_[:, 0:TB], self.ones_f[:], m_, start=True, stop=True),
                    wait=[(P.c["v"], vr)])
                tb2 = P.op("tensor", lambda h, r_=r_b, b_=bcb: h.matmul(
                    b_[:, 0:TB], self.ones_f[:], r_, start=True, stop=True), inc=True)
                aw = [(P.c["t"], tb2)]
                if i == 0 and gw:
                    aw.append((onc, gw))
                assert affine is None and xof is not None

                def bc3(psrc):
                    return bass_rust.AP(psrc[:].tensor, 0, [[512, 128], [0, NCH], [1, TB]])
                x3 = lambda buf: buf[:].rearrange("p (c t) -> p c t", t=TB)
                P.op("vector", lambda h, p_=pr, b_=bca: h.tensor_tensor(
                    x3(tmpfull), x3(xb[p_]), bc3(b_), op=OP.subtract), wait=aw)
                P.op("vector", lambda h, p_=pr, b_=bcb: h.tensor_tensor(
                    x3(xof[p_]), x3(tmpfull), bc3(b_), op=OP.mult))
                ad = P.op("vector", lambda h, p_=pr, b_=bcb: h.tensor_tensor(
                    x3(xout[p_]), x3(tmpfull), bc3(b_), op=OP.mult), inc=True)
                apply_done[i] = ad
                if xof is not None:
                    stores[i] = P.dma("sync", sto[pr], fm_view(dst2_f32, i * TB, (i + 1) * TB),
                                      sb3(xof[pr]), wait=[(P.c["v"], apply_done[i])])
                stores[i] = P.dma("sync", sto[pr], fm_view(dst_bf, i * TB, (i + 1) * TB),
                                  sb3(xout[pr]), wait=[(P.c["v"], apply_done[i])])

    def phase_l0_attn(self):
        nc, P, sc = self.nc, self.P, self.sc
        ps = self.ps
        with ExitStack() as ph:
            sb = lambda n, shp, dt: self.sbuf(n, shp, dt, ph)
            v_sb = sb("v_sb", [128, 32 * 1040], BF16)
            ncb = sb("p1ncb", [2, 128], F32)
            nrow = sb("p1nrow", [2, 16], F32)
            Bneg = sb("Bneg", [128, 16], F32)
            ld = P.dpool("p1ld", 2)
            sto = P.dpool("p1st", 2)
            onc = P.ctr("p1onc")

            vones = bass_rust.AP(v_sb[:].tensor, 64,
                                 [[32 * 1040, 128], [1040, 32], [65, 16], [1, 1]])
            gvo = P.op("gpsimd", lambda h: h.memset(vones, 1.0), inc=True)

            with ExitStack() as pha:
                sba = lambda n, shp, dt: self.sbuf(n, shp, dt, pha)
                w_in_sb = sba("w_in_sb", [128, 8 * 3072], BF16)
                xnb = [sba(f"p1xnb{i}", [128, NCH * TB], BF16) for i in range(2)]
                qstage = [sba(f"p1qs{i}", [128, NCH * TB], BF16) for i in range(2)]
                kstage = [sba(f"p1ks{i}", [128, NCH * TB], BF16) for i in range(2)]
                sqb = sba("p1sqb", [128, TB], BF16)
                bd2 = sba("bd2", [128, 2], BF16)
                gp = P.op("gpsimd", lambda h: h.memset(bd2[:], 1.0), inc=True)
                gp = P.op("gpsimd", lambda h: h.affine_select(
                    out=bd2[:], in_=bd2[:], pattern=[[-64, 2]],
                    compare_op=OP.is_ge, fill=0.0, base=0, channel_multiplier=1),
                    wait=[(P.c["g"], gp)], inc=True)
                gbd = P.op("gpsimd", lambda h: h.affine_select(
                    out=bd2[:], in_=bd2[:], pattern=[[64, 2]],
                    compare_op=OP.is_ge, fill=0.0, base=63, channel_multiplier=-1),
                    wait=[(P.c["g"], gp)], inc=True)
                gonc = P.ctr("p1gonc")
                wld = P.dma("gpsimd", gonc, w_in_sb[:].rearrange("p (kc n) -> p kc n", n=3072),
                            self.inp["w_in"].rearrange("(kc p) n -> p kc n", p=128))

                slots = [ps[0][:, 0:512], ps[1][:, 0:512], ps[2][:, 0:512], ps[3][:, 0:512]]
                mmdone, stored = {}, {}
                prev_ev = None
                norm_done = 0
                for i in range(NT):
                    pr = i % 2
                    wv = []
                    if i >= 2:
                        wv = [(P.c["t"], mmdone[i - 2]), (sto[pr], stored[i - 2]),
                              (P.c["s"], sq_last[pr])]
                    lv = P.dma("sync", ld[pr], sb3(xnb[pr]),
                               fm_view(sc["xn"], i * TB, (i + 1) * TB), wait=wv)
                    base_w = [(ld[pr], lv), (gonc, wld)]
                    if i == 0:
                        base_w += [(P.c["g"], gvo)]
                    ev = {}
                    for g in range(24):
                        sl = slots[g % 4]
                        tw = list(base_w) if (g < 4 and i == 0) else ([(ld[pr], lv)] if g < 4 else [])
                        if g >= 4:
                            tw.append((P.c["v"], ev[g - 4]))
                        elif prev_ev is not None:
                            tw.append((P.c["v"], prev_ev[20 + g]))
                        if g < 8:
                            c = g
                            for kc in range(NCH):
                                mmdone[i] = P.op("tensor", lambda h, sl_=sl, c_=c, kc_=kc, p_=pr: h.matmul(
                                    sl_, w_in_sb[:, kc_ * 3072 + c_ * 128:kc_ * 3072 + (c_ + 1) * 128],
                                    xnb[p_][:, kc_ * TB:(kc_ + 1) * TB],
                                    start=(kc_ == 0), stop=(kc_ == NCH - 1)),
                                    wait=(tw if kc == 0 else ()), inc=(kc == NCH - 1))
                            ev[g] = P.op("vector", lambda h, sl_=sl, c_=c, p_=pr: h.tensor_copy(
                                kstage[p_][:, c_ * TB:(c_ + 1) * TB], sl_),
                                wait=[(P.c["t"], mmdone[i])], inc=True)
                        elif g < 16:
                            c = g - 8
                            for kc in range(NCH):
                                mmdone[i] = P.op("tensor", lambda h, sl_=sl, c_=c, kc_=kc, p_=pr: h.matmul(
                                    sl_, w_in_sb[:, kc_ * 3072 + 2048 + c_ * 128:kc_ * 3072 + 2048 + (c_ + 1) * 128],
                                    xnb[p_][:, kc_ * TB:(kc_ + 1) * TB],
                                    start=(kc_ == 0), stop=(kc_ == NCH - 1)),
                                    wait=(tw if kc == 0 else ()), inc=(kc == NCH - 1))
                            ev[g] = P.op("vector", lambda h, sl_=sl, c_=c, p_=pr: h.tensor_scalar(
                                qstage[p_][:, c_ * TB:(c_ + 1) * TB], sl_, 0.125, None, op0=OP.mult),
                                wait=[(P.c["t"], mmdone[i])], inc=True)
                        else:
                            tt = (g - 16) // 2
                            half = (g - 16) % 2
                            gtt = i * 4 + tt
                            for kc in range(NCH):
                                mmdone[i] = P.op("tensor", lambda h, tt_=tt, hf_=half, sl_=sl, kc_=kc, p_=pr: h.matmul(
                                    sl_, xnb[p_][:, kc_ * TB + tt_ * 128:kc_ * TB + (tt_ + 1) * 128],
                                    w_in_sb[:, kc_ * 3072 + 1024 + hf_ * 512:kc_ * 3072 + 1024 + (hf_ + 1) * 512],
                                    start=(kc_ == 0), stop=(kc_ == NCH - 1)),
                                    wait=(tw if kc == 0 else ()), inc=(kc == NCH - 1))
                            vdst = bass_rust.AP(v_sb[:].tensor, gtt * 1040 + half * 520,
                                                [[32 * 1040, 128], [65, 8], [1, 64]])
                            ev[g] = P.op("vector", lambda h, sl_=sl, d_=vdst: h.tensor_copy(
                                d_, sl_.rearrange("p (a b) -> p a b", b=64)),
                                wait=[(P.c["t"], mmdone[i])], inc=True)
                    prev_ev = ev
                    sq_last = getattr(self, "_p1_sq_last", {0: 0, 1: 0})
                    for qk in range(2):
                        for c in range(NCH):
                            if qk == 0:
                                srcv = kstage[pr][:, c * TB:(c + 1) * TB]
                                wsq = [(P.c["v"], ev[c])]
                            else:
                                srcv = qstage[pr][:, c * TB:(c + 1) * TB]
                                wsq = [(P.c["v"], ev[8 + c])]
                            sqv = P.op("scalar", lambda h, s_=srcv: h.activation(
                                sqb[:], s_, AF.Square), wait=wsq, inc=True)
                            nm = P.op("tensor", lambda h: h.matmul(
                                ps[6][0:2, 0:TB], bd2[:], sqb[:], start=True, stop=True),
                                wait=[(P.c["s"], sqv), (P.c["v"], norm_done),
                                      (P.c["g"], gbd)], inc=True)
                            col = qk * 64 + c * 8 + i
                            norm_done = P.op("vector", lambda h, col_=col: h.tensor_reduce(
                                ncb[0:2, col_:col_ + 1], ps[6][0:2, 0:TB], axis=AX.X, op=OP.max),
                                wait=[(P.c["t"], nm)], inc=True)
                            P.emit("scalar", lambda h: h.nop(), wait=[(P.c["t"], nm)])
                    sq_last[pr] = P.c["s"].v
                    self._p1_sq_last = sq_last
                    stored[i] = P.dma("sync", sto[pr],
                                      fm_view(sc["qT"], i * TB, (i + 1) * TB),
                                      sb3(qstage[pr]), wait=[(P.c["v"], ev[15]), (P.c["s"], sq_last[pr])])
                    stored[i] = P.dma("sync", sto[pr],
                                      fm_view(sc["kT"], i * TB, (i + 1) * TB),
                                      sb3(kstage[pr]))
                nfin = P.op("vector", lambda h: h.tensor_reduce(
                    nrow[0:2, :], ncb[0:2, :].rearrange("p (c b) -> p c b", b=8),
                    axis=AX.X, op=OP.max), wait=[(P.c["v"], norm_done)], inc=True)
                P.barrier("p1a")

            with ExitStack() as phb:
                sbb = lambda n, shp, dt: self.sbuf(n, shp, dt, phb)
                amp_r = sbb("amp_r", [1, 336], F32)
                shp_r = sbb("shp_r", [1, 336], F32)
                off_r = sbb("off_r", [1, 336], F32)
                ampB = sbb("ampB", [128, 336], F32)
                nshB = sbb("nshB", [128, 336], F32)
                offB = sbb("offB", [128, 336], F32)
                io66 = sbb("io66", [128, 66], F32)
                tw1 = sbb("tisa_w1", [128, 336], F32)
                tw2 = sbb("tisa_w2", [128, 336], F32)
                vhc = sbb("tisa_vhc", [128, 16], F32)
                vstage = sbb("vstage", [16, 8448], BF16)
                brow = sbb("brow", [1, 64], F32)
                tld = P.dma("sync", onc, amp_r[:], self.inp["tisa"][0].rearrange("a b -> (a b)").unsqueeze(0))
                tld = P.dma("sync", onc, shp_r[:], self.inp["tisa"][1].rearrange("a b -> (a b)").unsqueeze(0))
                tld = P.dma("sync", onc, off_r[:], self.inp["tisa"][2].rearrange("a b -> (a b)").unsqueeze(0))
                gio = P.op("gpsimd", lambda h: h.iota(
                    io66[:], pattern=[[128, 66]], base=0, channel_multiplier=1,
                    allow_small_or_imprecise_dtypes=True), inc=True)
                P.op("vector", lambda h: h.tensor_scalar(
                    shp_r[:], shp_r[:], -1.0, None, op0=OP.mult), wait=[(onc, tld)])
                vns = P.op("vector", lambda h: h.tensor_scalar(
                    off_r[:], off_r[:], 4095.0, None, op0=OP.add), inc=True)
                g1 = self.bcast(ampB[:], amp_r[:], 336, ps[5], wait=[(P.c["v"], vns)])
                g1 = self.bcast(nshB[:], shp_r[:], 336, ps[5], wait=[(P.c["v"], g1)])
                g1 = self.bcast(offB[:], off_r[:], 336, ps[5], wait=[(P.c["v"], g1)])
                ebm = 0
                vcp = 0
                for rc in range(66):
                    vv = P.op("vector", lambda h, rc_=rc: h.tensor_scalar(
                        tw1[:], offB[:], io66[:, rc_:rc_ + 1], None, op0=OP.subtract),
                        wait=([(P.c["v"], g1), (P.c["g"], gio)] if rc == 0 else
                              [(P.c["s"], ebm)]), inc=True)
                    P.op("vector", lambda h: h.tensor_tensor(tw1[:], tw1[:], tw1[:], op=OP.mult))
                    vv = P.op("vector", lambda h: h.tensor_tensor(tw1[:], tw1[:], nshB[:], op=OP.mult), inc=True)
                    ebm = P.op("scalar", lambda h: h.activation(tw2[:], tw1[:], AF.Exp),
                               wait=[(P.c["v"], vv)], inc=True)
                    vv = P.op("vector", lambda h: h.tensor_tensor(tw2[:], tw2[:], ampB[:], op=OP.mult),
                              wait=[(P.c["s"], ebm)])
                    vv = P.op("vector", lambda h: h.tensor_reduce(
                        vhc[:], tw2[:].rearrange("p (a b) -> p a b", b=21), axis=AX.X, op=OP.add),
                        inc=True)
                    tt = P.op("tensor", lambda h: h.transpose(ps[7][0:16, 0:128], vhc[:], self.ident_f[:]),
                              wait=[(P.c["v"], vv)], inc=True)
                    vcp = P.op("vector", lambda h, rc_=rc: h.tensor_copy(
                        vstage[:, rc_ * 128:(rc_ + 1) * 128], ps[7][0:16, 0:128]),
                        wait=[(P.c["t"], tt)], inc=True)
                    P.emit("tensor", lambda h: h.nop(), wait=[(P.c["v"], vcp)])
                vsd = P.dma("sync", onc, sc["vh"], vstage[:], wait=[(P.c["v"], vcp)])
                bm = P.op("vector", lambda h: h.tensor_reduce(
                    vhc[0:16, 0:1], vstage[:], axis=AX.X, op=OP.max), inc=True)
                smv = sc["small"]
                d1 = P.dma("sync", onc, bass_rust.AP(smv.tensor, 0, [[1, 2], [16, 2], [2, 8]]),
                           nrow[0:2, :].rearrange("p (a b) -> p a b", b=8),
                           wait=[(P.c["v"], nfin)], slow_ok=True)
                d1 = P.dma("sync", onc, bass_rust.AP(smv.tensor, 32, [[1, 16], [1, 1]]),
                           vhc[0:16, 0:1], wait=[(P.c["v"], bm)])
                qn = brow[0:1, 0:16]
                kn = brow[0:1, 16:32]
                bx = brow[0:1, 32:48]
                d2 = P.dma("sync", onc, kn, bass_rust.AP(smv.tensor, 0, [[16, 1], [1, 16]]), wait=[(onc, d1)])
                d2 = P.dma("sync", onc, qn, bass_rust.AP(smv.tensor, 16, [[16, 1], [1, 16]]))
                d2 = P.dma("sync", onc, bx, bass_rust.AP(smv.tensor, 32, [[16, 1], [1, 16]]))
                sqn = P.op("scalar", lambda h: h.activation(qn, qn, AF.Sqrt), wait=[(onc, d2)], inc=True)
                sqn = P.op("scalar", lambda h: h.activation(kn, kn, AF.Sqrt), inc=True)
                P.op("vector", lambda h: h.tensor_tensor(qn, qn, kn, op=OP.mult), wait=[(P.c["s"], sqn)])
                P.op("vector", lambda h: h.tensor_tensor(qn, qn, bx, op=OP.add))
                bneg_row = P.op("vector", lambda h: h.tensor_scalar(
                    qn, qn, -1.0, -0.5, op0=OP.mult, op1=OP.add), inc=True)
                gB = self.bcast(Bneg[:], qn, 16, ps[5], wait=[(P.c["v"], bneg_row)])
                P.barrier("p1ab")

            with ExitStack() as ph2:
                sb2 = lambda n, shp, dt: self.sbuf(n, shp, dt, ph2)
                kT_sb = sb2("kT_sb", [128, 2 * 4096], BF16)
                qz2 = [sb2(f"qz{q}", [128, 2 * 4096], BF16) for q in range(2)]
                uband = sb2("uband", [128, 4224], BF16)
                dg = [sb2(f"dg{j}", [128, NDIAG * 128], BF16) for j in range(2)]
                probs = [sb2(f"probs{j}", [128, 4 * 512], BF16) for j in range(2)]
                attnb = [sb2(f"attnb{j}", [64, 2 * 512], BF16) for j in range(2)]
                qld = P.dpool("qld", 2)
                bld = P.dpool("bld", 1)
                ast = P.dpool("ast", 2)
                kpool = P.dpool("kld2", 2)
                P.op("gpsimd", lambda h: h.memset(qz2[0][:], 0.0), inc=True)
                gz = P.op("gpsimd", lambda h: h.memset(qz2[1][:], 0.0), inc=True)
                gz2 = P.op("gpsimd", lambda h: h.memset(dg[0][:], 0.0), inc=True)
                gz2 = P.op("gpsimd", lambda h: h.memset(dg[1][:], 0.0), inc=True)
                last_store = {0: 0, 1: 0}
                q_reader_at = {}
                prev_bv = {0: 0, 1: 0}
                dg_tread = 0
                exp_done = {}
                av_done = {}
                last_evn = {0: 0, 1: 0}
                for hp in P1C_HP:
                    qz = qz2[hp % 2]
                    qwv = [(P.c["t"], q_reader_at[hp - 2])] if hp >= 2 else [(P.c["g"], gz)]
                    qv = P.dma("sync", qld[hp % 2], qz[0:64, 0:4096],
                               sc["qT"][hp * 128:hp * 128 + 64, :], wait=qwv)
                    qv = P.dma("sync", qld[hp % 2], qz[64:128, 4096:8192],
                               sc["qT"][hp * 128 + 64:(hp + 1) * 128, :])
                    kld = P.dma("sync", kpool[hp % 2],
                                kT_sb[:, (hp % 2) * 4096:(hp % 2) * 4096 + 4096],
                                sc["kT"].rearrange("(c p) t -> p c t", p=128)[:, hp, :],
                                wait=[(P.c["t"], q_reader_at[hp - 2])] if hp >= 2 else [])
                    bv = {}
                    for j, hh in enumerate((2 * hp, 2 * hp + 1)):
                        ubw = [(P.c["v"], prev_bv[1] if j == 0 else bv[0])]
                        ub = P.dma("sync", bld[0], uband[:],
                                   bass_rust.AP(sc["vh"].tensor, hh * 8448 + 1920,
                                                [[1, 128], [1, 4224]]),
                                   wait=ubw)
                        for d0 in range(-13, 14):
                            rev = bass_rust.AP(uband[:].tensor, 2175 + 128 * d0,
                                               [[4224, 128], [-1, 128]])
                            w_ = [(bld[0], ub)] if d0 == -13 else []
                            if d0 == -13 and hp > 0:
                                w_.append((P.c["t"], dg_tread))
                            elif d0 == -13 and hp == 0:
                                w_.append((P.c["g"], gz2))
                            bv[j] = P.op("vector", lambda h, j_=j, d_=d0, r_=rev: h.tensor_copy(
                                dg[j_][:, (16 - d_) * 128:(17 - d_) * 128], r_),
                                wait=w_, inc=(d0 == 13), drain=(d0 == 13))
                    for ib in P1C_IB:
                        exp_at = {}

                        def emit_av(jt_, hp_=hp):
                            for j in range(2):
                                psv = ps[4 + j]
                                tw = [(P.c["s"], exp_at[(jt_, j)])]
                                if jt_ == 0:
                                    tw.append((P.c["v"], last_evn[j]))
                                P.op("tensor", lambda h, j_=j, jt2=jt_, p_=psv, hpp_=hp_: h.matmul(
                                    p_[0:65, :],
                                    v_sb[:, jt2 * 1040 + 65 * (2 * hpp_ + j_):jt2 * 1040 + 65 * (2 * hpp_ + j_) + 65],
                                    probs[j_][:, (jt2 % 4) * 512:(jt2 % 4) * 512 + 512],
                                    start=(jt2 == 0), stop=(jt2 == 31)),
                                    wait=tw, inc=(jt_ == 31))
                        for jt in range(32):
                            hpar = jt % 2
                            D = jt - 4 * ib
                            biased = -13 <= D <= 16
                            grp_end = {}
                            for j in range(2):
                                pst = ps[2 * j + hpar]
                                tw = []
                                if (hpar, j) in exp_done:
                                    tw.append((P.c["s"], exp_done[hpar, j]))
                                if ib == 0 and jt == 0 and j == 0:
                                    tw += [(qld[hp % 2], qv), (kpool[hp % 2], kld)]
                                mm = P.op("tensor", lambda h, j_=j, jt_=jt, p_=pst, hp_=hp % 2, ib_=ib, b_=biased, qz_=qz: h.matmul(
                                    p_[:, 0:512],
                                    kT_sb[:, hp_ * 4096 + jt_ * 128:hp_ * 4096 + (jt_ + 1) * 128],
                                    qz_[:, j_ * 4096 + ib_ * 512:j_ * 4096 + (ib_ + 1) * 512],
                                    start=True, stop=(not b_)),
                                    wait=tw, inc=(not biased))
                                if biased:
                                    mm = P.op("tensor", lambda h, j_=j, D_=D, p_=pst: h.matmul(
                                        p_[:, 0:512], self.ident_b[:],
                                        dg[j_][:, (16 - D_) * 128:(20 - D_) * 128],
                                        start=False, stop=True),
                                        wait=[(P.c["v"], bv[j])] if (ib == 0 and jt == 0) else (),
                                        inc=True)
                                grp_end[j] = mm
                            if jt >= 2:
                                emit_av(jt - 2)
                            for j, hh in enumerate((2 * hp, 2 * hp + 1)):
                                pst = ps[2 * j + hpar]
                                ev_ = P.op("scalar", lambda h, j_=j, p_=pst, sl_=jt % 4, hh_=hh: h.activation(
                                    probs[j_][:, sl_ * 512:(sl_ + 1) * 512], p_[:, 0:512],
                                    AF.Exp, bias=Bneg[:, hh_:hh_ + 1], scale=1.0),
                                    wait=[(P.c["t"], grp_end[j])], inc=True)
                                exp_done[hpar, j] = ev_
                                exp_at[(jt, j)] = ev_
                        emit_av(30)
                        emit_av(31)
                        av_end = P.c["t"].v
                        last_evn = {}
                        for j in range(2):
                            psv = ps[4 + j]
                            rr = self.rrow[64:65, 0:512]
                            rv = P.op("vector", lambda h, p_=psv, r_=rr: h.reciprocal(
                                r_, p_[64:65, 0:512]), wait=[(P.c["t"], av_end)], inc=True)
                            bc = P.op("tensor", lambda h, r_=rr: h.matmul(
                                ps[6][:, 0:512], self.onesP[64:65, :], r_, start=True, stop=True),
                                wait=[(P.c["v"], rv)], inc=True)
                            P.op("vector", lambda h: h.tensor_copy(
                                self.bcb[0:64, 0:512], ps[6][0:64, 0:512]),
                                wait=[(P.c["t"], bc)])
                            ibp = ib % 2
                            stw = []
                            if (hp, ib) != (0, 0):
                                stw = [(ast[ibp], last_store[ibp])]
                            last_evn[j] = P.op("vector", lambda h, j_=j, p_=psv, ibp_=ibp: h.tensor_tensor(
                                attnb[j_][0:64, ibp_ * 512:(ibp_ + 1) * 512],
                                p_[0:64, 0:512], self.bcb[0:64, 0:512], op=OP.mult),
                                wait=(stw if j == 0 else []),
                                inc=True)
                            hh2 = 2 * hp + j
                            last_store[ibp] = P.dma(
                                "sync", ast[ibp],
                                sc["attn"][hh2 * 64:(hh2 + 1) * 64, ib * 512:(ib + 1) * 512],
                                attnb[j][0:64, ibp * 512:(ibp + 1) * 512],
                                wait=[(P.c["v"], last_evn[j])])
                    q_reader_at[hp] = P.c["t"].v
                    dg_tread = P.c["t"].v
                    prev_bv = dict(bv)
        return

    def phase_l0_ffn(self):
        nc, P, sc = self.nc, self.P, self.sc
        ps = self.ps
        SGN_NEG = -1.0e30
        with ExitStack() as ph:
            sb = lambda n, shp, dt: self.sbuf(n, shp, dt, ph)
            invf_bc = sb("f_invf", [128, 64], F32)
            lnb_bc = sb("f_lnb", [128, 64], F32)
            sgn_bc = sb("f_sgn", [128, 64], F32)
            rowtmp = sb("f_rowt", [1, 256], F32)
            onc = P.ctr("f_onc")
            gonc = P.ctr("f_gonc")

            gi = P.op("gpsimd", lambda h: h.iota(
                rowtmp[0:1, 0:64].rearrange("p (a b) -> p a b", b=32),
                pattern=[[0, 2], [1, 32]], base=0, channel_multiplier=0,
                allow_small_or_imprecise_dtypes=True), inc=True)
            sA = P.op("scalar", lambda h: h.activation(
                rowtmp[0:1, 64:128], rowtmp[0:1, 0:64], AF.Exp,
                scale=-float(np.log(10000.0) / 32.0)), wait=[(P.c["g"], gi)], inc=True)
            vA = self.bcast(invf_bc[:], rowtmp[0:1, 64:128], 64, ps[5],
                            wait=[(P.c["s"], sA)])
            vB = P.op("vector", lambda h: h.tensor_scalar(
                rowtmp[0:1, 128:192], rowtmp[0:1, 0:64], 2.0 / 89.6, 25.6 / 89.6,
                op0=OP.mult, op1=OP.add), wait=[(P.c["g"], gi)], inc=True)
            sB = P.op("scalar", lambda h: h.activation(
                rowtmp[0:1, 128:192], rowtmp[0:1, 128:192], AF.Ln),
                wait=[(P.c["v"], vB)], inc=True)
            vB = self.bcast(lnb_bc[:], rowtmp[0:1, 128:192], 64, ps[5],
                            wait=[(P.c["s"], sB), (P.c["v"], vA)])
            gs = P.op("gpsimd", lambda h: h.memset(rowtmp[0:1, 192:224], -1.0), inc=True)
            gs = P.op("gpsimd", lambda h: h.memset(rowtmp[0:1, 224:256], 1.0), inc=True)
            vC = self.bcast(sgn_bc[:], rowtmp[0:1, 192:256], 64, ps[5],
                            wait=[(P.c["g"], gs), (P.c["v"], vB)])
            P.barrier("p3const")
            for l in range(2):
                src_bf = sc["xn"] if l == 0 else sc["ffnxb"]
                src_f = sc["xnf"] if l == 0 else sc["ffnx"]
                dst_f = sc["ffnx"] if l == 0 else sc["ffno"]
                dst_bf = sc["ffnxb"] if l == 0 else None
                self.ffn_subAB(l, src_bf, src_f, invf_bc, lnb_bc, sgn_bc,
                               rowtmp, onc, gonc, ph)
                P.barrier(f"p3ab{l}")
                self.ffn_subC(l, src_f, dst_f, dst_bf, onc, gonc)
                P.barrier(f"p3c{l}")

    def ffn_subAB(self, l, src_bf, src_f, invf_bc, lnb_bc, sgn_bc,
                  rowtmp, onc, gonc, ph_outer):
        nc, P, sc = self.nc, self.P, self.sc
        ps = self.ps
        with ExitStack() as ph:
            sb = lambda n, shp, dt: self.sbuf(f"{n}_{l}", shp, dt, ph)
            qf = sb("f_qf", [128, 4 * 4096], BF16)
            kf = sb("f_kf", [128, 4 * 4096], BF16)
            vloc = sb("f_vloc", [128, 32 * 520], BF16)
            nloc = sb("f_nloc", [2, 64], F32)
            brow2 = sb("f_brow", [1, 128], F32)
            Bloc = sb("f_Bloc", [128, 64], F32)
            phA = ExitStack()
            sbA = lambda n, shp, dt: self.sbuf(f"{n}_{l}", shp, dt, phA)
            wqkv_sb = sbA("f_wqkv", [128, 8 * 1536], BF16)
            xqb = [sbA("f_xqb0", [128, 8 * 513], BF16)]
            qtokb = sbA("f_qtok", [128, 2048], F32)
            ktokb = sbA("f_ktok", [128, 2048], F32)
            t3b = sbA("f_t3", [128, 2048], F32)
            rpb = sbA("f_rp", [128, 2048], F32)
            sq2 = rpb[:].bitcast(BF16)[:, 0:2048]
            ssq = sbA("f_ssq", [128, 2 * 32], F32)
            rloc = sbA("f_rloc", [128, 2 * 32], F32)
            tabs = sbA("f_tabs", [128, 8 * 256], F32)
            bd2 = sbA("f_bd2", [128, 2], BF16)
            sqb = sbA("f_sqb", [128, TB], BF16)
            qsrow = sbA("f_qsrow", [1, 4 * 64], F32)
            qs_bc = sbA("f_qsbc", [128, 64], F32)
            qsr_bc = sbA("f_qsrbc", [128, 64], F32)
            ks_bc = sbA("f_ksbc", [128, 64], F32)
            ksr_bc = sbA("f_ksrbc", [128, 64], F32)
            tcol = sbA("f_tcol", [128, 4], F32)
            ld = P.dpool(f"f_ld{l}", 2)
            ost = P.dpool(f"f_ost{l}", 2)

            gq = P.op("gpsimd", lambda h: h.memset(bd2[:], 1.0), inc=True)
            gq = P.op("gpsimd", lambda h: h.affine_select(
                out=bd2[:], in_=bd2[:], pattern=[[-64, 2]],
                compare_op=OP.is_ge, fill=0.0, base=0, channel_multiplier=1),
                wait=[(P.c["g"], gq)], inc=True)
            gbd = P.op("gpsimd", lambda h: h.affine_select(
                out=bd2[:], in_=bd2[:], pattern=[[64, 2]],
                compare_op=OP.is_ge, fill=0.0, base=63, channel_multiplier=-1),
                wait=[(P.c["g"], gq)], inc=True)
            cqkv = P.ctr(f"c_qkv{l}")
            wv = P.dma("gpsimd", cqkv, wqkv_sb[:].rearrange("p (kc n) -> p kc n", n=1536),
                       self.inp["wqkv"][l].rearrange("(kc p) n -> p kc n", p=128))
            qsl = P.dma("sync", onc, rowtmp[0:1, 0:64],
                        self.inp["qk_scale"][l, 0].unsqueeze(0))
            qsl = P.dma("sync", onc, rowtmp[0:1, 64:128],
                        self.inp["qk_scale"][l, 1].unsqueeze(0))
            P.op("vector", lambda h: h.tensor_scalar(
                qsrow[0:1, 0:64], rowtmp[0:1, 0:64], 8.0, None, op0=OP.mult),
                wait=[(onc, qsl)])
            P.op("vector", lambda h: h.tensor_scalar(
                qsrow[0:1, 64:96], rowtmp[0:1, 32:64], 8.0, None, op0=OP.mult))
            P.op("vector", lambda h: h.tensor_scalar(
                qsrow[0:1, 96:128], rowtmp[0:1, 0:32], 8.0, None, op0=OP.mult))
            P.op("vector", lambda h: h.tensor_copy(qsrow[0:1, 128:192], rowtmp[0:1, 64:128]))
            P.op("vector", lambda h: h.tensor_copy(qsrow[0:1, 192:224], rowtmp[0:1, 96:128]))
            vq = P.op("vector", lambda h: h.tensor_copy(qsrow[0:1, 224:256], rowtmp[0:1, 64:96]), inc=True)
            vq = self.bcast(qs_bc[:], qsrow[0:1, 0:64], 64, ps[5], wait=[(P.c["v"], vq)])
            vq = self.bcast(qsr_bc[:], qsrow[0:1, 64:128], 64, ps[5], wait=[(P.c["v"], vq)])
            vq = self.bcast(ks_bc[:], qsrow[0:1, 128:192], 64, ps[5], wait=[(P.c["v"], vq)])
            vq = self.bcast(ksr_bc[:], qsrow[0:1, 192:256], 64, ps[5], wait=[(P.c["v"], vq)])
            gvz = P.op("gpsimd", lambda h: h.memset(
                bass_rust.AP(xqb[0][:].tensor, 0, [[8 * 513, 128], [513, 8], [1, 1]]),
                0.0), inc=True)
            gvz = P.op("gpsimd", lambda h: h.memset(
                bass_rust.AP(vloc[:].tensor, 64, [[32 * 520, 128], [520, 32], [65, 8], [1, 1]]),
                1.0), inc=True)

            TABS = {"sin": 0, "cos": 1, "scl": 2, "isc": 3, "csq": 4, "ssq": 5,
                    "csk": 6, "ssk": 7}

            def tab(name):
                o = TABS[name] * 256
                return tabs[:, o:o + 256]

            def tab3(name):
                return tab(name).rearrange("p (a b) -> p a b", b=64)

            def tabx(name):
                o = TABS[name] * 256
                return bass_rust.AP(tabs[:].tensor, o, [[8 * 256, 128], [64, 4], [0, 8], [1, 64]])

            def bcx(t):
                return bass_rust.AP(t[:].tensor, 0, [[64, 128], [0, 4], [1, 64]])

            C2PI = float(1.0 / (2 * np.pi))
            CW1 = float(np.float32(2 * np.pi))
            CW2 = float(2 * np.pi - np.float64(np.float32(2 * np.pi)))
            RND = float(3 * 2.0 ** 22)

            mm_done = {}
            ev_done = {"q": 0, "k": 0, "v": 0}
            s_prev = 0
            tlast = 0
            norm_done = 0
            tcol_read = 0
            for i in range(NT):
                pr = i % 2
                wvl = []
                if i >= 1:
                    wvl = [(P.c["t"], mm_done[i - 1])]
                if i == 0:
                    lsrc = fm_view(src_bf, 0, TB)
                    ldst = bass_rust.AP(xqb[0][:].tensor, 1, [[8 * 513, 128], [513, 8], [1, TB]])
                    wvl.append((P.c["g"], gvz))
                else:
                    lsrc = fm_view(src_bf, i * TB - 1, i * TB + TB)
                    ldst = bass_rust.AP(xqb[0][:].tensor, 0, [[8 * 513, 128], [513, 8], [1, TB + 1]])
                lv = P.dma("sync", ld[pr], ldst, lsrc, wait=wvl)
                for tt in range(4):
                    for which, pso in (("q", ps[0]), ("k", ps[1]), ("v", ps[2])):
                        off = {"q": 0, "k": 512, "v": 1024}[which]
                        tw = []
                        if tt == 0 and which == "q":
                            tw = [(ld[pr], lv)]
                            if i == 0:
                                tw.append((cqkv, wv))
                        if ev_done[which]:
                            tw.append((P.c["v"], ev_done[which]))
                        for kc in range(NCH):
                            sh = 1 if kc < 4 else 0
                            mm_done[i] = P.op("tensor", lambda h, tt_=tt, kc_=kc, sh_=sh, o_=off, p_=pr, s_=pso: h.matmul(
                                s_[:, 0:512],
                                xqb[0][:, kc_ * 513 + sh_ + tt_ * 128:kc_ * 513 + sh_ + tt_ * 128 + 128],
                                wqkv_sb[:, kc_ * 1536 + o_:kc_ * 1536 + o_ + 512],
                                start=(kc_ == 0), stop=(kc_ == NCH - 1)),
                                wait=(tw if kc == 0 else ()), inc=(kc == NCH - 1))
                        if which == "v":
                            vdst = bass_rust.AP(vloc[:].tensor, (i * 4 + tt) * 520,
                                                [[32 * 520, 128], [65, 8], [1, 64]])
                            ev_done[which] = P.op("vector", lambda h, d_=vdst, s_=pso: h.tensor_copy(
                                d_, s_[:, 0:512].rearrange("p (a b) -> p a b", b=64)),
                                wait=[(P.c["t"], mm_done[i])], inc=True)
                        else:
                            dstb = qtokb if which == "q" else ktokb
                            wt = [(P.c["t"], mm_done[i])]
                            if i > 0 or tt > 0:
                                wt.append((P.c["v"], P.c["v"].v))
                            ev_done[which] = P.op("vector", lambda h, tt_=tt, d_=dstb, s_=pso: h.tensor_copy(
                                d_[:, tt_ * 512:(tt_ + 1) * 512], s_[:, 0:512]),
                                wait=wt, inc=True)
                gi2 = P.op("gpsimd", lambda h, i_=i: h.iota(
                    tcol[:], pattern=[[128, 4]], base=i_ * TB, channel_multiplier=1,
                    allow_small_or_imprecise_dtypes=True),
                    wait=[(P.c["v"], tcol_read)] if i > 0 else (), inc=True)
                tcx = bass_rust.AP(tcol[:].tensor, 0, [[4, 128], [1, 4], [0, 64]])
                vt = P.op("vector", lambda h: h.tensor_tensor(
                    tab3("sin"), tcx, bcx(invf_bc), op=OP.mult),
                    wait=[(P.c["g"], gi2), (P.c["v"], vq), (P.c["s"], s_prev)], inc=True)
                P.op("vector", lambda h: h.tensor_scalar(
                    tab("cos"), tab("sin"), C2PI, RND, op0=OP.mult, op1=OP.add))
                P.op("vector", lambda h: h.tensor_scalar(
                    tab("cos"), tab("cos"), RND, None, op0=OP.subtract))
                P.op("vector", lambda h: h.scalar_tensor_tensor(
                    tab("scl"), tab("cos"), -CW1, tab("sin"), op0=OP.mult, op1=OP.add))
                P.op("vector", lambda h: h.scalar_tensor_tensor(
                    tab("scl"), tab("cos"), -CW2, tab("scl"), op0=OP.mult, op1=OP.add))
                P.op("vector", lambda h: h.tensor_scalar(
                    tab("isc"), tab("sin"), C2PI, 0.25, op0=OP.mult, op1=OP.add))
                P.op("vector", lambda h: h.tensor_scalar(
                    tab("isc"), tab("isc"), RND, None, op0=OP.add))
                P.op("vector", lambda h: h.tensor_scalar(
                    tab("isc"), tab("isc"), RND, None, op0=OP.subtract))
                P.op("vector", lambda h: h.scalar_tensor_tensor(
                    tab("csq"), tab("isc"), -CW1, tab("sin"), op0=OP.mult, op1=OP.add))
                P.op("vector", lambda h: h.scalar_tensor_tensor(
                    tab("csq"), tab("isc"), -CW2, tab("csq"), op0=OP.mult, op1=OP.add))
                P.op("vector", lambda h: h.tensor_scalar(
                    tab("csq"), tab("csq"), 1.0, float(np.pi / 2), op0=OP.mult, op1=OP.add))
                PI_LO = float(np.nextafter(np.float32(np.pi), np.float32(0)))
                P.op("vector", lambda h, p_=PI_LO: h.tensor_scalar(
                    tab("csq"), tab("csq"), p_, -p_, op0=OP.min, op1=OP.max))
                vt2 = P.op("vector", lambda h, p_=PI_LO: h.tensor_scalar(
                    tab("scl"), tab("scl"), p_, -p_, op0=OP.min, op1=OP.max), inc=True)
                st_ = P.op("scalar", lambda h: h.activation(tab("sin"), tab("scl"), AF.Sin),
                           wait=[(P.c["v"], vt2)], inc=True)
                st_ = P.op("scalar", lambda h: h.activation(tab("cos"), tab("csq"), AF.Sin), inc=True)
                vt3 = P.op("vector", lambda h: h.tensor_scalar(
                    tcol[:], tcol[:], 1.0 / 256.0, -8.0, op0=OP.mult, op1=OP.add),
                    wait=[(P.c["s"], st_)], inc=True)
                vt3 = P.op("vector", lambda h: h.tensor_tensor(
                    tab3("scl"), tcx, bcx(lnb_bc), op=OP.mult), inc=True)
                tcol_read = vt3
                st2 = P.op("scalar", lambda h: h.activation(tab("isc"), tab("scl"), AF.Exp,
                                                            scale=-1.0),
                           wait=[(P.c["v"], vt3)], inc=True, drain=True)
                st2 = P.op("scalar", lambda h: h.activation(tab("scl"), tab("scl"), AF.Exp), inc=True)
                vt4 = P.op("vector", lambda h: h.tensor_tensor(tab3("csq"), tab3("cos"), bcx(qs_bc), op=OP.mult),
                           wait=[(P.c["s"], st2)])
                P.op("vector", lambda h: h.tensor_tensor(tab("csq"), tab("csq"), tab("scl"), op=OP.mult))
                P.op("vector", lambda h: h.tensor_tensor(tab3("ssq"), tab3("sin"), bcx(qsr_bc), op=OP.mult))
                P.op("vector", lambda h: h.tensor_tensor(tab("ssq"), tab("ssq"), tab("scl"), op=OP.mult))
                P.op("vector", lambda h: h.tensor_tensor(tab3("ssq"), tab3("ssq"), bcx(sgn_bc), op=OP.mult))
                P.op("vector", lambda h: h.tensor_tensor(tab3("csk"), tab3("cos"), bcx(ks_bc), op=OP.mult))
                P.op("vector", lambda h: h.tensor_tensor(tab("csk"), tab("csk"), tab("isc"), op=OP.mult))
                P.op("vector", lambda h: h.tensor_tensor(tab3("ssk"), tab3("sin"), bcx(ksr_bc), op=OP.mult))
                P.op("vector", lambda h: h.tensor_tensor(tab("ssk"), tab("ssk"), tab("isc"), op=OP.mult))
                vt4 = P.op("vector", lambda h: h.tensor_tensor(tab3("ssk"), tab3("ssk"), bcx(sgn_bc), op=OP.mult), inc=True)
                for qk in range(2):
                    tokb = qtokb if qk == 0 else ktokb
                    ctn, stn = ("csq", "ssq") if qk == 0 else ("csk", "ssk")
                    o_ = qk * 32
                    sq_ = P.op("scalar", lambda h, t_=tokb: h.activation(
                        sq2[:], t_[:], AF.Square),
                        wait=[(P.c["v"], ev_done["q" if qk == 0 else "k"])], inc=True)
                    rv_ = P.op("vector", lambda h, o2=o_: h.tensor_reduce(
                        ssq[:, o2:o2 + 32], sq2[:].rearrange("p (a c) -> p a c", c=64),
                        axis=AX.X, op=OP.add), wait=[(P.c["s"], sq_)], inc=True)
                    s0 = rloc[:, o_:o_ + 32]
                    ssv = ssq[:, o_:o_ + 32]
                    sA_ = P.op("scalar", lambda h, a_=ssv, b_=s0: h.activation(
                        b_, a_, AF.Sqrt), wait=[(P.c["v"], rv_)], inc=True)
                    nsc = self.rrow[0:128, 0:64]
                    P.op("vector", lambda h, b_=s0, n_=nsc: h.reciprocal(n_[:, 0:32], b_),
                         wait=[(P.c["s"], sA_)])
                    P.op("vector", lambda h, a_=ssv, n_=nsc: h.tensor_tensor(
                        n_[:, 0:32], a_, n_[:, 0:32], op=OP.mult))
                    P.op("vector", lambda h, b_=s0, n_=nsc: h.tensor_tensor(
                        b_, b_, n_[:, 0:32], op=OP.add))
                    P.op("vector", lambda h, b_=s0: h.tensor_scalar(
                        b_, b_, 0.5, 1e-12, op0=OP.mult, op1=OP.max))
                    rv2 = P.op("vector", lambda h, b_=s0: h.reciprocal(b_, b_), inc=True)
                    rx = bass_rust.AP(rloc[:].tensor, o_, [[64, 128], [8, 4], [1, 8], [0, 64]])
                    a1 = P.op("vector", lambda h, t_=tokb, r_=rx: h.tensor_tensor(
                        t_[:].rearrange("p (a b) -> p a b", b=64),
                        t_[:].rearrange("p (a b) -> p a b", b=64), r_, op=OP.mult),
                        wait=[(P.c["v"], rv2)], inc=True)
                    P.op("vector", lambda h, t_=tokb, c_=tabx(ctn): h.tensor_tensor(
                        t3b[:].rearrange("p (a b) -> p a b", b=64),
                        t_[:].rearrange("p (a b) -> p a b", b=64), c_, op=OP.mult),
                        wait=[(P.c["t"], tlast)] if tlast else ())
                    sx0 = bass_rust.AP(tabs[:].tensor, TABS[stn] * 256,
                                       [[8 * 256, 128], [64, 4], [0, 8], [1, 32]])
                    sx1 = bass_rust.AP(tabs[:].tensor, TABS[stn] * 256 + 32,
                                       [[8 * 256, 128], [64, 4], [0, 8], [1, 32]])
                    P.op("vector", lambda h, t_=tokb, s_=sx0: h.tensor_tensor(
                        bass_rust.AP(rpb[:].tensor, 0, [[2048, 128], [512, 4], [64, 8], [1, 32]]),
                        bass_rust.AP(t_[:].tensor, 32, [[2048, 128], [512, 4], [64, 8], [1, 32]]),
                        s_, op=OP.mult))
                    P.op("vector", lambda h, t_=tokb, s_=sx1: h.tensor_tensor(
                        bass_rust.AP(rpb[:].tensor, 32, [[2048, 128], [512, 4], [64, 8], [1, 32]]),
                        bass_rust.AP(t_[:].tensor, 0, [[2048, 128], [512, 4], [64, 8], [1, 32]]),
                        s_, op=OP.mult))
                    ap_ = P.op("vector", lambda h: h.tensor_tensor(
                        t3b[:], t3b[:], rpb[:], op=OP.add), inc=True)
                    dstf = qf if qk == 0 else kf
                    for tt in range(4):
                        for cc in range(4):
                            tr = P.op("tensor", lambda h, tt_=tt, cc_=cc: h.transpose(
                                ps[3][:, cc_ * 128:(cc_ + 1) * 128],
                                t3b[:, tt_ * 512 + cc_ * 128:tt_ * 512 + (cc_ + 1) * 128],
                                self.ident_f[:]),
                                wait=([(P.c["v"], ap_)] if (cc == 0 and tt == 0) else ()),
                                inc=(cc == 3))
                        tlast = tr
                        trd = P.op("vector", lambda h, tt_=tt, i_=i, d_=dstf: h.tensor_copy(
                            bass_rust.AP(d_[:].tensor, i_ * 512 + tt_ * 128,
                                         [[4 * 4096, 128], [4096, 4], [1, 128]]),
                            ps[3][:, 0:512].rearrange("p (a b) -> p a b", b=128)),
                            wait=[(P.c["t"], tr)], inc=True)
                        P.emit("tensor", lambda h: h.nop(), wait=[(P.c["v"], trd)])
                    for cc in range(4):
                        sqv = P.op("scalar", lambda h, cc_=cc, i_=i, d_=dstf: h.activation(
                            sqb[:], d_[:, cc_ * 4096 + i_ * 512:cc_ * 4096 + (i_ + 1) * 512],
                            AF.Square), wait=[(P.c["v"], trd)], inc=True)
                        nm2 = P.op("tensor", lambda h: h.matmul(
                            ps[6][0:2, 0:TB], bd2[:], sqb[:], start=True, stop=True),
                            wait=[(P.c["s"], sqv), (P.c["v"], norm_done), (P.c["g"], gbd)], inc=True)
                        col = qk * 32 + cc * 8 + i
                        norm_done = P.op("vector", lambda h, col_=col: h.tensor_reduce(
                            nloc[0:2, col_:col_ + 1], ps[6][0:2, 0:TB], axis=AX.X, op=OP.max),
                            wait=[(P.c["t"], nm2)], inc=True)
                        P.emit("scalar", lambda h: h.nop(), wait=[(P.c["t"], nm2)])
                s_prev = P.c["s"].v

            phA.close()
            smv = sc["small"]
            for qk_ in range(2):
                d1 = P.dma("sync", onc,
                           bass_rust.AP(smv.tensor, 64 + 64 * qk_, [[8, 2], [16, 4], [1, 8]]),
                           nloc[0:2, 32 * qk_:32 * (qk_ + 1)].rearrange("p (b c) -> p b c", c=8),
                           wait=[(P.c["v"], norm_done)], slow_ok=True)
            qrow = brow2[0:1, 0:64]
            krow = brow2[0:1, 64:128]
            d2 = P.dma("sync", onc, qrow, bass_rust.AP(smv.tensor, 64, [[64, 1], [1, 64]]),
                       wait=[(onc, d1)])
            d2 = P.dma("sync", onc, krow, bass_rust.AP(smv.tensor, 128, [[64, 1], [1, 64]]))
            sq_ = P.op("scalar", lambda h: h.activation(qrow, qrow, AF.Sqrt), wait=[(onc, d2)], inc=True)
            sq_ = P.op("scalar", lambda h: h.activation(krow, krow, AF.Sqrt), inc=True)
            kwin = rowtmp[0:1, 0:64]
            vb_ = P.op("vector", lambda h: h.tensor_copy(kwin, krow), wait=[(P.c["s"], sq_)])
            for hh_ in range(8):
                vb_ = P.op("vector", lambda h, o_=hh_ * 8: h.tensor_tensor(
                    kwin[0:1, o_ + 1:o_ + 8], krow[0:1, o_ + 1:o_ + 8],
                    krow[0:1, o_:o_ + 7], op=OP.max), inc=(hh_ == 7))
            P.op("vector", lambda h: h.tensor_tensor(qrow, qrow, kwin, op=OP.mult))
            vb_ = P.op("vector", lambda h: h.tensor_scalar(
                qrow, qrow, -1.0, -0.5, op0=OP.mult, op1=OP.add), inc=True)
            vb_ = self.bcast(Bloc[:], qrow, 64, ps[5], wait=[(P.c["v"], vb_)])
            P.barrier(f"p3a{l}")

            SGN_NEG = -1.0e30
            msk = sb("f_msk", [128, 11 * 128], BF16)
            gm = P.op("gpsimd", lambda h: h.memset(msk[:], 0.0), inc=True)
            for dmi in range(11):
                dmp = dmi - 3
                gm = P.op("gpsimd", lambda h, d_=dmp, i_=10 - dmi: h.affine_select(
                    out=msk[:, i_ * 128:(i_ + 1) * 128], in_=msk[:, i_ * 128:(i_ + 1) * 128],
                    pattern=[[-1, 128]], compare_op=OP.is_ge, fill=SGN_NEG,
                    base=128 * d_, channel_multiplier=1),
                    wait=[(P.c["g"], gm)], inc=True)
                gm = P.op("gpsimd", lambda h, d_=dmp, i_=10 - dmi: h.affine_select(
                    out=msk[:, i_ * 128:(i_ + 1) * 128], in_=msk[:, i_ * 128:(i_ + 1) * 128],
                    pattern=[[1, 128]], compare_op=OP.is_ge, fill=SGN_NEG,
                    base=512 - 128 * d_, channel_multiplier=-1),
                    wait=[(P.c["g"], gm)], inc=True)
            kzf = sb("f_kzf", [128, 2 * 1024], BF16)
            probsb = sb("f_probsb", [128, 2 * 1024], BF16)
            probsT = sb("f_probsT", [128, 2 * 1024], BF16)
            mrow = sb("f_mrow", [128, 8], F32)
            obuf = sb("f_obuf", [128, 4 * 512], BF16)
            wo_sb = sb("f_wo", [128, 4 * 1024], BF16)
            xmb = sb("f_xmb", [128, NCH * TB], F32)
            cwo = P.ctr(f"c_wo{l}")
            wvo = P.dma("gpsimd", cwo, wo_sb[:].rearrange("p (a b) -> p a b", b=1024),
                        self.inp["wo0"][l].rearrange("(a p) c -> p a c", p=128))
            gz3 = P.op("gpsimd", lambda h: h.memset(kzf[:], 0.0), inc=True)
            iters = []
            for w in FFN_W:
                for hp in range(4):
                    for it in range(4):
                        for j in range(2):
                            iters.append((w, hp, it, j))
            K = len(iters)
            st_grp, st_mx, st_exp, st_trE, st_evn = {}, {}, {}, {}, {}
            kz_done = {}
            prevG = {0: 0, 1: 0}
            xst, xm_store, lvx = {}, {}, {}
            last_of_w = {}
            for k in range(K - 1, -1, -1):
                w = iters[k][0]
                if w not in last_of_w:
                    last_of_w[w] = k

            def kz_build(w, hp):
                jlo = max(0, (w - 1) * 4)
                nj = (((w + 1) * 4) - jlo) * 128
                kzw = [(P.c["t"], P.c["t"].v)] if (w, hp) != (FFN_W[0], 0) else [(P.c["g"], gz3)]
                P.op("vector", lambda h, hp_=hp, c0=jlo * 128, n_=nj: h.tensor_copy(
                    kzf[0:64, 0:n_], kf[0:64, hp_ * 4096 + c0:hp_ * 4096 + c0 + n_]),
                    wait=kzw)
                kz_done[(w, hp)] = P.op("vector", lambda h, hp_=hp, c0=jlo * 128, n_=nj: h.tensor_copy(
                    kzf[64:128, 1024:1024 + n_], kf[64:128, hp_ * 4096 + c0:hp_ * 4096 + c0 + n_]),
                    inc=True)

            def stage_A(k):
                w, hp, it, j = iters[k]
                njt = 8 if w >= 1 else 4
                p = k % 2
                ig = w * 4 + it
                jlo = max(0, (w - 1) * 4)
                mm = None
                for jh in range(2 if njt == 8 else 1):
                    pst = ps[2 * p + jh]
                    tw = []
                    if jh == 0:
                        if k >= 2:
                            tw += [(P.c["s"], st_exp[k - 2]), (P.c["v"], st_trE[k - 2])]
                        if it == 0 and j == 0:
                            tw.append((P.c["v"], kz_done[(w, hp)]))
                            if (w, hp) == (FFN_W[0], 0):
                                tw.append((P.c["g"], gm))
                    P.op("tensor", lambda h, j_=j, jh_=jh, ig_=ig, hp_=hp, p_=pst: h.matmul(
                        p_[:, 0:512],
                        qf[:, hp_ * 4096 + ig_ * 128:hp_ * 4096 + (ig_ + 1) * 128],
                        kzf[:, j_ * 1024 + jh_ * 512:j_ * 1024 + jh_ * 512 + 512],
                        start=True, stop=False), wait=tw)
                    dmi0 = (ig - (jlo + jh * 4)) + 3
                    mm = P.op("tensor", lambda h, d_=dmi0, p_=pst: h.matmul(
                        p_[:, 0:512], self.ident_b[:],
                        msk[:, (10 - d_) * 128:(14 - d_) * 128],
                        start=False, stop=True), inc=True)
                st_grp[k] = mm

            def stage_BC(k):
                w, hp, it, j = iters[k]
                njt = 8 if w >= 1 else 4
                p = k % 2
                mx = P.op("vector", lambda h, p_=p: h.tensor_reduce(
                    mrow[:, 4 * p_:4 * p_ + 1], ps[2 * p_][:, 0:512], axis=AX.X,
                    op=OP.max, negate=True), wait=[(P.c["t"], st_grp[k])], inc=(njt == 4))
                if njt == 8:
                    P.op("vector", lambda h, p_=p: h.tensor_reduce(
                        mrow[:, 4 * p_ + 1:4 * p_ + 2], ps[2 * p_ + 1][:, 0:512],
                        axis=AX.X, op=OP.max, negate=True))
                    mx = P.op("vector", lambda h, p_=p: h.tensor_tensor(
                        mrow[:, 4 * p_:4 * p_ + 1], mrow[:, 4 * p_:4 * p_ + 1],
                        mrow[:, 4 * p_ + 1:4 * p_ + 2], op=OP.min), inc=True)
                st_mx[k] = mx
                ev = None
                for jh in range(2 if njt == 8 else 1):
                    ev = P.op("scalar", lambda h, p_=p, jh_=jh: h.activation(
                        probsb[:, p_ * 1024 + jh_ * 512:p_ * 1024 + jh_ * 512 + 512],
                        ps[2 * p_ + jh_][:, 0:512],
                        AF.Exp, bias=mrow[:, 4 * p_:4 * p_ + 1], scale=1.0),
                        wait=([(P.c["v"], mx)] if jh == 0 else ()), inc=True)
                st_exp[k] = ev

            def stage_DE(k):
                w, hp, it, j = iters[k]
                njt = 8 if w >= 1 else 4
                p = k % 2
                tr = None
                for jc in range(njt):
                    tr = P.op("tensor", lambda h, p_=p, jc_=jc: h.transpose(
                        ps[2 * p_][:].bitcast(BF16)[:, jc_ * 128:(jc_ + 1) * 128],
                        probsb[:, p_ * 1024 + jc_ * 128:p_ * 1024 + (jc_ + 1) * 128],
                        self.ident_b[:]),
                        wait=([(P.c["s"], st_exp[k])] if jc == 0 else ()),
                        inc=(jc == njt - 1))
                st_trE[k] = P.op("vector", lambda h, p_=p, n_=njt * 128: h.tensor_copy(
                    probsT[:, p_ * 1024:p_ * 1024 + n_],
                    ps[2 * p_][:].bitcast(BF16)[:, 0:n_]),
                    wait=[(P.c["t"], tr)], inc=True, drain=False)

            def stage_FG(k):
                w, hp, it, j = iters[k]
                njt = 8 if w >= 1 else 4
                p = k % 2
                jlo = max(0, (w - 1) * 4)
                avv = None
                for jc in range(njt):
                    jg = jlo + jc
                    vb0 = jg * 520 + 65 * (2 * hp + j)
                    tw = []
                    if jc == 0:
                        tw = [(P.c["v"], st_trE[k])]
                        if it == 0:
                            tw.append((P.c["v"], prevG[j]))
                    if j == 0:
                        avv = P.op("tensor", lambda h, jc_=jc, b_=vb0, p_=p, it_=it,
                                   st_=(jc == 0), sp_=(jc == njt - 1): h.matmul(
                            ps[6][0:65, it_ * 128:(it_ + 1) * 128],
                            vloc[:, b_:b_ + 65],
                            probsT[:, p_ * 1024 + jc_ * 128:p_ * 1024 + (jc_ + 1) * 128],
                            start=st_, stop=sp_), wait=tw, inc=(jc == njt - 1))
                    else:
                        P.op("tensor", lambda h, jc_=jc, b_=vb0, p_=p, it_=it,
                             st_=(jc == 0), sp_=(jc == njt - 1): h.matmul(
                            ps[7][64:128, it_ * 128:(it_ + 1) * 128],
                            vloc[:, b_:b_ + 64],
                            probsT[:, p_ * 1024 + jc_ * 128:p_ * 1024 + (jc_ + 1) * 128],
                            start=st_, stop=sp_), wait=tw)
                        avv = P.op("tensor", lambda h, jc_=jc, b_=vb0, p_=p, it_=it,
                                   st_=(jc == 0), sp_=(jc == njt - 1): h.matmul(
                            ps[4][0:1, it_ * 128:(it_ + 1) * 128],
                            vloc[:, b_ + 64:b_ + 65],
                            probsT[:, p_ * 1024 + jc_ * 128:p_ * 1024 + (jc_ + 1) * 128],
                            start=st_, stop=sp_), inc=(jc == njt - 1))
                if it < 3 or j == 0:
                    return
                pr_ = hp
                rre = self.rrow[64:65, 0:512]
                rro = self.rrow[32:33, 0:512]
                rv = P.op("vector", lambda h, r_=rre: h.reciprocal(
                    r_, ps[6][64:65, 0:512]), wait=[(P.c["t"], avv)], inc=True)
                rv2 = P.op("vector", lambda h, r_=rro: h.reciprocal(
                    r_, ps[4][0:1, 0:512]), inc=True)
                bce = P.op("tensor", lambda h, r_=rre: h.matmul(
                    ps[4][:, 0:512], self.onesP[64:65, :], r_, start=True, stop=True),
                    wait=[(P.c["v"], rv2)], inc=True)
                cpe = P.op("vector", lambda h: h.tensor_copy(
                    self.bcb[0:64, 0:512], ps[4][0:64, 0:512]),
                    wait=[(P.c["t"], bce)], inc=True)
                prevG[0] = P.op("vector", lambda h, pr2=pr_: h.tensor_tensor(
                    obuf[0:64, pr2 * 512:(pr2 + 1) * 512],
                    ps[6][0:64, 0:512], self.bcb[0:64, 0:512], op=OP.mult),
                    inc=True)
                bco = P.op("tensor", lambda h, r_=rro: h.matmul(
                    ps[4][:, 0:512], self.onesP[32:33, :], r_, start=True, stop=True),
                    wait=[(P.c["v"], cpe)], inc=True)
                P.op("vector", lambda h: h.tensor_copy(
                    self.bcb[64:128, 0:512], ps[4][64:128, 0:512]),
                    wait=[(P.c["t"], bco)])
                prevG[1] = st_evn[k] = P.op("vector", lambda h, pr2=pr_: h.tensor_tensor(
                    obuf[64:128, pr2 * 512:(pr2 + 1) * 512],
                    ps[7][64:128, 0:512], self.bcb[64:128, 0:512], op=OP.mult),
                    inc=True)

            def wo_block(w):
                for c in range(NCH):
                    wo_mm = None
                    for ic in range(4):
                        tw = []
                        if c == 0 and ic == 0:
                            tw = [(P.c["v"], st_evn[last_of_w[w]]), (ld[w % 2], lvx[w])]
                            if w == FFN_W[0]:
                                tw.append((cwo, wvo))
                            if (w - 1, NCH - 1) in xst:
                                tw.append((P.c["v"], xst[(w - 1, NCH - 1)]))
                        if ic == 0 and (w, c - 1) in xst:
                            tw.append((P.c["v"], xst[(w, c - 1)]))
                        wo_mm = P.op("tensor", lambda h, c_=c, ic_=ic: h.matmul(
                            ps[5][:, 0:512],
                            wo_sb[:, ic_ * 1024 + c_ * 128:ic_ * 1024 + (c_ + 1) * 128],
                            obuf[:, ic_ * 512:(ic_ + 1) * 512],
                            start=(ic_ == 0), stop=(ic_ == 3)),
                            wait=tw, inc=(ic == 3))
                    xst[(w, c)] = P.op("vector", lambda h, c_=c: h.tensor_tensor(
                        xmb[:, c_ * TB:(c_ + 1) * TB], xmb[:, c_ * TB:(c_ + 1) * TB],
                        ps[5][:, 0:512], op=OP.add), wait=[(P.c["t"], wo_mm)], inc=True)
                xm_store[w] = P.dma("sync", ost[w % 2], fm_view(sc["xmid"], w * TB, (w + 1) * TB),
                                    sb3(xmb), wait=[(P.c["v"], xst[(w, NCH - 1)])])
                if w != FFN_W[-1]:
                    wn = w + 1
                    lvx[wn] = P.dma("sync", ld[wn % 2], sb3(xmb),
                                    fm_view(src_f, wn * TB, (wn + 1) * TB),
                                    wait=[(P.c["v"], xst[(w, NCH - 1)]),
                                          (ost[w % 2], xm_store[w])])

            lvx[FFN_W[0]] = P.dma("sync", ld[FFN_W[0] % 2], sb3(xmb),
                                  fm_view(src_f, FFN_W[0] * TB, (FFN_W[0] + 1) * TB))
            for k in range(K):
                w, hp, it, j = iters[k]
                if it == 0 and j == 0:
                    kz_build(w, hp)
                stage_A(k)
                if k >= 1:
                    stage_DE(k - 1)
                if k >= 2:
                    stage_FG(k - 2)
                    if last_of_w.get(iters[k - 2][0]) == k - 2:
                        wo_block(iters[k - 2][0])
                stage_BC(k)
            stage_DE(K - 1)
            stage_FG(K - 2)
            stage_FG(K - 1)
            wo_block(iters[K - 1][0])
        return

    def ffn_subC(self, l, src_f, dst_f, dst_bf, onc, gonc):
        nc, P, sc = self.nc, self.P, self.sc
        ps = self.ps
        NJ = 22
        with ExitStack() as ph:
            sb = lambda n, shp, dt: self.sbuf(f"{n}_{l}", shp, dt, ph)
            w1_sb = sb("c_w1", [128, 8 * 5632], BF16)
            w2_sb = sb("c_w2", [128, 22 * 1024], BF16)
            xms = sb("c_xms", [128, 8 * 513], F32)
            xln = sb("c_xln", [128, NCH * TB], BF16)
            hbuf = sb("c_hbuf", [128, NJ * 512], BF16)
            xsh = hbuf[:][:, 0:NCH * TB]
            xsq = hbuf[:][:, NCH * TB:2 * NCH * TB]
            gel = sb("c_gel", [128, TB], BF16)
            gvec = sb("c_gvec", [128, 3 * NCH], F32)
            b1v = sb("c_b1v", [128, 44], F32)
            ostg = sb("c_ostg", [128, 2 * TB], F32)
            obf = sb("c_obf", [128, TB], BF16)
            ld = P.dpool(f"c_ld{l}", 2)
            ost = P.dpool(f"c_ost{l}", 2)
            cw1 = P.ctr(f"c_w1{l}")
            cw2 = P.ctr(f"c_w2{l}")
            gw1 = P.dma("gpsimd", cw1, w1_sb[:].rearrange("p (kc n) -> p kc n", n=5632),
                        self.inp["w1f"][l].rearrange("(kc p) n -> p kc n", p=128))
            gw2 = P.dma("gpsimd", cw2, w2_sb[:].rearrange("p (kc n) -> p kc n", n=1024),
                        self.inp["w2f"][l].rearrange("(kc p) n -> p kc n", p=128))
            lv0 = P.dma("sync", onc, gvec[:, 0:NCH], self.inp["ln_g"][l])
            lv0 = P.dma("sync", onc, gvec[:, NCH:2 * NCH], self.inp["ln_b"][l])
            lv0 = P.dma("sync", onc, gvec[:, 2 * NCH:3 * NCH], self.inp["b2f"][l])
            lv0 = P.dma("sync", onc, b1v[:], self.inp["b1f"][l])
            gz = P.op("gpsimd", lambda h: h.memset(
                bass_rust.AP(xms[:].tensor, 0, [[8 * 513, 128], [513, 8], [1, 1]]), 0.0),
                inc=True)
            stt = self.st_tmp
            t0, t1, t2, t3 = (stt[0:1, k * TB:(k + 1) * TB] for k in range(4))
            mm_last = 0
            st_last = {}
            casts = {}
            for w in range(NT):
                pr = w % 2
                wvl = ([(P.c["v"], casts[w - 1]), (P.c["v"], mm_last)]
                       if w >= 1 else [(P.c["g"], gz)])
                if w == 0:
                    lds = bass_rust.AP(xms[:].tensor, 1, [[8 * 513, 128], [513, 8], [1, TB]])
                    lsrc = fm_view(sc["xmid"], 0, TB)
                else:
                    lds = bass_rust.AP(xms[:].tensor, 0, [[8 * 513, 128], [513, 8], [1, TB + 1]])
                    lsrc = fm_view(sc["xmid"], w * TB - 1, w * TB + TB)
                lv = P.dma("sync", ld[pr], lds, lsrc, wait=wvl)
                for c in range(NCH):
                    sh = 1 if c < 4 else 0
                    casts[w] = P.op("vector", lambda h, c_=c, s_=sh: h.tensor_copy(
                        xsh[:, c_ * TB:(c_ + 1) * TB], xms[:, c_ * 513 + s_:c_ * 513 + s_ + TB]),
                        wait=([(ld[pr], lv)] + ([(P.c["v"], mm_last)] if mm_last else []))
                        if c == 0 else (), inc=(c == NCH - 1), drain=(c == NCH - 1))
                sq = P.op("vector", lambda h: h.tensor_tensor(
                    xsq[:], xsh[:], xsh[:], op=OP.mult), inc=True)
                for c in range(NCH):
                    P.op("tensor", lambda h, c_=c: h.matmul(
                        ps[0][0:1, 0:TB], self.ones_col_b[:], xsh[:, c_ * TB:(c_ + 1) * TB],
                        start=(c_ == 0), stop=(c_ == NCH - 1)),
                        wait=[(P.c["v"], casts[w])] if c == 0 else ())
                sums = None
                for c in range(NCH):
                    sums = P.op("tensor", lambda h, c_=c: h.matmul(
                        ps[1][0:1, 0:TB], self.ones_col_b[:], xsq[:, c_ * TB:(c_ + 1) * TB],
                        start=(c_ == 0), stop=(c_ == NCH - 1)),
                        wait=[(P.c["v"], sq)] if c == 0 else (), inc=(c == NCH - 1))
                mu_b = self.st_mu[0:1, pr * TB:(pr + 1) * TB]
                r_b = self.st_r[0:1, pr * TB:(pr + 1) * TB]
                P.op("vector", lambda h, m_=mu_b: h.tensor_scalar(
                    m_, ps[0][0:1, 0:TB], 1.0 / DM, None, op0=OP.mult),
                    wait=[(P.c["t"], sums)])
                P.op("vector", lambda h: h.tensor_scalar(
                    t0, ps[1][0:1, 0:TB], 1.0 / DM, None, op0=OP.mult))
                P.op("vector", lambda h, m_=mu_b: h.tensor_tensor(t1, m_, m_, op=OP.mult))
                P.op("vector", lambda h: h.scalar_tensor_tensor(
                    t2, t0, 1.0, t1, op0=OP.mult, op1=OP.subtract))
                vvar = P.op("vector", lambda h: h.tensor_scalar(
                    t2, t2, 1e-5, None, op0=OP.add), inc=True)
                vr = self.rsqrt_chain((t0, t1, t3), t2, r_b, [(P.c["v"], vvar)])
                bmu = P.op("tensor", lambda h, m_=mu_b: h.matmul(
                    ps[2][:, 0:TB], self.ones_f[:], m_, start=True, stop=True),
                    wait=[(P.c["v"], vr)])
                brr = P.op("tensor", lambda h, r_=r_b: h.matmul(
                    ps[3][:, 0:TB], self.ones_f[:], r_, start=True, stop=True), inc=True)
                ap_d = None
                for c in range(NCH):
                    sh = 1 if c < 4 else 0
                    aw0 = []
                    if c == 0:
                        aw0 = [(P.c["t"], brr)]
                        if st_last.get("prev"):
                            aw0.append((ost[st_last["prevpr"]], st_last["prev"]))
                    P.op("vector", lambda h, c_=c, s_=sh: h.tensor_tensor(
                        ostg[:, 0:TB], xms[:, c_ * 513 + s_:c_ * 513 + s_ + TB], ps[2][:, 0:TB],
                        op=OP.subtract),
                        wait=aw0)
                    P.op("vector", lambda h: h.tensor_tensor(
                        ostg[:, 0:TB], ostg[:, 0:TB], ps[3][:, 0:TB], op=OP.mult))
                    ap_d = P.op("vector", lambda h, c_=c: h.tensor_scalar(
                        xln[:, c_ * TB:(c_ + 1) * TB], ostg[:, 0:TB],
                        gvec[:, c_:c_ + 1], gvec[:, NCH + c_:NCH + c_ + 1],
                        op0=OP.mult, op1=OP.add),
                        wait=[(onc, lv0)] if (w == 0 and c == 0) else (), inc=(c == NCH - 1))
                glu_at = {}
                sgl_at = {}
                glu_d = None
                for j in range(NJ):
                    jp = j % 2
                    pa_ = ps[4] if jp == 0 else ps[0]
                    pg_ = ps[5] if jp == 0 else ps[1]
                    tw = []
                    if j == 0:
                        tw = [(P.c["v"], ap_d)]
                        if w == 0:
                            tw.append((cw1, gw1))
                    elif j == 1:
                        tw = [(P.c["v"], vr)]
                    else:
                        tw = [(P.c["v"], glu_at[j - 2])]
                    for kc in range(NCH):
                        P.op("tensor", lambda h, j_=j, kc_=kc, p_=pa_: h.matmul(
                            p_[:, 0:512],
                            w1_sb[:, kc_ * 5632 + j_ * 128:kc_ * 5632 + (j_ + 1) * 128],
                            xln[:, kc_ * TB:(kc_ + 1) * TB],
                            start=(kc_ == 0), stop=(kc_ == NCH - 1)),
                            wait=(tw if kc == 0 else ()))
                    gmm = None
                    for kc in range(NCH):
                        gmm = P.op("tensor", lambda h, j_=j, kc_=kc, p_=pg_: h.matmul(
                            p_[:, 0:512],
                            w1_sb[:, kc_ * 5632 + 2816 + j_ * 128:kc_ * 5632 + 2816 + (j_ + 1) * 128],
                            xln[:, kc_ * TB:(kc_ + 1) * TB],
                            start=(kc_ == 0), stop=(kc_ == NCH - 1)),
                            wait=([(P.c["s"], sgl_at[j - 2])] if j >= 2 and kc == 0 else ()),
                            inc=(kc == NCH - 1))
                    sgl_at[j] = P.op("scalar", lambda h, j_=j, p_=pg_: h.activation(
                        gel[:], p_[:, 0:512], AF.Gelu,
                        bias=b1v[:, 22 + j_:22 + j_ + 1], scale=1.0),
                        wait=[(P.c["t"], gmm)] + ([(P.c["v"], glu_at[j - 1])] if j >= 1 else []),
                        inc=True)
                    glu_d = glu_at[j] = P.op("vector", lambda h, j_=j, p_=pa_: h.scalar_tensor_tensor(
                        hbuf[:, j_ * 512:(j_ + 1) * 512], p_[:, 0:512],
                        b1v[:, j_:j_ + 1], gel[:],
                        op0=OP.add, op1=OP.mult),
                        wait=[(P.c["s"], sgl_at[j])], inc=True)
                evl = {}
                for c in range(NCH):
                    cp = c % 2
                    pw2 = ps[6] if cp == 0 else ps[7]
                    osl = ostg[:, cp * TB:(cp + 1) * TB]
                    w2m = None
                    for j in range(NJ):
                        tw = []
                        if j == 0:
                            if c <= 1:
                                tw = [(P.c["v"], glu_d)]
                                if w == 0 and c == 0:
                                    tw.append((cw2, gw2))
                            else:
                                tw = [(P.c["v"], evl[c - 2])]
                        w2m = P.op("tensor", lambda h, c_=c, j_=j, p_=pw2: h.matmul(
                            p_[:, 0:512],
                            w2_sb[:, j_ * 1024 + c_ * 128:j_ * 1024 + (c_ + 1) * 128],
                            hbuf[:, j_ * 512:(j_ + 1) * 512],
                            start=(j_ == 0), stop=(j_ == NJ - 1)),
                            wait=tw, inc=(j == NJ - 1))
                    vw2 = [(P.c["t"], w2m)]
                    if st_last.get("prev"):
                        vw2.append((ost[st_last["prevpr"]], st_last["prev"]))
                    if (f"cbf{cp}") in st_last:
                        vw2.append((P.c["s"], st_last[f"cbf{cp}"]))
                    vv = P.op("vector", lambda h, c_=c, p_=pw2, o_=osl: h.tensor_scalar(
                        o_, p_[:, 0:512], gvec[:, 2 * NCH + c_:2 * NCH + c_ + 1],
                        None, op0=OP.add), wait=vw2)
                    mm_last = evl[c] = P.op("vector", lambda h, c_=c, o_=osl: h.tensor_tensor(
                        o_, o_, xms[:, c_ * 513 + 1:c_ * 513 + 1 + TB], op=OP.add),
                        inc=True)
                    st1 = P.dma("sync", ost[pr], fm_view(dst_f, w * TB, (w + 1) * TB)[:, c:c + 1, :],
                                osl.unsqueeze(1), wait=[(P.c["v"], mm_last)])
                    if dst_bf is not None:
                        cbf = P.op("scalar", lambda h, o_=osl: h.activation(
                            obf[:], o_, AF.Copy),
                            wait=[(P.c["v"], mm_last)], inc=True)
                        st1 = P.dma("sync", ost[pr],
                                    fm_view(dst_bf, w * TB, (w + 1) * TB)[:, c:c + 1, :],
                                    obf[:].unsqueeze(1), wait=[(P.c["s"], cbf)])
                        st_last[f"cbf{cp}"] = cbf
                    st_last["prev"] = st1
                    st_last["prevpr"] = pr
        return

    def phase_p4(self):
        nc, P, sc = self.nc, self.P, self.sc
        ps = self.ps
        with ExitStack() as ph:
            sb = lambda n, shp, dt: self.sbuf(n, shp, dt, ph)
            xb = [sb(f"p4_xb{i}", [128, NCH * TB], F32) for i in range(2)]
            ag = sb("p4_ag", [128, NCH * TB], F32)
            af = sb("p4_af", [128, NCH * TB], F32)
            xbb = sb("p4_xbb", [128, NCH * TB], BF16)
            xsq = sb("p4_xsq", [128, NCH * TB], BF16)
            xout = [sb(f"p4_xo{i}", [128, NCH * TB], BF16) for i in range(2)]
            xof = [sb(f"p4_xof{i}", [128, NCH * TB], F32) for i in range(2)]
            h2o = [sb(f"p4_h2{i}", [128, NCH * TB], BF16) for i in range(2)]
            tmp = sb("p4_tmp", [128, NCH * TB], F32)
            rows = sb("p4_rows", [1, 4 * TB], F32)
            gam = sb("p4_gam", [128, 2 * NCH], F32)
            ld = P.dpool("p4_ld", 2)
            sto = P.dpool("p4_st", 2)
            gst = P.dpool("p4_gst", 2)
            onc = P.ctr("p4_onc")
            gw = P.dma("sync", onc, gam[:, 0:NCH], self.inp["gamma"])
            gw = P.dma("sync", onc, gam[:, NCH:2 * NCH], self.inp["beta"])
            stt = self.st_tmp
            t0, t1, t2, t3 = (stt[0:1, k * TB:(k + 1) * TB] for k in range(4))
            mu1, r1 = rows[0:1, 0:TB], rows[0:1, TB:2 * TB]
            mu2, r2 = rows[0:1, 2 * TB:3 * TB], rows[0:1, 3 * TB:4 * TB]

            def bc3(psrc):
                return bass_rust.AP(psrc[:].tensor, 0, [[512, 128], [0, NCH], [1, TB]])

            def gcol(off):
                return bass_rust.AP(gam[:].tensor, off, [[2 * NCH, 128], [1, NCH], [0, TB]])
            x3 = lambda buf: buf[:].rearrange("p (c t) -> p c t", t=TB)
            (loads, casts, sums1, ap1_d, ap2_d, st_d, gst_d, vsum_d,
             s2d) = ({} for _ in range(9))
            for i in range(NT):
                pr = i % 2
                pA, pB = ps[2 * pr], ps[2 * pr + 1]
                wv = []
                if i >= 2:
                    wv = [(P.c["v"], ap2_d[i - 2]), (sto[pr], st_d[i - 2]),
                          (gst[pr], gst_d[i - 2])]
                loads[i] = P.dma("sync", ld[pr], x3(xb[pr]),
                                 fm_view(self.inp["xT"], i * TB, (i + 1) * TB), wait=wv)
                aw = [(P.c["v"], vsum_d[i - 1])] if i >= 1 else []
                loads[i] = P.dma("sync", ld[pr], x3(ag),
                                 fm_view(sc["glu"], i * TB, (i + 1) * TB), wait=aw)
                loads[i] = P.dma("sync", ld[pr], x3(af),
                                 fm_view(sc["ffno"], i * TB, (i + 1) * TB))
                P.op("vector", lambda h, p_=pr: h.tensor_tensor(
                    xb[p_][:], xb[p_][:], ag[:], op=OP.add),
                    wait=[(ld[pr], loads[i])])
                vsum_d[i] = P.op("vector", lambda h, p_=pr: h.tensor_tensor(
                    xb[p_][:], xb[p_][:], af[:], op=OP.add), inc=True)
                gst_d[i] = P.dma("gpsimd", gst[pr], fm_view(sc["x1"], i * TB, (i + 1) * TB),
                                 x3(xb[pr]), wait=[(P.c["v"], vsum_d[i])])
                casts[i] = P.op("scalar", lambda h, p_=pr: h.activation(
                    xbb[:], xb[p_][:], AF.Copy),
                    wait=[(P.c["v"], vsum_d[i])] + ([(P.c["t"], s2d[i - 1])] if i else []),
                    inc=True, drain=True)
                sq1 = P.op("scalar", lambda h: h.activation(
                    xsq[:], xbb[:], AF.Square), inc=True, drain=True)
                for c in range(NCH):
                    P.op("tensor", lambda h, c_=c, p_=pA: h.matmul(
                        p_[0:1, 0:TB], self.ones_col_b[:], xbb[:, c_ * TB:(c_ + 1) * TB],
                        start=(c_ == 0), stop=(c_ == NCH - 1)),
                        wait=[(P.c["s"], casts[i])] if c == 0 else ())
                for c in range(NCH):
                    sums1[i] = P.op("tensor", lambda h, c_=c, p_=pB: h.matmul(
                        p_[0:1, 0:TB], self.ones_col_b[:], xsq[:, c_ * TB:(c_ + 1) * TB],
                        start=(c_ == 0), stop=(c_ == NCH - 1)),
                        wait=[(P.c["s"], sq1)] if c == 0 else (), inc=(c == NCH - 1))
                P.op("vector", lambda h, p_=pA: h.tensor_scalar(
                    mu1, p_[0:1, 0:TB], 1.0 / DM, None, op0=OP.mult),
                    wait=[(P.c["t"], sums1[i])])
                P.op("vector", lambda h, p_=pB: h.tensor_scalar(
                    t0, p_[0:1, 0:TB], 1.0 / DM, None, op0=OP.mult))
                P.op("vector", lambda h: h.tensor_tensor(t1, mu1, mu1, op=OP.mult))
                P.op("vector", lambda h: h.scalar_tensor_tensor(
                    t2, t0, 1.0, t1, op0=OP.mult, op1=OP.subtract))
                vvar = P.op("vector", lambda h: h.tensor_scalar(
                    t2, t2, 1e-5, None, op0=OP.add), inc=True)
                vr1 = self.rsqrt_chain((t0, t1, t3), t2, r1, [(P.c["v"], vvar)])
                P.op("tensor", lambda h: h.matmul(
                    ps[4][:, 0:TB], self.ones_f[:], mu1, start=True, stop=True),
                    wait=[(P.c["v"], vr1)])
                bc1 = P.op("tensor", lambda h: h.matmul(
                    ps[5][:, 0:TB], self.ones_f[:], r1, start=True, stop=True), inc=True)
                P.op("vector", lambda h, p_=pr: h.tensor_tensor(
                    tmp[:].rearrange("p (c t) -> p c t", t=TB), x3(xb[p_]), bc3(ps[4]),
                    op=OP.subtract), wait=[(P.c["t"], bc1)])
                apf = P.op("vector", lambda h, p_=pr: h.tensor_tensor(
                    xof[p_][:].rearrange("p (c t) -> p c t", t=TB),
                    tmp[:].rearrange("p (c t) -> p c t", t=TB), bc3(ps[5]),
                    op=OP.mult), inc=True)
                ap1_d[i] = P.op("scalar", lambda h, p_=pr: h.activation(
                    xout[p_][:], xof[p_][:], AF.Copy),
                    wait=[(P.c["v"], apf)], inc=True, drain=True)
                st_d[i] = P.dma("sync", sto[pr], fm_view(sc["xn2f"], i * TB, (i + 1) * TB),
                                x3(xof[pr]), wait=[(P.c["v"], apf)])
                gst_d[i] = P.dma("gpsimd", gst[pr], fm_view(sc["xn2"], i * TB, (i + 1) * TB),
                                 x3(xout[pr]), wait=[(P.c["s"], ap1_d[i])])
                sq2 = P.op("scalar", lambda h, p_=pr: h.activation(
                    xsq[:], xout[p_][:], AF.Square),
                    wait=[(P.c["t"], sums1[i])], inc=True, drain=True)
                for c in range(NCH):
                    P.op("tensor", lambda h, c_=c, p_=pA, pp_=pr: h.matmul(
                        p_[0:1, 0:TB], self.ones_col_b[:], xout[pp_][:, c_ * TB:(c_ + 1) * TB],
                        start=(c_ == 0), stop=(c_ == NCH - 1)),
                        wait=[(P.c["v"], vr1), (P.c["s"], ap1_d[i])] if c == 0 else ())
                sums2 = None
                for c in range(NCH):
                    sums2 = P.op("tensor", lambda h, c_=c, p_=pB: h.matmul(
                        p_[0:1, 0:TB], self.ones_col_b[:], xsq[:, c_ * TB:(c_ + 1) * TB],
                        start=(c_ == 0), stop=(c_ == NCH - 1)),
                        wait=[(P.c["s"], sq2)] if c == 0 else (), inc=(c == NCH - 1))
                s2d[i] = sums2
                P.op("vector", lambda h, p_=pA: h.tensor_scalar(
                    mu2, p_[0:1, 0:TB], 1.0 / DM, None, op0=OP.mult),
                    wait=[(P.c["t"], sums2)])
                P.op("vector", lambda h, p_=pB: h.tensor_scalar(
                    t0, p_[0:1, 0:TB], 1.0 / DM, None, op0=OP.mult))
                P.op("vector", lambda h: h.tensor_tensor(t1, mu2, mu2, op=OP.mult))
                P.op("vector", lambda h: h.scalar_tensor_tensor(
                    t2, t0, 1.0, t1, op0=OP.mult, op1=OP.subtract))
                vvar2 = P.op("vector", lambda h: h.tensor_scalar(
                    t2, t2, 1e-5, None, op0=OP.add), inc=True)
                vr2 = self.rsqrt_chain((t0, t1, t3), t2, r2, [(P.c["v"], vvar2)])
                P.op("tensor", lambda h: h.matmul(
                    ps[6][:, 0:TB], self.ones_f[:], mu2, start=True, stop=True),
                    wait=[(P.c["v"], vr2)])
                bc2 = P.op("tensor", lambda h: h.matmul(
                    ps[7][:, 0:TB], self.ones_f[:], r2, start=True, stop=True), inc=True)
                P.op("vector", lambda h, p_=pr: h.tensor_tensor(
                    tmp[:].rearrange("p (c t) -> p c t", t=TB), x3(xof[p_]), bc3(ps[6]),
                    op=OP.subtract),
                    wait=[(P.c["t"], bc2)] + ([(onc, gw)] if i == 0 else []))
                P.op("vector", lambda h: h.tensor_tensor(
                    tmp[:].rearrange("p (c t) -> p c t", t=TB),
                    tmp[:].rearrange("p (c t) -> p c t", t=TB), bc3(ps[7]), op=OP.mult))
                P.op("vector", lambda h: h.tensor_tensor(
                    tmp[:].rearrange("p (c t) -> p c t", t=TB),
                    tmp[:].rearrange("p (c t) -> p c t", t=TB), gcol(0), op=OP.mult))
                ap2_d[i] = P.op("vector", lambda h, p_=pr: h.tensor_tensor(
                    h2o[p_][:].rearrange("p (c t) -> p c t", t=TB),
                    tmp[:].rearrange("p (c t) -> p c t", t=TB), gcol(NCH), op=OP.add),
                    inc=True)
                st_d[i] = P.dma("sync", sto[pr], fm_view(sc["h2"], i * TB, (i + 1) * TB),
                                x3(h2o[pr]), wait=[(P.c["v"], ap2_d[i])])

    def phase_l1_attn(self):
        nc, P, sc = self.nc, self.P, self.sc
        ps = self.ps
        with ExitStack() as ph:
            sb = lambda n, shp, dt: self.sbuf(n, shp, dt, ph)
            Wk_sb = sb("a1_wk", [128, 8 * 1024], BF16)
            Wv_sb = sb("a1_wv", [128, 8 * 1024], BF16)
            Wka_sb = sb("a1_wka", [128, 8 * 1024], BF16)
            bcol = sb("a1_bcol", [128, 3 * NCH], F32)
            h2b = [sb(f"a1_h2{j}", [128, NCH * TB], BF16) for j in range(2)]
            kf32 = sb("a1_kf", [128, NCH * TB], F32)
            kbf = sb("a1_kbf", [128, NCH * TB], BF16)
            vbuf = sb("a1_vb", [128, NCH * TB], F32)
            kwb = sb("a1_kw", [128, NCH * TB], F32)
            cwb = sb("a1_cw", [128, NCH * TB], F32)
            pkb = sb("a1_pk", [128, NCH * TB], F32)
            ones5 = sb("a1_ones", [128, TB], F32)
            carry = sb("a1_carry", [128, 2 * NCH], F32)
            ostage = sb("a1_ost", [128, NCH * TB], F32)
            ld = P.dpool("a1_ld", 2)
            ost = P.dpool("a1_ost", 2)
            onc = P.ctr("a1_onc")
            g1 = P.ctr("a1_g1")
            for nm, dst in (("WkT", Wk_sb), ("WvT", Wv_sb), ("WkaT", Wka_sb)):
                wv1 = P.dma("gpsimd", g1, dst[:].rearrange("p (a b) -> p a b", b=1024),
                            self.inp[nm].rearrange("(a p) c -> p a c", p=128))
            for r in range(3):
                bv0 = P.dma("sync", onc, bcol[:, r * NCH:(r + 1) * NCH],
                            self.inp["bkv"][r])
            vb0 = P.op("vector", lambda h: h.tensor_scalar(
                bcol[:, NCH:2 * NCH], bcol[:, NCH:2 * NCH], 1.0 / 32.0, None,
                op0=OP.mult), wait=[(onc, bv0)], inc=True)
            P.op("gpsimd", lambda h: h.memset(ones5[:], 1.0), inc=True)
            gz = P.op("gpsimd", lambda h: h.memset(carry[:], 0.0), inc=True)
            ev_k = ev_v = ev_a = 0
            mmd = vlast = cast_last = 0
            tA, stored = {}, {}
            for i in range(NT):
                pr = i % 2
                wvl = [(P.c["t"], tA[i - 2])] if i >= 2 else []
                lv = P.dma("sync", ld[pr], sb3(h2b[pr]),
                           fm_view(sc["h2"], i * TB, (i + 1) * TB), wait=wvl)
                for c in range(NCH):
                    pk_ = ps[c % 2]
                    tw = []
                    if c == 0:
                        tw = [(ld[pr], lv)]
                        if i == 0:
                            tw += [(g1, wv1), (P.c["v"], vb0)]
                    if c >= 2 or i > 0:
                        tw.append((P.c["v"], ev_k))
                    for kc in range(NCH):
                        mmd = P.op("tensor", lambda h, c_=c, kc_=kc, p_=pr, s_=pk_: h.matmul(
                            s_[:, 0:512],
                            Wk_sb[:, kc_ * 1024 + c_ * 128:kc_ * 1024 + (c_ + 1) * 128],
                            h2b[p_][:, kc_ * TB:(kc_ + 1) * TB],
                            start=(kc_ == 0), stop=(kc_ == NCH - 1)),
                            wait=(tw if kc == 0 else ()), inc=(kc == NCH - 1))
                    ev_k = P.op("vector", lambda h, c_=c, s_=pk_: h.tensor_scalar(
                        kf32[:, c_ * TB:(c_ + 1) * TB], s_[:, 0:512],
                        bcol[:, c_:c_ + 1], None, op0=OP.add),
                        wait=[(P.c["t"], mmd)], inc=True)
                    cast_last = P.op("scalar", lambda h, c_=c: h.activation(
                        kbf[:, c_ * TB:(c_ + 1) * TB], kf32[:, c_ * TB:(c_ + 1) * TB],
                        AF.Copy), wait=[(P.c["v"], ev_k)], inc=True)
                for c in range(NCH):
                    pv_ = ps[2 + c % 2]
                    tw = []
                    if c >= 2 or i > 0:
                        tw.append((P.c["v"], ev_v))
                    for kc in range(NCH):
                        mmd = P.op("tensor", lambda h, c_=c, kc_=kc, p_=pr, s_=pv_: h.matmul(
                            s_[:, 0:512],
                            Wv_sb[:, kc_ * 1024 + c_ * 128:kc_ * 1024 + (c_ + 1) * 128],
                            h2b[p_][:, kc_ * TB:(kc_ + 1) * TB],
                            start=(kc_ == 0), stop=(kc_ == NCH - 1)),
                            wait=(tw if kc == 0 else ()), inc=(kc == NCH - 1))
                    ev_v = P.op("vector", lambda h, c_=c, s_=pv_: h.tensor_scalar(
                        vbuf[:, c_ * TB:(c_ + 1) * TB], s_[:, 0:512],
                        bcol[:, 2 * NCH + c_:2 * NCH + c_ + 1], None, op0=OP.add),
                        wait=[(P.c["t"], mmd)], inc=True)
                for c in range(NCH):
                    pa_ = ps[4 + c % 2]
                    tw = []
                    if c == 0:
                        tw = [(P.c["s"], cast_last)]
                    if c >= 2 or i > 0:
                        tw.append((P.c["s"], ev_a))
                    for kc in range(NCH):
                        mmd = P.op("tensor", lambda h, c_=c, kc_=kc, s_=pa_: h.matmul(
                            s_[:, 0:512],
                            Wka_sb[:, kc_ * 1024 + c_ * 128:kc_ * 1024 + (c_ + 1) * 128],
                            kbf[:, kc_ * TB:(kc_ + 1) * TB],
                            start=(kc_ == 0), stop=(kc_ == NCH - 1)),
                            wait=(tw if kc == 0 else ()), inc=(kc == NCH - 1))
                    ev_a = P.op("scalar", lambda h, c_=c, s_=pa_: h.activation(
                        kwb[:, c_ * TB:(c_ + 1) * TB], s_[:, 0:512], AF.Exp,
                        bias=bcol[:, NCH + c_:NCH + c_ + 1], scale=1.0 / 32.0),
                        wait=[(P.c["t"], mmd), (P.c["v"], vlast)], inc=True)
                tA[i] = mmd
                ew = [(P.c["s"], ev_a)]
                if i == 0:
                    ew.append((P.c["g"], gz))
                if i >= 1:
                    ew.append((ost[(i - 1) % 2], stored[i - 1]))
                for c in range(NCH):
                    P.op("vector", lambda h, c_=c: h.tensor_tensor_scan(
                        cwb[:, c_ * TB:(c_ + 1) * TB], ones5[:],
                        kwb[:, c_ * TB:(c_ + 1) * TB],
                        carry[:, c_:c_ + 1], op0=OP.mult, op1=OP.add),
                        wait=(ew if c == 0 else ()), drain=(c == NCH - 1))
                for c in range(NCH):
                    P.op("vector", lambda h, c_=c: h.tensor_copy(
                        carry[:, c_:c_ + 1], cwb[:, c_ * TB + TB - 1:c_ * TB + TB]),
                        drain=False)
                P.op("vector", lambda h: h.reciprocal(pkb[:], cwb[:]), drain=True)
                P.op("vector", lambda h: h.tensor_tensor(pkb[:], pkb[:], kwb[:], op=OP.mult),
                     drain=True)
                P.op("vector", lambda h: h.tensor_tensor(pkb[:], pkb[:], kf32[:], op=OP.mult),
                     drain=True)
                for c in range(NCH):
                    P.op("vector", lambda h, c_=c: h.tensor_tensor_scan(
                        cwb[:, c_ * TB:(c_ + 1) * TB], ones5[:],
                        pkb[:, c_ * TB:(c_ + 1) * TB],
                        carry[:, NCH + c_:NCH + c_ + 1], op0=OP.mult, op1=OP.add),
                        drain=(c == NCH - 1))
                for c in range(NCH):
                    P.op("vector", lambda h, c_=c: h.tensor_copy(
                        carry[:, NCH + c_:NCH + c_ + 1],
                        cwb[:, c_ * TB + TB - 1:c_ * TB + TB]), drain=False)
                vlast = P.op("vector", lambda h: h.tensor_tensor(
                    ostage[:], cwb[:], vbuf[:], op=OP.mult), inc=True)
                stored[i] = P.dma("sync", ost[pr], fm_view(sc["acc1"], i * TB, (i + 1) * TB),
                                  sb3(ostage), wait=[(P.c["v"], vlast)])

    def phase_l1_ffn(self):
        nc, P, sc = self.nc, self.P, self.sc
        ps = self.ps
        with ExitStack() as ph:
            sb = lambda n, shp, dt: self.sbuf(n, shp, dt, ph)
            W1s = sb("e_w1s", [128, 8 * 2048], BF16)
            W2s = sb("e_w2s", [128, 16 * 1024], BF16)
            Wo_sb = sb("e_wo", [128, 8 * 1024], BF16)
            wtmp = [sb(f"e_wt{j}", [128, 1024], BF16) for j in range(4)]
            wftmp = [sb(f"e_wtf{j}", [128, 1024], F32) for j in range(2)]
            Ies = sb("e_ies", [128, 8 * 128], BF16)
            lcol = sb("e_lcol", [128, 8], F32)
            Wp1f = sb("e_wp1", [128, 8 * 128], F32)
            Wp2f = sb("e_wp2", [128, 8], F32)
            g1c = sb("e_g1", [128, 1], F32)
            bp1c = sb("e_bp1", [128, 1], F32)
            rows = sb("e_rows", [1, 32], F32)
            lgc = sb("e_lgc", [8, 2], F32)
            ohf = sb("e_ohf", [8, 1], F32)
            OH = sb("e_OH", [128, 8], F32)

            brow = sb("e_brow", [1, 3072], F32)
            b1sc = sb("e_b1s", [128, 16], F32)
            b2sc = sb("e_b2s", [128, 8], F32)
            boc = sb("e_bo", [128, NCH], F32)
            xnb = [sb("e_xnb0", [128, NCH * TB], BF16)] * 2
            rt = [sb(f"e_rt{j}", [128, 3 * TB], F32) for j in range(4)]
            hb = sb("e_hb", [128, 16 * TB], BF16)
            be1f = hb[:].bitcast(F32)[0:8, 0:2048]
            be2f = hb[:].bitcast(F32)[0:8, 2048:3072]
            rbuf = sb("e_rbuf", [128, NCH * TB], BF16)
            otile = [sb(f"e_ot{j}", [128, TB], F32) for j in range(2)]
            ld = P.dpool("e_ld", 2)
            wld = P.dpool("e_wld", 2)
            rld = P.dpool("e_rld", 4)
            onc = P.ctr("e_onc")
            g1 = P.ctr("e_g1c")
            fino = P.dpool("e_fino", 2)
            smv = sc["small"]
            d0 = P.dma("sync", onc, lcol[:].unsqueeze(2),
                       fm_view(sc["xn2f"], S - 1, S), slow_ok=True)
            d0 = P.dma("sync", onc, bp1c[:], self.inp["bp1"])
            d0 = P.dma("sync", onc, boc[:], self.inp["bo1"])
            d0 = P.dma("sync", onc, be1f, self.inp["be1"])
            d0 = P.dma("sync", onc, be2f, self.inp["be2"])
            d0 = P.dma("sync", onc, lgc[0:8, 1:2], self.inp["bp2"].unsqueeze(1),
                       slow_ok=True)
            d0 = P.dma("sync", onc, Wp1f[:].rearrange("p (a b) -> p a b", b=128),
                       self.inp["Wp1"].rearrange("(a p) c -> p a c", p=128))
            d0 = P.dma("sync", onc, Wp2f[:], self.inp["Wp2"])
            gw2 = P.dma("gpsimd", g1, Wo_sb[:].rearrange("p (a b) -> p a b", b=1024),
                        self.inp["WoT1"].rearrange("(a p) c -> p a c", p=128))
            for kc in range(NCH):
                rmm = P.op("tensor", lambda h, kc_=kc: h.matmul(
                    ps[0][:, 0:1], Wp1f[:, kc_ * 128:(kc_ + 1) * 128],
                    lcol[:, kc_:kc_ + 1],
                    start=(kc_ == 0), stop=(kc_ == NCH - 1)),
                    wait=([(onc, d0)] if kc == 0 else ()), inc=(kc == NCH - 1))
            sg1 = P.op("scalar", lambda h: h.activation(
                g1c[:], ps[0][:, 0:1], AF.Gelu, bias=bp1c[:, 0:1], scale=1.0),
                wait=[(P.c["t"], rmm)], inc=True)
            rmm = P.op("tensor", lambda h: h.matmul(
                ps[1][0:8, 0:1], Wp2f[:], g1c[:], start=True, stop=True),
                wait=[(P.c["s"], sg1)], inc=True)
            vl = P.op("vector", lambda h: h.tensor_tensor(
                lgc[0:8, 0:1], ps[1][0:8, 0:1], lgc[0:8, 1:2], op=OP.add),
                wait=[(P.c["t"], rmm)], inc=True)
            tt1 = P.op("tensor", lambda h: h.transpose(
                ps[2][0:1, 0:8], lgc[0:8, 0:1], self.ident_f[0:8, 0:8]),
                wait=[(P.c["v"], vl)], inc=True)
            vl = P.op("vector", lambda h: h.tensor_copy(rows[0:1, 0:8], ps[2][0:1, 0:8]),
                      wait=[(P.c["t"], tt1)], inc=True)
            vl = P.op("vector", lambda h: h.tensor_reduce(
                rows[0:1, 8:9], rows[0:1, 0:8], axis=AX.X, op=OP.max), inc=True)
            vl = P.op("vector", lambda h: h.tensor_scalar(
                rows[0:1, 16:24], rows[0:1, 0:8], rows[0:1, 8:9], None,
                op0=OP.is_ge), inc=True)
            ohv = self.bcast(OH[:], rows[0:1, 16:24], 8, ps[3], wait=[(P.c["v"], vl)])
            tt2 = P.op("tensor", lambda h: h.transpose(
                ps[5][0:8, 0:1], rows[0:1, 16:24], self.ident_f[0:1, 0:1]),
                wait=[(P.c["v"], vl)], inc=True)
            vc2 = P.op("vector", lambda h: h.tensor_copy(ohf[0:8, 0:1], ps[5][0:8, 0:1]),
                       wait=[(P.c["t"], tt2)], inc=True)
            vbe = vc2
            for src, off, nnb in ((be1f, 0, 4), (be2f, 2048, 2)):
                for nb in range(nnb):
                    bm = P.op("tensor", lambda h, s_=src, nb_=nb: h.matmul(
                        ps[4][0:1, 0:512], ohf[0:8, 0:1],
                        s_[0:8, nb_ * 512:(nb_ + 1) * 512], start=True, stop=True),
                        wait=[(P.c["v"], vbe)], inc=True)
                    vbe = P.op("vector", lambda h, o_=off, nb_=nb: h.tensor_copy(
                        brow[0:1, o_ + nb_ * 512:o_ + (nb_ + 1) * 512], ps[4][0:1, 0:512]),
                        wait=[(P.c["t"], bm)], inc=True)
            d3 = P.dma("sync", onc, bass_rust.AP(smv.tensor, 256, [[1, 1], [1, 3072]]),
                       brow[0:1, 0:3072], wait=[(P.c["v"], vbe)])
            d4 = P.dma("sync", onc, b1sc[:],
                       bass_rust.AP(smv.tensor, 256, [[1, 128], [128, 16]]),
                       wait=[(onc, d3)], slow_ok=True)
            d4 = P.dma("sync", onc, b2sc[:],
                       bass_rust.AP(smv.tensor, 256 + 2048, [[1, 128], [128, 8]]),
                       slow_ok=True)
            vie = 0
            for e in range(8):
                vie = P.op("vector", lambda h, e_=e: h.tensor_scalar(
                    Ies[:, e_ * 128:(e_ + 1) * 128], self.ident_b[:], OH[:, e_:e_ + 1],
                    None, op0=OP.mult),
                    wait=([(P.c["v"], ohv)] if e == 0 else ()), inc=True)
            wmv, evv = {}, {}
            gidx = gtile = 0
            w1fin = wfin = 0
            sld = P.dpool("e_sld", 2)
            cast_at = {}
            for wi, (Wsrc, Wdst, nkc, ncols) in enumerate(
                    ((self.inp["We1"], W1s, 8, 2048), (self.inp["We2"], W2s, 16, 1024))):
                for kc in range(nkc):
                    for nb in range(ncols // 1024):
                        col0 = kc * ncols + nb * 1024
                        for e in range(8):
                            j = gidx % 4
                            src_ap = (Wsrc[e].rearrange("(a p) c -> p a c", p=128)
                                      [:, kc, nb * 1024:(nb + 1) * 1024])
                            dw = [(P.c["t"], wmv[gidx - 4])] if gidx >= 4 else []
                            if e % 2 == 0:
                                jg = (gidx // 2) % 2
                                dwe = P.dma("gpsimd", wld[jg], wtmp[j][:], src_ap,
                                            wait=dw)
                                mw = [(wld[jg], dwe)]
                            else:
                                j2 = (gidx // 2) % 2
                                fdw = list(dw)
                                if gidx >= 4:
                                    fdw.append((P.c["s"], cast_at[gidx - 4]))
                                dwe = P.dma("sync", sld[j2], wftmp[j2][:], src_ap,
                                            wait=fdw)
                                cv = P.op("scalar", lambda h, j_=j, j2_=j2: h.activation(
                                    wtmp[j_][:], wftmp[j2_][:], AF.Copy),
                                    wait=[(sld[j2], dwe)], inc=True)
                                cast_at[gidx] = cv
                                mw = [(P.c["s"], cv)]
                            if e == 0:
                                if gtile >= 1:
                                    mw.append((P.c["v"], evv[gtile - 1]))
                                if gtile == 0:
                                    mw.append((P.c["v"], vie))
                            P.op("tensor", lambda h, e_=e, j_=j: h.matmul(
                                ps[6][:, 0:512], Ies[:, e_ * 128:(e_ + 1) * 128],
                                wtmp[j_][:, 0:512],
                                start=(e_ == 0), stop=(e_ == 7)), wait=mw)
                            wmv[gidx] = P.op("tensor", lambda h, e_=e, j_=j: h.matmul(
                                ps[7][:, 0:512], Ies[:, e_ * 128:(e_ + 1) * 128],
                                wtmp[j_][:, 512:1024],
                                start=(e_ == 0), stop=(e_ == 7)), inc=True)
                            gidx += 1
                        P.op("vector", lambda h, d_=Wdst, c_=col0: h.tensor_copy(
                            d_[:, c_:c_ + 512], ps[6][:, 0:512]),
                            wait=[(P.c["t"], wmv[gidx - 1])], drain=False)
                        evv[gtile] = P.op("vector", lambda h, d_=Wdst, c_=col0: h.tensor_copy(
                            d_[:, c_ + 512:c_ + 1024], ps[7][:, 0:512]), inc=True,
                            drain=False)
                        gtile += 1
                if wi == 0:
                    w1fin = evv[gtile - 1]
            wfin = evv[gtile - 1]
            tB, stv, evo = {}, {}, {}
            ev1 = ev2 = 0

            def rt_load(g_, wv_):
                j_ = g_ % 4
                i_, c_ = divmod(g_, NCH)
                r = 0
                for k_, src_ in enumerate((sc["x1"], sc["acc1"], sc["xn2f"])):
                    r = P.dma("gpsimd", rld[j_],
                              rt[j_][:, k_ * TB:(k_ + 1) * TB].unsqueeze(1),
                              fm_view(src_, i_ * TB, (i_ + 1) * TB)[:, c_:c_ + 1, :],
                              wait=(wv_ if k_ == 0 else ()))
                stv[g_] = r
            for g in range(4):
                rt_load(g, [])
            for i in range(NT):
                pr = i % 2
                lw = [(P.c["t"], tB[i - 1])] if i >= 1 else []
                lx = P.dma("sync", ld[pr], sb3(xnb[pr]),
                           fm_view(sc["xn2"], i * TB, (i + 1) * TB), wait=lw)
                for c in range(16):
                    pH = ps[c % 2]
                    tw = []
                    if c == 0:
                        tw = [(ld[pr], lx)]
                        if i == 0:
                            tw.append((P.c["v"], w1fin))
                    if c >= 2 or i > 0:
                        tw.append((P.c["s"], ev1))
                    for kc in range(NCH):
                        mm = P.op("tensor", lambda h, c_=c, kc_=kc, p_=pr, s_=pH: h.matmul(
                            s_[:, 0:512],
                            W1s[:, kc_ * 2048 + c_ * 128:kc_ * 2048 + (c_ + 1) * 128],
                            xnb[p_][:, kc_ * TB:(kc_ + 1) * TB],
                            start=(kc_ == 0), stop=(kc_ == NCH - 1)),
                            wait=(tw if kc == 0 else ()), inc=(kc == NCH - 1))
                    ev1 = P.op("scalar", lambda h, c_=c, s_=pH: h.activation(
                        hb[:, c_ * TB:(c_ + 1) * TB], s_[:, 0:512], AF.Gelu,
                        bias=b1sc[:, c_:c_ + 1], scale=1.0),
                        wait=[(P.c["t"], mm)] + ([(onc, d4)] if (i == 0 and c == 0) else []),
                        inc=True)
                for c in range(NCH):
                    pR = ps[2 + c % 2]
                    tw = []
                    if c == 0:
                        tw = [(P.c["s"], ev1)]
                        if i == 0:
                            tw.append((P.c["v"], wfin))
                    if c >= 2 or i > 0:
                        tw.append((P.c["v"], ev2))
                    for kc in range(16):
                        mm = P.op("tensor", lambda h, c_=c, kc_=kc, s_=pR: h.matmul(
                            s_[:, 0:512],
                            W2s[:, kc_ * 1024 + c_ * 128:kc_ * 1024 + (c_ + 1) * 128],
                            hb[:, kc_ * TB:(kc_ + 1) * TB],
                            start=(kc_ == 0), stop=(kc_ == 15)),
                            wait=(tw if kc == 0 else ()), inc=(kc == 15))
                    ev2 = P.op("vector", lambda h, c_=c, s_=pR: h.tensor_scalar(
                        rbuf[:, c_ * TB:(c_ + 1) * TB], s_[:, 0:512],
                        b2sc[:, c_:c_ + 1], None, op0=OP.add),
                        wait=[(P.c["t"], mm)], inc=True)
                for c in range(NCH):
                    g = i * NCH + c
                    j = g % 4
                    jo = g % 2
                    pW = ps[4 + c % 2]
                    tw = []
                    if c == 0:
                        tw = [(P.c["v"], ev2)]
                        if i == 0:
                            tw.append((g1, gw2))
                    if c >= 2 or i > 0:
                        tw.append((P.c["v"], evo[g - 2]))
                    for kc in range(NCH):
                        mm = P.op("tensor", lambda h, c_=c, kc_=kc, s_=pW: h.matmul(
                            s_[:, 0:512],
                            Wo_sb[:, kc_ * 1024 + c_ * 128:kc_ * 1024 + (c_ + 1) * 128],
                            rbuf[:, kc_ * TB:(kc_ + 1) * TB],
                            start=(kc_ == 0), stop=(kc_ == NCH - 1)),
                            wait=(tw if kc == 0 else ()), inc=(kc == NCH - 1))
                    ow = [(P.c["t"], mm), (rld[j], stv[g])]
                    if g >= 2:
                        ow.append((fino[jo], stv[("o", g - 2)]))
                    P.op("vector", lambda h, s_=pW, c_=c, o_=otile[jo]: h.tensor_scalar(
                        o_[:], s_[:, 0:512], boc[:, c_:c_ + 1], 0.3,
                        op0=OP.add, op1=OP.mult), wait=ow)
                    for k in range(3):
                        evo[g] = P.op("vector", lambda h, j_=j, k_=k, o_=otile[jo]:
                                      h.tensor_tensor(o_[:], o_[:],
                                                      rt[j_][:, k_ * TB:(k_ + 1) * TB],
                                                      op=OP.add), inc=(k == 2))
                    stv[("o", g)] = P.dma("sync", fino[jo],
                                          fm_view(self.out, i * TB, (i + 1) * TB)[:, c:c + 1, :],
                                          otile[jo][:].unsqueeze(1),
                                          wait=[(P.c["v"], evo[g])])
                    if g + 4 < NT * NCH:
                        rt_load(g + 4, [(P.c["v"], evo[g])])
                tB[i] = mm

    def phase_gate(self):
        nc, P, sc = self.nc, self.P, self.sc
        ps = self.ps
        with ExitStack() as ph:
            sb = lambda n, shp, dt: self.sbuf(n, shp, dt, ph)
            wg = sb("wg_sb", [128, 8 * 2048], BF16)
            bg = sb("bg_sb", [128, 16], F32)
            atb = [sb(f"g_at{i}", [128, NCH * TB], BF16) for i in range(2)]
            sig = sb("g_sig", [128, TB], F32)
            glu = [sb(f"g_glu{i}", [128, NCH * TB], F32) for i in range(2)]
            ld = P.dpool("g_ld", 2)
            sto = P.dpool("g_st", 2)
            onc = P.ctr("g_onc")
            gonc = P.ctr("g_gonc")
            wv0 = P.dma("gpsimd", gonc, wg[:].rearrange("p (kc n) -> p kc n", n=2048),
                        self.inp["w_gate"].rearrange("(kc p) n -> p kc n", p=128))
            wv = P.dma("sync", onc, bg[:], self.inp["b_gate"])
            mmdone, gdone, stores = {}, {}, {}
            for i in range(NT):
                pr = i % 2
                wvl = []
                if i >= 2:
                    wvl = [(P.c["t"], mmdone[i - 2]), (sto[pr], stores[i - 2])]
                lv = P.dma("sync", ld[pr], sb3(atb[pr]),
                           fm_view(sc["attn"], i * TB, (i + 1) * TB), wait=wvl)
                for j in range(NCH):
                    psa, psg = (ps[0], ps[1]) if j % 2 == 0 else (ps[2], ps[3])
                    tw = [(ld[pr], lv), (onc, wv), (gonc, wv0)] if j == 0 else []
                    if i + j > 0:
                        tw.append((P.c["v"], gdone.get((i, j - 2), gdone.get((i - 1, j + 6), 0))))
                    for kc in range(NCH):
                        P.op("tensor", lambda h, j_=j, kc_=kc, p_=pr, ps_=psa: h.matmul(
                            ps_[:, 0:512], wg[:, kc_ * 2048 + j_ * 128:kc_ * 2048 + (j_ + 1) * 128],
                            atb[p_][:, kc_ * TB:(kc_ + 1) * TB],
                            start=(kc_ == 0), stop=(kc_ == NCH - 1)),
                            wait=(tw if kc == 0 else ()))
                    for kc in range(NCH):
                        mmdone[i] = P.op("tensor", lambda h, j_=j, kc_=kc, p_=pr, ps_=psg: h.matmul(
                            ps_[:, 0:512], wg[:, kc_ * 2048 + 1024 + j_ * 128:kc_ * 2048 + 1024 + (j_ + 1) * 128],
                            atb[p_][:, kc_ * TB:(kc_ + 1) * TB],
                            start=(kc_ == 0), stop=(kc_ == NCH - 1)), inc=(kc == NCH - 1))
                    sgw = [(P.c["t"], mmdone[i])]
                    prevg = gdone.get((i, j - 1), gdone.get((i - 1, NCH - 1), 0))
                    sgw.append((P.c["v"], prevg))
                    if i == 0 and j == 0:
                        sgw.append((onc, wv))
                    sg = P.op("scalar", lambda h, j_=j, ps_=psg: h.activation(
                        sig[:], ps_[:, 0:512], AF.Sigmoid, bias=bg[:, NCH + j_:NCH + j_ + 1], scale=1.0),
                        wait=sgw, inc=True)
                    gdone[(i, j)] = P.op("vector", lambda h, j_=j, p_=pr, ps_=psa: h.scalar_tensor_tensor(
                        glu[p_][:, j_ * TB:(j_ + 1) * TB], ps_[:, 0:512], bg[:, j_:j_ + 1],
                        sig[:], op0=OP.add, op1=OP.mult),
                        wait=[(P.c["s"], sg)], inc=True)
                stores[i] = P.dma("sync", sto[pr], fm_view(sc["glu"], i * TB, (i + 1) * TB),
                                  sb3(glu[pr]), wait=[(P.c["v"], gdone[(i, NCH - 1)])])


def build_program(taps=(), stop_after=None):
    b = Build(taps=taps)
    if stop_after == "p0":
        b.stop_after_p0 = True
    nc = b.build()
    return nc, {}


def prep_inputs(inputs):
    x = np.asarray(inputs["L0_attn_x"], np.float32)
    w1 = np.asarray(inputs["L0_ffn_w1"], np.float32)
    b1 = np.asarray(inputs["L0_ffn_b1"], np.float32)
    w2 = np.asarray(inputs["L0_ffn_w2"], np.float32)
    w1p = np.zeros((2, DM, 5632), np.float32)
    w1p[:, :, 0:2730] = w1[:, :, 0:2730]
    w1p[:, :, 2816:2816 + 2730] = w1[:, :, 2730:5460]
    b1p = np.zeros((2, 5632), np.float32)
    b1p[:, 0:2730] = b1[:, 0:2730]
    b1p[:, 2816:2816 + 2730] = b1[:, 2730:5460]
    w2p = np.zeros((2, 2816, DM), np.float32)
    w2p[:, 0:2730, :] = w2
    tisa = np.stack([np.asarray(inputs["L0_attn_amplitudes"], np.float32),
                     np.abs(np.asarray(inputs["L0_attn_sharpness"], np.float32)),
                     np.asarray(inputs["L0_attn_offsets"], np.float32)])
    qk_scale = np.stack([np.asarray(inputs["L0_ffn_q_scale"], np.float32),
                         np.asarray(inputs["L0_ffn_k_scale"], np.float32)], axis=1)
    cf = lambda k: np.ascontiguousarray(np.asarray(inputs[k], np.float32))

    def pcol(v):
        v = np.asarray(v, np.float32)
        return np.ascontiguousarray(v.reshape(*v.shape[:-1], -1, 128).swapaxes(-1, -2))
    common = {
        "w_in": cf("L0_attn_w_in"), "w_gate": cf("L0_attn_w_gate"),
        "b_gate": pcol(inputs["L0_attn_b_gate"]), "tisa": tisa,
        "wqkv": cf("L0_ffn_wqkv"), "qk_scale": qk_scale, "wo0": cf("L0_ffn_wo"),
        "ln_g": pcol(inputs["L0_ffn_ln_g"]), "ln_b": pcol(inputs["L0_ffn_ln_b"]),
        "w1f": w1p, "b1f": pcol(b1p), "w2f": w2p, "b2f": pcol(inputs["L0_ffn_b2"]),
        "WkT": np.ascontiguousarray(np.asarray(inputs["L1_attn_Wk"], np.float32).T),
        "WkaT": np.ascontiguousarray(np.asarray(inputs["L1_attn_Wka"], np.float32).T),
        "WvT": np.ascontiguousarray(np.asarray(inputs["L1_attn_Wv"], np.float32).T),
        "bkv": pcol(np.stack([np.asarray(inputs["L1_attn_bk"], np.float32),
                              np.asarray(inputs["L1_attn_bka"], np.float32),
                              np.asarray(inputs["L1_attn_bv"], np.float32)])),
        "gamma": pcol(inputs["L1_attn_gamma"]), "beta": pcol(inputs["L1_attn_beta"]),
        "Wp1": cf("L1_ffn_Wp1"), "bp1": np.asarray(inputs["L1_ffn_bp1"], np.float32).reshape(128, 1).copy(),
        "Wp2": cf("L1_ffn_Wp2"), "bp2": cf("L1_ffn_bp2"),
        "We1": cf("L1_ffn_W1"), "be1": cf("L1_ffn_b1"),
        "We2": cf("L1_ffn_W2"), "be2": cf("L1_ffn_b2"),
        "WoT1": cf("L1_ffn_Wo"), "bo1": pcol(inputs["L1_ffn_bo"]),
    }
    in_maps = []
    for b in range(8):
        m = dict(common)
        m["xT"] = np.ascontiguousarray(x[b].T)
        in_maps.append(m)
    return in_maps


def kernel(**inputs):
    in_maps = prep_inputs(inputs)
    nc, meta = build_program(taps=TAPS)
    res = run_bass_kernel_spmd(nc, in_maps, core_ids=list(range(8)))
    outs = np.stack([np.ascontiguousarray(np.asarray(r["out"]).T) for r in res.results])
    if TAPS:
        kernel.last_results = res
    return outs


# version 16
# speedup vs baseline: 1.0010x; 1.0010x over previous
import numpy as np
import ml_dtypes
import bass_rust
import concourse.bass as bass
import concourse.mybir as mybir
from concourse.alu_op_type import AluOpType as OP
from concourse.bass_utils import run_bass_kernel_spmd
from contextlib import ExitStack
from concourse import library_config

F32 = mybir.dt.float32
BF16 = mybir.dt.bfloat16
AF = mybir.ActivationFunctionType
AX = mybir.AxisListType

S = 4096
DM = 1024
TB = 512
NT = S // TB
NCH = DM // 128
NDIAG = 33

TAPS = ()
P1C_HP = range(8)
P1C_IB = range(8)
SKIP_P1 = False
FFN_W = range(NT)


class Ctr:
    def __init__(self, nc, es, name):
        self.sem = es.enter_context(nc.semaphore(name))
        self.v = 0
        self.name = name


class Prog:
    ENGS = ("sync", "gpsimd", "tensor", "vector", "scalar")

    def __init__(self, nc, es):
        self.nc = nc
        self.es = es
        self.ops = {e: [] for e in self.ENGS}
        self.all_ctrs = []
        self.c = {n: self.ctr("c_" + n) for n in ("t", "v", "s", "g")}
        self.engc = {"tensor": self.c["t"], "vector": self.c["v"],
                     "scalar": self.c["s"], "gpsimd": self.c["g"]}

    def ctr(self, name):
        c = Ctr(self.nc, self.es, name)
        self.all_ctrs.append(c)
        return c

    def dpool(self, name, n):
        return [self.ctr(f"{name}{i}") for i in range(n)]

    def emit(self, eng, fn, wait=(), inc=()):
        wait = [(c.sem, v) for c, v in wait if v > 0]
        for c, a in inc:
            c.v += a
        incs = [(c.sem, a) for c, a in inc]

        def run(h, _w=wait, _i=incs, _f=fn):
            for s_, v_ in _w:
                h.wait_ge(s_, v_)
            inst = _f(h)
            for s_, a_ in _i:
                inst.then_inc(s_, a_)
        self.ops[eng].append(run)

    def dma(self, q, c, out, in_, wait=(), slow_ok=False):
        if slow_ok:
            self.emit(q, lambda h: h.dma_start(out=out, in_=in_,
                                               allow_slow_non_contiguous=True),
                      wait=wait, inc=[(c, 16)])
        else:
            self.emit(q, lambda h: h.dma_start(out=out, in_=in_), wait=wait, inc=[(c, 16)])
        return c.v

    def op(self, eng, fn, wait=(), inc=False, drain=None):
        c = self.engc[eng]
        if drain is None:
            drain = eng == "vector"
        self.emit(eng, fn, wait=wait, inc=[(c, 1)] if inc else [])
        if drain:
            self.ops[eng].append(lambda h: h.drain())
        return c.v if inc else None

    def barrier(self, tag=""):
        vals = [(c, c.v) for c in self.all_ctrs]
        for e in self.ENGS:
            cc = self.engc.get(e)
            if e != "sync":
                self.ops[e].append(lambda h: h.drain())
            if cc is None:
                self.emit(e, lambda h: h.nop(), wait=vals)
            else:
                self.emit(e, lambda h: h.nop(), wait=vals, inc=[(cc, 1)])

    def replay_into(self, block):
        for e in self.ENGS:
            ops = self.ops[e]

            def make(ops_):
                def body(h):
                    for f in ops_:
                        f(h)
                return body
            getattr(block, e)(make(ops))


def fm_view(dram_ap, i0=None, i1=None):
    v = dram_ap.rearrange("(c p) t -> p c t", p=128)
    if i0 is not None:
        v = v[:, :, i0:i1]
    return v


def sb3(sbuf_tensor):
    return sbuf_tensor[:].rearrange("p (c t) -> p c t", t=TB)


class Build:
    def __init__(self, taps=()):
        self.taps = taps
        nc = bass.Bass("TRN2", target_bir_lowering=False)
        self.nc = nc
        self.es = ExitStack()

    def sbuf(self, name, shape, dt, scope=None):
        return (scope or self.es).enter_context(self.nc.sbuf_tensor(name, list(shape), dt))

    def build(self):
        nc = self.nc
        es = self.es

        def din(name, shape, dt=F32):
            return nc.dram_tensor(name, list(shape), dt, kind="ExternalInput").ap()

        def dsc(name, shape, dt=F32):
            return nc.dram_tensor(name, list(shape), dt, kind="Internal").ap()

        self.inp = {
            "xT": din("xT", (DM, S)),
            "w_in": din("w_in", (DM, 3 * DM)),
            "w_gate": din("w_gate", (DM, 2 * DM)),
            "b_gate": din("b_gate", (128, 16)),
            "tisa": din("tisa", (3, 16, 21)),
            "wqkv": din("wqkv", (2, DM, 1536)),
            "qk_scale": din("qk_scale", (2, 2, 64)),
            "wo0": din("wo0", (2, 512, DM)),
            "ln_g": din("ln_g", (2, 128, NCH)),
            "ln_b": din("ln_b", (2, 128, NCH)),
            "w1f": din("w1f", (2, DM, 5632)),
            "b1f": din("b1f", (2, 128, 44)),
            "w2f": din("w2f", (2, 2816, DM)),
            "b2f": din("b2f", (2, 128, NCH)),
            "WkT": din("WkT", (DM, DM)),
            "WkaT": din("WkaT", (DM, DM)),
            "WvT": din("WvT", (DM, DM)),
            "bkv": din("bkv", (3, 128, NCH)),
            "gamma": din("gamma", (128, NCH)),
            "beta": din("beta", (128, NCH)),
            "Wp1": din("Wp1", (DM, 128)),
            "bp1": din("bp1", (128, 1)),
            "Wp2": din("Wp2", (128, 8)),
            "bp2": din("bp2", (8,)),
            "We1": din("We1", (8, DM, 2048)),
            "be1": din("be1", (8, 2048)),
            "We2": din("We2", (8, 2048, DM)),
            "be2": din("be2", (8, DM)),
            "WoT1": din("WoT1", (DM, DM)),
            "bo1": din("bo1", (128, NCH)),
        }
        self.out = nc.dram_tensor("out", [DM, S], F32, kind="ExternalOutput").ap()

        sc = {}
        sc["xn"] = dsc("sc_xn", (DM, S), BF16)
        sc["xnf"] = dsc("sc_xnf", (DM, S), F32)
        sc["qT"] = dsc("sc_qT", (DM, S), BF16)
        sc["kT"] = dsc("sc_kT", (DM, S), BF16)
        sc["attn"] = dsc("sc_attn", (DM, S), BF16)
        sc["glu"] = dsc("sc_glu", (DM, S), F32)
        sc["vh"] = dsc("sc_vh", (16, 8448), BF16)
        sc["ffnx"] = dsc("sc_ffnx", (DM, S), F32)
        sc["ffnxb"] = dsc("sc_ffnxb", (DM, S), BF16)
        sc["ffno"] = dsc("sc_ffno", (DM, S), F32)
        sc["xmid"] = dsc("sc_xmid", (DM, S), F32)
        sc["x1"] = dsc("sc_x1", (DM, S), F32)
        sc["xn2"] = dsc("sc_xn2", (DM, S), BF16)
        sc["xn2f"] = dsc("sc_xn2f", (DM, S), F32)
        sc["h2"] = dsc("sc_h2", (DM, S), BF16)
        sc["acc1"] = dsc("sc_acc1", (DM, S), F32)
        sc["small"] = dsc("sc_small", (4096,), F32)
        sc["stats"] = dsc("sc_stats", (2, S), F32)
        self.sc = sc

        self.tap_outs = {}
        for t in self.taps:
            a = sc[t]
            self.tap_outs[t] = nc.dram_tensor("tap_" + t, list(a.shape), a.dtype,
                                              kind="ExternalOutput").ap()

        with es:
            self.P = Prog(nc, es)
            P = self.P
            self.ones_f = self.sbuf("ones_f", [1, 128], F32)
            self.onesP = self.sbuf("onesP", [128, 128], F32)
            self.ones_col_b = self.sbuf("ones_col_b", [128, 1], BF16)
            self.ident_b = self.sbuf("ident_b", [128, 128], BF16)
            self.ident_f = self.sbuf("ident_f", [128, 128], F32)
            self.st_mu = self.sbuf("st_mu", [1, 2 * TB], F32)
            self.st_r = self.sbuf("st_r", [1, 2 * TB], F32)
            self.st_tmp = self.sbuf("st_tmp", [1, 4 * TB], F32)
            self.rrow = self.sbuf("rrow", [128, TB], F32)
            self.bcb = self.sbuf("bcb", [128, TB], F32)
            self.ps = [es.enter_context(nc.psum_tensor(f"ps{i}", [128, 512], F32))
                       for i in range(8)]

            self.init_consts()
            P.barrier("consts")
            self.phase_ln(self.inp["xT"], sc["xn"], self.st_mu, self.st_r,
                          dst2_f32=sc["xnf"])
            P.barrier("p0")
            if not getattr(self, "stop_after_p0", False):
                if not SKIP_P1:
                    self.phase_l0_attn()
                    P.barrier("p1")
                    self.phase_gate()
                    P.barrier("p2")
                self.phase_l0_ffn()
                P.barrier("p3")
                self.phase_p4()
                P.barrier("p4")
                self.phase_l1_attn()
                P.barrier("p5b")
                self.phase_l1_ffn()
                P.barrier("p6")

            fin = P.ctr("fin")
            fv = 0
            for t in self.taps:
                fv = P.dma("sync", fin, self.tap_outs[t], sc[t])
            if fv:
                P.emit("sync", lambda h: h.nop(), wait=[(fin, fv)])

            block = es.enter_context(nc.Block())
            P.replay_into(block)
        return nc

    def bcast(self, out_ap, row_ap, n, ps, wait=()):
        P = self.P
        tb = P.op("tensor", lambda h: h.matmul(
            ps[:, 0:n], self.ones_f[:], row_ap, start=True, stop=True),
            wait=wait, inc=True)
        return P.op("vector", lambda h: h.tensor_copy(out_ap, ps[:, 0:n]),
                    wait=[(P.c["t"], tb)], inc=True)

    def init_consts(self):
        P, h = self.P, None
        P.op("gpsimd", lambda h: h.memset(self.ones_f[:], 1.0), inc=True)
        P.op("gpsimd", lambda h: h.memset(self.onesP[:], 1.0), inc=True)
        P.op("gpsimd", lambda h: h.memset(self.ones_col_b[:], 1.0), inc=True)
        gp = P.op("gpsimd", lambda h: h.memset(self.ident_b[:], 1.0), inc=True)
        P.op("gpsimd", lambda h: h.affine_select(
            out=self.ident_b[:], in_=self.ident_b[:], pattern=[[-1, 128]],
            compare_op=OP.is_equal, fill=0.0, base=0, channel_multiplier=1),
            wait=[(P.c["g"], gp)], inc=True)
        gq = P.op("gpsimd", lambda h: h.affine_select(
            out=self.ident_f[:], in_=self.onesP[:], pattern=[[-1, 128]],
            compare_op=OP.is_equal, fill=0.0, base=0, channel_multiplier=1),
            wait=[(P.c["g"], P.c["g"].v)], inc=True)

    def rsqrt_chain(self, t_slots, var_ap, out_ap, wait_v):
        P = self.P
        t0, t1, t2 = t_slots
        sv = P.op("scalar", lambda h: h.activation(t0, var_ap, AF.Sqrt),
                  wait=wait_v, inc=True)
        P.op("vector", lambda h: h.reciprocal(out_ap, t0), wait=[(P.c["s"], sv)])
        P.op("vector", lambda h: h.tensor_tensor(t1, out_ap, out_ap, op=OP.mult))
        P.op("vector", lambda h: h.tensor_tensor(t1, var_ap, t1, op=OP.mult))
        P.op("vector", lambda h: h.tensor_scalar(t1, t1, -0.5, 1.5, op0=OP.mult, op1=OP.add))
        return P.op("vector", lambda h: h.tensor_tensor(out_ap, out_ap, t1, op=OP.mult), inc=True)

    def phase_ln(self, src_f32, dst_bf, mu_row, r_row, dst2_f32=None, affine=None,
                 add_srcs=(), dst_sum_f32=None):
        nc, P = self.nc, self.P
        self.ln_id = getattr(self, "ln_id", 0) + 1
        pref = f"ln{self.ln_id}_"
        with ExitStack() as ph:
            sb = lambda n, shp, dt: self.sbuf(pref + n, shp, dt, ph)
            nadd = len(add_srcs)
            xb = [sb(f"ln_xb{i}", [128, NCH * TB], F32) for i in range(2)]
            ab = [[sb(f"ln_ab{j}_{i}", [128, NCH * TB], F32) for i in range(2)]
                  for j in range(nadd)]
            xbb = [sb(f"ln_xbb{i}", [128, NCH * TB], BF16) for i in range(2)]
            xout = [sb(f"ln_xo{i}", [128, NCH * TB], BF16) for i in range(2)]
            xof = [sb(f"ln_xof{i}", [128, NCH * TB], F32) for i in range(2)] if dst2_f32 is not None else None
            xsq = [sb(f"ln_xsq{i}", [128, NCH * TB], BF16) for i in range(2)]
            tmpf = [sb(f"ln_tmp{i}", [128, TB], F32) for i in range(2)]
            gam = None
            ld = P.dpool(pref + "ld", 2)
            sto = P.dpool(pref + "st", 2)
            onc = P.ctr(pref + "onc")
            gw = 0
            if affine is not None:
                gam = sb("ln_gam", [128, 2 * NCH], F32)
                P.dma("sync", onc, gam[:, 0:NCH], affine[0])
                gw = P.dma("sync", onc, gam[:, NCH:2 * NCH], affine[1])
            ps_stats = [(self.ps[0], self.ps[1]), (self.ps[2], self.ps[3])]
            ps_bc = [(self.ps[4], self.ps[5]), (self.ps[6], self.ps[7])]
            loads, casts, sums, stores, apply_done = {}, {}, {}, {}, {}
            for i in range(NT):
                pr = i % 2
                wv = []
                if i >= 2:
                    wv = [(P.c["v"], apply_done[i - 2]), (P.c["v"], casts[i - 2]),
                          (P.c["t"], sums[i - 2]), (sto[i % 2], stores[i - 2])]
                loads[i] = P.dma("sync", ld[pr], sb3(xb[pr]), fm_view(src_f32, i * TB, (i + 1) * TB), wait=wv)
                for j, asrc in enumerate(add_srcs):
                    loads[i] = P.dma("sync", ld[pr], sb3(ab[j][pr]),
                                     fm_view(asrc, i * TB, (i + 1) * TB), wait=wv)
                vw = [(ld[pr], loads[i])]
                if nadd:
                    vsum = None
                    for j in range(nadd):
                        vsum = P.op("vector", lambda h, p_=pr, j_=j: h.tensor_tensor(
                            xb[p_][:], xb[p_][:], ab[j_][p_][:], op=OP.add),
                            wait=vw, inc=(j == nadd - 1))
                    vw = [(P.c["v"], vsum)]
                    if dst_sum_f32 is not None:
                        stores[i] = P.dma("sync", sto[pr], fm_view(dst_sum_f32, i * TB, (i + 1) * TB),
                                          sb3(xb[pr]), wait=vw)
                casts[i] = P.op("vector", lambda h, p_=pr: h.tensor_copy(
                    xbb[p_][:], xb[p_][:]), wait=vw, inc=True)
                sq = P.op("vector", lambda h, p_=pr: h.tensor_tensor(
                    xsq[p_][:], xbb[p_][:], xbb[p_][:], op=OP.mult), inc=True)
                for c in range(NCH):
                    P.op("tensor", lambda h, p_=pr, c_=c: h.matmul(
                        ps_stats[p_][0][0:1, 0:TB], self.ones_col_b[:],
                        xbb[p_][:, c_ * TB:(c_ + 1) * TB],
                        start=(c_ == 0), stop=(c_ == NCH - 1)),
                        wait=[(P.c["v"], sq)] if c == 0 else ())
                for c in range(NCH):
                    sums[i] = P.op("tensor", lambda h, p_=pr, c_=c: h.matmul(
                        ps_stats[p_][1][0:1, 0:TB], self.ones_col_b[:],
                        xsq[p_][:, c_ * TB:(c_ + 1) * TB],
                        start=(c_ == 0), stop=(c_ == NCH - 1)), inc=(c == NCH - 1))
                stt = self.st_tmp
                t0, t1, t2, t3 = (stt[0:1, k * TB:(k + 1) * TB] for k in range(4))
                mu_b = mu_row[0:1, pr * TB:(pr + 1) * TB]
                r_b = r_row[0:1, pr * TB:(pr + 1) * TB]
                P.op("vector", lambda h, p_=pr, m_=mu_b: h.tensor_scalar(
                    m_, ps_stats[p_][0][0:1, 0:TB], 1.0 / DM, None, op0=OP.mult),
                    wait=[(P.c["t"], sums[i])])
                P.op("vector", lambda h, p_=pr: h.tensor_scalar(
                    t0, ps_stats[p_][1][0:1, 0:TB], 1.0 / DM, None, op0=OP.mult))
                P.op("vector", lambda h, m_=mu_b: h.tensor_tensor(t1, m_, m_, op=OP.mult))
                P.op("vector", lambda h: h.scalar_tensor_tensor(
                    t2, t0, 1.0, t1, op0=OP.mult, op1=OP.subtract))
                vvar = P.op("vector", lambda h: h.tensor_scalar(
                    t2, t2, 1e-5, None, op0=OP.add), inc=True)
                vr = self.rsqrt_chain((t0, t1, t3), t2, r_b, [(P.c["v"], vvar)])
                bca, bcb = ps_bc[pr]
                P.op("tensor", lambda h, m_=mu_b, b_=bca: h.matmul(
                    b_[:, 0:TB], self.ones_f[:], m_, start=True, stop=True),
                    wait=[(P.c["v"], vr)])
                tb2 = P.op("tensor", lambda h, r_=r_b, b_=bcb: h.matmul(
                    b_[:, 0:TB], self.ones_f[:], r_, start=True, stop=True), inc=True)
                aw = [(P.c["t"], tb2)]
                if i == 0 and gw:
                    aw.append((onc, gw))
                for c in range(NCH):
                    last = (c == NCH - 1)
                    P.op("vector", lambda h, p_=pr, c_=c, b_=bca: h.tensor_tensor(
                        tmpf[p_][:], xb[p_][:, c_ * TB:(c_ + 1) * TB], b_[:, 0:TB],
                        op=OP.subtract), wait=aw if c == 0 else ())
                    if affine is None:
                        if xof is not None:
                            P.op("vector", lambda h, p_=pr, c_=c, b_=bcb: h.tensor_tensor(
                                xof[p_][:, c_ * TB:(c_ + 1) * TB], tmpf[p_][:], b_[:, 0:TB],
                                op=OP.mult))
                        ad = P.op("vector", lambda h, p_=pr, c_=c, b_=bcb: h.tensor_tensor(
                            xout[p_][:, c_ * TB:(c_ + 1) * TB], tmpf[p_][:], b_[:, 0:TB],
                            op=OP.mult), inc=last)
                    else:
                        P.op("vector", lambda h, p_=pr, c_=c, b_=bcb: h.tensor_tensor(
                            tmpf[p_][:], tmpf[p_][:], b_[:, 0:TB], op=OP.mult))
                        if xof is not None:
                            P.op("vector", lambda h, p_=pr, c_=c: h.tensor_scalar(
                                xof[p_][:, c_ * TB:(c_ + 1) * TB], tmpf[p_][:],
                                gam[:, c_:c_ + 1], gam[:, NCH + c_:NCH + c_ + 1],
                                op0=OP.mult, op1=OP.add))
                        ad = P.op("vector", lambda h, p_=pr, c_=c: h.tensor_scalar(
                            xout[p_][:, c_ * TB:(c_ + 1) * TB], tmpf[p_][:],
                            gam[:, c_:c_ + 1], gam[:, NCH + c_:NCH + c_ + 1],
                            op0=OP.mult, op1=OP.add), inc=last)
                    apply_done[i] = ad
                if xof is not None:
                    stores[i] = P.dma("sync", sto[pr], fm_view(dst2_f32, i * TB, (i + 1) * TB),
                                      sb3(xof[pr]), wait=[(P.c["v"], apply_done[i])])
                stores[i] = P.dma("sync", sto[pr], fm_view(dst_bf, i * TB, (i + 1) * TB),
                                  sb3(xout[pr]), wait=[(P.c["v"], apply_done[i])])

    def phase_l0_attn(self):
        nc, P, sc = self.nc, self.P, self.sc
        ps = self.ps
        with ExitStack() as ph:
            sb = lambda n, shp, dt: self.sbuf(n, shp, dt, ph)
            v_sb = sb("v_sb", [128, 32 * 1040], BF16)
            ncb = sb("p1ncb", [2, 128], F32)
            nrow = sb("p1nrow", [2, 16], F32)
            Bneg = sb("Bneg", [128, 16], F32)
            ld = P.dpool("p1ld", 2)
            sto = P.dpool("p1st", 2)
            onc = P.ctr("p1onc")

            vones = bass_rust.AP(v_sb[:].tensor, 64,
                                 [[32 * 1040, 128], [1040, 32], [65, 16], [1, 1]])
            gvo = P.op("gpsimd", lambda h: h.memset(vones, 1.0), inc=True)

            with ExitStack() as pha:
                sba = lambda n, shp, dt: self.sbuf(n, shp, dt, pha)
                w_in_sb = sba("w_in_sb", [128, 8 * 3072], BF16)
                xnb = [sba(f"p1xnb{i}", [128, NCH * TB], BF16) for i in range(2)]
                qstage = [sba(f"p1qs{i}", [128, NCH * TB], BF16) for i in range(2)]
                kstage = [sba(f"p1ks{i}", [128, NCH * TB], BF16) for i in range(2)]
                sqb = sba("p1sqb", [128, TB], BF16)
                bd2 = sba("bd2", [128, 2], BF16)
                gp = P.op("gpsimd", lambda h: h.memset(bd2[:], 1.0), inc=True)
                gp = P.op("gpsimd", lambda h: h.affine_select(
                    out=bd2[:], in_=bd2[:], pattern=[[-64, 2]],
                    compare_op=OP.is_ge, fill=0.0, base=0, channel_multiplier=1),
                    wait=[(P.c["g"], gp)], inc=True)
                gbd = P.op("gpsimd", lambda h: h.affine_select(
                    out=bd2[:], in_=bd2[:], pattern=[[64, 2]],
                    compare_op=OP.is_ge, fill=0.0, base=63, channel_multiplier=-1),
                    wait=[(P.c["g"], gp)], inc=True)
                gonc = P.ctr("p1gonc")
                wld = P.dma("gpsimd", gonc, w_in_sb[:].rearrange("p (kc n) -> p kc n", n=3072),
                            self.inp["w_in"].rearrange("(kc p) n -> p kc n", p=128))

                slots = [ps[0][:, 0:512], ps[1][:, 0:512], ps[2][:, 0:512], ps[3][:, 0:512]]
                mmdone, stored = {}, {}
                prev_ev = None
                norm_done = 0
                for i in range(NT):
                    pr = i % 2
                    wv = []
                    if i >= 2:
                        wv = [(P.c["t"], mmdone[i - 2]), (sto[pr], stored[i - 2]),
                              (P.c["s"], sq_last[pr])]
                    lv = P.dma("sync", ld[pr], sb3(xnb[pr]),
                               fm_view(sc["xn"], i * TB, (i + 1) * TB), wait=wv)
                    base_w = [(ld[pr], lv), (gonc, wld)]
                    if i == 0:
                        base_w += [(P.c["g"], gvo)]
                    ev = {}
                    for g in range(24):
                        sl = slots[g % 4]
                        tw = list(base_w) if (g < 4 and i == 0) else ([(ld[pr], lv)] if g < 4 else [])
                        if g >= 4:
                            tw.append((P.c["v"], ev[g - 4]))
                        elif prev_ev is not None:
                            tw.append((P.c["v"], prev_ev[20 + g]))
                        if g < 8:
                            c = g
                            for kc in range(NCH):
                                mmdone[i] = P.op("tensor", lambda h, sl_=sl, c_=c, kc_=kc, p_=pr: h.matmul(
                                    sl_, w_in_sb[:, kc_ * 3072 + c_ * 128:kc_ * 3072 + (c_ + 1) * 128],
                                    xnb[p_][:, kc_ * TB:(kc_ + 1) * TB],
                                    start=(kc_ == 0), stop=(kc_ == NCH - 1)),
                                    wait=(tw if kc == 0 else ()), inc=(kc == NCH - 1))
                            ev[g] = P.op("vector", lambda h, sl_=sl, c_=c, p_=pr: h.tensor_copy(
                                kstage[p_][:, c_ * TB:(c_ + 1) * TB], sl_),
                                wait=[(P.c["t"], mmdone[i])], inc=True)
                        elif g < 16:
                            c = g - 8
                            for kc in range(NCH):
                                mmdone[i] = P.op("tensor", lambda h, sl_=sl, c_=c, kc_=kc, p_=pr: h.matmul(
                                    sl_, w_in_sb[:, kc_ * 3072 + 2048 + c_ * 128:kc_ * 3072 + 2048 + (c_ + 1) * 128],
                                    xnb[p_][:, kc_ * TB:(kc_ + 1) * TB],
                                    start=(kc_ == 0), stop=(kc_ == NCH - 1)),
                                    wait=(tw if kc == 0 else ()), inc=(kc == NCH - 1))
                            ev[g] = P.op("vector", lambda h, sl_=sl, c_=c, p_=pr: h.tensor_scalar(
                                qstage[p_][:, c_ * TB:(c_ + 1) * TB], sl_, 0.125, None, op0=OP.mult),
                                wait=[(P.c["t"], mmdone[i])], inc=True)
                        else:
                            tt = (g - 16) // 2
                            half = (g - 16) % 2
                            gtt = i * 4 + tt
                            for kc in range(NCH):
                                mmdone[i] = P.op("tensor", lambda h, tt_=tt, hf_=half, sl_=sl, kc_=kc, p_=pr: h.matmul(
                                    sl_, xnb[p_][:, kc_ * TB + tt_ * 128:kc_ * TB + (tt_ + 1) * 128],
                                    w_in_sb[:, kc_ * 3072 + 1024 + hf_ * 512:kc_ * 3072 + 1024 + (hf_ + 1) * 512],
                                    start=(kc_ == 0), stop=(kc_ == NCH - 1)),
                                    wait=(tw if kc == 0 else ()), inc=(kc == NCH - 1))
                            vdst = bass_rust.AP(v_sb[:].tensor, gtt * 1040 + half * 520,
                                                [[32 * 1040, 128], [65, 8], [1, 64]])
                            ev[g] = P.op("vector", lambda h, sl_=sl, d_=vdst: h.tensor_copy(
                                d_, sl_.rearrange("p (a b) -> p a b", b=64)),
                                wait=[(P.c["t"], mmdone[i])], inc=True)
                    prev_ev = ev
                    sq_last = getattr(self, "_p1_sq_last", {0: 0, 1: 0})
                    for qk in range(2):
                        for c in range(NCH):
                            if qk == 0:
                                srcv = kstage[pr][:, c * TB:(c + 1) * TB]
                                wsq = [(P.c["v"], ev[c])]
                            else:
                                srcv = qstage[pr][:, c * TB:(c + 1) * TB]
                                wsq = [(P.c["v"], ev[8 + c])]
                            sqv = P.op("scalar", lambda h, s_=srcv: h.activation(
                                sqb[:], s_, AF.Square), wait=wsq, inc=True)
                            nm = P.op("tensor", lambda h: h.matmul(
                                ps[6][0:2, 0:TB], bd2[:], sqb[:], start=True, stop=True),
                                wait=[(P.c["s"], sqv), (P.c["v"], norm_done),
                                      (P.c["g"], gbd)], inc=True)
                            col = qk * 64 + c * 8 + i
                            norm_done = P.op("vector", lambda h, col_=col: h.tensor_reduce(
                                ncb[0:2, col_:col_ + 1], ps[6][0:2, 0:TB], axis=AX.X, op=OP.max),
                                wait=[(P.c["t"], nm)], inc=True)
                            P.emit("scalar", lambda h: h.nop(), wait=[(P.c["t"], nm)])
                    sq_last[pr] = P.c["s"].v
                    self._p1_sq_last = sq_last
                    stored[i] = P.dma("sync", sto[pr],
                                      fm_view(sc["qT"], i * TB, (i + 1) * TB),
                                      sb3(qstage[pr]), wait=[(P.c["v"], ev[15]), (P.c["s"], sq_last[pr])])
                    stored[i] = P.dma("sync", sto[pr],
                                      fm_view(sc["kT"], i * TB, (i + 1) * TB),
                                      sb3(kstage[pr]))
                nfin = P.op("vector", lambda h: h.tensor_reduce(
                    nrow[0:2, :], ncb[0:2, :].rearrange("p (c b) -> p c b", b=8),
                    axis=AX.X, op=OP.max), wait=[(P.c["v"], norm_done)], inc=True)
                P.barrier("p1a")

            with ExitStack() as phb:
                sbb = lambda n, shp, dt: self.sbuf(n, shp, dt, phb)
                amp_r = sbb("amp_r", [1, 336], F32)
                shp_r = sbb("shp_r", [1, 336], F32)
                off_r = sbb("off_r", [1, 336], F32)
                ampB = sbb("ampB", [128, 336], F32)
                nshB = sbb("nshB", [128, 336], F32)
                offB = sbb("offB", [128, 336], F32)
                io66 = sbb("io66", [128, 66], F32)
                tw1 = sbb("tisa_w1", [128, 336], F32)
                tw2 = sbb("tisa_w2", [128, 336], F32)
                vhc = sbb("tisa_vhc", [128, 16], F32)
                vstage = sbb("vstage", [16, 8448], BF16)
                brow = sbb("brow", [1, 64], F32)
                tld = P.dma("sync", onc, amp_r[:], self.inp["tisa"][0].rearrange("a b -> (a b)").unsqueeze(0))
                tld = P.dma("sync", onc, shp_r[:], self.inp["tisa"][1].rearrange("a b -> (a b)").unsqueeze(0))
                tld = P.dma("sync", onc, off_r[:], self.inp["tisa"][2].rearrange("a b -> (a b)").unsqueeze(0))
                gio = P.op("gpsimd", lambda h: h.iota(
                    io66[:], pattern=[[128, 66]], base=0, channel_multiplier=1,
                    allow_small_or_imprecise_dtypes=True), inc=True)
                P.op("vector", lambda h: h.tensor_scalar(
                    shp_r[:], shp_r[:], -1.0, None, op0=OP.mult), wait=[(onc, tld)])
                vns = P.op("vector", lambda h: h.tensor_scalar(
                    off_r[:], off_r[:], 4095.0, None, op0=OP.add), inc=True)
                g1 = self.bcast(ampB[:], amp_r[:], 336, ps[5], wait=[(P.c["v"], vns)])
                g1 = self.bcast(nshB[:], shp_r[:], 336, ps[5], wait=[(P.c["v"], g1)])
                g1 = self.bcast(offB[:], off_r[:], 336, ps[5], wait=[(P.c["v"], g1)])
                ebm = 0
                vcp = 0
                for rc in range(66):
                    vv = P.op("vector", lambda h, rc_=rc: h.tensor_scalar(
                        tw1[:], offB[:], io66[:, rc_:rc_ + 1], None, op0=OP.subtract),
                        wait=([(P.c["v"], g1), (P.c["g"], gio)] if rc == 0 else
                              [(P.c["s"], ebm)]), inc=True)
                    P.op("vector", lambda h: h.tensor_tensor(tw1[:], tw1[:], tw1[:], op=OP.mult))
                    vv = P.op("vector", lambda h: h.tensor_tensor(tw1[:], tw1[:], nshB[:], op=OP.mult), inc=True)
                    ebm = P.op("scalar", lambda h: h.activation(tw2[:], tw1[:], AF.Exp),
                               wait=[(P.c["v"], vv)], inc=True)
                    vv = P.op("vector", lambda h: h.tensor_tensor(tw2[:], tw2[:], ampB[:], op=OP.mult),
                              wait=[(P.c["s"], ebm)])
                    vv = P.op("vector", lambda h: h.tensor_reduce(
                        vhc[:], tw2[:].rearrange("p (a b) -> p a b", b=21), axis=AX.X, op=OP.add),
                        inc=True)
                    tt = P.op("tensor", lambda h: h.transpose(ps[7][0:16, 0:128], vhc[:], self.ident_f[:]),
                              wait=[(P.c["v"], vv)], inc=True)
                    vcp = P.op("vector", lambda h, rc_=rc: h.tensor_copy(
                        vstage[:, rc_ * 128:(rc_ + 1) * 128], ps[7][0:16, 0:128]),
                        wait=[(P.c["t"], tt)], inc=True)
                    P.emit("tensor", lambda h: h.nop(), wait=[(P.c["v"], vcp)])
                vsd = P.dma("sync", onc, sc["vh"], vstage[:], wait=[(P.c["v"], vcp)])
                bm = P.op("vector", lambda h: h.tensor_reduce(
                    vhc[0:16, 0:1], vstage[:], axis=AX.X, op=OP.max), inc=True)
                smv = sc["small"]
                d1 = P.dma("sync", onc, bass_rust.AP(smv.tensor, 0, [[1, 2], [16, 2], [2, 8]]),
                           nrow[0:2, :].rearrange("p (a b) -> p a b", b=8),
                           wait=[(P.c["v"], nfin)], slow_ok=True)
                d1 = P.dma("sync", onc, bass_rust.AP(smv.tensor, 32, [[1, 16], [1, 1]]),
                           vhc[0:16, 0:1], wait=[(P.c["v"], bm)])
                qn = brow[0:1, 0:16]
                kn = brow[0:1, 16:32]
                bx = brow[0:1, 32:48]
                d2 = P.dma("sync", onc, kn, bass_rust.AP(smv.tensor, 0, [[16, 1], [1, 16]]), wait=[(onc, d1)])
                d2 = P.dma("sync", onc, qn, bass_rust.AP(smv.tensor, 16, [[16, 1], [1, 16]]))
                d2 = P.dma("sync", onc, bx, bass_rust.AP(smv.tensor, 32, [[16, 1], [1, 16]]))
                sqn = P.op("scalar", lambda h: h.activation(qn, qn, AF.Sqrt), wait=[(onc, d2)], inc=True)
                sqn = P.op("scalar", lambda h: h.activation(kn, kn, AF.Sqrt), inc=True)
                P.op("vector", lambda h: h.tensor_tensor(qn, qn, kn, op=OP.mult), wait=[(P.c["s"], sqn)])
                P.op("vector", lambda h: h.tensor_tensor(qn, qn, bx, op=OP.add))
                bneg_row = P.op("vector", lambda h: h.tensor_scalar(
                    qn, qn, -1.0, -0.5, op0=OP.mult, op1=OP.add), inc=True)
                gB = self.bcast(Bneg[:], qn, 16, ps[5], wait=[(P.c["v"], bneg_row)])
                P.barrier("p1ab")

            with ExitStack() as ph2:
                sb2 = lambda n, shp, dt: self.sbuf(n, shp, dt, ph2)
                kT_sb = sb2("kT_sb", [128, 8 * 4096], BF16)
                qz = sb2("qz", [128, 2 * 4096], BF16)
                uband = sb2("uband", [128, 4224], BF16)
                dg = [sb2(f"dg{j}", [128, NDIAG * 128], BF16) for j in range(2)]
                probs = [sb2(f"probs{j}", [128, 4 * 512], BF16) for j in range(2)]
                av_sb = sb2("av_sb", [128, 512], F32)
                attnb = [sb2(f"attnb{j}", [64, 2 * 512], BF16) for j in range(2)]
                qld = P.dpool("qld", 1)
                bld = P.dpool("bld", 1)
                ast = P.dpool("ast", 2)
                kld = P.dma("sync", onc, kT_sb[:].rearrange("p (c t) -> p c t", t=4096),
                            sc["kT"].rearrange("(c p) t -> p c t", p=128))
                gz = P.op("gpsimd", lambda h: h.memset(qz[:], 0.0), inc=True)
                gz2 = P.op("gpsimd", lambda h: h.memset(dg[0][:], 0.0), inc=True)
                gz2 = P.op("gpsimd", lambda h: h.memset(dg[1][:], 0.0), inc=True)
                last_store = {0: 0, 1: 0}
                last_q_reader = 0
                prev_bv = {0: 0, 1: 0}
                dg_tread = 0
                exp_done = {}
                av_done = {}
                last_evn = {0: 0, 1: 0}
                for hp in P1C_HP:
                    qwv = [(P.c["t"], last_q_reader)] if hp else [(P.c["g"], gz)]
                    qv = P.dma("sync", qld[0], qz[0:64, 0:4096],
                               sc["qT"][hp * 128:hp * 128 + 64, :], wait=qwv)
                    qv = P.dma("sync", qld[0], qz[64:128, 4096:8192],
                               sc["qT"][hp * 128 + 64:(hp + 1) * 128, :])
                    bv = {}
                    for j, hh in enumerate((2 * hp, 2 * hp + 1)):
                        ubw = [(P.c["v"], prev_bv[1] if j == 0 else bv[0])]
                        ub = P.dma("sync", bld[0], uband[:],
                                   bass_rust.AP(sc["vh"].tensor, hh * 8448 + 1920,
                                                [[1, 128], [1, 4224]]),
                                   wait=ubw)
                        for d0 in range(-13, 14):
                            rev = bass_rust.AP(uband[:].tensor, 2175 + 128 * d0,
                                               [[4224, 128], [-1, 128]])
                            w_ = [(bld[0], ub)] if d0 == -13 else []
                            if d0 == -13 and hp > 0:
                                w_.append((P.c["t"], dg_tread))
                            elif d0 == -13 and hp == 0:
                                w_.append((P.c["g"], gz2))
                            bv[j] = P.op("vector", lambda h, j_=j, d_=d0, r_=rev: h.tensor_copy(
                                dg[j_][:, (16 - d_) * 128:(17 - d_) * 128], r_),
                                wait=w_, inc=(d0 == 13), drain=(d0 == 13))
                    for ib in P1C_IB:
                        exp_at = {}

                        def emit_av(jt_, hp_=hp):
                            for j in range(2):
                                psv = ps[4 + j]
                                tw = [(P.c["s"], exp_at[(jt_, j)])]
                                if jt_ == 0:
                                    tw.append((P.c["v"], last_evn[j]))
                                P.op("tensor", lambda h, j_=j, jt2=jt_, p_=psv, hpp_=hp_: h.matmul(
                                    p_[0:65, :],
                                    v_sb[:, jt2 * 1040 + 65 * (2 * hpp_ + j_):jt2 * 1040 + 65 * (2 * hpp_ + j_) + 65],
                                    probs[j_][:, (jt2 % 4) * 512:(jt2 % 4) * 512 + 512],
                                    start=(jt2 == 0), stop=(jt2 == 31)),
                                    wait=tw, inc=(jt_ == 31))
                        for jt in range(32):
                            hpar = jt % 2
                            D = jt - 4 * ib
                            biased = -13 <= D <= 16
                            grp_end = {}
                            for j in range(2):
                                pst = ps[2 * j + hpar]
                                tw = []
                                if (hpar, j) in exp_done:
                                    tw.append((P.c["s"], exp_done[hpar, j]))
                                if ib == 0 and jt == 0 and j == 0:
                                    tw += [(qld[0], qv), (onc, kld)]
                                mm = P.op("tensor", lambda h, j_=j, jt_=jt, p_=pst, hp_=hp, ib_=ib, b_=biased: h.matmul(
                                    p_[:, 0:512],
                                    kT_sb[:, hp_ * 4096 + jt_ * 128:hp_ * 4096 + (jt_ + 1) * 128],
                                    qz[:, j_ * 4096 + ib_ * 512:j_ * 4096 + (ib_ + 1) * 512],
                                    start=True, stop=(not b_)),
                                    wait=tw, inc=(not biased))
                                if biased:
                                    mm = P.op("tensor", lambda h, j_=j, D_=D, p_=pst: h.matmul(
                                        p_[:, 0:512], self.ident_b[:],
                                        dg[j_][:, (16 - D_) * 128:(20 - D_) * 128],
                                        start=False, stop=True),
                                        wait=[(P.c["v"], bv[j])] if (ib == 0 and jt == 0) else (),
                                        inc=True)
                                grp_end[j] = mm
                            if jt >= 2:
                                emit_av(jt - 2)
                            for j, hh in enumerate((2 * hp, 2 * hp + 1)):
                                pst = ps[2 * j + hpar]
                                ev_ = P.op("scalar", lambda h, j_=j, p_=pst, sl_=jt % 4, hh_=hh: h.activation(
                                    probs[j_][:, sl_ * 512:(sl_ + 1) * 512], p_[:, 0:512],
                                    AF.Exp, bias=Bneg[:, hh_:hh_ + 1], scale=1.0),
                                    wait=[(P.c["t"], grp_end[j])], inc=True)
                                exp_done[hpar, j] = ev_
                                exp_at[(jt, j)] = ev_
                        emit_av(30)
                        emit_av(31)
                        av_end = P.c["t"].v
                        last_evn = {}
                        for j in range(2):
                            psv = ps[4 + j]
                            cp = P.op("vector", lambda h, p_=psv: h.tensor_copy(
                                av_sb[0:65, 0:512], p_[0:65, 0:512]),
                                wait=[(P.c["t"], av_end)], inc=True)
                            last_evn[j] = cp
                            rr = self.rrow[64:65, 0:512]
                            rv = P.op("vector", lambda h, r_=rr: h.reciprocal(
                                r_, av_sb[64:65, 0:512]), inc=True)
                            bc = P.op("tensor", lambda h, r_=rr: h.matmul(
                                ps[6][:, 0:512], self.onesP[64:65, :], r_, start=True, stop=True),
                                wait=[(P.c["v"], rv)], inc=True)
                            P.op("vector", lambda h: h.tensor_copy(
                                self.bcb[0:64, 0:512], ps[6][0:64, 0:512]),
                                wait=[(P.c["t"], bc)])
                            ibp = ib % 2
                            stw = []
                            if (hp, ib) != (0, 0):
                                stw = [(ast[ibp], last_store[ibp])]
                            env_ = P.op("vector", lambda h, j_=j, ibp_=ibp: h.tensor_tensor(
                                attnb[j_][0:64, ibp_ * 512:(ibp_ + 1) * 512],
                                av_sb[0:64, 0:512],
                                self.bcb[0:64, 0:512], op=OP.mult),
                                wait=(stw if j == 0 else []),
                                inc=True)
                            hh2 = 2 * hp + j
                            last_store[ibp] = P.dma(
                                "sync", ast[ibp],
                                sc["attn"][hh2 * 64:(hh2 + 1) * 64, ib * 512:(ib + 1) * 512],
                                attnb[j][0:64, ibp * 512:(ibp + 1) * 512],
                                wait=[(P.c["v"], env_)])
                    last_q_reader = P.c["t"].v
                    dg_tread = P.c["t"].v
                    prev_bv = dict(bv)
        return

    def phase_l0_ffn(self):
        nc, P, sc = self.nc, self.P, self.sc
        ps = self.ps
        SGN_NEG = -1.0e30
        with ExitStack() as ph:
            sb = lambda n, shp, dt: self.sbuf(n, shp, dt, ph)
            invf_bc = sb("f_invf", [128, 64], F32)
            lnb_bc = sb("f_lnb", [128, 64], F32)
            sgn_bc = sb("f_sgn", [128, 64], F32)
            rowtmp = sb("f_rowt", [1, 256], F32)
            onc = P.ctr("f_onc")
            gonc = P.ctr("f_gonc")

            gi = P.op("gpsimd", lambda h: h.iota(
                rowtmp[0:1, 0:64].rearrange("p (a b) -> p a b", b=32),
                pattern=[[0, 2], [1, 32]], base=0, channel_multiplier=0,
                allow_small_or_imprecise_dtypes=True), inc=True)
            sA = P.op("scalar", lambda h: h.activation(
                rowtmp[0:1, 64:128], rowtmp[0:1, 0:64], AF.Exp,
                scale=-float(np.log(10000.0) / 32.0)), wait=[(P.c["g"], gi)], inc=True)
            vA = self.bcast(invf_bc[:], rowtmp[0:1, 64:128], 64, ps[5],
                            wait=[(P.c["s"], sA)])
            vB = P.op("vector", lambda h: h.tensor_scalar(
                rowtmp[0:1, 128:192], rowtmp[0:1, 0:64], 2.0 / 89.6, 25.6 / 89.6,
                op0=OP.mult, op1=OP.add), wait=[(P.c["g"], gi)], inc=True)
            sB = P.op("scalar", lambda h: h.activation(
                rowtmp[0:1, 128:192], rowtmp[0:1, 128:192], AF.Ln),
                wait=[(P.c["v"], vB)], inc=True)
            vB = self.bcast(lnb_bc[:], rowtmp[0:1, 128:192], 64, ps[5],
                            wait=[(P.c["s"], sB), (P.c["v"], vA)])
            gs = P.op("gpsimd", lambda h: h.memset(rowtmp[0:1, 192:224], -1.0), inc=True)
            gs = P.op("gpsimd", lambda h: h.memset(rowtmp[0:1, 224:256], 1.0), inc=True)
            vC = self.bcast(sgn_bc[:], rowtmp[0:1, 192:256], 64, ps[5],
                            wait=[(P.c["g"], gs), (P.c["v"], vB)])
            P.barrier("p3const")
            for l in range(2):
                src_bf = sc["xn"] if l == 0 else sc["ffnxb"]
                src_f = sc["xnf"] if l == 0 else sc["ffnx"]
                dst_f = sc["ffnx"] if l == 0 else sc["ffno"]
                dst_bf = sc["ffnxb"] if l == 0 else None
                self.ffn_subAB(l, src_bf, src_f, invf_bc, lnb_bc, sgn_bc,
                               rowtmp, onc, gonc, ph)
                P.barrier(f"p3ab{l}")
                self.ffn_subC(l, src_f, dst_f, dst_bf, onc, gonc)
                P.barrier(f"p3c{l}")

    def ffn_subAB(self, l, src_bf, src_f, invf_bc, lnb_bc, sgn_bc,
                  rowtmp, onc, gonc, ph_outer):
        nc, P, sc = self.nc, self.P, self.sc
        ps = self.ps
        with ExitStack() as ph:
            sb = lambda n, shp, dt: self.sbuf(f"{n}_{l}", shp, dt, ph)
            qf = sb("f_qf", [128, 4 * 4096], BF16)
            kf = sb("f_kf", [128, 4 * 4096], BF16)
            vloc = sb("f_vloc", [128, 32 * 520], BF16)
            nloc = sb("f_nloc", [2, 64], F32)
            brow2 = sb("f_brow", [1, 128], F32)
            Bloc = sb("f_Bloc", [128, 64], F32)
            phA = ExitStack()
            sbA = lambda n, shp, dt: self.sbuf(f"{n}_{l}", shp, dt, phA)
            wqkv_sb = sbA("f_wqkv", [128, 8 * 1536], BF16)
            xqb = [sbA("f_xqb0", [128, 8 * 513], BF16)]
            qtokb = sbA("f_qtok", [128, 2048], F32)
            ktokb = sbA("f_ktok", [128, 2048], F32)
            t3b = sbA("f_t3", [128, 2048], F32)
            rpb = sbA("f_rp", [128, 2048], F32)
            sq2 = rpb[:].bitcast(BF16)[:, 0:2048]
            ssq = sbA("f_ssq", [128, 2 * 32], F32)
            rloc = sbA("f_rloc", [128, 2 * 32], F32)
            tabs = sbA("f_tabs", [128, 8 * 256], F32)
            bd2 = sbA("f_bd2", [128, 2], BF16)
            sqb = sbA("f_sqb", [128, TB], BF16)
            qsrow = sbA("f_qsrow", [1, 4 * 64], F32)
            qs_bc = sbA("f_qsbc", [128, 64], F32)
            qsr_bc = sbA("f_qsrbc", [128, 64], F32)
            ks_bc = sbA("f_ksbc", [128, 64], F32)
            ksr_bc = sbA("f_ksrbc", [128, 64], F32)
            tcol = sbA("f_tcol", [128, 4], F32)
            ld = P.dpool(f"f_ld{l}", 2)
            ost = P.dpool(f"f_ost{l}", 2)

            gq = P.op("gpsimd", lambda h: h.memset(bd2[:], 1.0), inc=True)
            gq = P.op("gpsimd", lambda h: h.affine_select(
                out=bd2[:], in_=bd2[:], pattern=[[-64, 2]],
                compare_op=OP.is_ge, fill=0.0, base=0, channel_multiplier=1),
                wait=[(P.c["g"], gq)], inc=True)
            gbd = P.op("gpsimd", lambda h: h.affine_select(
                out=bd2[:], in_=bd2[:], pattern=[[64, 2]],
                compare_op=OP.is_ge, fill=0.0, base=63, channel_multiplier=-1),
                wait=[(P.c["g"], gq)], inc=True)
            cqkv = P.ctr(f"c_qkv{l}")
            wv = P.dma("gpsimd", cqkv, wqkv_sb[:].rearrange("p (kc n) -> p kc n", n=1536),
                       self.inp["wqkv"][l].rearrange("(kc p) n -> p kc n", p=128))
            qsl = P.dma("sync", onc, rowtmp[0:1, 0:64],
                        self.inp["qk_scale"][l, 0].unsqueeze(0))
            qsl = P.dma("sync", onc, rowtmp[0:1, 64:128],
                        self.inp["qk_scale"][l, 1].unsqueeze(0))
            P.op("vector", lambda h: h.tensor_scalar(
                qsrow[0:1, 0:64], rowtmp[0:1, 0:64], 8.0, None, op0=OP.mult),
                wait=[(onc, qsl)])
            P.op("vector", lambda h: h.tensor_scalar(
                qsrow[0:1, 64:96], rowtmp[0:1, 32:64], 8.0, None, op0=OP.mult))
            P.op("vector", lambda h: h.tensor_scalar(
                qsrow[0:1, 96:128], rowtmp[0:1, 0:32], 8.0, None, op0=OP.mult))
            P.op("vector", lambda h: h.tensor_copy(qsrow[0:1, 128:192], rowtmp[0:1, 64:128]))
            P.op("vector", lambda h: h.tensor_copy(qsrow[0:1, 192:224], rowtmp[0:1, 96:128]))
            vq = P.op("vector", lambda h: h.tensor_copy(qsrow[0:1, 224:256], rowtmp[0:1, 64:96]), inc=True)
            vq = self.bcast(qs_bc[:], qsrow[0:1, 0:64], 64, ps[5], wait=[(P.c["v"], vq)])
            vq = self.bcast(qsr_bc[:], qsrow[0:1, 64:128], 64, ps[5], wait=[(P.c["v"], vq)])
            vq = self.bcast(ks_bc[:], qsrow[0:1, 128:192], 64, ps[5], wait=[(P.c["v"], vq)])
            vq = self.bcast(ksr_bc[:], qsrow[0:1, 192:256], 64, ps[5], wait=[(P.c["v"], vq)])
            gvz = P.op("gpsimd", lambda h: h.memset(
                bass_rust.AP(xqb[0][:].tensor, 0, [[8 * 513, 128], [513, 8], [1, 1]]),
                0.0), inc=True)
            gvz = P.op("gpsimd", lambda h: h.memset(
                bass_rust.AP(vloc[:].tensor, 64, [[32 * 520, 128], [520, 32], [65, 8], [1, 1]]),
                1.0), inc=True)

            TABS = {"sin": 0, "cos": 1, "scl": 2, "isc": 3, "csq": 4, "ssq": 5,
                    "csk": 6, "ssk": 7}

            def tab(name):
                o = TABS[name] * 256
                return tabs[:, o:o + 256]

            def tab3(name):
                return tab(name).rearrange("p (a b) -> p a b", b=64)

            def tabx(name):
                o = TABS[name] * 256
                return bass_rust.AP(tabs[:].tensor, o, [[8 * 256, 128], [64, 4], [0, 8], [1, 64]])

            def bcx(t):
                return bass_rust.AP(t[:].tensor, 0, [[64, 128], [0, 4], [1, 64]])

            C2PI = float(1.0 / (2 * np.pi))
            CW1 = float(np.float32(2 * np.pi))
            CW2 = float(2 * np.pi - np.float64(np.float32(2 * np.pi)))
            RND = float(3 * 2.0 ** 22)

            mm_done = {}
            ev_done = {"q": 0, "k": 0, "v": 0}
            s_prev = 0
            tlast = 0
            norm_done = 0
            tcol_read = 0
            for i in range(NT):
                pr = i % 2
                wvl = []
                if i >= 1:
                    wvl = [(P.c["t"], mm_done[i - 1])]
                if i == 0:
                    lsrc = fm_view(src_bf, 0, TB)
                    ldst = bass_rust.AP(xqb[0][:].tensor, 1, [[8 * 513, 128], [513, 8], [1, TB]])
                    wvl.append((P.c["g"], gvz))
                else:
                    lsrc = fm_view(src_bf, i * TB - 1, i * TB + TB)
                    ldst = bass_rust.AP(xqb[0][:].tensor, 0, [[8 * 513, 128], [513, 8], [1, TB + 1]])
                lv = P.dma("sync", ld[pr], ldst, lsrc, wait=wvl)
                for tt in range(4):
                    for which, pso in (("q", ps[0]), ("k", ps[1]), ("v", ps[2])):
                        off = {"q": 0, "k": 512, "v": 1024}[which]
                        tw = []
                        if tt == 0 and which == "q":
                            tw = [(ld[pr], lv)]
                            if i == 0:
                                tw.append((cqkv, wv))
                        if ev_done[which]:
                            tw.append((P.c["v"], ev_done[which]))
                        for kc in range(NCH):
                            sh = 1 if kc < 4 else 0
                            mm_done[i] = P.op("tensor", lambda h, tt_=tt, kc_=kc, sh_=sh, o_=off, p_=pr, s_=pso: h.matmul(
                                s_[:, 0:512],
                                xqb[0][:, kc_ * 513 + sh_ + tt_ * 128:kc_ * 513 + sh_ + tt_ * 128 + 128],
                                wqkv_sb[:, kc_ * 1536 + o_:kc_ * 1536 + o_ + 512],
                                start=(kc_ == 0), stop=(kc_ == NCH - 1)),
                                wait=(tw if kc == 0 else ()), inc=(kc == NCH - 1))
                        if which == "v":
                            vdst = bass_rust.AP(vloc[:].tensor, (i * 4 + tt) * 520,
                                                [[32 * 520, 128], [65, 8], [1, 64]])
                            ev_done[which] = P.op("vector", lambda h, d_=vdst, s_=pso: h.tensor_copy(
                                d_, s_[:, 0:512].rearrange("p (a b) -> p a b", b=64)),
                                wait=[(P.c["t"], mm_done[i])], inc=True)
                        else:
                            dstb = qtokb if which == "q" else ktokb
                            wt = [(P.c["t"], mm_done[i])]
                            if i > 0 or tt > 0:
                                wt.append((P.c["v"], P.c["v"].v))
                            ev_done[which] = P.op("vector", lambda h, tt_=tt, d_=dstb, s_=pso: h.tensor_copy(
                                d_[:, tt_ * 512:(tt_ + 1) * 512], s_[:, 0:512]),
                                wait=wt, inc=True)
                gi2 = P.op("gpsimd", lambda h, i_=i: h.iota(
                    tcol[:], pattern=[[128, 4]], base=i_ * TB, channel_multiplier=1,
                    allow_small_or_imprecise_dtypes=True),
                    wait=[(P.c["v"], tcol_read)] if i > 0 else (), inc=True)
                tcx = bass_rust.AP(tcol[:].tensor, 0, [[4, 128], [1, 4], [0, 64]])
                vt = P.op("vector", lambda h: h.tensor_tensor(
                    tab3("sin"), tcx, bcx(invf_bc), op=OP.mult),
                    wait=[(P.c["g"], gi2), (P.c["v"], vq), (P.c["s"], s_prev)], inc=True)
                P.op("vector", lambda h: h.tensor_scalar(
                    tab("cos"), tab("sin"), C2PI, RND, op0=OP.mult, op1=OP.add))
                P.op("vector", lambda h: h.tensor_scalar(
                    tab("cos"), tab("cos"), RND, None, op0=OP.subtract))
                P.op("vector", lambda h: h.scalar_tensor_tensor(
                    tab("scl"), tab("cos"), -CW1, tab("sin"), op0=OP.mult, op1=OP.add))
                P.op("vector", lambda h: h.scalar_tensor_tensor(
                    tab("scl"), tab("cos"), -CW2, tab("scl"), op0=OP.mult, op1=OP.add))
                P.op("vector", lambda h: h.tensor_scalar(
                    tab("isc"), tab("sin"), C2PI, 0.25, op0=OP.mult, op1=OP.add))
                P.op("vector", lambda h: h.tensor_scalar(
                    tab("isc"), tab("isc"), RND, None, op0=OP.add))
                P.op("vector", lambda h: h.tensor_scalar(
                    tab("isc"), tab("isc"), RND, None, op0=OP.subtract))
                P.op("vector", lambda h: h.scalar_tensor_tensor(
                    tab("csq"), tab("isc"), -CW1, tab("sin"), op0=OP.mult, op1=OP.add))
                P.op("vector", lambda h: h.scalar_tensor_tensor(
                    tab("csq"), tab("isc"), -CW2, tab("csq"), op0=OP.mult, op1=OP.add))
                P.op("vector", lambda h: h.tensor_scalar(
                    tab("csq"), tab("csq"), 1.0, float(np.pi / 2), op0=OP.mult, op1=OP.add))
                PI_LO = float(np.nextafter(np.float32(np.pi), np.float32(0)))
                P.op("vector", lambda h, p_=PI_LO: h.tensor_scalar(
                    tab("csq"), tab("csq"), p_, -p_, op0=OP.min, op1=OP.max))
                vt2 = P.op("vector", lambda h, p_=PI_LO: h.tensor_scalar(
                    tab("scl"), tab("scl"), p_, -p_, op0=OP.min, op1=OP.max), inc=True)
                st_ = P.op("scalar", lambda h: h.activation(tab("sin"), tab("scl"), AF.Sin),
                           wait=[(P.c["v"], vt2)], inc=True)
                st_ = P.op("scalar", lambda h: h.activation(tab("cos"), tab("csq"), AF.Sin), inc=True)
                vt3 = P.op("vector", lambda h: h.tensor_scalar(
                    tcol[:], tcol[:], 1.0 / 256.0, -8.0, op0=OP.mult, op1=OP.add),
                    wait=[(P.c["s"], st_)], inc=True)
                vt3 = P.op("vector", lambda h: h.tensor_tensor(
                    tab3("scl"), tcx, bcx(lnb_bc), op=OP.mult), inc=True)
                tcol_read = vt3
                st2 = P.op("scalar", lambda h: h.activation(tab("isc"), tab("scl"), AF.Exp,
                                                            scale=-1.0),
                           wait=[(P.c["v"], vt3)], inc=True, drain=True)
                st2 = P.op("scalar", lambda h: h.activation(tab("scl"), tab("scl"), AF.Exp), inc=True)
                vt4 = P.op("vector", lambda h: h.tensor_tensor(tab3("csq"), tab3("cos"), bcx(qs_bc), op=OP.mult),
                           wait=[(P.c["s"], st2)])
                P.op("vector", lambda h: h.tensor_tensor(tab("csq"), tab("csq"), tab("scl"), op=OP.mult))
                P.op("vector", lambda h: h.tensor_tensor(tab3("ssq"), tab3("sin"), bcx(qsr_bc), op=OP.mult))
                P.op("vector", lambda h: h.tensor_tensor(tab("ssq"), tab("ssq"), tab("scl"), op=OP.mult))
                P.op("vector", lambda h: h.tensor_tensor(tab3("ssq"), tab3("ssq"), bcx(sgn_bc), op=OP.mult))
                P.op("vector", lambda h: h.tensor_tensor(tab3("csk"), tab3("cos"), bcx(ks_bc), op=OP.mult))
                P.op("vector", lambda h: h.tensor_tensor(tab("csk"), tab("csk"), tab("isc"), op=OP.mult))
                P.op("vector", lambda h: h.tensor_tensor(tab3("ssk"), tab3("sin"), bcx(ksr_bc), op=OP.mult))
                P.op("vector", lambda h: h.tensor_tensor(tab("ssk"), tab("ssk"), tab("isc"), op=OP.mult))
                vt4 = P.op("vector", lambda h: h.tensor_tensor(tab3("ssk"), tab3("ssk"), bcx(sgn_bc), op=OP.mult), inc=True)
                for qk in range(2):
                    tokb = qtokb if qk == 0 else ktokb
                    ctn, stn = ("csq", "ssq") if qk == 0 else ("csk", "ssk")
                    o_ = qk * 32
                    sq_ = P.op("scalar", lambda h, t_=tokb: h.activation(
                        sq2[:], t_[:], AF.Square),
                        wait=[(P.c["v"], ev_done["q" if qk == 0 else "k"])], inc=True)
                    rv_ = P.op("vector", lambda h, o2=o_: h.tensor_reduce(
                        ssq[:, o2:o2 + 32], sq2[:].rearrange("p (a c) -> p a c", c=64),
                        axis=AX.X, op=OP.add), wait=[(P.c["s"], sq_)], inc=True)
                    s0 = rloc[:, o_:o_ + 32]
                    ssv = ssq[:, o_:o_ + 32]
                    sA_ = P.op("scalar", lambda h, a_=ssv, b_=s0: h.activation(
                        b_, a_, AF.Sqrt), wait=[(P.c["v"], rv_)], inc=True)
                    nsc = self.rrow[0:128, 0:64]
                    P.op("vector", lambda h, b_=s0, n_=nsc: h.reciprocal(n_[:, 0:32], b_),
                         wait=[(P.c["s"], sA_)])
                    P.op("vector", lambda h, a_=ssv, n_=nsc: h.tensor_tensor(
                        n_[:, 0:32], a_, n_[:, 0:32], op=OP.mult))
                    P.op("vector", lambda h, b_=s0, n_=nsc: h.tensor_tensor(
                        b_, b_, n_[:, 0:32], op=OP.add))
                    P.op("vector", lambda h, b_=s0: h.tensor_scalar(
                        b_, b_, 0.5, 1e-12, op0=OP.mult, op1=OP.max))
                    rv2 = P.op("vector", lambda h, b_=s0: h.reciprocal(b_, b_), inc=True)
                    rx = bass_rust.AP(rloc[:].tensor, o_, [[64, 128], [8, 4], [1, 8], [0, 64]])
                    a1 = P.op("vector", lambda h, t_=tokb, r_=rx: h.tensor_tensor(
                        t_[:].rearrange("p (a b) -> p a b", b=64),
                        t_[:].rearrange("p (a b) -> p a b", b=64), r_, op=OP.mult),
                        wait=[(P.c["v"], rv2)], inc=True)
                    P.op("vector", lambda h, t_=tokb, c_=tabx(ctn): h.tensor_tensor(
                        t3b[:].rearrange("p (a b) -> p a b", b=64),
                        t_[:].rearrange("p (a b) -> p a b", b=64), c_, op=OP.mult),
                        wait=[(P.c["t"], tlast)] if tlast else ())
                    sx0 = bass_rust.AP(tabs[:].tensor, TABS[stn] * 256,
                                       [[8 * 256, 128], [64, 4], [0, 8], [1, 32]])
                    sx1 = bass_rust.AP(tabs[:].tensor, TABS[stn] * 256 + 32,
                                       [[8 * 256, 128], [64, 4], [0, 8], [1, 32]])
                    P.op("vector", lambda h, t_=tokb, s_=sx0: h.tensor_tensor(
                        bass_rust.AP(rpb[:].tensor, 0, [[2048, 128], [512, 4], [64, 8], [1, 32]]),
                        bass_rust.AP(t_[:].tensor, 32, [[2048, 128], [512, 4], [64, 8], [1, 32]]),
                        s_, op=OP.mult))
                    P.op("vector", lambda h, t_=tokb, s_=sx1: h.tensor_tensor(
                        bass_rust.AP(rpb[:].tensor, 32, [[2048, 128], [512, 4], [64, 8], [1, 32]]),
                        bass_rust.AP(t_[:].tensor, 0, [[2048, 128], [512, 4], [64, 8], [1, 32]]),
                        s_, op=OP.mult))
                    ap_ = P.op("vector", lambda h: h.tensor_tensor(
                        t3b[:], t3b[:], rpb[:], op=OP.add), inc=True)
                    dstf = qf if qk == 0 else kf
                    for tt in range(4):
                        for cc in range(4):
                            tr = P.op("tensor", lambda h, tt_=tt, cc_=cc: h.transpose(
                                ps[3][:, cc_ * 128:(cc_ + 1) * 128],
                                t3b[:, tt_ * 512 + cc_ * 128:tt_ * 512 + (cc_ + 1) * 128],
                                self.ident_f[:]),
                                wait=([(P.c["v"], ap_)] if (cc == 0 and tt == 0) else ()),
                                inc=(cc == 3))
                        tlast = tr
                        trd = P.op("vector", lambda h, tt_=tt, i_=i, d_=dstf: h.tensor_copy(
                            bass_rust.AP(d_[:].tensor, i_ * 512 + tt_ * 128,
                                         [[4 * 4096, 128], [4096, 4], [1, 128]]),
                            ps[3][:, 0:512].rearrange("p (a b) -> p a b", b=128)),
                            wait=[(P.c["t"], tr)], inc=True)
                        P.emit("tensor", lambda h: h.nop(), wait=[(P.c["v"], trd)])
                    for cc in range(4):
                        sqv = P.op("scalar", lambda h, cc_=cc, i_=i, d_=dstf: h.activation(
                            sqb[:], d_[:, cc_ * 4096 + i_ * 512:cc_ * 4096 + (i_ + 1) * 512],
                            AF.Square), wait=[(P.c["v"], trd)], inc=True)
                        nm2 = P.op("tensor", lambda h: h.matmul(
                            ps[6][0:2, 0:TB], bd2[:], sqb[:], start=True, stop=True),
                            wait=[(P.c["s"], sqv), (P.c["v"], norm_done), (P.c["g"], gbd)], inc=True)
                        col = qk * 32 + cc * 8 + i
                        norm_done = P.op("vector", lambda h, col_=col: h.tensor_reduce(
                            nloc[0:2, col_:col_ + 1], ps[6][0:2, 0:TB], axis=AX.X, op=OP.max),
                            wait=[(P.c["t"], nm2)], inc=True)
                        P.emit("scalar", lambda h: h.nop(), wait=[(P.c["t"], nm2)])
                s_prev = P.c["s"].v

            phA.close()
            smv = sc["small"]
            for qk_ in range(2):
                d1 = P.dma("sync", onc,
                           bass_rust.AP(smv.tensor, 64 + 64 * qk_, [[8, 2], [16, 4], [1, 8]]),
                           nloc[0:2, 32 * qk_:32 * (qk_ + 1)].rearrange("p (b c) -> p b c", c=8),
                           wait=[(P.c["v"], norm_done)], slow_ok=True)
            qrow = brow2[0:1, 0:64]
            krow = brow2[0:1, 64:128]
            d2 = P.dma("sync", onc, qrow, bass_rust.AP(smv.tensor, 64, [[64, 1], [1, 64]]),
                       wait=[(onc, d1)])
            d2 = P.dma("sync", onc, krow, bass_rust.AP(smv.tensor, 128, [[64, 1], [1, 64]]))
            sq_ = P.op("scalar", lambda h: h.activation(qrow, qrow, AF.Sqrt), wait=[(onc, d2)], inc=True)
            sq_ = P.op("scalar", lambda h: h.activation(krow, krow, AF.Sqrt), inc=True)
            kwin = rowtmp[0:1, 0:64]
            vb_ = P.op("vector", lambda h: h.tensor_copy(kwin, krow), wait=[(P.c["s"], sq_)])
            for hh_ in range(8):
                vb_ = P.op("vector", lambda h, o_=hh_ * 8: h.tensor_tensor(
                    kwin[0:1, o_ + 1:o_ + 8], krow[0:1, o_ + 1:o_ + 8],
                    krow[0:1, o_:o_ + 7], op=OP.max), inc=(hh_ == 7))
            P.op("vector", lambda h: h.tensor_tensor(qrow, qrow, kwin, op=OP.mult))
            vb_ = P.op("vector", lambda h: h.tensor_scalar(
                qrow, qrow, -1.0, -0.5, op0=OP.mult, op1=OP.add), inc=True)
            vb_ = self.bcast(Bloc[:], qrow, 64, ps[5], wait=[(P.c["v"], vb_)])
            P.barrier(f"p3a{l}")

            SGN_NEG = -1.0e30
            msk = sb("f_msk", [128, 11 * 128], BF16)
            gm = P.op("gpsimd", lambda h: h.memset(msk[:], 0.0), inc=True)
            for dmi in range(11):
                dmp = dmi - 3
                gm = P.op("gpsimd", lambda h, d_=dmp, i_=10 - dmi: h.affine_select(
                    out=msk[:, i_ * 128:(i_ + 1) * 128], in_=msk[:, i_ * 128:(i_ + 1) * 128],
                    pattern=[[-1, 128]], compare_op=OP.is_ge, fill=SGN_NEG,
                    base=128 * d_, channel_multiplier=1),
                    wait=[(P.c["g"], gm)], inc=True)
                gm = P.op("gpsimd", lambda h, d_=dmp, i_=10 - dmi: h.affine_select(
                    out=msk[:, i_ * 128:(i_ + 1) * 128], in_=msk[:, i_ * 128:(i_ + 1) * 128],
                    pattern=[[1, 128]], compare_op=OP.is_ge, fill=SGN_NEG,
                    base=512 - 128 * d_, channel_multiplier=-1),
                    wait=[(P.c["g"], gm)], inc=True)
            kzf = sb("f_kzf", [128, 2 * 1024], BF16)
            probsb = sb("f_probsb", [128, 2 * 1024], BF16)
            probsT = sb("f_probsT", [128, 2 * 1024], BF16)
            mrow = sb("f_mrow", [128, 8], F32)
            obuf = sb("f_obuf", [128, 4 * 512], BF16)
            wo_sb = sb("f_wo", [128, 4 * 1024], BF16)
            xmb = sb("f_xmb", [128, NCH * TB], F32)
            cwo = P.ctr(f"c_wo{l}")
            wvo = P.dma("gpsimd", cwo, wo_sb[:].rearrange("p (a b) -> p a b", b=1024),
                        self.inp["wo0"][l].rearrange("(a p) c -> p a c", p=128))
            gz3 = P.op("gpsimd", lambda h: h.memset(kzf[:], 0.0), inc=True)
            iters = []
            for w in FFN_W:
                for hp in range(4):
                    for it in range(4):
                        for j in range(2):
                            iters.append((w, hp, it, j))
            K = len(iters)
            st_grp, st_mx, st_exp, st_trE, st_evn = {}, {}, {}, {}, {}
            kz_done = {}
            prevG = {0: 0, 1: 0}
            xst, xm_store, lvx = {}, {}, {}
            last_of_w = {}
            for k in range(K - 1, -1, -1):
                w = iters[k][0]
                if w not in last_of_w:
                    last_of_w[w] = k

            def kz_build(w, hp):
                jlo = max(0, (w - 1) * 4)
                nj = (((w + 1) * 4) - jlo) * 128
                kzw = [(P.c["t"], P.c["t"].v)] if (w, hp) != (FFN_W[0], 0) else [(P.c["g"], gz3)]
                P.op("vector", lambda h, hp_=hp, c0=jlo * 128, n_=nj: h.tensor_copy(
                    kzf[0:64, 0:n_], kf[0:64, hp_ * 4096 + c0:hp_ * 4096 + c0 + n_]),
                    wait=kzw)
                kz_done[(w, hp)] = P.op("vector", lambda h, hp_=hp, c0=jlo * 128, n_=nj: h.tensor_copy(
                    kzf[64:128, 1024:1024 + n_], kf[64:128, hp_ * 4096 + c0:hp_ * 4096 + c0 + n_]),
                    inc=True)

            def stage_A(k):
                w, hp, it, j = iters[k]
                njt = 8 if w >= 1 else 4
                p = k % 2
                ig = w * 4 + it
                jlo = max(0, (w - 1) * 4)
                mm = None
                for jh in range(2 if njt == 8 else 1):
                    pst = ps[2 * p + jh]
                    tw = []
                    if jh == 0:
                        if k >= 2:
                            tw += [(P.c["s"], st_exp[k - 2]), (P.c["v"], st_trE[k - 2])]
                        if it == 0 and j == 0:
                            tw.append((P.c["v"], kz_done[(w, hp)]))
                            if (w, hp) == (FFN_W[0], 0):
                                tw.append((P.c["g"], gm))
                    P.op("tensor", lambda h, j_=j, jh_=jh, ig_=ig, hp_=hp, p_=pst: h.matmul(
                        p_[:, 0:512],
                        qf[:, hp_ * 4096 + ig_ * 128:hp_ * 4096 + (ig_ + 1) * 128],
                        kzf[:, j_ * 1024 + jh_ * 512:j_ * 1024 + jh_ * 512 + 512],
                        start=True, stop=False), wait=tw)
                    dmi0 = (ig - (jlo + jh * 4)) + 3
                    mm = P.op("tensor", lambda h, d_=dmi0, p_=pst: h.matmul(
                        p_[:, 0:512], self.ident_b[:],
                        msk[:, (10 - d_) * 128:(14 - d_) * 128],
                        start=False, stop=True), inc=True)
                st_grp[k] = mm

            def stage_BC(k):
                w, hp, it, j = iters[k]
                njt = 8 if w >= 1 else 4
                p = k % 2
                mx = P.op("vector", lambda h, p_=p: h.tensor_reduce(
                    mrow[:, 4 * p_:4 * p_ + 1], ps[2 * p_][:, 0:512], axis=AX.X,
                    op=OP.max, negate=True), wait=[(P.c["t"], st_grp[k])], inc=(njt == 4))
                if njt == 8:
                    P.op("vector", lambda h, p_=p: h.tensor_reduce(
                        mrow[:, 4 * p_ + 1:4 * p_ + 2], ps[2 * p_ + 1][:, 0:512],
                        axis=AX.X, op=OP.max, negate=True))
                    mx = P.op("vector", lambda h, p_=p: h.tensor_tensor(
                        mrow[:, 4 * p_:4 * p_ + 1], mrow[:, 4 * p_:4 * p_ + 1],
                        mrow[:, 4 * p_ + 1:4 * p_ + 2], op=OP.min), inc=True)
                st_mx[k] = mx
                ev = None
                for jh in range(2 if njt == 8 else 1):
                    ev = P.op("scalar", lambda h, p_=p, jh_=jh: h.activation(
                        probsb[:, p_ * 1024 + jh_ * 512:p_ * 1024 + jh_ * 512 + 512],
                        ps[2 * p_ + jh_][:, 0:512],
                        AF.Exp, bias=mrow[:, 4 * p_:4 * p_ + 1], scale=1.0),
                        wait=([(P.c["v"], mx)] if jh == 0 else ()), inc=True)
                st_exp[k] = ev

            def stage_DE(k):
                w, hp, it, j = iters[k]
                njt = 8 if w >= 1 else 4
                p = k % 2
                tr = None
                for jc in range(njt):
                    tr = P.op("tensor", lambda h, p_=p, jc_=jc: h.transpose(
                        ps[2 * p_][:].bitcast(BF16)[:, jc_ * 128:(jc_ + 1) * 128],
                        probsb[:, p_ * 1024 + jc_ * 128:p_ * 1024 + (jc_ + 1) * 128],
                        self.ident_b[:]),
                        wait=([(P.c["s"], st_exp[k])] if jc == 0 else ()),
                        inc=(jc == njt - 1))
                st_trE[k] = P.op("vector", lambda h, p_=p, n_=njt * 128: h.tensor_copy(
                    probsT[:, p_ * 1024:p_ * 1024 + n_],
                    ps[2 * p_][:].bitcast(BF16)[:, 0:n_]),
                    wait=[(P.c["t"], tr)], inc=True, drain=False)

            def stage_FG(k):
                w, hp, it, j = iters[k]
                njt = 8 if w >= 1 else 4
                p = k % 2
                jlo = max(0, (w - 1) * 4)
                avv = None
                for jc in range(njt):
                    jg = jlo + jc
                    vb0 = jg * 520 + 65 * (2 * hp + j)
                    tw = []
                    if jc == 0:
                        tw = [(P.c["v"], st_trE[k])]
                        if it == 0:
                            tw.append((P.c["v"], prevG[j]))
                    if j == 0:
                        avv = P.op("tensor", lambda h, jc_=jc, b_=vb0, p_=p, it_=it,
                                   st_=(jc == 0), sp_=(jc == njt - 1): h.matmul(
                            ps[6][0:65, it_ * 128:(it_ + 1) * 128],
                            vloc[:, b_:b_ + 65],
                            probsT[:, p_ * 1024 + jc_ * 128:p_ * 1024 + (jc_ + 1) * 128],
                            start=st_, stop=sp_), wait=tw, inc=(jc == njt - 1))
                    else:
                        P.op("tensor", lambda h, jc_=jc, b_=vb0, p_=p, it_=it,
                             st_=(jc == 0), sp_=(jc == njt - 1): h.matmul(
                            ps[7][64:128, it_ * 128:(it_ + 1) * 128],
                            vloc[:, b_:b_ + 64],
                            probsT[:, p_ * 1024 + jc_ * 128:p_ * 1024 + (jc_ + 1) * 128],
                            start=st_, stop=sp_), wait=tw)
                        avv = P.op("tensor", lambda h, jc_=jc, b_=vb0, p_=p, it_=it,
                                   st_=(jc == 0), sp_=(jc == njt - 1): h.matmul(
                            ps[4][0:1, it_ * 128:(it_ + 1) * 128],
                            vloc[:, b_ + 64:b_ + 65],
                            probsT[:, p_ * 1024 + jc_ * 128:p_ * 1024 + (jc_ + 1) * 128],
                            start=st_, stop=sp_), inc=(jc == njt - 1))
                if it < 3 or j == 0:
                    return
                pr_ = hp
                rre = self.rrow[64:65, 0:512]
                rro = self.rrow[32:33, 0:512]
                rv = P.op("vector", lambda h, r_=rre: h.reciprocal(
                    r_, ps[6][64:65, 0:512]), wait=[(P.c["t"], avv)], inc=True)
                rv2 = P.op("vector", lambda h, r_=rro: h.reciprocal(
                    r_, ps[4][0:1, 0:512]), inc=True)
                bce = P.op("tensor", lambda h, r_=rre: h.matmul(
                    ps[4][:, 0:512], self.onesP[64:65, :], r_, start=True, stop=True),
                    wait=[(P.c["v"], rv2)], inc=True)
                cpe = P.op("vector", lambda h: h.tensor_copy(
                    self.bcb[0:64, 0:512], ps[4][0:64, 0:512]),
                    wait=[(P.c["t"], bce)], inc=True)
                prevG[0] = P.op("vector", lambda h, pr2=pr_: h.tensor_tensor(
                    obuf[0:64, pr2 * 512:(pr2 + 1) * 512],
                    ps[6][0:64, 0:512], self.bcb[0:64, 0:512], op=OP.mult),
                    inc=True)
                bco = P.op("tensor", lambda h, r_=rro: h.matmul(
                    ps[4][:, 0:512], self.onesP[32:33, :], r_, start=True, stop=True),
                    wait=[(P.c["v"], cpe)], inc=True)
                P.op("vector", lambda h: h.tensor_copy(
                    self.bcb[64:128, 0:512], ps[4][64:128, 0:512]),
                    wait=[(P.c["t"], bco)])
                prevG[1] = st_evn[k] = P.op("vector", lambda h, pr2=pr_: h.tensor_tensor(
                    obuf[64:128, pr2 * 512:(pr2 + 1) * 512],
                    ps[7][64:128, 0:512], self.bcb[64:128, 0:512], op=OP.mult),
                    inc=True)

            def wo_block(w):
                for c in range(NCH):
                    wo_mm = None
                    for ic in range(4):
                        tw = []
                        if c == 0 and ic == 0:
                            tw = [(P.c["v"], st_evn[last_of_w[w]]), (ld[w % 2], lvx[w])]
                            if w == FFN_W[0]:
                                tw.append((cwo, wvo))
                            if (w - 1, NCH - 1) in xst:
                                tw.append((P.c["v"], xst[(w - 1, NCH - 1)]))
                        if ic == 0 and (w, c - 1) in xst:
                            tw.append((P.c["v"], xst[(w, c - 1)]))
                        wo_mm = P.op("tensor", lambda h, c_=c, ic_=ic: h.matmul(
                            ps[5][:, 0:512],
                            wo_sb[:, ic_ * 1024 + c_ * 128:ic_ * 1024 + (c_ + 1) * 128],
                            obuf[:, ic_ * 512:(ic_ + 1) * 512],
                            start=(ic_ == 0), stop=(ic_ == 3)),
                            wait=tw, inc=(ic == 3))
                    xst[(w, c)] = P.op("vector", lambda h, c_=c: h.tensor_tensor(
                        xmb[:, c_ * TB:(c_ + 1) * TB], xmb[:, c_ * TB:(c_ + 1) * TB],
                        ps[5][:, 0:512], op=OP.add), wait=[(P.c["t"], wo_mm)], inc=True)
                xm_store[w] = P.dma("sync", ost[w % 2], fm_view(sc["xmid"], w * TB, (w + 1) * TB),
                                    sb3(xmb), wait=[(P.c["v"], xst[(w, NCH - 1)])])
                if w != FFN_W[-1]:
                    wn = w + 1
                    lvx[wn] = P.dma("sync", ld[wn % 2], sb3(xmb),
                                    fm_view(src_f, wn * TB, (wn + 1) * TB),
                                    wait=[(P.c["v"], xst[(w, NCH - 1)]),
                                          (ost[w % 2], xm_store[w])])

            lvx[FFN_W[0]] = P.dma("sync", ld[FFN_W[0] % 2], sb3(xmb),
                                  fm_view(src_f, FFN_W[0] * TB, (FFN_W[0] + 1) * TB))
            for k in range(K):
                w, hp, it, j = iters[k]
                if it == 0 and j == 0:
                    kz_build(w, hp)
                stage_A(k)
                if k >= 1:
                    stage_DE(k - 1)
                if k >= 2:
                    stage_FG(k - 2)
                    if last_of_w.get(iters[k - 2][0]) == k - 2:
                        wo_block(iters[k - 2][0])
                stage_BC(k)
            stage_DE(K - 1)
            stage_FG(K - 2)
            stage_FG(K - 1)
            wo_block(iters[K - 1][0])
        return

    def ffn_subC(self, l, src_f, dst_f, dst_bf, onc, gonc):
        nc, P, sc = self.nc, self.P, self.sc
        ps = self.ps
        NJ = 22
        with ExitStack() as ph:
            sb = lambda n, shp, dt: self.sbuf(f"{n}_{l}", shp, dt, ph)
            w1_sb = sb("c_w1", [128, 8 * 5632], BF16)
            w2_sb = sb("c_w2", [128, 22 * 1024], BF16)
            xms = sb("c_xms", [128, 8 * 513], F32)
            xln = sb("c_xln", [128, NCH * TB], BF16)
            hbuf = sb("c_hbuf", [128, NJ * 512], BF16)
            xsh = hbuf[:][:, 0:NCH * TB]
            xsq = hbuf[:][:, NCH * TB:2 * NCH * TB]
            gel = sb("c_gel", [128, TB], BF16)
            gvec = sb("c_gvec", [128, 3 * NCH], F32)
            b1v = sb("c_b1v", [128, 44], F32)
            ostg = sb("c_ostg", [128, 2 * TB], F32)
            obf = sb("c_obf", [128, TB], BF16)
            ld = P.dpool(f"c_ld{l}", 2)
            ost = P.dpool(f"c_ost{l}", 2)
            cw1 = P.ctr(f"c_w1{l}")
            cw2 = P.ctr(f"c_w2{l}")
            gw1 = P.dma("gpsimd", cw1, w1_sb[:].rearrange("p (kc n) -> p kc n", n=5632),
                        self.inp["w1f"][l].rearrange("(kc p) n -> p kc n", p=128))
            gw2 = P.dma("gpsimd", cw2, w2_sb[:].rearrange("p (kc n) -> p kc n", n=1024),
                        self.inp["w2f"][l].rearrange("(kc p) n -> p kc n", p=128))
            lv0 = P.dma("sync", onc, gvec[:, 0:NCH], self.inp["ln_g"][l])
            lv0 = P.dma("sync", onc, gvec[:, NCH:2 * NCH], self.inp["ln_b"][l])
            lv0 = P.dma("sync", onc, gvec[:, 2 * NCH:3 * NCH], self.inp["b2f"][l])
            lv0 = P.dma("sync", onc, b1v[:], self.inp["b1f"][l])
            gz = P.op("gpsimd", lambda h: h.memset(
                bass_rust.AP(xms[:].tensor, 0, [[8 * 513, 128], [513, 8], [1, 1]]), 0.0),
                inc=True)
            stt = self.st_tmp
            t0, t1, t2, t3 = (stt[0:1, k * TB:(k + 1) * TB] for k in range(4))
            mm_last = 0
            st_last = {}
            casts = {}
            for w in range(NT):
                pr = w % 2
                wvl = ([(P.c["v"], casts[w - 1]), (P.c["v"], mm_last)]
                       if w >= 1 else [(P.c["g"], gz)])
                if w == 0:
                    lds = bass_rust.AP(xms[:].tensor, 1, [[8 * 513, 128], [513, 8], [1, TB]])
                    lsrc = fm_view(sc["xmid"], 0, TB)
                else:
                    lds = bass_rust.AP(xms[:].tensor, 0, [[8 * 513, 128], [513, 8], [1, TB + 1]])
                    lsrc = fm_view(sc["xmid"], w * TB - 1, w * TB + TB)
                lv = P.dma("sync", ld[pr], lds, lsrc, wait=wvl)
                for c in range(NCH):
                    sh = 1 if c < 4 else 0
                    casts[w] = P.op("vector", lambda h, c_=c, s_=sh: h.tensor_copy(
                        xsh[:, c_ * TB:(c_ + 1) * TB], xms[:, c_ * 513 + s_:c_ * 513 + s_ + TB]),
                        wait=([(ld[pr], lv)] + ([(P.c["v"], mm_last)] if mm_last else []))
                        if c == 0 else (), inc=(c == NCH - 1), drain=(c == NCH - 1))
                sq = P.op("vector", lambda h: h.tensor_tensor(
                    xsq[:], xsh[:], xsh[:], op=OP.mult), inc=True)
                for c in range(NCH):
                    P.op("tensor", lambda h, c_=c: h.matmul(
                        ps[0][0:1, 0:TB], self.ones_col_b[:], xsh[:, c_ * TB:(c_ + 1) * TB],
                        start=(c_ == 0), stop=(c_ == NCH - 1)),
                        wait=[(P.c["v"], casts[w])] if c == 0 else ())
                sums = None
                for c in range(NCH):
                    sums = P.op("tensor", lambda h, c_=c: h.matmul(
                        ps[1][0:1, 0:TB], self.ones_col_b[:], xsq[:, c_ * TB:(c_ + 1) * TB],
                        start=(c_ == 0), stop=(c_ == NCH - 1)),
                        wait=[(P.c["v"], sq)] if c == 0 else (), inc=(c == NCH - 1))
                mu_b = self.st_mu[0:1, pr * TB:(pr + 1) * TB]
                r_b = self.st_r[0:1, pr * TB:(pr + 1) * TB]
                P.op("vector", lambda h, m_=mu_b: h.tensor_scalar(
                    m_, ps[0][0:1, 0:TB], 1.0 / DM, None, op0=OP.mult),
                    wait=[(P.c["t"], sums)])
                P.op("vector", lambda h: h.tensor_scalar(
                    t0, ps[1][0:1, 0:TB], 1.0 / DM, None, op0=OP.mult))
                P.op("vector", lambda h, m_=mu_b: h.tensor_tensor(t1, m_, m_, op=OP.mult))
                P.op("vector", lambda h: h.scalar_tensor_tensor(
                    t2, t0, 1.0, t1, op0=OP.mult, op1=OP.subtract))
                vvar = P.op("vector", lambda h: h.tensor_scalar(
                    t2, t2, 1e-5, None, op0=OP.add), inc=True)
                vr = self.rsqrt_chain((t0, t1, t3), t2, r_b, [(P.c["v"], vvar)])
                bmu = P.op("tensor", lambda h, m_=mu_b: h.matmul(
                    ps[2][:, 0:TB], self.ones_f[:], m_, start=True, stop=True),
                    wait=[(P.c["v"], vr)])
                brr = P.op("tensor", lambda h, r_=r_b: h.matmul(
                    ps[3][:, 0:TB], self.ones_f[:], r_, start=True, stop=True), inc=True)
                ap_d = None
                for c in range(NCH):
                    sh = 1 if c < 4 else 0
                    aw0 = []
                    if c == 0:
                        aw0 = [(P.c["t"], brr)]
                        if st_last.get("prev"):
                            aw0.append((ost[st_last["prevpr"]], st_last["prev"]))
                    P.op("vector", lambda h, c_=c, s_=sh: h.tensor_tensor(
                        ostg[:, 0:TB], xms[:, c_ * 513 + s_:c_ * 513 + s_ + TB], ps[2][:, 0:TB],
                        op=OP.subtract),
                        wait=aw0)
                    P.op("vector", lambda h: h.tensor_tensor(
                        ostg[:, 0:TB], ostg[:, 0:TB], ps[3][:, 0:TB], op=OP.mult))
                    ap_d = P.op("vector", lambda h, c_=c: h.tensor_scalar(
                        xln[:, c_ * TB:(c_ + 1) * TB], ostg[:, 0:TB],
                        gvec[:, c_:c_ + 1], gvec[:, NCH + c_:NCH + c_ + 1],
                        op0=OP.mult, op1=OP.add),
                        wait=[(onc, lv0)] if (w == 0 and c == 0) else (), inc=(c == NCH - 1))
                glu_at = {}
                sgl_at = {}
                glu_d = None
                for j in range(NJ):
                    jp = j % 2
                    pa_ = ps[4] if jp == 0 else ps[0]
                    pg_ = ps[5] if jp == 0 else ps[1]
                    tw = []
                    if j == 0:
                        tw = [(P.c["v"], ap_d)]
                        if w == 0:
                            tw.append((cw1, gw1))
                    elif j == 1:
                        tw = [(P.c["v"], vr)]
                    else:
                        tw = [(P.c["v"], glu_at[j - 2])]
                    for kc in range(NCH):
                        P.op("tensor", lambda h, j_=j, kc_=kc, p_=pa_: h.matmul(
                            p_[:, 0:512],
                            w1_sb[:, kc_ * 5632 + j_ * 128:kc_ * 5632 + (j_ + 1) * 128],
                            xln[:, kc_ * TB:(kc_ + 1) * TB],
                            start=(kc_ == 0), stop=(kc_ == NCH - 1)),
                            wait=(tw if kc == 0 else ()))
                    gmm = None
                    for kc in range(NCH):
                        gmm = P.op("tensor", lambda h, j_=j, kc_=kc, p_=pg_: h.matmul(
                            p_[:, 0:512],
                            w1_sb[:, kc_ * 5632 + 2816 + j_ * 128:kc_ * 5632 + 2816 + (j_ + 1) * 128],
                            xln[:, kc_ * TB:(kc_ + 1) * TB],
                            start=(kc_ == 0), stop=(kc_ == NCH - 1)),
                            wait=([(P.c["s"], sgl_at[j - 2])] if j >= 2 and kc == 0 else ()),
                            inc=(kc == NCH - 1))
                    sgl_at[j] = P.op("scalar", lambda h, j_=j, p_=pg_: h.activation(
                        gel[:], p_[:, 0:512], AF.Gelu,
                        bias=b1v[:, 22 + j_:22 + j_ + 1], scale=1.0),
                        wait=[(P.c["t"], gmm)] + ([(P.c["v"], glu_at[j - 1])] if j >= 1 else []),
                        inc=True)
                    glu_d = glu_at[j] = P.op("vector", lambda h, j_=j, p_=pa_: h.scalar_tensor_tensor(
                        hbuf[:, j_ * 512:(j_ + 1) * 512], p_[:, 0:512],
                        b1v[:, j_:j_ + 1], gel[:],
                        op0=OP.add, op1=OP.mult),
                        wait=[(P.c["s"], sgl_at[j])], inc=True)
                evl = {}
                for c in range(NCH):
                    cp = c % 2
                    pw2 = ps[6] if cp == 0 else ps[7]
                    osl = ostg[:, cp * TB:(cp + 1) * TB]
                    w2m = None
                    for j in range(NJ):
                        tw = []
                        if j == 0:
                            if c <= 1:
                                tw = [(P.c["v"], glu_d)]
                                if w == 0 and c == 0:
                                    tw.append((cw2, gw2))
                            else:
                                tw = [(P.c["v"], evl[c - 2])]
                        w2m = P.op("tensor", lambda h, c_=c, j_=j, p_=pw2: h.matmul(
                            p_[:, 0:512],
                            w2_sb[:, j_ * 1024 + c_ * 128:j_ * 1024 + (c_ + 1) * 128],
                            hbuf[:, j_ * 512:(j_ + 1) * 512],
                            start=(j_ == 0), stop=(j_ == NJ - 1)),
                            wait=tw, inc=(j == NJ - 1))
                    vw2 = [(P.c["t"], w2m)]
                    if st_last.get("prev"):
                        vw2.append((ost[st_last["prevpr"]], st_last["prev"]))
                    if (f"cbf{cp}") in st_last:
                        vw2.append((P.c["s"], st_last[f"cbf{cp}"]))
                    vv = P.op("vector", lambda h, c_=c, p_=pw2, o_=osl: h.tensor_scalar(
                        o_, p_[:, 0:512], gvec[:, 2 * NCH + c_:2 * NCH + c_ + 1],
                        None, op0=OP.add), wait=vw2)
                    mm_last = evl[c] = P.op("vector", lambda h, c_=c, o_=osl: h.tensor_tensor(
                        o_, o_, xms[:, c_ * 513 + 1:c_ * 513 + 1 + TB], op=OP.add),
                        inc=True)
                    st1 = P.dma("sync", ost[pr], fm_view(dst_f, w * TB, (w + 1) * TB)[:, c:c + 1, :],
                                osl.unsqueeze(1), wait=[(P.c["v"], mm_last)])
                    if dst_bf is not None:
                        cbf = P.op("scalar", lambda h, o_=osl: h.activation(
                            obf[:], o_, AF.Copy),
                            wait=[(P.c["v"], mm_last)], inc=True)
                        st1 = P.dma("sync", ost[pr],
                                    fm_view(dst_bf, w * TB, (w + 1) * TB)[:, c:c + 1, :],
                                    obf[:].unsqueeze(1), wait=[(P.c["s"], cbf)])
                        st_last[f"cbf{cp}"] = cbf
                    st_last["prev"] = st1
                    st_last["prevpr"] = pr
        return

    def phase_p4(self):
        nc, P, sc = self.nc, self.P, self.sc
        ps = self.ps
        with ExitStack() as ph:
            sb = lambda n, shp, dt: self.sbuf(n, shp, dt, ph)
            xb = [sb(f"p4_xb{i}", [128, NCH * TB], F32) for i in range(2)]
            ag = sb("p4_ag", [128, NCH * TB], F32)
            af = sb("p4_af", [128, NCH * TB], F32)
            xbb = sb("p4_xbb", [128, NCH * TB], BF16)
            xsq = sb("p4_xsq", [128, NCH * TB], BF16)
            xout = [sb(f"p4_xo{i}", [128, NCH * TB], BF16) for i in range(2)]
            xof = [sb(f"p4_xof{i}", [128, NCH * TB], F32) for i in range(2)]
            h2o = [sb(f"p4_h2{i}", [128, NCH * TB], BF16) for i in range(2)]
            tmp = sb("p4_tmp", [128, NCH * TB], F32)
            rows = sb("p4_rows", [1, 4 * TB], F32)
            gam = sb("p4_gam", [128, 2 * NCH], F32)
            ld = P.dpool("p4_ld", 2)
            sto = P.dpool("p4_st", 2)
            gst = P.dpool("p4_gst", 2)
            onc = P.ctr("p4_onc")
            gw = P.dma("sync", onc, gam[:, 0:NCH], self.inp["gamma"])
            gw = P.dma("sync", onc, gam[:, NCH:2 * NCH], self.inp["beta"])
            stt = self.st_tmp
            t0, t1, t2, t3 = (stt[0:1, k * TB:(k + 1) * TB] for k in range(4))
            mu1, r1 = rows[0:1, 0:TB], rows[0:1, TB:2 * TB]
            mu2, r2 = rows[0:1, 2 * TB:3 * TB], rows[0:1, 3 * TB:4 * TB]

            def bc3(psrc):
                return bass_rust.AP(psrc[:].tensor, 0, [[512, 128], [0, NCH], [1, TB]])

            def gcol(off):
                return bass_rust.AP(gam[:].tensor, off, [[2 * NCH, 128], [1, NCH], [0, TB]])
            x3 = lambda buf: buf[:].rearrange("p (c t) -> p c t", t=TB)
            (loads, casts, sums1, ap1_d, ap2_d, st_d, gst_d, vsum_d,
             s2d) = ({} for _ in range(9))
            for i in range(NT):
                pr = i % 2
                pA, pB = ps[2 * pr], ps[2 * pr + 1]
                wv = []
                if i >= 2:
                    wv = [(P.c["v"], ap2_d[i - 2]), (sto[pr], st_d[i - 2]),
                          (gst[pr], gst_d[i - 2])]
                loads[i] = P.dma("sync", ld[pr], x3(xb[pr]),
                                 fm_view(self.inp["xT"], i * TB, (i + 1) * TB), wait=wv)
                aw = [(P.c["v"], vsum_d[i - 1])] if i >= 1 else []
                loads[i] = P.dma("sync", ld[pr], x3(ag),
                                 fm_view(sc["glu"], i * TB, (i + 1) * TB), wait=aw)
                loads[i] = P.dma("sync", ld[pr], x3(af),
                                 fm_view(sc["ffno"], i * TB, (i + 1) * TB))
                P.op("vector", lambda h, p_=pr: h.tensor_tensor(
                    xb[p_][:], xb[p_][:], ag[:], op=OP.add),
                    wait=[(ld[pr], loads[i])])
                vsum_d[i] = P.op("vector", lambda h, p_=pr: h.tensor_tensor(
                    xb[p_][:], xb[p_][:], af[:], op=OP.add), inc=True)
                gst_d[i] = P.dma("gpsimd", gst[pr], fm_view(sc["x1"], i * TB, (i + 1) * TB),
                                 x3(xb[pr]), wait=[(P.c["v"], vsum_d[i])])
                casts[i] = P.op("vector", lambda h, p_=pr: h.tensor_copy(
                    xbb[:], xb[p_][:]),
                    wait=[(P.c["t"], s2d[i - 1])] if i else [], inc=True)
                sq1 = P.op("vector", lambda h: h.tensor_tensor(
                    xsq[:], xbb[:], xbb[:], op=OP.mult), inc=True)
                for c in range(NCH):
                    P.op("tensor", lambda h, c_=c, p_=pA: h.matmul(
                        p_[0:1, 0:TB], self.ones_col_b[:], xbb[:, c_ * TB:(c_ + 1) * TB],
                        start=(c_ == 0), stop=(c_ == NCH - 1)),
                        wait=[(P.c["v"], casts[i])] if c == 0 else ())
                for c in range(NCH):
                    sums1[i] = P.op("tensor", lambda h, c_=c, p_=pB: h.matmul(
                        p_[0:1, 0:TB], self.ones_col_b[:], xsq[:, c_ * TB:(c_ + 1) * TB],
                        start=(c_ == 0), stop=(c_ == NCH - 1)),
                        wait=[(P.c["v"], sq1)] if c == 0 else (), inc=(c == NCH - 1))
                P.op("vector", lambda h, p_=pA: h.tensor_scalar(
                    mu1, p_[0:1, 0:TB], 1.0 / DM, None, op0=OP.mult),
                    wait=[(P.c["t"], sums1[i])])
                P.op("vector", lambda h, p_=pB: h.tensor_scalar(
                    t0, p_[0:1, 0:TB], 1.0 / DM, None, op0=OP.mult))
                P.op("vector", lambda h: h.tensor_tensor(t1, mu1, mu1, op=OP.mult))
                P.op("vector", lambda h: h.scalar_tensor_tensor(
                    t2, t0, 1.0, t1, op0=OP.mult, op1=OP.subtract))
                vvar = P.op("vector", lambda h: h.tensor_scalar(
                    t2, t2, 1e-5, None, op0=OP.add), inc=True)
                vr1 = self.rsqrt_chain((t0, t1, t3), t2, r1, [(P.c["v"], vvar)])
                P.op("tensor", lambda h: h.matmul(
                    ps[4][:, 0:TB], self.ones_f[:], mu1, start=True, stop=True),
                    wait=[(P.c["v"], vr1)])
                bc1 = P.op("tensor", lambda h: h.matmul(
                    ps[5][:, 0:TB], self.ones_f[:], r1, start=True, stop=True), inc=True)
                P.op("vector", lambda h, p_=pr: h.tensor_tensor(
                    tmp[:].rearrange("p (c t) -> p c t", t=TB), x3(xb[p_]), bc3(ps[4]),
                    op=OP.subtract), wait=[(P.c["t"], bc1)])
                apf = P.op("vector", lambda h, p_=pr: h.tensor_tensor(
                    xof[p_][:].rearrange("p (c t) -> p c t", t=TB),
                    tmp[:].rearrange("p (c t) -> p c t", t=TB), bc3(ps[5]),
                    op=OP.mult), inc=True)
                ap1_d[i] = P.op("vector", lambda h, p_=pr: h.tensor_copy(
                    xout[p_][:], xof[p_][:]), inc=True)
                st_d[i] = P.dma("sync", sto[pr], fm_view(sc["xn2f"], i * TB, (i + 1) * TB),
                                x3(xof[pr]), wait=[(P.c["v"], apf)])
                gst_d[i] = P.dma("gpsimd", gst[pr], fm_view(sc["xn2"], i * TB, (i + 1) * TB),
                                 x3(xout[pr]), wait=[(P.c["v"], ap1_d[i])])
                sq2 = P.op("vector", lambda h, p_=pr: h.tensor_tensor(
                    xsq[:], xout[p_][:], xout[p_][:], op=OP.mult),
                    wait=[(P.c["t"], sums1[i])], inc=True)
                for c in range(NCH):
                    P.op("tensor", lambda h, c_=c, p_=pA, pp_=pr: h.matmul(
                        p_[0:1, 0:TB], self.ones_col_b[:], xout[pp_][:, c_ * TB:(c_ + 1) * TB],
                        start=(c_ == 0), stop=(c_ == NCH - 1)),
                        wait=[(P.c["v"], vr1), (P.c["v"], ap1_d[i])] if c == 0 else ())
                sums2 = None
                for c in range(NCH):
                    sums2 = P.op("tensor", lambda h, c_=c, p_=pB: h.matmul(
                        p_[0:1, 0:TB], self.ones_col_b[:], xsq[:, c_ * TB:(c_ + 1) * TB],
                        start=(c_ == 0), stop=(c_ == NCH - 1)),
                        wait=[(P.c["v"], sq2)] if c == 0 else (), inc=(c == NCH - 1))
                s2d[i] = sums2
                P.op("vector", lambda h, p_=pA: h.tensor_scalar(
                    mu2, p_[0:1, 0:TB], 1.0 / DM, None, op0=OP.mult),
                    wait=[(P.c["t"], sums2)])
                P.op("vector", lambda h, p_=pB: h.tensor_scalar(
                    t0, p_[0:1, 0:TB], 1.0 / DM, None, op0=OP.mult))
                P.op("vector", lambda h: h.tensor_tensor(t1, mu2, mu2, op=OP.mult))
                P.op("vector", lambda h: h.scalar_tensor_tensor(
                    t2, t0, 1.0, t1, op0=OP.mult, op1=OP.subtract))
                vvar2 = P.op("vector", lambda h: h.tensor_scalar(
                    t2, t2, 1e-5, None, op0=OP.add), inc=True)
                vr2 = self.rsqrt_chain((t0, t1, t3), t2, r2, [(P.c["v"], vvar2)])
                P.op("tensor", lambda h: h.matmul(
                    ps[6][:, 0:TB], self.ones_f[:], mu2, start=True, stop=True),
                    wait=[(P.c["v"], vr2)])
                bc2 = P.op("tensor", lambda h: h.matmul(
                    ps[7][:, 0:TB], self.ones_f[:], r2, start=True, stop=True), inc=True)
                P.op("vector", lambda h, p_=pr: h.tensor_tensor(
                    tmp[:].rearrange("p (c t) -> p c t", t=TB), x3(xof[p_]), bc3(ps[6]),
                    op=OP.subtract),
                    wait=[(P.c["t"], bc2)] + ([(onc, gw)] if i == 0 else []))
                P.op("vector", lambda h: h.tensor_tensor(
                    tmp[:].rearrange("p (c t) -> p c t", t=TB),
                    tmp[:].rearrange("p (c t) -> p c t", t=TB), bc3(ps[7]), op=OP.mult))
                P.op("vector", lambda h: h.tensor_tensor(
                    tmp[:].rearrange("p (c t) -> p c t", t=TB),
                    tmp[:].rearrange("p (c t) -> p c t", t=TB), gcol(0), op=OP.mult))
                ap2_d[i] = P.op("vector", lambda h, p_=pr: h.tensor_tensor(
                    h2o[p_][:].rearrange("p (c t) -> p c t", t=TB),
                    tmp[:].rearrange("p (c t) -> p c t", t=TB), gcol(NCH), op=OP.add),
                    inc=True)
                st_d[i] = P.dma("sync", sto[pr], fm_view(sc["h2"], i * TB, (i + 1) * TB),
                                x3(h2o[pr]), wait=[(P.c["v"], ap2_d[i])])

    def phase_l1_attn(self):
        nc, P, sc = self.nc, self.P, self.sc
        ps = self.ps
        with ExitStack() as ph:
            sb = lambda n, shp, dt: self.sbuf(n, shp, dt, ph)
            Wk_sb = sb("a1_wk", [128, 8 * 1024], BF16)
            Wv_sb = sb("a1_wv", [128, 8 * 1024], BF16)
            Wka_sb = sb("a1_wka", [128, 8 * 1024], BF16)
            bcol = sb("a1_bcol", [128, 3 * NCH], F32)
            h2b = [sb(f"a1_h2{j}", [128, NCH * TB], BF16) for j in range(2)]
            kf32 = sb("a1_kf", [128, NCH * TB], F32)
            kbf = sb("a1_kbf", [128, NCH * TB], BF16)
            vbuf = sb("a1_vb", [128, NCH * TB], F32)
            kwb = sb("a1_kw", [128, NCH * TB], F32)
            cwb = sb("a1_cw", [128, NCH * TB], F32)
            pkb = sb("a1_pk", [128, NCH * TB], F32)
            ones5 = sb("a1_ones", [128, TB], F32)
            carry = sb("a1_carry", [128, 2 * NCH], F32)
            ostage = sb("a1_ost", [128, NCH * TB], F32)
            ld = P.dpool("a1_ld", 2)
            ost = P.dpool("a1_ost", 2)
            onc = P.ctr("a1_onc")
            g1 = P.ctr("a1_g1")
            for nm, dst in (("WkT", Wk_sb), ("WvT", Wv_sb), ("WkaT", Wka_sb)):
                wv1 = P.dma("gpsimd", g1, dst[:].rearrange("p (a b) -> p a b", b=1024),
                            self.inp[nm].rearrange("(a p) c -> p a c", p=128))
            for r in range(3):
                bv0 = P.dma("sync", onc, bcol[:, r * NCH:(r + 1) * NCH],
                            self.inp["bkv"][r])
            vb0 = P.op("vector", lambda h: h.tensor_scalar(
                bcol[:, NCH:2 * NCH], bcol[:, NCH:2 * NCH], 1.0 / 32.0, None,
                op0=OP.mult), wait=[(onc, bv0)], inc=True)
            P.op("gpsimd", lambda h: h.memset(ones5[:], 1.0), inc=True)
            gz = P.op("gpsimd", lambda h: h.memset(carry[:], 0.0), inc=True)
            ev_k = ev_v = ev_a = 0
            mmd = vlast = cast_last = 0
            tA, stored = {}, {}
            for i in range(NT):
                pr = i % 2
                wvl = [(P.c["t"], tA[i - 2])] if i >= 2 else []
                lv = P.dma("sync", ld[pr], sb3(h2b[pr]),
                           fm_view(sc["h2"], i * TB, (i + 1) * TB), wait=wvl)
                for c in range(NCH):
                    pk_ = ps[c % 2]
                    tw = []
                    if c == 0:
                        tw = [(ld[pr], lv)]
                        if i == 0:
                            tw += [(g1, wv1), (P.c["v"], vb0)]
                    if c >= 2 or i > 0:
                        tw.append((P.c["v"], ev_k))
                    for kc in range(NCH):
                        mmd = P.op("tensor", lambda h, c_=c, kc_=kc, p_=pr, s_=pk_: h.matmul(
                            s_[:, 0:512],
                            Wk_sb[:, kc_ * 1024 + c_ * 128:kc_ * 1024 + (c_ + 1) * 128],
                            h2b[p_][:, kc_ * TB:(kc_ + 1) * TB],
                            start=(kc_ == 0), stop=(kc_ == NCH - 1)),
                            wait=(tw if kc == 0 else ()), inc=(kc == NCH - 1))
                    ev_k = P.op("vector", lambda h, c_=c, s_=pk_: h.tensor_scalar(
                        kf32[:, c_ * TB:(c_ + 1) * TB], s_[:, 0:512],
                        bcol[:, c_:c_ + 1], None, op0=OP.add),
                        wait=[(P.c["t"], mmd)], inc=True)
                    cast_last = P.op("vector", lambda h, c_=c: h.tensor_copy(
                        kbf[:, c_ * TB:(c_ + 1) * TB], kf32[:, c_ * TB:(c_ + 1) * TB]),
                        inc=True)
                for c in range(NCH):
                    pv_ = ps[2 + c % 2]
                    tw = []
                    if c >= 2 or i > 0:
                        tw.append((P.c["v"], ev_v))
                    for kc in range(NCH):
                        mmd = P.op("tensor", lambda h, c_=c, kc_=kc, p_=pr, s_=pv_: h.matmul(
                            s_[:, 0:512],
                            Wv_sb[:, kc_ * 1024 + c_ * 128:kc_ * 1024 + (c_ + 1) * 128],
                            h2b[p_][:, kc_ * TB:(kc_ + 1) * TB],
                            start=(kc_ == 0), stop=(kc_ == NCH - 1)),
                            wait=(tw if kc == 0 else ()), inc=(kc == NCH - 1))
                    ev_v = P.op("vector", lambda h, c_=c, s_=pv_: h.tensor_scalar(
                        vbuf[:, c_ * TB:(c_ + 1) * TB], s_[:, 0:512],
                        bcol[:, 2 * NCH + c_:2 * NCH + c_ + 1], None, op0=OP.add),
                        wait=[(P.c["t"], mmd)], inc=True)
                for c in range(NCH):
                    pa_ = ps[4 + c % 2]
                    tw = []
                    if c == 0:
                        tw = [(P.c["v"], cast_last)]
                    if c >= 2 or i > 0:
                        tw.append((P.c["s"], ev_a))
                    for kc in range(NCH):
                        mmd = P.op("tensor", lambda h, c_=c, kc_=kc, s_=pa_: h.matmul(
                            s_[:, 0:512],
                            Wka_sb[:, kc_ * 1024 + c_ * 128:kc_ * 1024 + (c_ + 1) * 128],
                            kbf[:, kc_ * TB:(kc_ + 1) * TB],
                            start=(kc_ == 0), stop=(kc_ == NCH - 1)),
                            wait=(tw if kc == 0 else ()), inc=(kc == NCH - 1))
                    ev_a = P.op("scalar", lambda h, c_=c, s_=pa_: h.activation(
                        kwb[:, c_ * TB:(c_ + 1) * TB], s_[:, 0:512], AF.Exp,
                        bias=bcol[:, NCH + c_:NCH + c_ + 1], scale=1.0 / 32.0),
                        wait=[(P.c["t"], mmd), (P.c["v"], vlast)], inc=True)
                tA[i] = mmd
                ew = [(P.c["s"], ev_a)]
                if i == 0:
                    ew.append((P.c["g"], gz))
                if i >= 1:
                    ew.append((ost[(i - 1) % 2], stored[i - 1]))
                for c in range(NCH):
                    P.op("vector", lambda h, c_=c: h.tensor_tensor_scan(
                        cwb[:, c_ * TB:(c_ + 1) * TB], ones5[:],
                        kwb[:, c_ * TB:(c_ + 1) * TB],
                        carry[:, c_:c_ + 1], op0=OP.mult, op1=OP.add),
                        wait=(ew if c == 0 else ()), drain=(c == NCH - 1))
                for c in range(NCH):
                    P.op("vector", lambda h, c_=c: h.tensor_copy(
                        carry[:, c_:c_ + 1], cwb[:, c_ * TB + TB - 1:c_ * TB + TB]),
                        drain=False)
                P.op("vector", lambda h: h.reciprocal(pkb[:], cwb[:]), drain=True)
                P.op("vector", lambda h: h.tensor_tensor(pkb[:], pkb[:], kwb[:], op=OP.mult),
                     drain=True)
                P.op("vector", lambda h: h.tensor_tensor(pkb[:], pkb[:], kf32[:], op=OP.mult),
                     drain=True)
                for c in range(NCH):
                    P.op("vector", lambda h, c_=c: h.tensor_tensor_scan(
                        cwb[:, c_ * TB:(c_ + 1) * TB], ones5[:],
                        pkb[:, c_ * TB:(c_ + 1) * TB],
                        carry[:, NCH + c_:NCH + c_ + 1], op0=OP.mult, op1=OP.add),
                        drain=(c == NCH - 1))
                for c in range(NCH):
                    P.op("vector", lambda h, c_=c: h.tensor_copy(
                        carry[:, NCH + c_:NCH + c_ + 1],
                        cwb[:, c_ * TB + TB - 1:c_ * TB + TB]), drain=False)
                vlast = P.op("vector", lambda h: h.tensor_tensor(
                    ostage[:], cwb[:], vbuf[:], op=OP.mult), inc=True)
                stored[i] = P.dma("sync", ost[pr], fm_view(sc["acc1"], i * TB, (i + 1) * TB),
                                  sb3(ostage), wait=[(P.c["v"], vlast)])

    def phase_l1_ffn(self):
        nc, P, sc = self.nc, self.P, self.sc
        ps = self.ps
        with ExitStack() as ph:
            sb = lambda n, shp, dt: self.sbuf(n, shp, dt, ph)
            W1s = sb("e_w1s", [128, 8 * 2048], BF16)
            W2s = sb("e_w2s", [128, 16 * 1024], BF16)
            Wo_sb = sb("e_wo", [128, 8 * 1024], BF16)
            wtmp = [sb(f"e_wt{j}", [128, 1024], BF16) for j in range(4)]
            wftmp = [sb(f"e_wtf{j}", [128, 1024], F32) for j in range(2)]
            Ies = sb("e_ies", [128, 8 * 128], BF16)
            lcol = sb("e_lcol", [128, 8], F32)
            Wp1f = sb("e_wp1", [128, 8 * 128], F32)
            Wp2f = sb("e_wp2", [128, 8], F32)
            g1c = sb("e_g1", [128, 1], F32)
            bp1c = sb("e_bp1", [128, 1], F32)
            rows = sb("e_rows", [1, 32], F32)
            lgc = sb("e_lgc", [8, 2], F32)
            ohf = sb("e_ohf", [8, 1], F32)
            OH = sb("e_OH", [128, 8], F32)

            brow = sb("e_brow", [1, 3072], F32)
            b1sc = sb("e_b1s", [128, 16], F32)
            b2sc = sb("e_b2s", [128, 8], F32)
            boc = sb("e_bo", [128, NCH], F32)
            xnb = [sb("e_xnb0", [128, NCH * TB], BF16)] * 2
            rt = [sb(f"e_rt{j}", [128, 3 * TB], F32) for j in range(4)]
            hb = sb("e_hb", [128, 16 * TB], BF16)
            be1f = hb[:].bitcast(F32)[0:8, 0:2048]
            be2f = hb[:].bitcast(F32)[0:8, 2048:3072]
            rbuf = sb("e_rbuf", [128, NCH * TB], BF16)
            otile = [sb(f"e_ot{j}", [128, TB], F32) for j in range(2)]
            ld = P.dpool("e_ld", 2)
            wld = P.dpool("e_wld", 2)
            rld = P.dpool("e_rld", 4)
            onc = P.ctr("e_onc")
            g1 = P.ctr("e_g1c")
            fino = P.dpool("e_fino", 2)
            smv = sc["small"]
            d0 = P.dma("sync", onc, lcol[:].unsqueeze(2),
                       fm_view(sc["xn2f"], S - 1, S), slow_ok=True)
            d0 = P.dma("sync", onc, bp1c[:], self.inp["bp1"])
            d0 = P.dma("sync", onc, boc[:], self.inp["bo1"])
            d0 = P.dma("sync", onc, be1f, self.inp["be1"])
            d0 = P.dma("sync", onc, be2f, self.inp["be2"])
            d0 = P.dma("sync", onc, lgc[0:8, 1:2], self.inp["bp2"].unsqueeze(1),
                       slow_ok=True)
            d0 = P.dma("sync", onc, Wp1f[:].rearrange("p (a b) -> p a b", b=128),
                       self.inp["Wp1"].rearrange("(a p) c -> p a c", p=128))
            d0 = P.dma("sync", onc, Wp2f[:], self.inp["Wp2"])
            gw2 = P.dma("gpsimd", g1, Wo_sb[:].rearrange("p (a b) -> p a b", b=1024),
                        self.inp["WoT1"].rearrange("(a p) c -> p a c", p=128))
            for kc in range(NCH):
                rmm = P.op("tensor", lambda h, kc_=kc: h.matmul(
                    ps[0][:, 0:1], Wp1f[:, kc_ * 128:(kc_ + 1) * 128],
                    lcol[:, kc_:kc_ + 1],
                    start=(kc_ == 0), stop=(kc_ == NCH - 1)),
                    wait=([(onc, d0)] if kc == 0 else ()), inc=(kc == NCH - 1))
            sg1 = P.op("scalar", lambda h: h.activation(
                g1c[:], ps[0][:, 0:1], AF.Gelu, bias=bp1c[:, 0:1], scale=1.0),
                wait=[(P.c["t"], rmm)], inc=True)
            rmm = P.op("tensor", lambda h: h.matmul(
                ps[1][0:8, 0:1], Wp2f[:], g1c[:], start=True, stop=True),
                wait=[(P.c["s"], sg1)], inc=True)
            vl = P.op("vector", lambda h: h.tensor_tensor(
                lgc[0:8, 0:1], ps[1][0:8, 0:1], lgc[0:8, 1:2], op=OP.add),
                wait=[(P.c["t"], rmm)], inc=True)
            tt1 = P.op("tensor", lambda h: h.transpose(
                ps[2][0:1, 0:8], lgc[0:8, 0:1], self.ident_f[0:8, 0:8]),
                wait=[(P.c["v"], vl)], inc=True)
            vl = P.op("vector", lambda h: h.tensor_copy(rows[0:1, 0:8], ps[2][0:1, 0:8]),
                      wait=[(P.c["t"], tt1)], inc=True)
            vl = P.op("vector", lambda h: h.tensor_reduce(
                rows[0:1, 8:9], rows[0:1, 0:8], axis=AX.X, op=OP.max), inc=True)
            vl = P.op("vector", lambda h: h.tensor_scalar(
                rows[0:1, 16:24], rows[0:1, 0:8], rows[0:1, 8:9], None,
                op0=OP.is_ge), inc=True)
            ohv = self.bcast(OH[:], rows[0:1, 16:24], 8, ps[3], wait=[(P.c["v"], vl)])
            tt2 = P.op("tensor", lambda h: h.transpose(
                ps[5][0:8, 0:1], rows[0:1, 16:24], self.ident_f[0:1, 0:1]),
                wait=[(P.c["v"], vl)], inc=True)
            vc2 = P.op("vector", lambda h: h.tensor_copy(ohf[0:8, 0:1], ps[5][0:8, 0:1]),
                       wait=[(P.c["t"], tt2)], inc=True)
            vbe = vc2
            for src, off, nnb in ((be1f, 0, 4), (be2f, 2048, 2)):
                for nb in range(nnb):
                    bm = P.op("tensor", lambda h, s_=src, nb_=nb: h.matmul(
                        ps[4][0:1, 0:512], ohf[0:8, 0:1],
                        s_[0:8, nb_ * 512:(nb_ + 1) * 512], start=True, stop=True),
                        wait=[(P.c["v"], vbe)], inc=True)
                    vbe = P.op("vector", lambda h, o_=off, nb_=nb: h.tensor_copy(
                        brow[0:1, o_ + nb_ * 512:o_ + (nb_ + 1) * 512], ps[4][0:1, 0:512]),
                        wait=[(P.c["t"], bm)], inc=True)
            d3 = P.dma("sync", onc, bass_rust.AP(smv.tensor, 256, [[1, 1], [1, 3072]]),
                       brow[0:1, 0:3072], wait=[(P.c["v"], vbe)])
            d4 = P.dma("sync", onc, b1sc[:],
                       bass_rust.AP(smv.tensor, 256, [[1, 128], [128, 16]]),
                       wait=[(onc, d3)], slow_ok=True)
            d4 = P.dma("sync", onc, b2sc[:],
                       bass_rust.AP(smv.tensor, 256 + 2048, [[1, 128], [128, 8]]),
                       slow_ok=True)
            vie = 0
            for e in range(8):
                vie = P.op("vector", lambda h, e_=e: h.tensor_scalar(
                    Ies[:, e_ * 128:(e_ + 1) * 128], self.ident_b[:], OH[:, e_:e_ + 1],
                    None, op0=OP.mult),
                    wait=([(P.c["v"], ohv)] if e == 0 else ()), inc=True)
            wmv, evv = {}, {}
            gidx = gtile = 0
            w1fin = wfin = 0
            sld = P.dpool("e_sld", 2)
            cast_at = {}
            for wi, (Wsrc, Wdst, nkc, ncols) in enumerate(
                    ((self.inp["We1"], W1s, 8, 2048), (self.inp["We2"], W2s, 16, 1024))):
                for kc in range(nkc):
                    for nb in range(ncols // 1024):
                        col0 = kc * ncols + nb * 1024
                        for e in range(8):
                            j = gidx % 4
                            src_ap = (Wsrc[e].rearrange("(a p) c -> p a c", p=128)
                                      [:, kc, nb * 1024:(nb + 1) * 1024])
                            dw = [(P.c["t"], wmv[gidx - 4])] if gidx >= 4 else []
                            if e % 2 == 0:
                                jg = (gidx // 2) % 2
                                dwe = P.dma("gpsimd", wld[jg], wtmp[j][:], src_ap,
                                            wait=dw)
                                mw = [(wld[jg], dwe)]
                            else:
                                j2 = (gidx // 2) % 2
                                fdw = list(dw)
                                if gidx >= 4:
                                    fdw.append((P.c["s"], cast_at[gidx - 4]))
                                dwe = P.dma("sync", sld[j2], wftmp[j2][:], src_ap,
                                            wait=fdw)
                                cv = P.op("scalar", lambda h, j_=j, j2_=j2: h.activation(
                                    wtmp[j_][:], wftmp[j2_][:], AF.Copy),
                                    wait=[(sld[j2], dwe)], inc=True)
                                cast_at[gidx] = cv
                                mw = [(P.c["s"], cv)]
                            if e == 0:
                                if gtile >= 1:
                                    mw.append((P.c["v"], evv[gtile - 1]))
                                if gtile == 0:
                                    mw.append((P.c["v"], vie))
                            P.op("tensor", lambda h, e_=e, j_=j: h.matmul(
                                ps[6][:, 0:512], Ies[:, e_ * 128:(e_ + 1) * 128],
                                wtmp[j_][:, 0:512],
                                start=(e_ == 0), stop=(e_ == 7)), wait=mw)
                            wmv[gidx] = P.op("tensor", lambda h, e_=e, j_=j: h.matmul(
                                ps[7][:, 0:512], Ies[:, e_ * 128:(e_ + 1) * 128],
                                wtmp[j_][:, 512:1024],
                                start=(e_ == 0), stop=(e_ == 7)), inc=True)
                            gidx += 1
                        P.op("vector", lambda h, d_=Wdst, c_=col0: h.tensor_copy(
                            d_[:, c_:c_ + 512], ps[6][:, 0:512]),
                            wait=[(P.c["t"], wmv[gidx - 1])], drain=False)
                        evv[gtile] = P.op("vector", lambda h, d_=Wdst, c_=col0: h.tensor_copy(
                            d_[:, c_ + 512:c_ + 1024], ps[7][:, 0:512]), inc=True,
                            drain=False)
                        gtile += 1
                if wi == 0:
                    w1fin = evv[gtile - 1]
            wfin = evv[gtile - 1]
            tB, stv, evo = {}, {}, {}
            ev1 = ev2 = 0

            def rt_load(g_, wv_):
                j_ = g_ % 4
                i_, c_ = divmod(g_, NCH)
                r = 0
                for k_, src_ in enumerate((sc["x1"], sc["acc1"], sc["xn2f"])):
                    r = P.dma("gpsimd", rld[j_],
                              rt[j_][:, k_ * TB:(k_ + 1) * TB].unsqueeze(1),
                              fm_view(src_, i_ * TB, (i_ + 1) * TB)[:, c_:c_ + 1, :],
                              wait=(wv_ if k_ == 0 else ()))
                stv[g_] = r
            for g in range(4):
                rt_load(g, [])
            for i in range(NT):
                pr = i % 2
                lw = [(P.c["t"], tB[i - 1])] if i >= 1 else []
                lx = P.dma("sync", ld[pr], sb3(xnb[pr]),
                           fm_view(sc["xn2"], i * TB, (i + 1) * TB), wait=lw)
                for c in range(16):
                    pH = ps[c % 2]
                    tw = []
                    if c == 0:
                        tw = [(ld[pr], lx)]
                        if i == 0:
                            tw.append((P.c["v"], w1fin))
                    if c >= 2 or i > 0:
                        tw.append((P.c["s"], ev1))
                    for kc in range(NCH):
                        mm = P.op("tensor", lambda h, c_=c, kc_=kc, p_=pr, s_=pH: h.matmul(
                            s_[:, 0:512],
                            W1s[:, kc_ * 2048 + c_ * 128:kc_ * 2048 + (c_ + 1) * 128],
                            xnb[p_][:, kc_ * TB:(kc_ + 1) * TB],
                            start=(kc_ == 0), stop=(kc_ == NCH - 1)),
                            wait=(tw if kc == 0 else ()), inc=(kc == NCH - 1))
                    ev1 = P.op("scalar", lambda h, c_=c, s_=pH: h.activation(
                        hb[:, c_ * TB:(c_ + 1) * TB], s_[:, 0:512], AF.Gelu,
                        bias=b1sc[:, c_:c_ + 1], scale=1.0),
                        wait=[(P.c["t"], mm)] + ([(onc, d4)] if (i == 0 and c == 0) else []),
                        inc=True)
                for c in range(NCH):
                    pR = ps[2 + c % 2]
                    tw = []
                    if c == 0:
                        tw = [(P.c["s"], ev1)]
                        if i == 0:
                            tw.append((P.c["v"], wfin))
                    if c >= 2 or i > 0:
                        tw.append((P.c["v"], ev2))
                    for kc in range(16):
                        mm = P.op("tensor", lambda h, c_=c, kc_=kc, s_=pR: h.matmul(
                            s_[:, 0:512],
                            W2s[:, kc_ * 1024 + c_ * 128:kc_ * 1024 + (c_ + 1) * 128],
                            hb[:, kc_ * TB:(kc_ + 1) * TB],
                            start=(kc_ == 0), stop=(kc_ == 15)),
                            wait=(tw if kc == 0 else ()), inc=(kc == 15))
                    ev2 = P.op("vector", lambda h, c_=c, s_=pR: h.tensor_scalar(
                        rbuf[:, c_ * TB:(c_ + 1) * TB], s_[:, 0:512],
                        b2sc[:, c_:c_ + 1], None, op0=OP.add),
                        wait=[(P.c["t"], mm)], inc=True)
                for c in range(NCH):
                    g = i * NCH + c
                    j = g % 4
                    jo = g % 2
                    pW = ps[4 + c % 2]
                    tw = []
                    if c == 0:
                        tw = [(P.c["v"], ev2)]
                        if i == 0:
                            tw.append((g1, gw2))
                    if c >= 2 or i > 0:
                        tw.append((P.c["v"], evo[g - 2]))
                    for kc in range(NCH):
                        mm = P.op("tensor", lambda h, c_=c, kc_=kc, s_=pW: h.matmul(
                            s_[:, 0:512],
                            Wo_sb[:, kc_ * 1024 + c_ * 128:kc_ * 1024 + (c_ + 1) * 128],
                            rbuf[:, kc_ * TB:(kc_ + 1) * TB],
                            start=(kc_ == 0), stop=(kc_ == NCH - 1)),
                            wait=(tw if kc == 0 else ()), inc=(kc == NCH - 1))
                    ow = [(P.c["t"], mm), (rld[j], stv[g])]
                    if g >= 2:
                        ow.append((fino[jo], stv[("o", g - 2)]))
                    P.op("vector", lambda h, s_=pW, c_=c, o_=otile[jo]: h.tensor_scalar(
                        o_[:], s_[:, 0:512], boc[:, c_:c_ + 1], 0.3,
                        op0=OP.add, op1=OP.mult), wait=ow)
                    for k in range(3):
                        evo[g] = P.op("vector", lambda h, j_=j, k_=k, o_=otile[jo]:
                                      h.tensor_tensor(o_[:], o_[:],
                                                      rt[j_][:, k_ * TB:(k_ + 1) * TB],
                                                      op=OP.add), inc=(k == 2))
                    stv[("o", g)] = P.dma("sync", fino[jo],
                                          fm_view(self.out, i * TB, (i + 1) * TB)[:, c:c + 1, :],
                                          otile[jo][:].unsqueeze(1),
                                          wait=[(P.c["v"], evo[g])])
                    if g + 4 < NT * NCH:
                        rt_load(g + 4, [(P.c["v"], evo[g])])
                tB[i] = mm

    def phase_gate(self):
        nc, P, sc = self.nc, self.P, self.sc
        ps = self.ps
        with ExitStack() as ph:
            sb = lambda n, shp, dt: self.sbuf(n, shp, dt, ph)
            wg = sb("wg_sb", [128, 8 * 2048], BF16)
            bg = sb("bg_sb", [128, 16], F32)
            atb = [sb(f"g_at{i}", [128, NCH * TB], BF16) for i in range(2)]
            sig = sb("g_sig", [128, TB], F32)
            glu = [sb(f"g_glu{i}", [128, NCH * TB], F32) for i in range(2)]
            ld = P.dpool("g_ld", 2)
            sto = P.dpool("g_st", 2)
            onc = P.ctr("g_onc")
            gonc = P.ctr("g_gonc")
            wv0 = P.dma("gpsimd", gonc, wg[:].rearrange("p (kc n) -> p kc n", n=2048),
                        self.inp["w_gate"].rearrange("(kc p) n -> p kc n", p=128))
            wv = P.dma("sync", onc, bg[:], self.inp["b_gate"])
            mmdone, gdone, stores = {}, {}, {}
            for i in range(NT):
                pr = i % 2
                wvl = []
                if i >= 2:
                    wvl = [(P.c["t"], mmdone[i - 2]), (sto[pr], stores[i - 2])]
                lv = P.dma("sync", ld[pr], sb3(atb[pr]),
                           fm_view(sc["attn"], i * TB, (i + 1) * TB), wait=wvl)
                for j in range(NCH):
                    psa, psg = (ps[0], ps[1]) if j % 2 == 0 else (ps[2], ps[3])
                    tw = [(ld[pr], lv), (onc, wv), (gonc, wv0)] if j == 0 else []
                    if i + j > 0:
                        tw.append((P.c["v"], gdone.get((i, j - 2), gdone.get((i - 1, j + 6), 0))))
                    for kc in range(NCH):
                        P.op("tensor", lambda h, j_=j, kc_=kc, p_=pr, ps_=psa: h.matmul(
                            ps_[:, 0:512], wg[:, kc_ * 2048 + j_ * 128:kc_ * 2048 + (j_ + 1) * 128],
                            atb[p_][:, kc_ * TB:(kc_ + 1) * TB],
                            start=(kc_ == 0), stop=(kc_ == NCH - 1)),
                            wait=(tw if kc == 0 else ()))
                    for kc in range(NCH):
                        mmdone[i] = P.op("tensor", lambda h, j_=j, kc_=kc, p_=pr, ps_=psg: h.matmul(
                            ps_[:, 0:512], wg[:, kc_ * 2048 + 1024 + j_ * 128:kc_ * 2048 + 1024 + (j_ + 1) * 128],
                            atb[p_][:, kc_ * TB:(kc_ + 1) * TB],
                            start=(kc_ == 0), stop=(kc_ == NCH - 1)), inc=(kc == NCH - 1))
                    sgw = [(P.c["t"], mmdone[i])]
                    prevg = gdone.get((i, j - 1), gdone.get((i - 1, NCH - 1), 0))
                    sgw.append((P.c["v"], prevg))
                    if i == 0 and j == 0:
                        sgw.append((onc, wv))
                    sg = P.op("scalar", lambda h, j_=j, ps_=psg: h.activation(
                        sig[:], ps_[:, 0:512], AF.Sigmoid, bias=bg[:, NCH + j_:NCH + j_ + 1], scale=1.0),
                        wait=sgw, inc=True)
                    gdone[(i, j)] = P.op("vector", lambda h, j_=j, p_=pr, ps_=psa: h.scalar_tensor_tensor(
                        glu[p_][:, j_ * TB:(j_ + 1) * TB], ps_[:, 0:512], bg[:, j_:j_ + 1],
                        sig[:], op0=OP.add, op1=OP.mult),
                        wait=[(P.c["s"], sg)], inc=True)
                stores[i] = P.dma("sync", sto[pr], fm_view(sc["glu"], i * TB, (i + 1) * TB),
                                  sb3(glu[pr]), wait=[(P.c["v"], gdone[(i, NCH - 1)])])


def build_program(taps=(), stop_after=None):
    b = Build(taps=taps)
    if stop_after == "p0":
        b.stop_after_p0 = True
    nc = b.build()
    return nc, {}


def prep_inputs(inputs):
    x = np.asarray(inputs["L0_attn_x"], np.float32)
    w1 = np.asarray(inputs["L0_ffn_w1"], np.float32)
    b1 = np.asarray(inputs["L0_ffn_b1"], np.float32)
    w2 = np.asarray(inputs["L0_ffn_w2"], np.float32)
    w1p = np.zeros((2, DM, 5632), np.float32)
    w1p[:, :, 0:2730] = w1[:, :, 0:2730]
    w1p[:, :, 2816:2816 + 2730] = w1[:, :, 2730:5460]
    b1p = np.zeros((2, 5632), np.float32)
    b1p[:, 0:2730] = b1[:, 0:2730]
    b1p[:, 2816:2816 + 2730] = b1[:, 2730:5460]
    w2p = np.zeros((2, 2816, DM), np.float32)
    w2p[:, 0:2730, :] = w2
    tisa = np.stack([np.asarray(inputs["L0_attn_amplitudes"], np.float32),
                     np.abs(np.asarray(inputs["L0_attn_sharpness"], np.float32)),
                     np.asarray(inputs["L0_attn_offsets"], np.float32)])
    qk_scale = np.stack([np.asarray(inputs["L0_ffn_q_scale"], np.float32),
                         np.asarray(inputs["L0_ffn_k_scale"], np.float32)], axis=1)
    cf = lambda k: np.ascontiguousarray(np.asarray(inputs[k], np.float32))

    def pcol(v):
        v = np.asarray(v, np.float32)
        return np.ascontiguousarray(v.reshape(*v.shape[:-1], -1, 128).swapaxes(-1, -2))
    common = {
        "w_in": cf("L0_attn_w_in"), "w_gate": cf("L0_attn_w_gate"),
        "b_gate": pcol(inputs["L0_attn_b_gate"]), "tisa": tisa,
        "wqkv": cf("L0_ffn_wqkv"), "qk_scale": qk_scale, "wo0": cf("L0_ffn_wo"),
        "ln_g": pcol(inputs["L0_ffn_ln_g"]), "ln_b": pcol(inputs["L0_ffn_ln_b"]),
        "w1f": w1p, "b1f": pcol(b1p), "w2f": w2p, "b2f": pcol(inputs["L0_ffn_b2"]),
        "WkT": np.ascontiguousarray(np.asarray(inputs["L1_attn_Wk"], np.float32).T),
        "WkaT": np.ascontiguousarray(np.asarray(inputs["L1_attn_Wka"], np.float32).T),
        "WvT": np.ascontiguousarray(np.asarray(inputs["L1_attn_Wv"], np.float32).T),
        "bkv": pcol(np.stack([np.asarray(inputs["L1_attn_bk"], np.float32),
                              np.asarray(inputs["L1_attn_bka"], np.float32),
                              np.asarray(inputs["L1_attn_bv"], np.float32)])),
        "gamma": pcol(inputs["L1_attn_gamma"]), "beta": pcol(inputs["L1_attn_beta"]),
        "Wp1": cf("L1_ffn_Wp1"), "bp1": np.asarray(inputs["L1_ffn_bp1"], np.float32).reshape(128, 1).copy(),
        "Wp2": cf("L1_ffn_Wp2"), "bp2": cf("L1_ffn_bp2"),
        "We1": cf("L1_ffn_W1"), "be1": cf("L1_ffn_b1"),
        "We2": cf("L1_ffn_W2"), "be2": cf("L1_ffn_b2"),
        "WoT1": cf("L1_ffn_Wo"), "bo1": pcol(inputs["L1_ffn_bo"]),
    }
    in_maps = []
    for b in range(8):
        m = dict(common)
        m["xT"] = np.ascontiguousarray(x[b].T)
        in_maps.append(m)
    return in_maps


def kernel(**inputs):
    in_maps = prep_inputs(inputs)
    nc, meta = build_program(taps=TAPS)
    res = run_bass_kernel_spmd(nc, in_maps, core_ids=list(range(8)))
    outs = np.stack([np.ascontiguousarray(np.asarray(r["out"]).T) for r in res.results])
    if TAPS:
        kernel.last_results = res
    return outs


# version 17 (final)
# speedup vs baseline: 1.0192x; 1.0182x over previous
import numpy as np
import ml_dtypes
import bass_rust
import concourse.bass as bass
import concourse.mybir as mybir
from concourse.alu_op_type import AluOpType as OP
from concourse.bass_utils import run_bass_kernel_spmd
from contextlib import ExitStack
from concourse import library_config

F32 = mybir.dt.float32
BF16 = mybir.dt.bfloat16
AF = mybir.ActivationFunctionType
AX = mybir.AxisListType

S = 4096
DM = 1024
TB = 512
NT = S // TB
NCH = DM // 128
NDIAG = 33

TAPS = ()
P1C_HP = range(8)
P1C_IB = range(8)
SKIP_P1 = False
FFN_W = range(NT)


class Ctr:
    def __init__(self, nc, es, name):
        self.sem = es.enter_context(nc.semaphore(name))
        self.v = 0
        self.name = name


class Prog:
    ENGS = ("sync", "gpsimd", "tensor", "vector", "scalar")

    def __init__(self, nc, es):
        self.nc = nc
        self.es = es
        self.ops = {e: [] for e in self.ENGS}
        self.all_ctrs = []
        self.waited = {e: {} for e in self.ENGS}
        self.c = {n: self.ctr("c_" + n) for n in ("t", "v", "s", "g")}
        self.engc = {"tensor": self.c["t"], "vector": self.c["v"],
                     "scalar": self.c["s"], "gpsimd": self.c["g"]}

    def ctr(self, name):
        c = Ctr(self.nc, self.es, name)
        self.all_ctrs.append(c)
        return c

    def dpool(self, name, n):
        return [self.ctr(f"{name}{i}") for i in range(n)]

    def emit(self, eng, fn, wait=(), inc=()):
        wd = self.waited[eng]
        wlist = []
        for c, v in wait:
            if v > 0 and wd.get(c.name, 0) < v:
                wlist.append((c.sem, v))
                wd[c.name] = v
        wait = wlist
        for c, a in inc:
            c.v += a
        incs = [(c.sem, a) for c, a in inc]

        def run(h, _w=wait, _i=incs, _f=fn):
            for s_, v_ in _w:
                h.wait_ge(s_, v_)
            inst = _f(h)
            for s_, a_ in _i:
                inst.then_inc(s_, a_)
        self.ops[eng].append(run)

    def dma(self, q, c, out, in_, wait=(), slow_ok=False):
        if slow_ok:
            self.emit(q, lambda h: h.dma_start(out=out, in_=in_,
                                               allow_slow_non_contiguous=True),
                      wait=wait, inc=[(c, 16)])
        else:
            self.emit(q, lambda h: h.dma_start(out=out, in_=in_), wait=wait, inc=[(c, 16)])
        return c.v

    def op(self, eng, fn, wait=(), inc=False, drain=None):
        c = self.engc[eng]
        if drain is None:
            drain = eng == "vector"
        self.emit(eng, fn, wait=wait, inc=[(c, 1)] if inc else [])
        if drain:
            self.ops[eng].append(lambda h: h.drain())
        return c.v if inc else None

    def barrier(self, tag=""):
        vals = [(c, c.v) for c in self.all_ctrs]
        for e in self.ENGS:
            cc = self.engc.get(e)
            if e != "sync":
                self.ops[e].append(lambda h: h.drain())
            if cc is None:
                self.emit(e, lambda h: h.nop(), wait=vals)
            else:
                self.emit(e, lambda h: h.nop(), wait=vals, inc=[(cc, 1)])

    def replay_into(self, block):
        for e in self.ENGS:
            ops = self.ops[e]

            def make(ops_):
                def body(h):
                    for f in ops_:
                        f(h)
                return body
            getattr(block, e)(make(ops))


def fm_view(dram_ap, i0=None, i1=None):
    v = dram_ap.rearrange("(c p) t -> p c t", p=128)
    if i0 is not None:
        v = v[:, :, i0:i1]
    return v


def sb3(sbuf_tensor):
    return sbuf_tensor[:].rearrange("p (c t) -> p c t", t=TB)


class Build:
    def __init__(self, taps=()):
        self.taps = taps
        nc = bass.Bass("TRN2", target_bir_lowering=False)
        self.nc = nc
        self.es = ExitStack()

    def sbuf(self, name, shape, dt, scope=None):
        return (scope or self.es).enter_context(self.nc.sbuf_tensor(name, list(shape), dt))

    def build(self):
        nc = self.nc
        es = self.es

        def din(name, shape, dt=F32):
            return nc.dram_tensor(name, list(shape), dt, kind="ExternalInput").ap()

        def dsc(name, shape, dt=F32):
            return nc.dram_tensor(name, list(shape), dt, kind="Internal").ap()

        self.inp = {
            "xT": din("xT", (DM, S)),
            "w_in": din("w_in", (DM, 3 * DM)),
            "w_gate": din("w_gate", (DM, 2 * DM)),
            "b_gate": din("b_gate", (128, 16)),
            "tisa": din("tisa", (3, 16, 21)),
            "wqkv": din("wqkv", (2, DM, 1536)),
            "qk_scale": din("qk_scale", (2, 2, 64)),
            "wo0": din("wo0", (2, 512, DM)),
            "ln_g": din("ln_g", (2, 128, NCH)),
            "ln_b": din("ln_b", (2, 128, NCH)),
            "w1f": din("w1f", (2, DM, 5632)),
            "b1f": din("b1f", (2, 128, 44)),
            "w2f": din("w2f", (2, 2816, DM)),
            "b2f": din("b2f", (2, 128, NCH)),
            "WkT": din("WkT", (DM, DM)),
            "WkaT": din("WkaT", (DM, DM)),
            "WvT": din("WvT", (DM, DM)),
            "bkv": din("bkv", (3, 128, NCH)),
            "gamma": din("gamma", (128, NCH)),
            "beta": din("beta", (128, NCH)),
            "Wp1": din("Wp1", (DM, 128)),
            "bp1": din("bp1", (128, 1)),
            "Wp2": din("Wp2", (128, 8)),
            "bp2": din("bp2", (8,)),
            "We1": din("We1", (8, DM, 2048)),
            "be1": din("be1", (8, 2048)),
            "We2": din("We2", (8, 2048, DM)),
            "be2": din("be2", (8, DM)),
            "WoT1": din("WoT1", (DM, DM)),
            "bo1": din("bo1", (128, NCH)),
        }
        self.out = nc.dram_tensor("out", [DM, S], F32, kind="ExternalOutput").ap()

        sc = {}
        sc["xn"] = dsc("sc_xn", (DM, S), BF16)
        sc["xnf"] = dsc("sc_xnf", (DM, S), F32)
        sc["qT"] = dsc("sc_qT", (DM, S), BF16)
        sc["kT"] = dsc("sc_kT", (DM, S), BF16)
        sc["attn"] = dsc("sc_attn", (DM, S), BF16)
        sc["glu"] = dsc("sc_glu", (DM, S), F32)
        sc["vh"] = dsc("sc_vh", (16, 8448), BF16)
        sc["ffnx"] = dsc("sc_ffnx", (DM, S), F32)
        sc["ffnxb"] = dsc("sc_ffnxb", (DM, S), BF16)
        sc["ffno"] = dsc("sc_ffno", (DM, S), F32)
        sc["xmid"] = dsc("sc_xmid", (DM, S), F32)
        sc["x1"] = dsc("sc_x1", (DM, S), F32)
        sc["xn2"] = dsc("sc_xn2", (DM, S), BF16)
        sc["xn2f"] = dsc("sc_xn2f", (DM, S), F32)
        sc["h2"] = dsc("sc_h2", (DM, S), BF16)
        sc["acc1"] = dsc("sc_acc1", (DM, S), F32)
        sc["small"] = dsc("sc_small", (4096,), F32)
        sc["stats"] = dsc("sc_stats", (2, S), F32)
        self.sc = sc

        self.tap_outs = {}
        for t in self.taps:
            a = sc[t]
            self.tap_outs[t] = nc.dram_tensor("tap_" + t, list(a.shape), a.dtype,
                                              kind="ExternalOutput").ap()

        with es:
            self.P = Prog(nc, es)
            P = self.P
            self.ones_f = self.sbuf("ones_f", [1, 128], F32)
            self.onesP = self.sbuf("onesP", [128, 128], F32)
            self.ones_col_b = self.sbuf("ones_col_b", [128, 1], BF16)
            self.ident_b = self.sbuf("ident_b", [128, 128], BF16)
            self.ident_f = self.sbuf("ident_f", [128, 128], F32)
            self.st_mu = self.sbuf("st_mu", [1, 2 * TB], F32)
            self.st_r = self.sbuf("st_r", [1, 2 * TB], F32)
            self.st_tmp = self.sbuf("st_tmp", [1, 4 * TB], F32)
            self.rrow = self.sbuf("rrow", [128, TB], F32)
            self.bcb = self.sbuf("bcb", [128, TB], F32)
            self.ps = [es.enter_context(nc.psum_tensor(f"ps{i}", [128, 512], F32))
                       for i in range(8)]

            self.init_consts()
            P.barrier("consts")
            self.phase_ln(self.inp["xT"], sc["xn"], self.st_mu, self.st_r,
                          dst2_f32=sc["xnf"])
            P.barrier("p0")
            if not getattr(self, "stop_after_p0", False):
                if not SKIP_P1:
                    self.phase_l0_attn()
                    P.barrier("p1")
                    self.phase_gate()
                    P.barrier("p2")
                self.phase_l0_ffn()
                P.barrier("p3")
                self.phase_p4()
                P.barrier("p4")
                self.phase_l1_attn()
                P.barrier("p5b")
                self.phase_l1_ffn()
                P.barrier("p6")

            fin = P.ctr("fin")
            fv = 0
            for t in self.taps:
                fv = P.dma("sync", fin, self.tap_outs[t], sc[t])
            if fv:
                P.emit("sync", lambda h: h.nop(), wait=[(fin, fv)])

            block = es.enter_context(nc.Block())
            P.replay_into(block)
        return nc

    def bcast(self, out_ap, row_ap, n, ps, wait=()):
        P = self.P
        tb = P.op("tensor", lambda h: h.matmul(
            ps[:, 0:n], self.ones_f[:], row_ap, start=True, stop=True),
            wait=wait, inc=True)
        return P.op("vector", lambda h: h.tensor_copy(out_ap, ps[:, 0:n]),
                    wait=[(P.c["t"], tb)], inc=True)

    def init_consts(self):
        P, h = self.P, None
        P.op("gpsimd", lambda h: h.memset(self.ones_f[:], 1.0), inc=True)
        P.op("gpsimd", lambda h: h.memset(self.onesP[:], 1.0), inc=True)
        P.op("gpsimd", lambda h: h.memset(self.ones_col_b[:], 1.0), inc=True)
        gp = P.op("gpsimd", lambda h: h.memset(self.ident_b[:], 1.0), inc=True)
        P.op("gpsimd", lambda h: h.affine_select(
            out=self.ident_b[:], in_=self.ident_b[:], pattern=[[-1, 128]],
            compare_op=OP.is_equal, fill=0.0, base=0, channel_multiplier=1),
            wait=[(P.c["g"], gp)], inc=True)
        gq = P.op("gpsimd", lambda h: h.affine_select(
            out=self.ident_f[:], in_=self.onesP[:], pattern=[[-1, 128]],
            compare_op=OP.is_equal, fill=0.0, base=0, channel_multiplier=1),
            wait=[(P.c["g"], P.c["g"].v)], inc=True)

    def rsqrt_chain(self, t_slots, var_ap, out_ap, wait_v):
        P = self.P
        t0, t1, t2 = t_slots
        sv = P.op("scalar", lambda h: h.activation(t0, var_ap, AF.Sqrt),
                  wait=wait_v, inc=True)
        P.op("vector", lambda h: h.reciprocal(out_ap, t0), wait=[(P.c["s"], sv)])
        P.op("vector", lambda h: h.tensor_tensor(t1, out_ap, out_ap, op=OP.mult))
        P.op("vector", lambda h: h.tensor_tensor(t1, var_ap, t1, op=OP.mult))
        P.op("vector", lambda h: h.tensor_scalar(t1, t1, -0.5, 1.5, op0=OP.mult, op1=OP.add))
        return P.op("vector", lambda h: h.tensor_tensor(out_ap, out_ap, t1, op=OP.mult), inc=True)

    def phase_ln(self, src_f32, dst_bf, mu_row, r_row, dst2_f32=None, affine=None,
                 add_srcs=(), dst_sum_f32=None):
        nc, P = self.nc, self.P
        self.ln_id = getattr(self, "ln_id", 0) + 1
        pref = f"ln{self.ln_id}_"
        with ExitStack() as ph:
            sb = lambda n, shp, dt: self.sbuf(pref + n, shp, dt, ph)
            nadd = len(add_srcs)
            xb = [sb(f"ln_xb{i}", [128, NCH * TB], F32) for i in range(2)]
            ab = [[sb(f"ln_ab{j}_{i}", [128, NCH * TB], F32) for i in range(2)]
                  for j in range(nadd)]
            xbb = [sb(f"ln_xbb{i}", [128, NCH * TB], BF16) for i in range(2)]
            xout = [sb(f"ln_xo{i}", [128, NCH * TB], BF16) for i in range(2)]
            xof = [sb(f"ln_xof{i}", [128, NCH * TB], F32) for i in range(2)] if dst2_f32 is not None else None
            xsq = [sb(f"ln_xsq{i}", [128, NCH * TB], BF16) for i in range(2)]
            tmpf = [sb(f"ln_tmp{i}", [128, TB], F32) for i in range(2)]
            gam = None
            ld = P.dpool(pref + "ld", 2)
            sto = P.dpool(pref + "st", 2)
            onc = P.ctr(pref + "onc")
            gw = 0
            if affine is not None:
                gam = sb("ln_gam", [128, 2 * NCH], F32)
                P.dma("sync", onc, gam[:, 0:NCH], affine[0])
                gw = P.dma("sync", onc, gam[:, NCH:2 * NCH], affine[1])
            ps_stats = [(self.ps[0], self.ps[1]), (self.ps[2], self.ps[3])]
            ps_bc = [(self.ps[4], self.ps[5]), (self.ps[6], self.ps[7])]
            loads, casts, sums, stores, apply_done = {}, {}, {}, {}, {}
            for i in range(NT):
                pr = i % 2
                wv = []
                if i >= 2:
                    wv = [(P.c["v"], apply_done[i - 2]), (P.c["v"], casts[i - 2]),
                          (P.c["t"], sums[i - 2]), (sto[i % 2], stores[i - 2])]
                loads[i] = P.dma("sync", ld[pr], sb3(xb[pr]), fm_view(src_f32, i * TB, (i + 1) * TB), wait=wv)
                for j, asrc in enumerate(add_srcs):
                    loads[i] = P.dma("sync", ld[pr], sb3(ab[j][pr]),
                                     fm_view(asrc, i * TB, (i + 1) * TB), wait=wv)
                vw = [(ld[pr], loads[i])]
                if nadd:
                    vsum = None
                    for j in range(nadd):
                        vsum = P.op("vector", lambda h, p_=pr, j_=j: h.tensor_tensor(
                            xb[p_][:], xb[p_][:], ab[j_][p_][:], op=OP.add),
                            wait=vw, inc=(j == nadd - 1))
                    vw = [(P.c["v"], vsum)]
                    if dst_sum_f32 is not None:
                        stores[i] = P.dma("sync", sto[pr], fm_view(dst_sum_f32, i * TB, (i + 1) * TB),
                                          sb3(xb[pr]), wait=vw)
                casts[i] = P.op("vector", lambda h, p_=pr: h.tensor_copy(
                    xbb[p_][:], xb[p_][:]), wait=vw, inc=True)
                sq = P.op("vector", lambda h, p_=pr: h.tensor_tensor(
                    xsq[p_][:], xbb[p_][:], xbb[p_][:], op=OP.mult), inc=True)
                for c in range(NCH):
                    P.op("tensor", lambda h, p_=pr, c_=c: h.matmul(
                        ps_stats[p_][0][0:1, 0:TB], self.ones_col_b[:],
                        xbb[p_][:, c_ * TB:(c_ + 1) * TB],
                        start=(c_ == 0), stop=(c_ == NCH - 1)),
                        wait=[(P.c["v"], sq)] if c == 0 else ())
                for c in range(NCH):
                    sums[i] = P.op("tensor", lambda h, p_=pr, c_=c: h.matmul(
                        ps_stats[p_][1][0:1, 0:TB], self.ones_col_b[:],
                        xsq[p_][:, c_ * TB:(c_ + 1) * TB],
                        start=(c_ == 0), stop=(c_ == NCH - 1)), inc=(c == NCH - 1))
                stt = self.st_tmp
                t0, t1, t2, t3 = (stt[0:1, k * TB:(k + 1) * TB] for k in range(4))
                mu_b = mu_row[0:1, pr * TB:(pr + 1) * TB]
                r_b = r_row[0:1, pr * TB:(pr + 1) * TB]
                P.op("vector", lambda h, p_=pr, m_=mu_b: h.tensor_scalar(
                    m_, ps_stats[p_][0][0:1, 0:TB], 1.0 / DM, None, op0=OP.mult),
                    wait=[(P.c["t"], sums[i])])
                P.op("vector", lambda h, p_=pr: h.tensor_scalar(
                    t0, ps_stats[p_][1][0:1, 0:TB], 1.0 / DM, None, op0=OP.mult))
                P.op("vector", lambda h, m_=mu_b: h.tensor_tensor(t1, m_, m_, op=OP.mult))
                P.op("vector", lambda h: h.scalar_tensor_tensor(
                    t2, t0, 1.0, t1, op0=OP.mult, op1=OP.subtract))
                vvar = P.op("vector", lambda h: h.tensor_scalar(
                    t2, t2, 1e-5, None, op0=OP.add), inc=True)
                vr = self.rsqrt_chain((t0, t1, t3), t2, r_b, [(P.c["v"], vvar)])
                bca, bcb = ps_bc[pr]
                P.op("tensor", lambda h, m_=mu_b, b_=bca: h.matmul(
                    b_[:, 0:TB], self.ones_f[:], m_, start=True, stop=True),
                    wait=[(P.c["v"], vr)])
                tb2 = P.op("tensor", lambda h, r_=r_b, b_=bcb: h.matmul(
                    b_[:, 0:TB], self.ones_f[:], r_, start=True, stop=True), inc=True)
                aw = [(P.c["t"], tb2)]
                if i == 0 and gw:
                    aw.append((onc, gw))
                for c in range(NCH):
                    last = (c == NCH - 1)
                    P.op("vector", lambda h, p_=pr, c_=c, b_=bca: h.tensor_tensor(
                        tmpf[p_][:], xb[p_][:, c_ * TB:(c_ + 1) * TB], b_[:, 0:TB],
                        op=OP.subtract), wait=aw if c == 0 else ())
                    if affine is None:
                        if xof is not None:
                            P.op("vector", lambda h, p_=pr, c_=c, b_=bcb: h.tensor_tensor(
                                xof[p_][:, c_ * TB:(c_ + 1) * TB], tmpf[p_][:], b_[:, 0:TB],
                                op=OP.mult))
                        ad = P.op("vector", lambda h, p_=pr, c_=c, b_=bcb: h.tensor_tensor(
                            xout[p_][:, c_ * TB:(c_ + 1) * TB], tmpf[p_][:], b_[:, 0:TB],
                            op=OP.mult), inc=last)
                    else:
                        P.op("vector", lambda h, p_=pr, c_=c, b_=bcb: h.tensor_tensor(
                            tmpf[p_][:], tmpf[p_][:], b_[:, 0:TB], op=OP.mult))
                        if xof is not None:
                            P.op("vector", lambda h, p_=pr, c_=c: h.tensor_scalar(
                                xof[p_][:, c_ * TB:(c_ + 1) * TB], tmpf[p_][:],
                                gam[:, c_:c_ + 1], gam[:, NCH + c_:NCH + c_ + 1],
                                op0=OP.mult, op1=OP.add))
                        ad = P.op("vector", lambda h, p_=pr, c_=c: h.tensor_scalar(
                            xout[p_][:, c_ * TB:(c_ + 1) * TB], tmpf[p_][:],
                            gam[:, c_:c_ + 1], gam[:, NCH + c_:NCH + c_ + 1],
                            op0=OP.mult, op1=OP.add), inc=last)
                    apply_done[i] = ad
                if xof is not None:
                    stores[i] = P.dma("sync", sto[pr], fm_view(dst2_f32, i * TB, (i + 1) * TB),
                                      sb3(xof[pr]), wait=[(P.c["v"], apply_done[i])])
                stores[i] = P.dma("sync", sto[pr], fm_view(dst_bf, i * TB, (i + 1) * TB),
                                  sb3(xout[pr]), wait=[(P.c["v"], apply_done[i])])

    def phase_l0_attn(self):
        nc, P, sc = self.nc, self.P, self.sc
        ps = self.ps
        with ExitStack() as ph:
            sb = lambda n, shp, dt: self.sbuf(n, shp, dt, ph)
            v_sb = sb("v_sb", [128, 32 * 1040], BF16)
            ncb = sb("p1ncb", [2, 128], F32)
            nrow = sb("p1nrow", [2, 16], F32)
            Bneg = sb("Bneg", [128, 16], F32)
            ld = P.dpool("p1ld", 2)
            sto = P.dpool("p1st", 2)
            onc = P.ctr("p1onc")

            vones = bass_rust.AP(v_sb[:].tensor, 64,
                                 [[32 * 1040, 128], [1040, 32], [65, 16], [1, 1]])
            gvo = P.op("gpsimd", lambda h: h.memset(vones, 1.0), inc=True)

            with ExitStack() as pha:
                sba = lambda n, shp, dt: self.sbuf(n, shp, dt, pha)
                w_in_sb = sba("w_in_sb", [128, 8 * 3072], BF16)
                xnb = [sba(f"p1xnb{i}", [128, NCH * TB], BF16) for i in range(2)]
                qstage = [sba(f"p1qs{i}", [128, NCH * TB], BF16) for i in range(2)]
                kstage = [sba(f"p1ks{i}", [128, NCH * TB], BF16) for i in range(2)]
                sqb = sba("p1sqb", [128, TB], BF16)
                bd2 = sba("bd2", [128, 2], BF16)
                gp = P.op("gpsimd", lambda h: h.memset(bd2[:], 1.0), inc=True)
                gp = P.op("gpsimd", lambda h: h.affine_select(
                    out=bd2[:], in_=bd2[:], pattern=[[-64, 2]],
                    compare_op=OP.is_ge, fill=0.0, base=0, channel_multiplier=1),
                    wait=[(P.c["g"], gp)], inc=True)
                gbd = P.op("gpsimd", lambda h: h.affine_select(
                    out=bd2[:], in_=bd2[:], pattern=[[64, 2]],
                    compare_op=OP.is_ge, fill=0.0, base=63, channel_multiplier=-1),
                    wait=[(P.c["g"], gp)], inc=True)
                gonc = P.ctr("p1gonc")
                wld = P.dma("gpsimd", gonc, w_in_sb[:].rearrange("p (kc n) -> p kc n", n=3072),
                            self.inp["w_in"].rearrange("(kc p) n -> p kc n", p=128))

                slots = [ps[0][:, 0:512], ps[1][:, 0:512], ps[2][:, 0:512], ps[3][:, 0:512]]
                mmdone, stored = {}, {}
                prev_ev = None
                norm_done = 0
                for i in range(NT):
                    pr = i % 2
                    wv = []
                    if i >= 2:
                        wv = [(P.c["t"], mmdone[i - 2]), (sto[pr], stored[i - 2]),
                              (P.c["s"], sq_last[pr])]
                    lv = P.dma("sync", ld[pr], sb3(xnb[pr]),
                               fm_view(sc["xn"], i * TB, (i + 1) * TB), wait=wv)
                    base_w = [(ld[pr], lv), (gonc, wld)]
                    if i == 0:
                        base_w += [(P.c["g"], gvo)]
                    ev = {}
                    for g in range(24):
                        sl = slots[g % 4]
                        tw = list(base_w) if (g < 4 and i == 0) else ([(ld[pr], lv)] if g < 4 else [])
                        if g >= 4:
                            tw.append((P.c["v"], ev[g - 4]))
                        elif prev_ev is not None:
                            tw.append((P.c["v"], prev_ev[20 + g]))
                        if g < 8:
                            c = g
                            for kc in range(NCH):
                                mmdone[i] = P.op("tensor", lambda h, sl_=sl, c_=c, kc_=kc, p_=pr: h.matmul(
                                    sl_, w_in_sb[:, kc_ * 3072 + c_ * 128:kc_ * 3072 + (c_ + 1) * 128],
                                    xnb[p_][:, kc_ * TB:(kc_ + 1) * TB],
                                    start=(kc_ == 0), stop=(kc_ == NCH - 1)),
                                    wait=(tw if kc == 0 else ()), inc=(kc == NCH - 1))
                            ev[g] = P.op("vector", lambda h, sl_=sl, c_=c, p_=pr: h.tensor_copy(
                                kstage[p_][:, c_ * TB:(c_ + 1) * TB], sl_),
                                wait=[(P.c["t"], mmdone[i])], inc=True)
                        elif g < 16:
                            c = g - 8
                            for kc in range(NCH):
                                mmdone[i] = P.op("tensor", lambda h, sl_=sl, c_=c, kc_=kc, p_=pr: h.matmul(
                                    sl_, w_in_sb[:, kc_ * 3072 + 2048 + c_ * 128:kc_ * 3072 + 2048 + (c_ + 1) * 128],
                                    xnb[p_][:, kc_ * TB:(kc_ + 1) * TB],
                                    start=(kc_ == 0), stop=(kc_ == NCH - 1)),
                                    wait=(tw if kc == 0 else ()), inc=(kc == NCH - 1))
                            ev[g] = P.op("vector", lambda h, sl_=sl, c_=c, p_=pr: h.tensor_scalar(
                                qstage[p_][:, c_ * TB:(c_ + 1) * TB], sl_, 0.125, None, op0=OP.mult),
                                wait=[(P.c["t"], mmdone[i])], inc=True)
                        else:
                            tt = (g - 16) // 2
                            half = (g - 16) % 2
                            gtt = i * 4 + tt
                            for kc in range(NCH):
                                mmdone[i] = P.op("tensor", lambda h, tt_=tt, hf_=half, sl_=sl, kc_=kc, p_=pr: h.matmul(
                                    sl_, xnb[p_][:, kc_ * TB + tt_ * 128:kc_ * TB + (tt_ + 1) * 128],
                                    w_in_sb[:, kc_ * 3072 + 1024 + hf_ * 512:kc_ * 3072 + 1024 + (hf_ + 1) * 512],
                                    start=(kc_ == 0), stop=(kc_ == NCH - 1)),
                                    wait=(tw if kc == 0 else ()), inc=(kc == NCH - 1))
                            vdst = bass_rust.AP(v_sb[:].tensor, gtt * 1040 + half * 520,
                                                [[32 * 1040, 128], [65, 8], [1, 64]])
                            ev[g] = P.op("vector", lambda h, sl_=sl, d_=vdst: h.tensor_copy(
                                d_, sl_.rearrange("p (a b) -> p a b", b=64)),
                                wait=[(P.c["t"], mmdone[i])], inc=True)
                    prev_ev = ev
                    sq_last = getattr(self, "_p1_sq_last", {0: 0, 1: 0})
                    for qk in range(2):
                        for c in range(NCH):
                            if qk == 0:
                                srcv = kstage[pr][:, c * TB:(c + 1) * TB]
                                wsq = [(P.c["v"], ev[c])]
                            else:
                                srcv = qstage[pr][:, c * TB:(c + 1) * TB]
                                wsq = [(P.c["v"], ev[8 + c])]
                            sqv = P.op("scalar", lambda h, s_=srcv: h.activation(
                                sqb[:], s_, AF.Square), wait=wsq, inc=True)
                            nm = P.op("tensor", lambda h: h.matmul(
                                ps[6][0:2, 0:TB], bd2[:], sqb[:], start=True, stop=True),
                                wait=[(P.c["s"], sqv), (P.c["v"], norm_done),
                                      (P.c["g"], gbd)], inc=True)
                            col = qk * 64 + c * 8 + i
                            norm_done = P.op("vector", lambda h, col_=col: h.tensor_reduce(
                                ncb[0:2, col_:col_ + 1], ps[6][0:2, 0:TB], axis=AX.X, op=OP.max),
                                wait=[(P.c["t"], nm)], inc=True)
                            P.emit("scalar", lambda h: h.nop(), wait=[(P.c["t"], nm)])
                    sq_last[pr] = P.c["s"].v
                    self._p1_sq_last = sq_last
                    stored[i] = P.dma("sync", sto[pr],
                                      fm_view(sc["qT"], i * TB, (i + 1) * TB),
                                      sb3(qstage[pr]), wait=[(P.c["v"], ev[15]), (P.c["s"], sq_last[pr])])
                    stored[i] = P.dma("sync", sto[pr],
                                      fm_view(sc["kT"], i * TB, (i + 1) * TB),
                                      sb3(kstage[pr]))
                nfin = P.op("vector", lambda h: h.tensor_reduce(
                    nrow[0:2, :], ncb[0:2, :].rearrange("p (c b) -> p c b", b=8),
                    axis=AX.X, op=OP.max), wait=[(P.c["v"], norm_done)], inc=True)
                P.barrier("p1a")

            with ExitStack() as phb:
                sbb = lambda n, shp, dt: self.sbuf(n, shp, dt, phb)
                amp_r = sbb("amp_r", [1, 336], F32)
                shp_r = sbb("shp_r", [1, 336], F32)
                off_r = sbb("off_r", [1, 336], F32)
                ampB = sbb("ampB", [128, 336], F32)
                nshB = sbb("nshB", [128, 336], F32)
                offB = sbb("offB", [128, 336], F32)
                io66 = sbb("io66", [128, 66], F32)
                tw1 = sbb("tisa_w1", [128, 336], F32)
                tw2 = sbb("tisa_w2", [128, 336], F32)
                vhc = sbb("tisa_vhc", [128, 16], F32)
                vstage = sbb("vstage", [16, 8448], BF16)
                brow = sbb("brow", [1, 64], F32)
                tld = P.dma("sync", onc, amp_r[:], self.inp["tisa"][0].rearrange("a b -> (a b)").unsqueeze(0))
                tld = P.dma("sync", onc, shp_r[:], self.inp["tisa"][1].rearrange("a b -> (a b)").unsqueeze(0))
                tld = P.dma("sync", onc, off_r[:], self.inp["tisa"][2].rearrange("a b -> (a b)").unsqueeze(0))
                gio = P.op("gpsimd", lambda h: h.iota(
                    io66[:], pattern=[[128, 66]], base=0, channel_multiplier=1,
                    allow_small_or_imprecise_dtypes=True), inc=True)
                P.op("vector", lambda h: h.tensor_scalar(
                    shp_r[:], shp_r[:], -1.0, None, op0=OP.mult), wait=[(onc, tld)])
                vns = P.op("vector", lambda h: h.tensor_scalar(
                    off_r[:], off_r[:], 4095.0, None, op0=OP.add), inc=True)
                g1 = self.bcast(ampB[:], amp_r[:], 336, ps[5], wait=[(P.c["v"], vns)])
                g1 = self.bcast(nshB[:], shp_r[:], 336, ps[5], wait=[(P.c["v"], g1)])
                g1 = self.bcast(offB[:], off_r[:], 336, ps[5], wait=[(P.c["v"], g1)])
                ebm = 0
                vcp = 0
                for rc in range(66):
                    vv = P.op("vector", lambda h, rc_=rc: h.tensor_scalar(
                        tw1[:], offB[:], io66[:, rc_:rc_ + 1], None, op0=OP.subtract),
                        wait=([(P.c["v"], g1), (P.c["g"], gio)] if rc == 0 else
                              [(P.c["s"], ebm)]), inc=True)
                    P.op("vector", lambda h: h.tensor_tensor(tw1[:], tw1[:], tw1[:], op=OP.mult))
                    vv = P.op("vector", lambda h: h.tensor_tensor(tw1[:], tw1[:], nshB[:], op=OP.mult), inc=True)
                    ebm = P.op("scalar", lambda h: h.activation(tw2[:], tw1[:], AF.Exp),
                               wait=[(P.c["v"], vv)], inc=True)
                    vv = P.op("vector", lambda h: h.tensor_tensor(tw2[:], tw2[:], ampB[:], op=OP.mult),
                              wait=[(P.c["s"], ebm)])
                    vv = P.op("vector", lambda h: h.tensor_reduce(
                        vhc[:], tw2[:].rearrange("p (a b) -> p a b", b=21), axis=AX.X, op=OP.add),
                        inc=True)
                    tt = P.op("tensor", lambda h: h.transpose(ps[7][0:16, 0:128], vhc[:], self.ident_f[:]),
                              wait=[(P.c["v"], vv)], inc=True)
                    vcp = P.op("vector", lambda h, rc_=rc: h.tensor_copy(
                        vstage[:, rc_ * 128:(rc_ + 1) * 128], ps[7][0:16, 0:128]),
                        wait=[(P.c["t"], tt)], inc=True)
                    P.emit("tensor", lambda h: h.nop(), wait=[(P.c["v"], vcp)])
                vsd = P.dma("sync", onc, sc["vh"], vstage[:], wait=[(P.c["v"], vcp)])
                bm = P.op("vector", lambda h: h.tensor_reduce(
                    vhc[0:16, 0:1], vstage[:], axis=AX.X, op=OP.max), inc=True)
                smv = sc["small"]
                d1 = P.dma("sync", onc, bass_rust.AP(smv.tensor, 0, [[1, 2], [16, 2], [2, 8]]),
                           nrow[0:2, :].rearrange("p (a b) -> p a b", b=8),
                           wait=[(P.c["v"], nfin)], slow_ok=True)
                d1 = P.dma("sync", onc, bass_rust.AP(smv.tensor, 32, [[1, 16], [1, 1]]),
                           vhc[0:16, 0:1], wait=[(P.c["v"], bm)])
                qn = brow[0:1, 0:16]
                kn = brow[0:1, 16:32]
                bx = brow[0:1, 32:48]
                d2 = P.dma("sync", onc, kn, bass_rust.AP(smv.tensor, 0, [[16, 1], [1, 16]]), wait=[(onc, d1)])
                d2 = P.dma("sync", onc, qn, bass_rust.AP(smv.tensor, 16, [[16, 1], [1, 16]]))
                d2 = P.dma("sync", onc, bx, bass_rust.AP(smv.tensor, 32, [[16, 1], [1, 16]]))
                sqn = P.op("scalar", lambda h: h.activation(qn, qn, AF.Sqrt), wait=[(onc, d2)], inc=True)
                sqn = P.op("scalar", lambda h: h.activation(kn, kn, AF.Sqrt), inc=True)
                P.op("vector", lambda h: h.tensor_tensor(qn, qn, kn, op=OP.mult), wait=[(P.c["s"], sqn)])
                P.op("vector", lambda h: h.tensor_tensor(qn, qn, bx, op=OP.add))
                bneg_row = P.op("vector", lambda h: h.tensor_scalar(
                    qn, qn, -1.0, -0.5, op0=OP.mult, op1=OP.add), inc=True)
                gB = self.bcast(Bneg[:], qn, 16, ps[5], wait=[(P.c["v"], bneg_row)])
                P.barrier("p1ab")

            with ExitStack() as ph2:
                sb2 = lambda n, shp, dt: self.sbuf(n, shp, dt, ph2)
                kT_sb = sb2("kT_sb", [128, 8 * 4096], BF16)
                qz = sb2("qz", [128, 2 * 4096], BF16)
                uband = sb2("uband", [128, 4224], BF16)
                dg = [sb2(f"dg{j}", [128, NDIAG * 128], BF16) for j in range(2)]
                probs = [sb2(f"probs{j}", [128, 4 * 512], BF16) for j in range(2)]
                attnb = [sb2(f"attnb{j}", [64, 2 * 512], BF16) for j in range(2)]
                qld = P.dpool("qld", 1)
                bld = P.dpool("bld", 1)
                ast = P.dpool("ast", 2)
                kld = P.dma("sync", onc, kT_sb[:].rearrange("p (c t) -> p c t", t=4096),
                            sc["kT"].rearrange("(c p) t -> p c t", p=128))
                gz = P.op("gpsimd", lambda h: h.memset(qz[:], 0.0), inc=True)
                gz2 = P.op("gpsimd", lambda h: h.memset(dg[0][:], 0.0), inc=True)
                gz2 = P.op("gpsimd", lambda h: h.memset(dg[1][:], 0.0), inc=True)
                last_store = {0: 0, 1: 0}
                last_q_reader = 0
                prev_bv = {0: 0, 1: 0}
                dg_tread = 0
                exp_done = {}
                av_done = {}
                last_evn = {0: 0, 1: 0}
                for hp in P1C_HP:
                    qwv = [(P.c["t"], last_q_reader)] if hp else [(P.c["g"], gz)]
                    qv = P.dma("sync", qld[0], qz[0:64, 0:4096],
                               sc["qT"][hp * 128:hp * 128 + 64, :], wait=qwv)
                    qv = P.dma("sync", qld[0], qz[64:128, 4096:8192],
                               sc["qT"][hp * 128 + 64:(hp + 1) * 128, :])
                    bv = {}
                    for j, hh in enumerate((2 * hp, 2 * hp + 1)):
                        ubw = [(P.c["v"], prev_bv[1] if j == 0 else bv[0])]
                        ub = P.dma("sync", bld[0], uband[:],
                                   bass_rust.AP(sc["vh"].tensor, hh * 8448 + 1920,
                                                [[1, 128], [1, 4224]]),
                                   wait=ubw)
                        for d0 in range(-13, 14):
                            rev = bass_rust.AP(uband[:].tensor, 2175 + 128 * d0,
                                               [[4224, 128], [-1, 128]])
                            w_ = [(bld[0], ub)] if d0 == -13 else []
                            if d0 == -13 and hp > 0:
                                w_.append((P.c["t"], dg_tread))
                            elif d0 == -13 and hp == 0:
                                w_.append((P.c["g"], gz2))
                            bv[j] = P.op("vector", lambda h, j_=j, d_=d0, r_=rev: h.tensor_copy(
                                dg[j_][:, (16 - d_) * 128:(17 - d_) * 128], r_),
                                wait=w_, inc=(d0 == 13), drain=(d0 == 13))
                    for ib in P1C_IB:
                        exp_at = {}

                        def emit_av(jt_, hp_=hp):
                            for j in range(2):
                                psv = ps[4 + j]
                                tw = [(P.c["s"], exp_at[(jt_, j)])]
                                if jt_ == 0:
                                    tw.append((P.c["v"], last_evn[j]))
                                P.op("tensor", lambda h, j_=j, jt2=jt_, p_=psv, hpp_=hp_: h.matmul(
                                    p_[0:65, :],
                                    v_sb[:, jt2 * 1040 + 65 * (2 * hpp_ + j_):jt2 * 1040 + 65 * (2 * hpp_ + j_) + 65],
                                    probs[j_][:, (jt2 % 4) * 512:(jt2 % 4) * 512 + 512],
                                    start=(jt2 == 0), stop=(jt2 == 31)),
                                    wait=tw, inc=(jt_ == 31))
                        for jt in range(32):
                            hpar = jt % 2
                            D = jt - 4 * ib
                            biased = -13 <= D <= 16
                            grp_end = {}
                            for j in range(2):
                                pst = ps[2 * j + hpar]
                                tw = []
                                if (hpar, j) in exp_done:
                                    tw.append((P.c["s"], exp_done[hpar, j]))
                                if ib == 0 and jt == 0 and j == 0:
                                    tw += [(qld[0], qv), (onc, kld)]
                                mm = P.op("tensor", lambda h, j_=j, jt_=jt, p_=pst, hp_=hp, ib_=ib, b_=biased: h.matmul(
                                    p_[:, 0:512],
                                    kT_sb[:, hp_ * 4096 + jt_ * 128:hp_ * 4096 + (jt_ + 1) * 128],
                                    qz[:, j_ * 4096 + ib_ * 512:j_ * 4096 + (ib_ + 1) * 512],
                                    start=True, stop=(not b_)),
                                    wait=tw, inc=(not biased))
                                if biased:
                                    mm = P.op("tensor", lambda h, j_=j, D_=D, p_=pst: h.matmul(
                                        p_[:, 0:512], self.ident_b[:],
                                        dg[j_][:, (16 - D_) * 128:(20 - D_) * 128],
                                        start=False, stop=True),
                                        wait=[(P.c["v"], bv[j])] if (ib == 0 and jt == 0) else (),
                                        inc=True)
                                grp_end[j] = mm
                            if jt >= 2:
                                emit_av(jt - 2)
                            for j, hh in enumerate((2 * hp, 2 * hp + 1)):
                                pst = ps[2 * j + hpar]
                                ev_ = P.op("scalar", lambda h, j_=j, p_=pst, sl_=jt % 4, hh_=hh: h.activation(
                                    probs[j_][:, sl_ * 512:(sl_ + 1) * 512], p_[:, 0:512],
                                    AF.Exp, bias=Bneg[:, hh_:hh_ + 1], scale=1.0),
                                    wait=[(P.c["t"], grp_end[j])], inc=True)
                                exp_done[hpar, j] = ev_
                                exp_at[(jt, j)] = ev_
                        emit_av(30)
                        emit_av(31)
                        av_end = P.c["t"].v
                        last_evn = {}
                        for j in range(2):
                            psv = ps[4 + j]
                            rr = self.rrow[64:65, 0:512]
                            rv = P.op("vector", lambda h, p_=psv, r_=rr: h.reciprocal(
                                r_, p_[64:65, 0:512]), wait=[(P.c["t"], av_end)], inc=True)
                            bc = P.op("tensor", lambda h, r_=rr: h.matmul(
                                ps[6][:, 0:512], self.onesP[64:65, :], r_, start=True, stop=True),
                                wait=[(P.c["v"], rv)], inc=True)
                            P.op("vector", lambda h: h.tensor_copy(
                                self.bcb[0:64, 0:512], ps[6][0:64, 0:512]),
                                wait=[(P.c["t"], bc)])
                            ibp = ib % 2
                            stw = []
                            if (hp, ib) != (0, 0):
                                stw = [(ast[ibp], last_store[ibp])]
                            last_evn[j] = P.op("vector", lambda h, j_=j, p_=psv, ibp_=ibp: h.tensor_tensor(
                                attnb[j_][0:64, ibp_ * 512:(ibp_ + 1) * 512],
                                p_[0:64, 0:512], self.bcb[0:64, 0:512], op=OP.mult),
                                wait=(stw if j == 0 else []),
                                inc=True)
                            hh2 = 2 * hp + j
                            last_store[ibp] = P.dma(
                                "sync", ast[ibp],
                                sc["attn"][hh2 * 64:(hh2 + 1) * 64, ib * 512:(ib + 1) * 512],
                                attnb[j][0:64, ibp * 512:(ibp + 1) * 512],
                                wait=[(P.c["v"], last_evn[j])])
                    last_q_reader = P.c["t"].v
                    dg_tread = P.c["t"].v
                    prev_bv = dict(bv)
        return

    def phase_l0_ffn(self):
        nc, P, sc = self.nc, self.P, self.sc
        ps = self.ps
        SGN_NEG = -1.0e30
        with ExitStack() as ph:
            sb = lambda n, shp, dt: self.sbuf(n, shp, dt, ph)
            invf_bc = sb("f_invf", [128, 64], F32)
            lnb_bc = sb("f_lnb", [128, 64], F32)
            sgn_bc = sb("f_sgn", [128, 64], F32)
            rowtmp = sb("f_rowt", [1, 256], F32)
            onc = P.ctr("f_onc")
            gonc = P.ctr("f_gonc")

            gi = P.op("gpsimd", lambda h: h.iota(
                rowtmp[0:1, 0:64].rearrange("p (a b) -> p a b", b=32),
                pattern=[[0, 2], [1, 32]], base=0, channel_multiplier=0,
                allow_small_or_imprecise_dtypes=True), inc=True)
            sA = P.op("scalar", lambda h: h.activation(
                rowtmp[0:1, 64:128], rowtmp[0:1, 0:64], AF.Exp,
                scale=-float(np.log(10000.0) / 32.0)), wait=[(P.c["g"], gi)], inc=True)
            vA = self.bcast(invf_bc[:], rowtmp[0:1, 64:128], 64, ps[5],
                            wait=[(P.c["s"], sA)])
            vB = P.op("vector", lambda h: h.tensor_scalar(
                rowtmp[0:1, 128:192], rowtmp[0:1, 0:64], 2.0 / 89.6, 25.6 / 89.6,
                op0=OP.mult, op1=OP.add), wait=[(P.c["g"], gi)], inc=True)
            sB = P.op("scalar", lambda h: h.activation(
                rowtmp[0:1, 128:192], rowtmp[0:1, 128:192], AF.Ln),
                wait=[(P.c["v"], vB)], inc=True)
            vB = self.bcast(lnb_bc[:], rowtmp[0:1, 128:192], 64, ps[5],
                            wait=[(P.c["s"], sB), (P.c["v"], vA)])
            gs = P.op("gpsimd", lambda h: h.memset(rowtmp[0:1, 192:224], -1.0), inc=True)
            gs = P.op("gpsimd", lambda h: h.memset(rowtmp[0:1, 224:256], 1.0), inc=True)
            vC = self.bcast(sgn_bc[:], rowtmp[0:1, 192:256], 64, ps[5],
                            wait=[(P.c["g"], gs), (P.c["v"], vB)])
            P.barrier("p3const")
            for l in range(2):
                src_bf = sc["xn"] if l == 0 else sc["ffnxb"]
                src_f = sc["xnf"] if l == 0 else sc["ffnx"]
                dst_f = sc["ffnx"] if l == 0 else sc["ffno"]
                dst_bf = sc["ffnxb"] if l == 0 else None
                self.ffn_subAB(l, src_bf, src_f, invf_bc, lnb_bc, sgn_bc,
                               rowtmp, onc, gonc, ph)
                P.barrier(f"p3ab{l}")
                self.ffn_subC(l, src_f, dst_f, dst_bf, onc, gonc)
                P.barrier(f"p3c{l}")

    def ffn_subAB(self, l, src_bf, src_f, invf_bc, lnb_bc, sgn_bc,
                  rowtmp, onc, gonc, ph_outer):
        nc, P, sc = self.nc, self.P, self.sc
        ps = self.ps
        with ExitStack() as ph:
            sb = lambda n, shp, dt: self.sbuf(f"{n}_{l}", shp, dt, ph)
            qf = sb("f_qf", [128, 4 * 4096], BF16)
            kf = sb("f_kf", [128, 4 * 4096], BF16)
            vloc = sb("f_vloc", [128, 32 * 520], BF16)
            nloc = sb("f_nloc", [2, 64], F32)
            brow2 = sb("f_brow", [1, 128], F32)
            Bloc = sb("f_Bloc", [128, 64], F32)
            phA = ExitStack()
            sbA = lambda n, shp, dt: self.sbuf(f"{n}_{l}", shp, dt, phA)
            wqkv_sb = sbA("f_wqkv", [128, 8 * 1536], BF16)
            xqb = [sbA("f_xqb0", [128, 8 * 513], BF16)]
            qtokb = sbA("f_qtok", [128, 2048], F32)
            ktokb = sbA("f_ktok", [128, 2048], F32)
            t3b = sbA("f_t3", [128, 2048], F32)
            rpb = sbA("f_rp", [128, 2048], F32)
            sq2 = rpb[:].bitcast(BF16)[:, 0:2048]
            ssq = sbA("f_ssq", [128, 2 * 32], F32)
            rloc = sbA("f_rloc", [128, 2 * 32], F32)
            tabs = sbA("f_tabs", [128, 8 * 256], F32)
            bd2 = sbA("f_bd2", [128, 2], BF16)
            sqb = sbA("f_sqb", [128, TB], BF16)
            qsrow = sbA("f_qsrow", [1, 4 * 64], F32)
            qs_bc = sbA("f_qsbc", [128, 64], F32)
            qsr_bc = sbA("f_qsrbc", [128, 64], F32)
            ks_bc = sbA("f_ksbc", [128, 64], F32)
            ksr_bc = sbA("f_ksrbc", [128, 64], F32)
            tcol = sbA("f_tcol", [128, 4], F32)
            ld = P.dpool(f"f_ld{l}", 2)
            ost = P.dpool(f"f_ost{l}", 2)

            gq = P.op("gpsimd", lambda h: h.memset(bd2[:], 1.0), inc=True)
            gq = P.op("gpsimd", lambda h: h.affine_select(
                out=bd2[:], in_=bd2[:], pattern=[[-64, 2]],
                compare_op=OP.is_ge, fill=0.0, base=0, channel_multiplier=1),
                wait=[(P.c["g"], gq)], inc=True)
            gbd = P.op("gpsimd", lambda h: h.affine_select(
                out=bd2[:], in_=bd2[:], pattern=[[64, 2]],
                compare_op=OP.is_ge, fill=0.0, base=63, channel_multiplier=-1),
                wait=[(P.c["g"], gq)], inc=True)
            cqkv = P.ctr(f"c_qkv{l}")
            wv = P.dma("gpsimd", cqkv, wqkv_sb[:].rearrange("p (kc n) -> p kc n", n=1536),
                       self.inp["wqkv"][l].rearrange("(kc p) n -> p kc n", p=128))
            qsl = P.dma("sync", onc, rowtmp[0:1, 0:64],
                        self.inp["qk_scale"][l, 0].unsqueeze(0))
            qsl = P.dma("sync", onc, rowtmp[0:1, 64:128],
                        self.inp["qk_scale"][l, 1].unsqueeze(0))
            P.op("vector", lambda h: h.tensor_scalar(
                qsrow[0:1, 0:64], rowtmp[0:1, 0:64], 8.0, None, op0=OP.mult),
                wait=[(onc, qsl)])
            P.op("vector", lambda h: h.tensor_scalar(
                qsrow[0:1, 64:96], rowtmp[0:1, 32:64], 8.0, None, op0=OP.mult))
            P.op("vector", lambda h: h.tensor_scalar(
                qsrow[0:1, 96:128], rowtmp[0:1, 0:32], 8.0, None, op0=OP.mult))
            P.op("vector", lambda h: h.tensor_copy(qsrow[0:1, 128:192], rowtmp[0:1, 64:128]))
            P.op("vector", lambda h: h.tensor_copy(qsrow[0:1, 192:224], rowtmp[0:1, 96:128]))
            vq = P.op("vector", lambda h: h.tensor_copy(qsrow[0:1, 224:256], rowtmp[0:1, 64:96]), inc=True)
            vq = self.bcast(qs_bc[:], qsrow[0:1, 0:64], 64, ps[5], wait=[(P.c["v"], vq)])
            vq = self.bcast(qsr_bc[:], qsrow[0:1, 64:128], 64, ps[5], wait=[(P.c["v"], vq)])
            vq = self.bcast(ks_bc[:], qsrow[0:1, 128:192], 64, ps[5], wait=[(P.c["v"], vq)])
            vq = self.bcast(ksr_bc[:], qsrow[0:1, 192:256], 64, ps[5], wait=[(P.c["v"], vq)])
            gvz = P.op("gpsimd", lambda h: h.memset(
                bass_rust.AP(xqb[0][:].tensor, 0, [[8 * 513, 128], [513, 8], [1, 1]]),
                0.0), inc=True)
            gvz = P.op("gpsimd", lambda h: h.memset(
                bass_rust.AP(vloc[:].tensor, 64, [[32 * 520, 128], [520, 32], [65, 8], [1, 1]]),
                1.0), inc=True)

            TABS = {"sin": 0, "cos": 1, "scl": 2, "isc": 3, "csq": 4, "ssq": 5,
                    "csk": 6, "ssk": 7}

            def tab(name):
                o = TABS[name] * 256
                return tabs[:, o:o + 256]

            def tab3(name):
                return tab(name).rearrange("p (a b) -> p a b", b=64)

            def tabx(name):
                o = TABS[name] * 256
                return bass_rust.AP(tabs[:].tensor, o, [[8 * 256, 128], [64, 4], [0, 8], [1, 64]])

            def bcx(t):
                return bass_rust.AP(t[:].tensor, 0, [[64, 128], [0, 4], [1, 64]])

            C2PI = float(1.0 / (2 * np.pi))
            CW1 = float(np.float32(2 * np.pi))
            CW2 = float(2 * np.pi - np.float64(np.float32(2 * np.pi)))
            RND = float(3 * 2.0 ** 22)

            mm_done = {}
            ev_done = {"q": 0, "k": 0, "v": 0}
            s_prev = 0
            tlast = 0
            norm_done = 0
            tcol_read = 0
            for i in range(NT):
                pr = i % 2
                wvl = []
                if i >= 1:
                    wvl = [(P.c["t"], mm_done[i - 1])]
                if i == 0:
                    lsrc = fm_view(src_bf, 0, TB)
                    ldst = bass_rust.AP(xqb[0][:].tensor, 1, [[8 * 513, 128], [513, 8], [1, TB]])
                    wvl.append((P.c["g"], gvz))
                else:
                    lsrc = fm_view(src_bf, i * TB - 1, i * TB + TB)
                    ldst = bass_rust.AP(xqb[0][:].tensor, 0, [[8 * 513, 128], [513, 8], [1, TB + 1]])
                lv = P.dma("sync", ld[pr], ldst, lsrc, wait=wvl)
                for tt in range(4):
                    for which, pso in (("q", ps[0]), ("k", ps[1]), ("v", ps[2])):
                        off = {"q": 0, "k": 512, "v": 1024}[which]
                        tw = []
                        if tt == 0 and which == "q":
                            tw = [(ld[pr], lv)]
                            if i == 0:
                                tw.append((cqkv, wv))
                        if ev_done[which]:
                            tw.append((P.c["v"], ev_done[which]))
                        for kc in range(NCH):
                            sh = 1 if kc < 4 else 0
                            mm_done[i] = P.op("tensor", lambda h, tt_=tt, kc_=kc, sh_=sh, o_=off, p_=pr, s_=pso: h.matmul(
                                s_[:, 0:512],
                                xqb[0][:, kc_ * 513 + sh_ + tt_ * 128:kc_ * 513 + sh_ + tt_ * 128 + 128],
                                wqkv_sb[:, kc_ * 1536 + o_:kc_ * 1536 + o_ + 512],
                                start=(kc_ == 0), stop=(kc_ == NCH - 1)),
                                wait=(tw if kc == 0 else ()), inc=(kc == NCH - 1))
                        if which == "v":
                            vdst = bass_rust.AP(vloc[:].tensor, (i * 4 + tt) * 520,
                                                [[32 * 520, 128], [65, 8], [1, 64]])
                            ev_done[which] = P.op("vector", lambda h, d_=vdst, s_=pso: h.tensor_copy(
                                d_, s_[:, 0:512].rearrange("p (a b) -> p a b", b=64)),
                                wait=[(P.c["t"], mm_done[i])], inc=True)
                        else:
                            dstb = qtokb if which == "q" else ktokb
                            wt = [(P.c["t"], mm_done[i])]
                            if i > 0 or tt > 0:
                                wt.append((P.c["v"], P.c["v"].v))
                            ev_done[which] = P.op("vector", lambda h, tt_=tt, d_=dstb, s_=pso: h.tensor_copy(
                                d_[:, tt_ * 512:(tt_ + 1) * 512], s_[:, 0:512]),
                                wait=wt, inc=True)
                gi2 = P.op("gpsimd", lambda h, i_=i: h.iota(
                    tcol[:], pattern=[[128, 4]], base=i_ * TB, channel_multiplier=1,
                    allow_small_or_imprecise_dtypes=True),
                    wait=[(P.c["v"], tcol_read)] if i > 0 else (), inc=True)
                tcx = bass_rust.AP(tcol[:].tensor, 0, [[4, 128], [1, 4], [0, 64]])
                vt = P.op("vector", lambda h: h.tensor_tensor(
                    tab3("sin"), tcx, bcx(invf_bc), op=OP.mult),
                    wait=[(P.c["g"], gi2), (P.c["v"], vq), (P.c["s"], s_prev)], inc=True)
                P.op("vector", lambda h: h.tensor_scalar(
                    tab("cos"), tab("sin"), C2PI, RND, op0=OP.mult, op1=OP.add))
                P.op("vector", lambda h: h.tensor_scalar(
                    tab("cos"), tab("cos"), RND, None, op0=OP.subtract))
                P.op("vector", lambda h: h.scalar_tensor_tensor(
                    tab("scl"), tab("cos"), -CW1, tab("sin"), op0=OP.mult, op1=OP.add))
                P.op("vector", lambda h: h.scalar_tensor_tensor(
                    tab("scl"), tab("cos"), -CW2, tab("scl"), op0=OP.mult, op1=OP.add))
                P.op("vector", lambda h: h.tensor_scalar(
                    tab("isc"), tab("sin"), C2PI, 0.25, op0=OP.mult, op1=OP.add))
                P.op("vector", lambda h: h.tensor_scalar(
                    tab("isc"), tab("isc"), RND, None, op0=OP.add))
                P.op("vector", lambda h: h.tensor_scalar(
                    tab("isc"), tab("isc"), RND, None, op0=OP.subtract))
                P.op("vector", lambda h: h.scalar_tensor_tensor(
                    tab("csq"), tab("isc"), -CW1, tab("sin"), op0=OP.mult, op1=OP.add))
                P.op("vector", lambda h: h.scalar_tensor_tensor(
                    tab("csq"), tab("isc"), -CW2, tab("csq"), op0=OP.mult, op1=OP.add))
                P.op("vector", lambda h: h.tensor_scalar(
                    tab("csq"), tab("csq"), 1.0, float(np.pi / 2), op0=OP.mult, op1=OP.add))
                PI_LO = float(np.nextafter(np.float32(np.pi), np.float32(0)))
                P.op("vector", lambda h, p_=PI_LO: h.tensor_scalar(
                    tab("csq"), tab("csq"), p_, -p_, op0=OP.min, op1=OP.max))
                vt2 = P.op("vector", lambda h, p_=PI_LO: h.tensor_scalar(
                    tab("scl"), tab("scl"), p_, -p_, op0=OP.min, op1=OP.max), inc=True)
                st_ = P.op("scalar", lambda h: h.activation(tab("sin"), tab("scl"), AF.Sin),
                           wait=[(P.c["v"], vt2)], inc=True)
                st_ = P.op("scalar", lambda h: h.activation(tab("cos"), tab("csq"), AF.Sin), inc=True)
                vt3 = P.op("vector", lambda h: h.tensor_scalar(
                    tcol[:], tcol[:], 1.0 / 256.0, -8.0, op0=OP.mult, op1=OP.add),
                    wait=[(P.c["s"], st_)], inc=True)
                vt3 = P.op("vector", lambda h: h.tensor_tensor(
                    tab3("scl"), tcx, bcx(lnb_bc), op=OP.mult), inc=True)
                tcol_read = vt3
                st2 = P.op("scalar", lambda h: h.activation(tab("isc"), tab("scl"), AF.Exp,
                                                            scale=-1.0),
                           wait=[(P.c["v"], vt3)], inc=True, drain=True)
                st2 = P.op("scalar", lambda h: h.activation(tab("scl"), tab("scl"), AF.Exp), inc=True)
                vt4 = P.op("vector", lambda h: h.tensor_tensor(tab3("csq"), tab3("cos"), bcx(qs_bc), op=OP.mult),
                           wait=[(P.c["s"], st2)])
                P.op("vector", lambda h: h.tensor_tensor(tab("csq"), tab("csq"), tab("scl"), op=OP.mult))
                P.op("vector", lambda h: h.tensor_tensor(tab3("ssq"), tab3("sin"), bcx(qsr_bc), op=OP.mult))
                P.op("vector", lambda h: h.tensor_tensor(tab("ssq"), tab("ssq"), tab("scl"), op=OP.mult))
                P.op("vector", lambda h: h.tensor_tensor(tab3("ssq"), tab3("ssq"), bcx(sgn_bc), op=OP.mult))
                P.op("vector", lambda h: h.tensor_tensor(tab3("csk"), tab3("cos"), bcx(ks_bc), op=OP.mult))
                P.op("vector", lambda h: h.tensor_tensor(tab("csk"), tab("csk"), tab("isc"), op=OP.mult))
                P.op("vector", lambda h: h.tensor_tensor(tab3("ssk"), tab3("sin"), bcx(ksr_bc), op=OP.mult))
                P.op("vector", lambda h: h.tensor_tensor(tab("ssk"), tab("ssk"), tab("isc"), op=OP.mult))
                vt4 = P.op("vector", lambda h: h.tensor_tensor(tab3("ssk"), tab3("ssk"), bcx(sgn_bc), op=OP.mult), inc=True)
                for qk in range(2):
                    tokb = qtokb if qk == 0 else ktokb
                    ctn, stn = ("csq", "ssq") if qk == 0 else ("csk", "ssk")
                    o_ = qk * 32
                    sq_ = P.op("scalar", lambda h, t_=tokb: h.activation(
                        sq2[:], t_[:], AF.Square),
                        wait=[(P.c["v"], ev_done["q" if qk == 0 else "k"])], inc=True)
                    rv_ = P.op("vector", lambda h, o2=o_: h.tensor_reduce(
                        ssq[:, o2:o2 + 32], sq2[:].rearrange("p (a c) -> p a c", c=64),
                        axis=AX.X, op=OP.add), wait=[(P.c["s"], sq_)], inc=True)
                    s0 = rloc[:, o_:o_ + 32]
                    ssv = ssq[:, o_:o_ + 32]
                    sA_ = P.op("scalar", lambda h, a_=ssv, b_=s0: h.activation(
                        b_, a_, AF.Sqrt), wait=[(P.c["v"], rv_)], inc=True)
                    nsc = self.rrow[0:128, 0:64]
                    P.op("vector", lambda h, b_=s0, n_=nsc: h.reciprocal(n_[:, 0:32], b_),
                         wait=[(P.c["s"], sA_)])
                    P.op("vector", lambda h, a_=ssv, n_=nsc: h.tensor_tensor(
                        n_[:, 0:32], a_, n_[:, 0:32], op=OP.mult))
                    P.op("vector", lambda h, b_=s0, n_=nsc: h.tensor_tensor(
                        b_, b_, n_[:, 0:32], op=OP.add))
                    P.op("vector", lambda h, b_=s0: h.tensor_scalar(
                        b_, b_, 0.5, 1e-12, op0=OP.mult, op1=OP.max))
                    rv2 = P.op("vector", lambda h, b_=s0: h.reciprocal(b_, b_), inc=True)
                    rx = bass_rust.AP(rloc[:].tensor, o_, [[64, 128], [8, 4], [1, 8], [0, 64]])
                    a1 = P.op("vector", lambda h, t_=tokb, r_=rx: h.tensor_tensor(
                        t_[:].rearrange("p (a b) -> p a b", b=64),
                        t_[:].rearrange("p (a b) -> p a b", b=64), r_, op=OP.mult),
                        wait=[(P.c["v"], rv2)], inc=True)
                    P.op("vector", lambda h, t_=tokb, c_=tabx(ctn): h.tensor_tensor(
                        t3b[:].rearrange("p (a b) -> p a b", b=64),
                        t_[:].rearrange("p (a b) -> p a b", b=64), c_, op=OP.mult),
                        wait=[(P.c["t"], tlast)] if tlast else ())
                    sx0 = bass_rust.AP(tabs[:].tensor, TABS[stn] * 256,
                                       [[8 * 256, 128], [64, 4], [0, 8], [1, 32]])
                    sx1 = bass_rust.AP(tabs[:].tensor, TABS[stn] * 256 + 32,
                                       [[8 * 256, 128], [64, 4], [0, 8], [1, 32]])
                    P.op("vector", lambda h, t_=tokb, s_=sx0: h.tensor_tensor(
                        bass_rust.AP(rpb[:].tensor, 0, [[2048, 128], [512, 4], [64, 8], [1, 32]]),
                        bass_rust.AP(t_[:].tensor, 32, [[2048, 128], [512, 4], [64, 8], [1, 32]]),
                        s_, op=OP.mult))
                    P.op("vector", lambda h, t_=tokb, s_=sx1: h.tensor_tensor(
                        bass_rust.AP(rpb[:].tensor, 32, [[2048, 128], [512, 4], [64, 8], [1, 32]]),
                        bass_rust.AP(t_[:].tensor, 0, [[2048, 128], [512, 4], [64, 8], [1, 32]]),
                        s_, op=OP.mult))
                    ap_ = P.op("vector", lambda h: h.tensor_tensor(
                        t3b[:], t3b[:], rpb[:], op=OP.add), inc=True)
                    dstf = qf if qk == 0 else kf
                    for tt in range(4):
                        for cc in range(4):
                            tr = P.op("tensor", lambda h, tt_=tt, cc_=cc: h.transpose(
                                ps[3][:, cc_ * 128:(cc_ + 1) * 128],
                                t3b[:, tt_ * 512 + cc_ * 128:tt_ * 512 + (cc_ + 1) * 128],
                                self.ident_f[:]),
                                wait=([(P.c["v"], ap_)] if (cc == 0 and tt == 0) else ()),
                                inc=(cc == 3))
                        tlast = tr
                        trd = P.op("vector", lambda h, tt_=tt, i_=i, d_=dstf: h.tensor_copy(
                            bass_rust.AP(d_[:].tensor, i_ * 512 + tt_ * 128,
                                         [[4 * 4096, 128], [4096, 4], [1, 128]]),
                            ps[3][:, 0:512].rearrange("p (a b) -> p a b", b=128)),
                            wait=[(P.c["t"], tr)], inc=True)
                        P.emit("tensor", lambda h: h.nop(), wait=[(P.c["v"], trd)])
                    for cc in range(4):
                        sqv = P.op("scalar", lambda h, cc_=cc, i_=i, d_=dstf: h.activation(
                            sqb[:], d_[:, cc_ * 4096 + i_ * 512:cc_ * 4096 + (i_ + 1) * 512],
                            AF.Square), wait=[(P.c["v"], trd)], inc=True)
                        nm2 = P.op("tensor", lambda h: h.matmul(
                            ps[6][0:2, 0:TB], bd2[:], sqb[:], start=True, stop=True),
                            wait=[(P.c["s"], sqv), (P.c["v"], norm_done), (P.c["g"], gbd)], inc=True)
                        col = qk * 32 + cc * 8 + i
                        norm_done = P.op("vector", lambda h, col_=col: h.tensor_reduce(
                            nloc[0:2, col_:col_ + 1], ps[6][0:2, 0:TB], axis=AX.X, op=OP.max),
                            wait=[(P.c["t"], nm2)], inc=True)
                        P.emit("scalar", lambda h: h.nop(), wait=[(P.c["t"], nm2)])
                s_prev = P.c["s"].v

            phA.close()
            smv = sc["small"]
            for qk_ in range(2):
                d1 = P.dma("sync", onc,
                           bass_rust.AP(smv.tensor, 64 + 64 * qk_, [[8, 2], [16, 4], [1, 8]]),
                           nloc[0:2, 32 * qk_:32 * (qk_ + 1)].rearrange("p (b c) -> p b c", c=8),
                           wait=[(P.c["v"], norm_done)], slow_ok=True)
            qrow = brow2[0:1, 0:64]
            krow = brow2[0:1, 64:128]
            d2 = P.dma("sync", onc, qrow, bass_rust.AP(smv.tensor, 64, [[64, 1], [1, 64]]),
                       wait=[(onc, d1)])
            d2 = P.dma("sync", onc, krow, bass_rust.AP(smv.tensor, 128, [[64, 1], [1, 64]]))
            sq_ = P.op("scalar", lambda h: h.activation(qrow, qrow, AF.Sqrt), wait=[(onc, d2)], inc=True)
            sq_ = P.op("scalar", lambda h: h.activation(krow, krow, AF.Sqrt), inc=True)
            kwin = rowtmp[0:1, 0:64]
            vb_ = P.op("vector", lambda h: h.tensor_copy(kwin, krow), wait=[(P.c["s"], sq_)])
            for hh_ in range(8):
                vb_ = P.op("vector", lambda h, o_=hh_ * 8: h.tensor_tensor(
                    kwin[0:1, o_ + 1:o_ + 8], krow[0:1, o_ + 1:o_ + 8],
                    krow[0:1, o_:o_ + 7], op=OP.max), inc=(hh_ == 7))
            P.op("vector", lambda h: h.tensor_tensor(qrow, qrow, kwin, op=OP.mult))
            vb_ = P.op("vector", lambda h: h.tensor_scalar(
                qrow, qrow, -1.0, -0.5, op0=OP.mult, op1=OP.add), inc=True)
            vb_ = self.bcast(Bloc[:], qrow, 64, ps[5], wait=[(P.c["v"], vb_)])
            P.barrier(f"p3a{l}")

            SGN_NEG = -1.0e30
            msk = sb("f_msk", [128, 11 * 128], BF16)
            gm = P.op("gpsimd", lambda h: h.memset(msk[:], 0.0), inc=True)
            for dmi in range(11):
                dmp = dmi - 3
                gm = P.op("gpsimd", lambda h, d_=dmp, i_=10 - dmi: h.affine_select(
                    out=msk[:, i_ * 128:(i_ + 1) * 128], in_=msk[:, i_ * 128:(i_ + 1) * 128],
                    pattern=[[-1, 128]], compare_op=OP.is_ge, fill=SGN_NEG,
                    base=128 * d_, channel_multiplier=1),
                    wait=[(P.c["g"], gm)], inc=True)
                gm = P.op("gpsimd", lambda h, d_=dmp, i_=10 - dmi: h.affine_select(
                    out=msk[:, i_ * 128:(i_ + 1) * 128], in_=msk[:, i_ * 128:(i_ + 1) * 128],
                    pattern=[[1, 128]], compare_op=OP.is_ge, fill=SGN_NEG,
                    base=512 - 128 * d_, channel_multiplier=-1),
                    wait=[(P.c["g"], gm)], inc=True)
            kzf = sb("f_kzf", [128, 2 * 1024], BF16)
            probsb = sb("f_probsb", [128, 2 * 1024], BF16)
            probsT = sb("f_probsT", [128, 2 * 1024], BF16)
            mrow = sb("f_mrow", [128, 8], F32)
            obuf = sb("f_obuf", [128, 4 * 512], BF16)
            wo_sb = sb("f_wo", [128, 4 * 1024], BF16)
            xmb = sb("f_xmb", [128, NCH * TB], F32)
            cwo = P.ctr(f"c_wo{l}")
            wvo = P.dma("gpsimd", cwo, wo_sb[:].rearrange("p (a b) -> p a b", b=1024),
                        self.inp["wo0"][l].rearrange("(a p) c -> p a c", p=128))
            gz3 = P.op("gpsimd", lambda h: h.memset(kzf[:], 0.0), inc=True)
            iters = []
            for w in FFN_W:
                for hp in range(4):
                    for it in range(4):
                        for j in range(2):
                            iters.append((w, hp, it, j))
            K = len(iters)
            st_grp, st_mx, st_exp, st_trE, st_evn = {}, {}, {}, {}, {}
            kz_done = {}
            prevG = {0: 0, 1: 0}
            xst, xm_store, lvx = {}, {}, {}
            last_of_w = {}
            for k in range(K - 1, -1, -1):
                w = iters[k][0]
                if w not in last_of_w:
                    last_of_w[w] = k

            def kz_build(w, hp):
                jlo = max(0, (w - 1) * 4)
                nj = (((w + 1) * 4) - jlo) * 128
                kzw = [(P.c["t"], P.c["t"].v)] if (w, hp) != (FFN_W[0], 0) else [(P.c["g"], gz3)]
                P.op("vector", lambda h, hp_=hp, c0=jlo * 128, n_=nj: h.tensor_copy(
                    kzf[0:64, 0:n_], kf[0:64, hp_ * 4096 + c0:hp_ * 4096 + c0 + n_]),
                    wait=kzw)
                kz_done[(w, hp)] = P.op("vector", lambda h, hp_=hp, c0=jlo * 128, n_=nj: h.tensor_copy(
                    kzf[64:128, 1024:1024 + n_], kf[64:128, hp_ * 4096 + c0:hp_ * 4096 + c0 + n_]),
                    inc=True)

            def stage_A(k):
                w, hp, it, j = iters[k]
                njt = 8 if w >= 1 else 4
                p = k % 2
                ig = w * 4 + it
                jlo = max(0, (w - 1) * 4)
                mm = None
                for jh in range(2 if njt == 8 else 1):
                    pst = ps[2 * p + jh]
                    tw = []
                    if jh == 0:
                        if k >= 2:
                            tw += [(P.c["s"], st_exp[k - 2]), (P.c["v"], st_trE[k - 2])]
                        if it == 0 and j == 0:
                            tw.append((P.c["v"], kz_done[(w, hp)]))
                            if (w, hp) == (FFN_W[0], 0):
                                tw.append((P.c["g"], gm))
                    P.op("tensor", lambda h, j_=j, jh_=jh, ig_=ig, hp_=hp, p_=pst: h.matmul(
                        p_[:, 0:512],
                        qf[:, hp_ * 4096 + ig_ * 128:hp_ * 4096 + (ig_ + 1) * 128],
                        kzf[:, j_ * 1024 + jh_ * 512:j_ * 1024 + jh_ * 512 + 512],
                        start=True, stop=False), wait=tw)
                    dmi0 = (ig - (jlo + jh * 4)) + 3
                    mm = P.op("tensor", lambda h, d_=dmi0, p_=pst: h.matmul(
                        p_[:, 0:512], self.ident_b[:],
                        msk[:, (10 - d_) * 128:(14 - d_) * 128],
                        start=False, stop=True), inc=True)
                st_grp[k] = mm

            def stage_BC(k):
                w, hp, it, j = iters[k]
                njt = 8 if w >= 1 else 4
                p = k % 2
                mx = P.op("vector", lambda h, p_=p: h.tensor_reduce(
                    mrow[:, 4 * p_:4 * p_ + 1], ps[2 * p_][:, 0:512], axis=AX.X,
                    op=OP.max, negate=True), wait=[(P.c["t"], st_grp[k])], inc=(njt == 4))
                if njt == 8:
                    P.op("vector", lambda h, p_=p: h.tensor_reduce(
                        mrow[:, 4 * p_ + 1:4 * p_ + 2], ps[2 * p_ + 1][:, 0:512],
                        axis=AX.X, op=OP.max, negate=True))
                    mx = P.op("vector", lambda h, p_=p: h.tensor_tensor(
                        mrow[:, 4 * p_:4 * p_ + 1], mrow[:, 4 * p_:4 * p_ + 1],
                        mrow[:, 4 * p_ + 1:4 * p_ + 2], op=OP.min), inc=True)
                st_mx[k] = mx
                ev = None
                for jh in range(2 if njt == 8 else 1):
                    ev = P.op("scalar", lambda h, p_=p, jh_=jh: h.activation(
                        probsb[:, p_ * 1024 + jh_ * 512:p_ * 1024 + jh_ * 512 + 512],
                        ps[2 * p_ + jh_][:, 0:512],
                        AF.Exp, bias=mrow[:, 4 * p_:4 * p_ + 1], scale=1.0),
                        wait=([(P.c["v"], mx)] if jh == 0 else ()), inc=True)
                st_exp[k] = ev

            def stage_DE(k):
                w, hp, it, j = iters[k]
                njt = 8 if w >= 1 else 4
                p = k % 2
                tr = None
                for jc in range(njt):
                    tr = P.op("tensor", lambda h, p_=p, jc_=jc: h.transpose(
                        ps[2 * p_][:].bitcast(BF16)[:, jc_ * 128:(jc_ + 1) * 128],
                        probsb[:, p_ * 1024 + jc_ * 128:p_ * 1024 + (jc_ + 1) * 128],
                        self.ident_b[:]),
                        wait=([(P.c["s"], st_exp[k])] if jc == 0 else ()),
                        inc=(jc == njt - 1))
                st_trE[k] = P.op("vector", lambda h, p_=p, n_=njt * 128: h.tensor_copy(
                    probsT[:, p_ * 1024:p_ * 1024 + n_],
                    ps[2 * p_][:].bitcast(BF16)[:, 0:n_]),
                    wait=[(P.c["t"], tr)], inc=True, drain=False)

            def stage_FG(k):
                w, hp, it, j = iters[k]
                njt = 8 if w >= 1 else 4
                p = k % 2
                jlo = max(0, (w - 1) * 4)
                avv = None
                for jc in range(njt):
                    jg = jlo + jc
                    vb0 = jg * 520 + 65 * (2 * hp + j)
                    tw = []
                    if jc == 0:
                        tw = [(P.c["v"], st_trE[k])]
                        if it == 0:
                            tw.append((P.c["v"], prevG[j]))
                    if j == 0:
                        avv = P.op("tensor", lambda h, jc_=jc, b_=vb0, p_=p, it_=it,
                                   st_=(jc == 0), sp_=(jc == njt - 1): h.matmul(
                            ps[6][0:65, it_ * 128:(it_ + 1) * 128],
                            vloc[:, b_:b_ + 65],
                            probsT[:, p_ * 1024 + jc_ * 128:p_ * 1024 + (jc_ + 1) * 128],
                            start=st_, stop=sp_), wait=tw, inc=(jc == njt - 1))
                    else:
                        P.op("tensor", lambda h, jc_=jc, b_=vb0, p_=p, it_=it,
                             st_=(jc == 0), sp_=(jc == njt - 1): h.matmul(
                            ps[7][64:128, it_ * 128:(it_ + 1) * 128],
                            vloc[:, b_:b_ + 64],
                            probsT[:, p_ * 1024 + jc_ * 128:p_ * 1024 + (jc_ + 1) * 128],
                            start=st_, stop=sp_), wait=tw)
                        avv = P.op("tensor", lambda h, jc_=jc, b_=vb0, p_=p, it_=it,
                                   st_=(jc == 0), sp_=(jc == njt - 1): h.matmul(
                            ps[4][0:1, it_ * 128:(it_ + 1) * 128],
                            vloc[:, b_ + 64:b_ + 65],
                            probsT[:, p_ * 1024 + jc_ * 128:p_ * 1024 + (jc_ + 1) * 128],
                            start=st_, stop=sp_), inc=(jc == njt - 1))
                if it < 3 or j == 0:
                    return
                pr_ = hp
                rre = self.rrow[64:65, 0:512]
                rro = self.rrow[32:33, 0:512]
                rv = P.op("vector", lambda h, r_=rre: h.reciprocal(
                    r_, ps[6][64:65, 0:512]), wait=[(P.c["t"], avv)], inc=True)
                rv2 = P.op("vector", lambda h, r_=rro: h.reciprocal(
                    r_, ps[4][0:1, 0:512]), inc=True)
                bce = P.op("tensor", lambda h, r_=rre: h.matmul(
                    ps[4][:, 0:512], self.onesP[64:65, :], r_, start=True, stop=True),
                    wait=[(P.c["v"], rv2)], inc=True)
                cpe = P.op("vector", lambda h: h.tensor_copy(
                    self.bcb[0:64, 0:512], ps[4][0:64, 0:512]),
                    wait=[(P.c["t"], bce)], inc=True)
                prevG[0] = P.op("vector", lambda h, pr2=pr_: h.tensor_tensor(
                    obuf[0:64, pr2 * 512:(pr2 + 1) * 512],
                    ps[6][0:64, 0:512], self.bcb[0:64, 0:512], op=OP.mult),
                    inc=True)
                bco = P.op("tensor", lambda h, r_=rro: h.matmul(
                    ps[4][:, 0:512], self.onesP[32:33, :], r_, start=True, stop=True),
                    wait=[(P.c["v"], cpe)], inc=True)
                P.op("vector", lambda h: h.tensor_copy(
                    self.bcb[64:128, 0:512], ps[4][64:128, 0:512]),
                    wait=[(P.c["t"], bco)])
                prevG[1] = st_evn[k] = P.op("vector", lambda h, pr2=pr_: h.tensor_tensor(
                    obuf[64:128, pr2 * 512:(pr2 + 1) * 512],
                    ps[7][64:128, 0:512], self.bcb[64:128, 0:512], op=OP.mult),
                    inc=True)

            def wo_block(w):
                for c in range(NCH):
                    wo_mm = None
                    for ic in range(4):
                        tw = []
                        if c == 0 and ic == 0:
                            tw = [(P.c["v"], st_evn[last_of_w[w]]), (ld[w % 2], lvx[w])]
                            if w == FFN_W[0]:
                                tw.append((cwo, wvo))
                            if (w - 1, NCH - 1) in xst:
                                tw.append((P.c["v"], xst[(w - 1, NCH - 1)]))
                        if ic == 0 and (w, c - 1) in xst:
                            tw.append((P.c["v"], xst[(w, c - 1)]))
                        wo_mm = P.op("tensor", lambda h, c_=c, ic_=ic: h.matmul(
                            ps[5][:, 0:512],
                            wo_sb[:, ic_ * 1024 + c_ * 128:ic_ * 1024 + (c_ + 1) * 128],
                            obuf[:, ic_ * 512:(ic_ + 1) * 512],
                            start=(ic_ == 0), stop=(ic_ == 3)),
                            wait=tw, inc=(ic == 3))
                    xst[(w, c)] = P.op("vector", lambda h, c_=c: h.tensor_tensor(
                        xmb[:, c_ * TB:(c_ + 1) * TB], xmb[:, c_ * TB:(c_ + 1) * TB],
                        ps[5][:, 0:512], op=OP.add), wait=[(P.c["t"], wo_mm)], inc=True)
                xm_store[w] = P.dma("sync", ost[w % 2], fm_view(sc["xmid"], w * TB, (w + 1) * TB),
                                    sb3(xmb), wait=[(P.c["v"], xst[(w, NCH - 1)])])
                if w != FFN_W[-1]:
                    wn = w + 1
                    lvx[wn] = P.dma("sync", ld[wn % 2], sb3(xmb),
                                    fm_view(src_f, wn * TB, (wn + 1) * TB),
                                    wait=[(P.c["v"], xst[(w, NCH - 1)]),
                                          (ost[w % 2], xm_store[w])])

            lvx[FFN_W[0]] = P.dma("sync", ld[FFN_W[0] % 2], sb3(xmb),
                                  fm_view(src_f, FFN_W[0] * TB, (FFN_W[0] + 1) * TB))
            for k in range(K):
                w, hp, it, j = iters[k]
                if it == 0 and j == 0:
                    kz_build(w, hp)
                stage_A(k)
                if k >= 1:
                    stage_DE(k - 1)
                if k >= 2:
                    stage_FG(k - 2)
                    if last_of_w.get(iters[k - 2][0]) == k - 2:
                        wo_block(iters[k - 2][0])
                stage_BC(k)
            stage_DE(K - 1)
            stage_FG(K - 2)
            stage_FG(K - 1)
            wo_block(iters[K - 1][0])
        return

    def ffn_subC(self, l, src_f, dst_f, dst_bf, onc, gonc):
        nc, P, sc = self.nc, self.P, self.sc
        ps = self.ps
        NJ = 22
        with ExitStack() as ph:
            sb = lambda n, shp, dt: self.sbuf(f"{n}_{l}", shp, dt, ph)
            w1_sb = sb("c_w1", [128, 8 * 5632], BF16)
            w2_sb = sb("c_w2", [128, 22 * 1024], BF16)
            xms = sb("c_xms", [128, 8 * 513], F32)
            xln = sb("c_xln", [128, NCH * TB], BF16)
            hbuf = sb("c_hbuf", [128, NJ * 512], BF16)
            xsh = hbuf[:][:, 0:NCH * TB]
            xsq = hbuf[:][:, NCH * TB:2 * NCH * TB]
            gel = sb("c_gel", [128, TB], BF16)
            gvec = sb("c_gvec", [128, 3 * NCH], F32)
            b1v = sb("c_b1v", [128, 44], F32)
            ostg = sb("c_ostg", [128, 2 * TB], F32)
            obf = sb("c_obf", [128, TB], BF16)
            ld = P.dpool(f"c_ld{l}", 2)
            ost = P.dpool(f"c_ost{l}", 2)
            cw1 = P.ctr(f"c_w1{l}")
            cw2 = P.ctr(f"c_w2{l}")
            gw1 = P.dma("gpsimd", cw1, w1_sb[:].rearrange("p (kc n) -> p kc n", n=5632),
                        self.inp["w1f"][l].rearrange("(kc p) n -> p kc n", p=128))
            gw2 = P.dma("gpsimd", cw2, w2_sb[:].rearrange("p (kc n) -> p kc n", n=1024),
                        self.inp["w2f"][l].rearrange("(kc p) n -> p kc n", p=128))
            lv0 = P.dma("sync", onc, gvec[:, 0:NCH], self.inp["ln_g"][l])
            lv0 = P.dma("sync", onc, gvec[:, NCH:2 * NCH], self.inp["ln_b"][l])
            lv0 = P.dma("sync", onc, gvec[:, 2 * NCH:3 * NCH], self.inp["b2f"][l])
            lv0 = P.dma("sync", onc, b1v[:], self.inp["b1f"][l])
            gz = P.op("gpsimd", lambda h: h.memset(
                bass_rust.AP(xms[:].tensor, 0, [[8 * 513, 128], [513, 8], [1, 1]]), 0.0),
                inc=True)
            stt = self.st_tmp
            t0, t1, t2, t3 = (stt[0:1, k * TB:(k + 1) * TB] for k in range(4))
            mm_last = 0
            st_last = {}
            casts = {}
            for w in range(NT):
                pr = w % 2
                wvl = ([(P.c["v"], casts[w - 1]), (P.c["v"], mm_last)]
                       if w >= 1 else [(P.c["g"], gz)])
                if w == 0:
                    lds = bass_rust.AP(xms[:].tensor, 1, [[8 * 513, 128], [513, 8], [1, TB]])
                    lsrc = fm_view(sc["xmid"], 0, TB)
                else:
                    lds = bass_rust.AP(xms[:].tensor, 0, [[8 * 513, 128], [513, 8], [1, TB + 1]])
                    lsrc = fm_view(sc["xmid"], w * TB - 1, w * TB + TB)
                lv = P.dma("sync", ld[pr], lds, lsrc, wait=wvl)
                for c in range(NCH):
                    sh = 1 if c < 4 else 0
                    casts[w] = P.op("vector", lambda h, c_=c, s_=sh: h.tensor_copy(
                        xsh[:, c_ * TB:(c_ + 1) * TB], xms[:, c_ * 513 + s_:c_ * 513 + s_ + TB]),
                        wait=([(ld[pr], lv)] + ([(P.c["v"], mm_last)] if mm_last else []))
                        if c == 0 else (), inc=(c == NCH - 1), drain=(c == NCH - 1))
                sq = P.op("vector", lambda h: h.tensor_tensor(
                    xsq[:], xsh[:], xsh[:], op=OP.mult), inc=True)
                for c in range(NCH):
                    P.op("tensor", lambda h, c_=c: h.matmul(
                        ps[0][0:1, 0:TB], self.ones_col_b[:], xsh[:, c_ * TB:(c_ + 1) * TB],
                        start=(c_ == 0), stop=(c_ == NCH - 1)),
                        wait=[(P.c["v"], casts[w])] if c == 0 else ())
                sums = None
                for c in range(NCH):
                    sums = P.op("tensor", lambda h, c_=c: h.matmul(
                        ps[1][0:1, 0:TB], self.ones_col_b[:], xsq[:, c_ * TB:(c_ + 1) * TB],
                        start=(c_ == 0), stop=(c_ == NCH - 1)),
                        wait=[(P.c["v"], sq)] if c == 0 else (), inc=(c == NCH - 1))
                mu_b = self.st_mu[0:1, pr * TB:(pr + 1) * TB]
                r_b = self.st_r[0:1, pr * TB:(pr + 1) * TB]
                P.op("vector", lambda h, m_=mu_b: h.tensor_scalar(
                    m_, ps[0][0:1, 0:TB], 1.0 / DM, None, op0=OP.mult),
                    wait=[(P.c["t"], sums)])
                P.op("vector", lambda h: h.tensor_scalar(
                    t0, ps[1][0:1, 0:TB], 1.0 / DM, None, op0=OP.mult))
                P.op("vector", lambda h, m_=mu_b: h.tensor_tensor(t1, m_, m_, op=OP.mult))
                P.op("vector", lambda h: h.scalar_tensor_tensor(
                    t2, t0, 1.0, t1, op0=OP.mult, op1=OP.subtract))
                vvar = P.op("vector", lambda h: h.tensor_scalar(
                    t2, t2, 1e-5, None, op0=OP.add), inc=True)
                vr = self.rsqrt_chain((t0, t1, t3), t2, r_b, [(P.c["v"], vvar)])
                bmu = P.op("tensor", lambda h, m_=mu_b: h.matmul(
                    ps[2][:, 0:TB], self.ones_f[:], m_, start=True, stop=True),
                    wait=[(P.c["v"], vr)])
                brr = P.op("tensor", lambda h, r_=r_b: h.matmul(
                    ps[3][:, 0:TB], self.ones_f[:], r_, start=True, stop=True), inc=True)
                ap_d = None
                for c in range(NCH):
                    sh = 1 if c < 4 else 0
                    aw0 = []
                    if c == 0:
                        aw0 = [(P.c["t"], brr)]
                        if st_last.get("prev"):
                            aw0.append((ost[st_last["prevpr"]], st_last["prev"]))
                    P.op("vector", lambda h, c_=c, s_=sh: h.tensor_tensor(
                        ostg[:, 0:TB], xms[:, c_ * 513 + s_:c_ * 513 + s_ + TB], ps[2][:, 0:TB],
                        op=OP.subtract),
                        wait=aw0)
                    P.op("vector", lambda h: h.tensor_tensor(
                        ostg[:, 0:TB], ostg[:, 0:TB], ps[3][:, 0:TB], op=OP.mult))
                    ap_d = P.op("vector", lambda h, c_=c: h.tensor_scalar(
                        xln[:, c_ * TB:(c_ + 1) * TB], ostg[:, 0:TB],
                        gvec[:, c_:c_ + 1], gvec[:, NCH + c_:NCH + c_ + 1],
                        op0=OP.mult, op1=OP.add),
                        wait=[(onc, lv0)] if (w == 0 and c == 0) else (), inc=(c == NCH - 1))
                glu_at = {}
                sgl_at = {}
                glu_d = None
                for j in range(NJ):
                    jp = j % 2
                    pa_ = ps[4] if jp == 0 else ps[0]
                    pg_ = ps[5] if jp == 0 else ps[1]
                    tw = []
                    if j == 0:
                        tw = [(P.c["v"], ap_d)]
                        if w == 0:
                            tw.append((cw1, gw1))
                    elif j == 1:
                        tw = [(P.c["v"], vr)]
                    else:
                        tw = [(P.c["v"], glu_at[j - 2])]
                    for kc in range(NCH):
                        P.op("tensor", lambda h, j_=j, kc_=kc, p_=pa_: h.matmul(
                            p_[:, 0:512],
                            w1_sb[:, kc_ * 5632 + j_ * 128:kc_ * 5632 + (j_ + 1) * 128],
                            xln[:, kc_ * TB:(kc_ + 1) * TB],
                            start=(kc_ == 0), stop=(kc_ == NCH - 1)),
                            wait=(tw if kc == 0 else ()))
                    gmm = None
                    for kc in range(NCH):
                        gmm = P.op("tensor", lambda h, j_=j, kc_=kc, p_=pg_: h.matmul(
                            p_[:, 0:512],
                            w1_sb[:, kc_ * 5632 + 2816 + j_ * 128:kc_ * 5632 + 2816 + (j_ + 1) * 128],
                            xln[:, kc_ * TB:(kc_ + 1) * TB],
                            start=(kc_ == 0), stop=(kc_ == NCH - 1)),
                            wait=([(P.c["s"], sgl_at[j - 2])] if j >= 2 and kc == 0 else ()),
                            inc=(kc == NCH - 1))
                    sgl_at[j] = P.op("scalar", lambda h, j_=j, p_=pg_: h.activation(
                        gel[:], p_[:, 0:512], AF.Gelu,
                        bias=b1v[:, 22 + j_:22 + j_ + 1], scale=1.0),
                        wait=[(P.c["t"], gmm)] + ([(P.c["v"], glu_at[j - 1])] if j >= 1 else []),
                        inc=True)
                    glu_d = glu_at[j] = P.op("vector", lambda h, j_=j, p_=pa_: h.scalar_tensor_tensor(
                        hbuf[:, j_ * 512:(j_ + 1) * 512], p_[:, 0:512],
                        b1v[:, j_:j_ + 1], gel[:],
                        op0=OP.add, op1=OP.mult),
                        wait=[(P.c["s"], sgl_at[j])], inc=True)
                evl = {}
                for c in range(NCH):
                    cp = c % 2
                    pw2 = ps[6] if cp == 0 else ps[7]
                    osl = ostg[:, cp * TB:(cp + 1) * TB]
                    w2m = None
                    for j in range(NJ):
                        tw = []
                        if j == 0:
                            if c <= 1:
                                tw = [(P.c["v"], glu_d)]
                                if w == 0 and c == 0:
                                    tw.append((cw2, gw2))
                            else:
                                tw = [(P.c["v"], evl[c - 2])]
                        w2m = P.op("tensor", lambda h, c_=c, j_=j, p_=pw2: h.matmul(
                            p_[:, 0:512],
                            w2_sb[:, j_ * 1024 + c_ * 128:j_ * 1024 + (c_ + 1) * 128],
                            hbuf[:, j_ * 512:(j_ + 1) * 512],
                            start=(j_ == 0), stop=(j_ == NJ - 1)),
                            wait=tw, inc=(j == NJ - 1))
                    vw2 = [(P.c["t"], w2m)]
                    if st_last.get("prev"):
                        vw2.append((ost[st_last["prevpr"]], st_last["prev"]))
                    if (f"cbf{cp}") in st_last:
                        vw2.append((P.c["s"], st_last[f"cbf{cp}"]))
                    vv = P.op("vector", lambda h, c_=c, p_=pw2, o_=osl: h.tensor_scalar(
                        o_, p_[:, 0:512], gvec[:, 2 * NCH + c_:2 * NCH + c_ + 1],
                        None, op0=OP.add), wait=vw2)
                    mm_last = evl[c] = P.op("vector", lambda h, c_=c, o_=osl: h.tensor_tensor(
                        o_, o_, xms[:, c_ * 513 + 1:c_ * 513 + 1 + TB], op=OP.add),
                        inc=True)
                    st1 = P.dma("sync", ost[pr], fm_view(dst_f, w * TB, (w + 1) * TB)[:, c:c + 1, :],
                                osl.unsqueeze(1), wait=[(P.c["v"], mm_last)])
                    if dst_bf is not None:
                        cbf = P.op("scalar", lambda h, o_=osl: h.activation(
                            obf[:], o_, AF.Copy),
                            wait=[(P.c["v"], mm_last)], inc=True)
                        st1 = P.dma("sync", ost[pr],
                                    fm_view(dst_bf, w * TB, (w + 1) * TB)[:, c:c + 1, :],
                                    obf[:].unsqueeze(1), wait=[(P.c["s"], cbf)])
                        st_last[f"cbf{cp}"] = cbf
                    st_last["prev"] = st1
                    st_last["prevpr"] = pr
        return

    def phase_p4(self):
        nc, P, sc = self.nc, self.P, self.sc
        ps = self.ps
        with ExitStack() as ph:
            sb = lambda n, shp, dt: self.sbuf(n, shp, dt, ph)
            xb = [sb(f"p4_xb{i}", [128, NCH * TB], F32) for i in range(2)]
            ag = sb("p4_ag", [128, NCH * TB], F32)
            af = sb("p4_af", [128, NCH * TB], F32)
            xbb = sb("p4_xbb", [128, NCH * TB], BF16)
            xsq = sb("p4_xsq", [128, NCH * TB], BF16)
            xout = [sb(f"p4_xo{i}", [128, NCH * TB], BF16) for i in range(2)]
            xof = [sb(f"p4_xof{i}", [128, NCH * TB], F32) for i in range(2)]
            h2o = [sb(f"p4_h2{i}", [128, NCH * TB], BF16) for i in range(2)]
            tmp = sb("p4_tmp", [128, NCH * TB], F32)
            rows = sb("p4_rows", [1, 4 * TB], F32)
            gam = sb("p4_gam", [128, 2 * NCH], F32)
            ld = P.dpool("p4_ld", 2)
            sto = P.dpool("p4_st", 2)
            gst = P.dpool("p4_gst", 2)
            onc = P.ctr("p4_onc")
            gw = P.dma("sync", onc, gam[:, 0:NCH], self.inp["gamma"])
            gw = P.dma("sync", onc, gam[:, NCH:2 * NCH], self.inp["beta"])
            stt = self.st_tmp
            t0, t1, t2, t3 = (stt[0:1, k * TB:(k + 1) * TB] for k in range(4))
            mu1, r1 = rows[0:1, 0:TB], rows[0:1, TB:2 * TB]
            mu2, r2 = rows[0:1, 2 * TB:3 * TB], rows[0:1, 3 * TB:4 * TB]

            def bc3(psrc):
                return bass_rust.AP(psrc[:].tensor, 0, [[512, 128], [0, NCH], [1, TB]])

            def gcol(off):
                return bass_rust.AP(gam[:].tensor, off, [[2 * NCH, 128], [1, NCH], [0, TB]])
            x3 = lambda buf: buf[:].rearrange("p (c t) -> p c t", t=TB)
            (loads, casts, sums1, ap1_d, ap2_d, st_d, gst_d, vsum_d,
             s2d) = ({} for _ in range(9))
            for i in range(NT):
                pr = i % 2
                pA, pB = ps[2 * pr], ps[2 * pr + 1]
                wv = []
                if i >= 2:
                    wv = [(P.c["v"], ap2_d[i - 2]), (sto[pr], st_d[i - 2]),
                          (gst[pr], gst_d[i - 2])]
                loads[i] = P.dma("sync", ld[pr], x3(xb[pr]),
                                 fm_view(self.inp["xT"], i * TB, (i + 1) * TB), wait=wv)
                aw = [(P.c["v"], vsum_d[i - 1])] if i >= 1 else []
                loads[i] = P.dma("sync", ld[pr], x3(ag),
                                 fm_view(sc["glu"], i * TB, (i + 1) * TB), wait=aw)
                loads[i] = P.dma("sync", ld[pr], x3(af),
                                 fm_view(sc["ffno"], i * TB, (i + 1) * TB))
                P.op("vector", lambda h, p_=pr: h.tensor_tensor(
                    xb[p_][:], xb[p_][:], ag[:], op=OP.add),
                    wait=[(ld[pr], loads[i])])
                vsum_d[i] = P.op("vector", lambda h, p_=pr: h.tensor_tensor(
                    xb[p_][:], xb[p_][:], af[:], op=OP.add), inc=True)
                gst_d[i] = P.dma("gpsimd", gst[pr], fm_view(sc["x1"], i * TB, (i + 1) * TB),
                                 x3(xb[pr]), wait=[(P.c["v"], vsum_d[i])])
                casts[i] = P.op("vector", lambda h, p_=pr: h.tensor_copy(
                    xbb[:], xb[p_][:]),
                    wait=[(P.c["t"], s2d[i - 1])] if i else [], inc=True)
                sq1 = P.op("vector", lambda h: h.tensor_tensor(
                    xsq[:], xbb[:], xbb[:], op=OP.mult), inc=True)
                for c in range(NCH):
                    P.op("tensor", lambda h, c_=c, p_=pA: h.matmul(
                        p_[0:1, 0:TB], self.ones_col_b[:], xbb[:, c_ * TB:(c_ + 1) * TB],
                        start=(c_ == 0), stop=(c_ == NCH - 1)),
                        wait=[(P.c["v"], casts[i])] if c == 0 else ())
                for c in range(NCH):
                    sums1[i] = P.op("tensor", lambda h, c_=c, p_=pB: h.matmul(
                        p_[0:1, 0:TB], self.ones_col_b[:], xsq[:, c_ * TB:(c_ + 1) * TB],
                        start=(c_ == 0), stop=(c_ == NCH - 1)),
                        wait=[(P.c["v"], sq1)] if c == 0 else (), inc=(c == NCH - 1))
                P.op("vector", lambda h, p_=pA: h.tensor_scalar(
                    mu1, p_[0:1, 0:TB], 1.0 / DM, None, op0=OP.mult),
                    wait=[(P.c["t"], sums1[i])])
                P.op("vector", lambda h, p_=pB: h.tensor_scalar(
                    t0, p_[0:1, 0:TB], 1.0 / DM, None, op0=OP.mult))
                P.op("vector", lambda h: h.tensor_tensor(t1, mu1, mu1, op=OP.mult))
                P.op("vector", lambda h: h.scalar_tensor_tensor(
                    t2, t0, 1.0, t1, op0=OP.mult, op1=OP.subtract))
                vvar = P.op("vector", lambda h: h.tensor_scalar(
                    t2, t2, 1e-5, None, op0=OP.add), inc=True)
                vr1 = self.rsqrt_chain((t0, t1, t3), t2, r1, [(P.c["v"], vvar)])
                P.op("tensor", lambda h: h.matmul(
                    ps[4][:, 0:TB], self.ones_f[:], mu1, start=True, stop=True),
                    wait=[(P.c["v"], vr1)])
                bc1 = P.op("tensor", lambda h: h.matmul(
                    ps[5][:, 0:TB], self.ones_f[:], r1, start=True, stop=True), inc=True)
                P.op("vector", lambda h, p_=pr: h.tensor_tensor(
                    tmp[:].rearrange("p (c t) -> p c t", t=TB), x3(xb[p_]), bc3(ps[4]),
                    op=OP.subtract), wait=[(P.c["t"], bc1)])
                apf = P.op("vector", lambda h, p_=pr: h.tensor_tensor(
                    xof[p_][:].rearrange("p (c t) -> p c t", t=TB),
                    tmp[:].rearrange("p (c t) -> p c t", t=TB), bc3(ps[5]),
                    op=OP.mult), inc=True)
                ap1_d[i] = P.op("vector", lambda h, p_=pr: h.tensor_copy(
                    xout[p_][:], xof[p_][:]), inc=True)
                st_d[i] = P.dma("sync", sto[pr], fm_view(sc["xn2f"], i * TB, (i + 1) * TB),
                                x3(xof[pr]), wait=[(P.c["v"], apf)])
                gst_d[i] = P.dma("gpsimd", gst[pr], fm_view(sc["xn2"], i * TB, (i + 1) * TB),
                                 x3(xout[pr]), wait=[(P.c["v"], ap1_d[i])])
                sq2 = P.op("vector", lambda h, p_=pr: h.tensor_tensor(
                    xsq[:], xout[p_][:], xout[p_][:], op=OP.mult),
                    wait=[(P.c["t"], sums1[i])], inc=True)
                for c in range(NCH):
                    P.op("tensor", lambda h, c_=c, p_=pA, pp_=pr: h.matmul(
                        p_[0:1, 0:TB], self.ones_col_b[:], xout[pp_][:, c_ * TB:(c_ + 1) * TB],
                        start=(c_ == 0), stop=(c_ == NCH - 1)),
                        wait=[(P.c["v"], vr1), (P.c["v"], ap1_d[i])] if c == 0 else ())
                sums2 = None
                for c in range(NCH):
                    sums2 = P.op("tensor", lambda h, c_=c, p_=pB: h.matmul(
                        p_[0:1, 0:TB], self.ones_col_b[:], xsq[:, c_ * TB:(c_ + 1) * TB],
                        start=(c_ == 0), stop=(c_ == NCH - 1)),
                        wait=[(P.c["v"], sq2)] if c == 0 else (), inc=(c == NCH - 1))
                s2d[i] = sums2
                P.op("vector", lambda h, p_=pA: h.tensor_scalar(
                    mu2, p_[0:1, 0:TB], 1.0 / DM, None, op0=OP.mult),
                    wait=[(P.c["t"], sums2)])
                P.op("vector", lambda h, p_=pB: h.tensor_scalar(
                    t0, p_[0:1, 0:TB], 1.0 / DM, None, op0=OP.mult))
                P.op("vector", lambda h: h.tensor_tensor(t1, mu2, mu2, op=OP.mult))
                P.op("vector", lambda h: h.scalar_tensor_tensor(
                    t2, t0, 1.0, t1, op0=OP.mult, op1=OP.subtract))
                vvar2 = P.op("vector", lambda h: h.tensor_scalar(
                    t2, t2, 1e-5, None, op0=OP.add), inc=True)
                vr2 = self.rsqrt_chain((t0, t1, t3), t2, r2, [(P.c["v"], vvar2)])
                P.op("tensor", lambda h: h.matmul(
                    ps[6][:, 0:TB], self.ones_f[:], mu2, start=True, stop=True),
                    wait=[(P.c["v"], vr2)])
                bc2 = P.op("tensor", lambda h: h.matmul(
                    ps[7][:, 0:TB], self.ones_f[:], r2, start=True, stop=True), inc=True)
                P.op("vector", lambda h, p_=pr: h.tensor_tensor(
                    tmp[:].rearrange("p (c t) -> p c t", t=TB), x3(xof[p_]), bc3(ps[6]),
                    op=OP.subtract),
                    wait=[(P.c["t"], bc2)] + ([(onc, gw)] if i == 0 else []))
                P.op("vector", lambda h: h.tensor_tensor(
                    tmp[:].rearrange("p (c t) -> p c t", t=TB),
                    tmp[:].rearrange("p (c t) -> p c t", t=TB), bc3(ps[7]), op=OP.mult))
                P.op("vector", lambda h: h.tensor_tensor(
                    tmp[:].rearrange("p (c t) -> p c t", t=TB),
                    tmp[:].rearrange("p (c t) -> p c t", t=TB), gcol(0), op=OP.mult))
                ap2_d[i] = P.op("vector", lambda h, p_=pr: h.tensor_tensor(
                    h2o[p_][:].rearrange("p (c t) -> p c t", t=TB),
                    tmp[:].rearrange("p (c t) -> p c t", t=TB), gcol(NCH), op=OP.add),
                    inc=True)
                st_d[i] = P.dma("sync", sto[pr], fm_view(sc["h2"], i * TB, (i + 1) * TB),
                                x3(h2o[pr]), wait=[(P.c["v"], ap2_d[i])])

    def phase_l1_attn(self):
        nc, P, sc = self.nc, self.P, self.sc
        ps = self.ps
        with ExitStack() as ph:
            sb = lambda n, shp, dt: self.sbuf(n, shp, dt, ph)
            Wk_sb = sb("a1_wk", [128, 8 * 1024], BF16)
            Wv_sb = sb("a1_wv", [128, 8 * 1024], BF16)
            Wka_sb = sb("a1_wka", [128, 8 * 1024], BF16)
            bcol = sb("a1_bcol", [128, 3 * NCH], F32)
            h2b = [sb(f"a1_h2{j}", [128, NCH * TB], BF16) for j in range(2)]
            kf32 = sb("a1_kf", [128, NCH * TB], F32)
            kbf = sb("a1_kbf", [128, NCH * TB], BF16)
            vbuf = sb("a1_vb", [128, NCH * TB], F32)
            kwb = sb("a1_kw", [128, NCH * TB], F32)
            cwb = sb("a1_cw", [128, NCH * TB], F32)
            pkb = sb("a1_pk", [128, NCH * TB], F32)
            ones5 = sb("a1_ones", [128, TB], F32)
            carry = sb("a1_carry", [128, 2 * NCH], F32)
            ostage = sb("a1_ost", [128, NCH * TB], F32)
            ld = P.dpool("a1_ld", 2)
            ost = P.dpool("a1_ost", 2)
            onc = P.ctr("a1_onc")
            g1 = P.ctr("a1_g1")
            for nm, dst in (("WkT", Wk_sb), ("WvT", Wv_sb), ("WkaT", Wka_sb)):
                wv1 = P.dma("gpsimd", g1, dst[:].rearrange("p (a b) -> p a b", b=1024),
                            self.inp[nm].rearrange("(a p) c -> p a c", p=128))
            for r in range(3):
                bv0 = P.dma("sync", onc, bcol[:, r * NCH:(r + 1) * NCH],
                            self.inp["bkv"][r])
            vb0 = P.op("vector", lambda h: h.tensor_scalar(
                bcol[:, NCH:2 * NCH], bcol[:, NCH:2 * NCH], 1.0 / 32.0, None,
                op0=OP.mult), wait=[(onc, bv0)], inc=True)
            P.op("gpsimd", lambda h: h.memset(ones5[:], 1.0), inc=True)
            gz = P.op("gpsimd", lambda h: h.memset(carry[:], 0.0), inc=True)
            ev_k = ev_v = ev_a = 0
            mmd = vlast = cast_last = 0
            tA, stored = {}, {}
            for i in range(NT):
                pr = i % 2
                wvl = [(P.c["t"], tA[i - 2])] if i >= 2 else []
                lv = P.dma("sync", ld[pr], sb3(h2b[pr]),
                           fm_view(sc["h2"], i * TB, (i + 1) * TB), wait=wvl)
                for c in range(NCH):
                    pk_ = ps[c % 2]
                    tw = []
                    if c == 0:
                        tw = [(ld[pr], lv)]
                        if i == 0:
                            tw += [(g1, wv1), (P.c["v"], vb0)]
                    if c >= 2 or i > 0:
                        tw.append((P.c["v"], ev_k))
                    for kc in range(NCH):
                        mmd = P.op("tensor", lambda h, c_=c, kc_=kc, p_=pr, s_=pk_: h.matmul(
                            s_[:, 0:512],
                            Wk_sb[:, kc_ * 1024 + c_ * 128:kc_ * 1024 + (c_ + 1) * 128],
                            h2b[p_][:, kc_ * TB:(kc_ + 1) * TB],
                            start=(kc_ == 0), stop=(kc_ == NCH - 1)),
                            wait=(tw if kc == 0 else ()), inc=(kc == NCH - 1))
                    ev_k = P.op("vector", lambda h, c_=c, s_=pk_: h.tensor_scalar(
                        kf32[:, c_ * TB:(c_ + 1) * TB], s_[:, 0:512],
                        bcol[:, c_:c_ + 1], None, op0=OP.add),
                        wait=[(P.c["t"], mmd)], inc=True)
                    cast_last = P.op("vector", lambda h, c_=c: h.tensor_copy(
                        kbf[:, c_ * TB:(c_ + 1) * TB], kf32[:, c_ * TB:(c_ + 1) * TB]),
                        inc=True)
                for c in range(NCH):
                    pv_ = ps[2 + c % 2]
                    tw = []
                    if c >= 2 or i > 0:
                        tw.append((P.c["v"], ev_v))
                    for kc in range(NCH):
                        mmd = P.op("tensor", lambda h, c_=c, kc_=kc, p_=pr, s_=pv_: h.matmul(
                            s_[:, 0:512],
                            Wv_sb[:, kc_ * 1024 + c_ * 128:kc_ * 1024 + (c_ + 1) * 128],
                            h2b[p_][:, kc_ * TB:(kc_ + 1) * TB],
                            start=(kc_ == 0), stop=(kc_ == NCH - 1)),
                            wait=(tw if kc == 0 else ()), inc=(kc == NCH - 1))
                    ev_v = P.op("vector", lambda h, c_=c, s_=pv_: h.tensor_scalar(
                        vbuf[:, c_ * TB:(c_ + 1) * TB], s_[:, 0:512],
                        bcol[:, 2 * NCH + c_:2 * NCH + c_ + 1], None, op0=OP.add),
                        wait=[(P.c["t"], mmd)], inc=True)
                for c in range(NCH):
                    pa_ = ps[4 + c % 2]
                    tw = []
                    if c == 0:
                        tw = [(P.c["v"], cast_last)]
                    if c >= 2 or i > 0:
                        tw.append((P.c["s"], ev_a))
                    for kc in range(NCH):
                        mmd = P.op("tensor", lambda h, c_=c, kc_=kc, s_=pa_: h.matmul(
                            s_[:, 0:512],
                            Wka_sb[:, kc_ * 1024 + c_ * 128:kc_ * 1024 + (c_ + 1) * 128],
                            kbf[:, kc_ * TB:(kc_ + 1) * TB],
                            start=(kc_ == 0), stop=(kc_ == NCH - 1)),
                            wait=(tw if kc == 0 else ()), inc=(kc == NCH - 1))
                    ev_a = P.op("scalar", lambda h, c_=c, s_=pa_: h.activation(
                        kwb[:, c_ * TB:(c_ + 1) * TB], s_[:, 0:512], AF.Exp,
                        bias=bcol[:, NCH + c_:NCH + c_ + 1], scale=1.0 / 32.0),
                        wait=[(P.c["t"], mmd), (P.c["v"], vlast)], inc=True)
                tA[i] = mmd
                ew = [(P.c["s"], ev_a)]
                if i == 0:
                    ew.append((P.c["g"], gz))
                if i >= 1:
                    ew.append((ost[(i - 1) % 2], stored[i - 1]))
                for c in range(NCH):
                    P.op("vector", lambda h, c_=c: h.tensor_tensor_scan(
                        cwb[:, c_ * TB:(c_ + 1) * TB], ones5[:],
                        kwb[:, c_ * TB:(c_ + 1) * TB],
                        carry[:, c_:c_ + 1], op0=OP.mult, op1=OP.add),
                        wait=(ew if c == 0 else ()), drain=(c == NCH - 1))
                for c in range(NCH):
                    P.op("vector", lambda h, c_=c: h.tensor_copy(
                        carry[:, c_:c_ + 1], cwb[:, c_ * TB + TB - 1:c_ * TB + TB]),
                        drain=False)
                P.op("vector", lambda h: h.reciprocal(pkb[:], cwb[:]), drain=True)
                P.op("vector", lambda h: h.tensor_tensor(pkb[:], pkb[:], kwb[:], op=OP.mult),
                     drain=True)
                P.op("vector", lambda h: h.tensor_tensor(pkb[:], pkb[:], kf32[:], op=OP.mult),
                     drain=True)
                for c in range(NCH):
                    P.op("vector", lambda h, c_=c: h.tensor_tensor_scan(
                        cwb[:, c_ * TB:(c_ + 1) * TB], ones5[:],
                        pkb[:, c_ * TB:(c_ + 1) * TB],
                        carry[:, NCH + c_:NCH + c_ + 1], op0=OP.mult, op1=OP.add),
                        drain=(c == NCH - 1))
                for c in range(NCH):
                    P.op("vector", lambda h, c_=c: h.tensor_copy(
                        carry[:, NCH + c_:NCH + c_ + 1],
                        cwb[:, c_ * TB + TB - 1:c_ * TB + TB]), drain=False)
                vlast = P.op("vector", lambda h: h.tensor_tensor(
                    ostage[:], cwb[:], vbuf[:], op=OP.mult), inc=True)
                stored[i] = P.dma("sync", ost[pr], fm_view(sc["acc1"], i * TB, (i + 1) * TB),
                                  sb3(ostage), wait=[(P.c["v"], vlast)])

    def phase_l1_ffn(self):
        nc, P, sc = self.nc, self.P, self.sc
        ps = self.ps
        with ExitStack() as ph:
            sb = lambda n, shp, dt: self.sbuf(n, shp, dt, ph)
            W1s = sb("e_w1s", [128, 8 * 2048], BF16)
            W2s = sb("e_w2s", [128, 16 * 1024], BF16)
            Wo_sb = sb("e_wo", [128, 8 * 1024], BF16)
            wtmp = [sb(f"e_wt{j}", [128, 1024], BF16) for j in range(4)]
            wftmp = [sb(f"e_wtf{j}", [128, 1024], F32) for j in range(2)]
            Ies = sb("e_ies", [128, 8 * 128], BF16)
            lcol = sb("e_lcol", [128, 8], F32)
            Wp1f = sb("e_wp1", [128, 8 * 128], F32)
            Wp2f = sb("e_wp2", [128, 8], F32)
            g1c = sb("e_g1", [128, 1], F32)
            bp1c = sb("e_bp1", [128, 1], F32)
            rows = sb("e_rows", [1, 32], F32)
            lgc = sb("e_lgc", [8, 2], F32)
            ohf = sb("e_ohf", [8, 1], F32)
            OH = sb("e_OH", [128, 8], F32)

            brow = sb("e_brow", [1, 3072], F32)
            b1sc = sb("e_b1s", [128, 16], F32)
            b2sc = sb("e_b2s", [128, 8], F32)
            boc = sb("e_bo", [128, NCH], F32)
            xnb = [sb("e_xnb0", [128, NCH * TB], BF16)] * 2
            rt = [sb(f"e_rt{j}", [128, 3 * TB], F32) for j in range(4)]
            hb = sb("e_hb", [128, 16 * TB], BF16)
            be1f = hb[:].bitcast(F32)[0:8, 0:2048]
            be2f = hb[:].bitcast(F32)[0:8, 2048:3072]
            rbuf = sb("e_rbuf", [128, NCH * TB], BF16)
            otile = [sb(f"e_ot{j}", [128, TB], F32) for j in range(2)]
            ld = P.dpool("e_ld", 2)
            wld = P.dpool("e_wld", 2)
            rld = P.dpool("e_rld", 4)
            onc = P.ctr("e_onc")
            g1 = P.ctr("e_g1c")
            fino = P.dpool("e_fino", 2)
            smv = sc["small"]
            d0 = P.dma("sync", onc, lcol[:].unsqueeze(2),
                       fm_view(sc["xn2f"], S - 1, S), slow_ok=True)
            d0 = P.dma("sync", onc, bp1c[:], self.inp["bp1"])
            d0 = P.dma("sync", onc, boc[:], self.inp["bo1"])
            d0 = P.dma("sync", onc, be1f, self.inp["be1"])
            d0 = P.dma("sync", onc, be2f, self.inp["be2"])
            d0 = P.dma("sync", onc, lgc[0:8, 1:2], self.inp["bp2"].unsqueeze(1),
                       slow_ok=True)
            d0 = P.dma("sync", onc, Wp1f[:].rearrange("p (a b) -> p a b", b=128),
                       self.inp["Wp1"].rearrange("(a p) c -> p a c", p=128))
            d0 = P.dma("sync", onc, Wp2f[:], self.inp["Wp2"])
            gw2 = P.dma("gpsimd", g1, Wo_sb[:].rearrange("p (a b) -> p a b", b=1024),
                        self.inp["WoT1"].rearrange("(a p) c -> p a c", p=128))
            for kc in range(NCH):
                rmm = P.op("tensor", lambda h, kc_=kc: h.matmul(
                    ps[0][:, 0:1], Wp1f[:, kc_ * 128:(kc_ + 1) * 128],
                    lcol[:, kc_:kc_ + 1],
                    start=(kc_ == 0), stop=(kc_ == NCH - 1)),
                    wait=([(onc, d0)] if kc == 0 else ()), inc=(kc == NCH - 1))
            sg1 = P.op("scalar", lambda h: h.activation(
                g1c[:], ps[0][:, 0:1], AF.Gelu, bias=bp1c[:, 0:1], scale=1.0),
                wait=[(P.c["t"], rmm)], inc=True)
            rmm = P.op("tensor", lambda h: h.matmul(
                ps[1][0:8, 0:1], Wp2f[:], g1c[:], start=True, stop=True),
                wait=[(P.c["s"], sg1)], inc=True)
            vl = P.op("vector", lambda h: h.tensor_tensor(
                lgc[0:8, 0:1], ps[1][0:8, 0:1], lgc[0:8, 1:2], op=OP.add),
                wait=[(P.c["t"], rmm)], inc=True)
            tt1 = P.op("tensor", lambda h: h.transpose(
                ps[2][0:1, 0:8], lgc[0:8, 0:1], self.ident_f[0:8, 0:8]),
                wait=[(P.c["v"], vl)], inc=True)
            vl = P.op("vector", lambda h: h.tensor_copy(rows[0:1, 0:8], ps[2][0:1, 0:8]),
                      wait=[(P.c["t"], tt1)], inc=True)
            vl = P.op("vector", lambda h: h.tensor_reduce(
                rows[0:1, 8:9], rows[0:1, 0:8], axis=AX.X, op=OP.max), inc=True)
            vl = P.op("vector", lambda h: h.tensor_scalar(
                rows[0:1, 16:24], rows[0:1, 0:8], rows[0:1, 8:9], None,
                op0=OP.is_ge), inc=True)
            ohv = self.bcast(OH[:], rows[0:1, 16:24], 8, ps[3], wait=[(P.c["v"], vl)])
            tt2 = P.op("tensor", lambda h: h.transpose(
                ps[5][0:8, 0:1], rows[0:1, 16:24], self.ident_f[0:1, 0:1]),
                wait=[(P.c["v"], vl)], inc=True)
            vc2 = P.op("vector", lambda h: h.tensor_copy(ohf[0:8, 0:1], ps[5][0:8, 0:1]),
                       wait=[(P.c["t"], tt2)], inc=True)
            vbe = vc2
            for src, off, nnb in ((be1f, 0, 4), (be2f, 2048, 2)):
                for nb in range(nnb):
                    bm = P.op("tensor", lambda h, s_=src, nb_=nb: h.matmul(
                        ps[4][0:1, 0:512], ohf[0:8, 0:1],
                        s_[0:8, nb_ * 512:(nb_ + 1) * 512], start=True, stop=True),
                        wait=[(P.c["v"], vbe)], inc=True)
                    vbe = P.op("vector", lambda h, o_=off, nb_=nb: h.tensor_copy(
                        brow[0:1, o_ + nb_ * 512:o_ + (nb_ + 1) * 512], ps[4][0:1, 0:512]),
                        wait=[(P.c["t"], bm)], inc=True)
            d3 = P.dma("sync", onc, bass_rust.AP(smv.tensor, 256, [[1, 1], [1, 3072]]),
                       brow[0:1, 0:3072], wait=[(P.c["v"], vbe)])
            d4 = P.dma("sync", onc, b1sc[:],
                       bass_rust.AP(smv.tensor, 256, [[1, 128], [128, 16]]),
                       wait=[(onc, d3)], slow_ok=True)
            d4 = P.dma("sync", onc, b2sc[:],
                       bass_rust.AP(smv.tensor, 256 + 2048, [[1, 128], [128, 8]]),
                       slow_ok=True)
            vie = 0
            for e in range(8):
                vie = P.op("vector", lambda h, e_=e: h.tensor_scalar(
                    Ies[:, e_ * 128:(e_ + 1) * 128], self.ident_b[:], OH[:, e_:e_ + 1],
                    None, op0=OP.mult),
                    wait=([(P.c["v"], ohv)] if e == 0 else ()), inc=True)
            wmv, evv = {}, {}
            gidx = gtile = 0
            w1fin = wfin = 0
            sld = P.dpool("e_sld", 2)
            cast_at = {}
            for wi, (Wsrc, Wdst, nkc, ncols) in enumerate(
                    ((self.inp["We1"], W1s, 8, 2048), (self.inp["We2"], W2s, 16, 1024))):
                for kc in range(nkc):
                    for nb in range(ncols // 1024):
                        col0 = kc * ncols + nb * 1024
                        for e in range(8):
                            j = gidx % 4
                            src_ap = (Wsrc[e].rearrange("(a p) c -> p a c", p=128)
                                      [:, kc, nb * 1024:(nb + 1) * 1024])
                            dw = [(P.c["t"], wmv[gidx - 4])] if gidx >= 4 else []
                            if e % 2 == 0:
                                jg = (gidx // 2) % 2
                                dwe = P.dma("gpsimd", wld[jg], wtmp[j][:], src_ap,
                                            wait=dw)
                                mw = [(wld[jg], dwe)]
                            else:
                                j2 = (gidx // 2) % 2
                                fdw = list(dw)
                                if gidx >= 4:
                                    fdw.append((P.c["s"], cast_at[gidx - 4]))
                                dwe = P.dma("sync", sld[j2], wftmp[j2][:], src_ap,
                                            wait=fdw)
                                cv = P.op("scalar", lambda h, j_=j, j2_=j2: h.activation(
                                    wtmp[j_][:], wftmp[j2_][:], AF.Copy),
                                    wait=[(sld[j2], dwe)], inc=True)
                                cast_at[gidx] = cv
                                mw = [(P.c["s"], cv)]
                            if e == 0:
                                if gtile >= 1:
                                    mw.append((P.c["v"], evv[gtile - 1]))
                                if gtile == 0:
                                    mw.append((P.c["v"], vie))
                            P.op("tensor", lambda h, e_=e, j_=j: h.matmul(
                                ps[6][:, 0:512], Ies[:, e_ * 128:(e_ + 1) * 128],
                                wtmp[j_][:, 0:512],
                                start=(e_ == 0), stop=(e_ == 7)), wait=mw)
                            wmv[gidx] = P.op("tensor", lambda h, e_=e, j_=j: h.matmul(
                                ps[7][:, 0:512], Ies[:, e_ * 128:(e_ + 1) * 128],
                                wtmp[j_][:, 512:1024],
                                start=(e_ == 0), stop=(e_ == 7)), inc=True)
                            gidx += 1
                        P.op("vector", lambda h, d_=Wdst, c_=col0: h.tensor_copy(
                            d_[:, c_:c_ + 512], ps[6][:, 0:512]),
                            wait=[(P.c["t"], wmv[gidx - 1])], drain=False)
                        evv[gtile] = P.op("vector", lambda h, d_=Wdst, c_=col0: h.tensor_copy(
                            d_[:, c_ + 512:c_ + 1024], ps[7][:, 0:512]), inc=True,
                            drain=False)
                        gtile += 1
                if wi == 0:
                    w1fin = evv[gtile - 1]
            wfin = evv[gtile - 1]
            tB, stv, evo = {}, {}, {}
            ev1 = ev2 = 0

            def rt_load(g_, wv_):
                j_ = g_ % 4
                i_, c_ = divmod(g_, NCH)
                r = 0
                for k_, src_ in enumerate((sc["x1"], sc["acc1"], sc["xn2f"])):
                    r = P.dma("gpsimd", rld[j_],
                              rt[j_][:, k_ * TB:(k_ + 1) * TB].unsqueeze(1),
                              fm_view(src_, i_ * TB, (i_ + 1) * TB)[:, c_:c_ + 1, :],
                              wait=(wv_ if k_ == 0 else ()))
                stv[g_] = r
            for g in range(4):
                rt_load(g, [])
            for i in range(NT):
                pr = i % 2
                lw = [(P.c["t"], tB[i - 1])] if i >= 1 else []
                lx = P.dma("sync", ld[pr], sb3(xnb[pr]),
                           fm_view(sc["xn2"], i * TB, (i + 1) * TB), wait=lw)
                for c in range(16):
                    pH = ps[c % 2]
                    tw = []
                    if c == 0:
                        tw = [(ld[pr], lx)]
                        if i == 0:
                            tw.append((P.c["v"], w1fin))
                    if c >= 2 or i > 0:
                        tw.append((P.c["s"], ev1))
                    for kc in range(NCH):
                        mm = P.op("tensor", lambda h, c_=c, kc_=kc, p_=pr, s_=pH: h.matmul(
                            s_[:, 0:512],
                            W1s[:, kc_ * 2048 + c_ * 128:kc_ * 2048 + (c_ + 1) * 128],
                            xnb[p_][:, kc_ * TB:(kc_ + 1) * TB],
                            start=(kc_ == 0), stop=(kc_ == NCH - 1)),
                            wait=(tw if kc == 0 else ()), inc=(kc == NCH - 1))
                    ev1 = P.op("scalar", lambda h, c_=c, s_=pH: h.activation(
                        hb[:, c_ * TB:(c_ + 1) * TB], s_[:, 0:512], AF.Gelu,
                        bias=b1sc[:, c_:c_ + 1], scale=1.0),
                        wait=[(P.c["t"], mm)] + ([(onc, d4)] if (i == 0 and c == 0) else []),
                        inc=True)
                for c in range(NCH):
                    pR = ps[2 + c % 2]
                    tw = []
                    if c == 0:
                        tw = [(P.c["s"], ev1)]
                        if i == 0:
                            tw.append((P.c["v"], wfin))
                    if c >= 2 or i > 0:
                        tw.append((P.c["v"], ev2))
                    for kc in range(16):
                        mm = P.op("tensor", lambda h, c_=c, kc_=kc, s_=pR: h.matmul(
                            s_[:, 0:512],
                            W2s[:, kc_ * 1024 + c_ * 128:kc_ * 1024 + (c_ + 1) * 128],
                            hb[:, kc_ * TB:(kc_ + 1) * TB],
                            start=(kc_ == 0), stop=(kc_ == 15)),
                            wait=(tw if kc == 0 else ()), inc=(kc == 15))
                    ev2 = P.op("vector", lambda h, c_=c, s_=pR: h.tensor_scalar(
                        rbuf[:, c_ * TB:(c_ + 1) * TB], s_[:, 0:512],
                        b2sc[:, c_:c_ + 1], None, op0=OP.add),
                        wait=[(P.c["t"], mm)], inc=True)
                for c in range(NCH):
                    g = i * NCH + c
                    j = g % 4
                    jo = g % 2
                    pW = ps[4 + c % 2]
                    tw = []
                    if c == 0:
                        tw = [(P.c["v"], ev2)]
                        if i == 0:
                            tw.append((g1, gw2))
                    if c >= 2 or i > 0:
                        tw.append((P.c["v"], evo[g - 2]))
                    for kc in range(NCH):
                        mm = P.op("tensor", lambda h, c_=c, kc_=kc, s_=pW: h.matmul(
                            s_[:, 0:512],
                            Wo_sb[:, kc_ * 1024 + c_ * 128:kc_ * 1024 + (c_ + 1) * 128],
                            rbuf[:, kc_ * TB:(kc_ + 1) * TB],
                            start=(kc_ == 0), stop=(kc_ == NCH - 1)),
                            wait=(tw if kc == 0 else ()), inc=(kc == NCH - 1))
                    ow = [(P.c["t"], mm), (rld[j], stv[g])]
                    if g >= 2:
                        ow.append((fino[jo], stv[("o", g - 2)]))
                    P.op("vector", lambda h, s_=pW, c_=c, o_=otile[jo]: h.tensor_scalar(
                        o_[:], s_[:, 0:512], boc[:, c_:c_ + 1], 0.3,
                        op0=OP.add, op1=OP.mult), wait=ow)
                    for k in range(3):
                        evo[g] = P.op("vector", lambda h, j_=j, k_=k, o_=otile[jo]:
                                      h.tensor_tensor(o_[:], o_[:],
                                                      rt[j_][:, k_ * TB:(k_ + 1) * TB],
                                                      op=OP.add), inc=(k == 2))
                    stv[("o", g)] = P.dma("sync", fino[jo],
                                          fm_view(self.out, i * TB, (i + 1) * TB)[:, c:c + 1, :],
                                          otile[jo][:].unsqueeze(1),
                                          wait=[(P.c["v"], evo[g])])
                    if g + 4 < NT * NCH:
                        rt_load(g + 4, [(P.c["v"], evo[g])])
                tB[i] = mm

    def phase_gate(self):
        nc, P, sc = self.nc, self.P, self.sc
        ps = self.ps
        with ExitStack() as ph:
            sb = lambda n, shp, dt: self.sbuf(n, shp, dt, ph)
            wg = sb("wg_sb", [128, 8 * 2048], BF16)
            bg = sb("bg_sb", [128, 16], F32)
            atb = [sb(f"g_at{i}", [128, NCH * TB], BF16) for i in range(2)]
            sig = sb("g_sig", [128, TB], F32)
            glu = [sb(f"g_glu{i}", [128, NCH * TB], F32) for i in range(2)]
            ld = P.dpool("g_ld", 2)
            sto = P.dpool("g_st", 2)
            onc = P.ctr("g_onc")
            gonc = P.ctr("g_gonc")
            wv0 = P.dma("gpsimd", gonc, wg[:].rearrange("p (kc n) -> p kc n", n=2048),
                        self.inp["w_gate"].rearrange("(kc p) n -> p kc n", p=128))
            wv = P.dma("sync", onc, bg[:], self.inp["b_gate"])
            mmdone, gdone, stores = {}, {}, {}
            for i in range(NT):
                pr = i % 2
                wvl = []
                if i >= 2:
                    wvl = [(P.c["t"], mmdone[i - 2]), (sto[pr], stores[i - 2])]
                lv = P.dma("sync", ld[pr], sb3(atb[pr]),
                           fm_view(sc["attn"], i * TB, (i + 1) * TB), wait=wvl)
                for j in range(NCH):
                    psa, psg = (ps[0], ps[1]) if j % 2 == 0 else (ps[2], ps[3])
                    tw = [(ld[pr], lv), (onc, wv), (gonc, wv0)] if j == 0 else []
                    if i + j > 0:
                        tw.append((P.c["v"], gdone.get((i, j - 2), gdone.get((i - 1, j + 6), 0))))
                    for kc in range(NCH):
                        P.op("tensor", lambda h, j_=j, kc_=kc, p_=pr, ps_=psa: h.matmul(
                            ps_[:, 0:512], wg[:, kc_ * 2048 + j_ * 128:kc_ * 2048 + (j_ + 1) * 128],
                            atb[p_][:, kc_ * TB:(kc_ + 1) * TB],
                            start=(kc_ == 0), stop=(kc_ == NCH - 1)),
                            wait=(tw if kc == 0 else ()))
                    for kc in range(NCH):
                        mmdone[i] = P.op("tensor", lambda h, j_=j, kc_=kc, p_=pr, ps_=psg: h.matmul(
                            ps_[:, 0:512], wg[:, kc_ * 2048 + 1024 + j_ * 128:kc_ * 2048 + 1024 + (j_ + 1) * 128],
                            atb[p_][:, kc_ * TB:(kc_ + 1) * TB],
                            start=(kc_ == 0), stop=(kc_ == NCH - 1)), inc=(kc == NCH - 1))
                    sgw = [(P.c["t"], mmdone[i])]
                    prevg = gdone.get((i, j - 1), gdone.get((i - 1, NCH - 1), 0))
                    sgw.append((P.c["v"], prevg))
                    if i == 0 and j == 0:
                        sgw.append((onc, wv))
                    sg = P.op("scalar", lambda h, j_=j, ps_=psg: h.activation(
                        sig[:], ps_[:, 0:512], AF.Sigmoid, bias=bg[:, NCH + j_:NCH + j_ + 1], scale=1.0),
                        wait=sgw, inc=True)
                    gdone[(i, j)] = P.op("vector", lambda h, j_=j, p_=pr, ps_=psa: h.scalar_tensor_tensor(
                        glu[p_][:, j_ * TB:(j_ + 1) * TB], ps_[:, 0:512], bg[:, j_:j_ + 1],
                        sig[:], op0=OP.add, op1=OP.mult),
                        wait=[(P.c["s"], sg)], inc=True)
                stores[i] = P.dma("sync", sto[pr], fm_view(sc["glu"], i * TB, (i + 1) * TB),
                                  sb3(glu[pr]), wait=[(P.c["v"], gdone[(i, NCH - 1)])])


def build_program(taps=(), stop_after=None):
    b = Build(taps=taps)
    if stop_after == "p0":
        b.stop_after_p0 = True
    nc = b.build()
    return nc, {}


def prep_inputs(inputs):
    x = np.asarray(inputs["L0_attn_x"], np.float32)
    w1 = np.asarray(inputs["L0_ffn_w1"], np.float32)
    b1 = np.asarray(inputs["L0_ffn_b1"], np.float32)
    w2 = np.asarray(inputs["L0_ffn_w2"], np.float32)
    w1p = np.zeros((2, DM, 5632), np.float32)
    w1p[:, :, 0:2730] = w1[:, :, 0:2730]
    w1p[:, :, 2816:2816 + 2730] = w1[:, :, 2730:5460]
    b1p = np.zeros((2, 5632), np.float32)
    b1p[:, 0:2730] = b1[:, 0:2730]
    b1p[:, 2816:2816 + 2730] = b1[:, 2730:5460]
    w2p = np.zeros((2, 2816, DM), np.float32)
    w2p[:, 0:2730, :] = w2
    tisa = np.stack([np.asarray(inputs["L0_attn_amplitudes"], np.float32),
                     np.abs(np.asarray(inputs["L0_attn_sharpness"], np.float32)),
                     np.asarray(inputs["L0_attn_offsets"], np.float32)])
    qk_scale = np.stack([np.asarray(inputs["L0_ffn_q_scale"], np.float32),
                         np.asarray(inputs["L0_ffn_k_scale"], np.float32)], axis=1)
    cf = lambda k: np.ascontiguousarray(np.asarray(inputs[k], np.float32))

    def pcol(v):
        v = np.asarray(v, np.float32)
        return np.ascontiguousarray(v.reshape(*v.shape[:-1], -1, 128).swapaxes(-1, -2))
    common = {
        "w_in": cf("L0_attn_w_in"), "w_gate": cf("L0_attn_w_gate"),
        "b_gate": pcol(inputs["L0_attn_b_gate"]), "tisa": tisa,
        "wqkv": cf("L0_ffn_wqkv"), "qk_scale": qk_scale, "wo0": cf("L0_ffn_wo"),
        "ln_g": pcol(inputs["L0_ffn_ln_g"]), "ln_b": pcol(inputs["L0_ffn_ln_b"]),
        "w1f": w1p, "b1f": pcol(b1p), "w2f": w2p, "b2f": pcol(inputs["L0_ffn_b2"]),
        "WkT": np.ascontiguousarray(np.asarray(inputs["L1_attn_Wk"], np.float32).T),
        "WkaT": np.ascontiguousarray(np.asarray(inputs["L1_attn_Wka"], np.float32).T),
        "WvT": np.ascontiguousarray(np.asarray(inputs["L1_attn_Wv"], np.float32).T),
        "bkv": pcol(np.stack([np.asarray(inputs["L1_attn_bk"], np.float32),
                              np.asarray(inputs["L1_attn_bka"], np.float32),
                              np.asarray(inputs["L1_attn_bv"], np.float32)])),
        "gamma": pcol(inputs["L1_attn_gamma"]), "beta": pcol(inputs["L1_attn_beta"]),
        "Wp1": cf("L1_ffn_Wp1"), "bp1": np.asarray(inputs["L1_ffn_bp1"], np.float32).reshape(128, 1).copy(),
        "Wp2": cf("L1_ffn_Wp2"), "bp2": cf("L1_ffn_bp2"),
        "We1": cf("L1_ffn_W1"), "be1": cf("L1_ffn_b1"),
        "We2": cf("L1_ffn_W2"), "be2": cf("L1_ffn_b2"),
        "WoT1": cf("L1_ffn_Wo"), "bo1": pcol(inputs["L1_ffn_bo"]),
    }
    in_maps = []
    for b in range(8):
        m = dict(common)
        m["xT"] = np.ascontiguousarray(x[b].T)
        in_maps.append(m)
    return in_maps


def kernel(**inputs):
    in_maps = prep_inputs(inputs)
    nc, meta = build_program(taps=TAPS)
    res = run_bass_kernel_spmd(nc, in_maps, core_ids=list(range(8)))
    outs = np.stack([np.ascontiguousarray(np.asarray(r["out"]).T) for r in res.results])
    if TAPS:
        kernel.last_results = res
    return outs
